# version 08
# baseline (speedup 1.0000x reference)
import numpy as np
from contextlib import ExitStack

import concourse.bass as bass
import concourse.mybir as mybir
from concourse.bass_utils import run_bass_kernel_spmd

F32 = mybir.dt.float32
BF16 = mybir.dt.bfloat16
I8 = mybir.dt.int8
AF = mybir.ActivationFunctionType
ALU = mybir.AluOpType
AX = mybir.AxisListType
f32 = np.float32

B, T, D, H, HD = 4, 4096, 1024, 16, 64
NPAIR = 8
TO = T // 2
OWN_A = [0, 3, 4, 7, 8, 11, 12, 15]
OWN_B = [1, 2, 5, 6, 9, 10, 13, 14]
NSLOT = 8
N_MSK = 32
FC = D // 128
NT = T // 512
NTO = TO // 512
EPS = 1e-5
DI = 2816
DIC = DI // 128
DM = 4096
DMC = DM // 128
C6_1 = float(f32(1.0 / 6.0))
C6_2 = float(f32(1.0 / 6.0 - float(f32(1.0 / 6.0))))

ENGS = ("sync", "tensor", "vector", "scalar", "gpsimd")
DBG = {}


class Sem:
    __slots__ = ("name", "h", "v")

    def __init__(self, name):
        self.name, self.h, self.v = name, None, 0


class Gen:
    def __init__(self):
        self.ops = {e: [] for e in ENGS}

    def emit(self, eng, fn):
        self.ops[eng].append(fn)

    def wait(self, eng, sem, val=None):
        v = sem.v if val is None else val
        assert v <= sem.v, (sem.name, v, sem.v)
        if v > 0:
            self.emit(eng, lambda e, h=sem, vv=v: e.wait_ge(h.h, vv))

    def op(self, eng, build, sem=None, inc=1):
        own = getattr(self, "own_sems", {}).get(eng)
        if own is not None:
            self.wait(eng, own)
            sem, inc = own, 1
        if sem is not None:
            sem.v += inc

        def fn(e, build=build, sem=sem, inc=inc):
            r = build(e)
            if sem is not None:
                r.then_inc(sem.h, inc)
        self.emit(eng, fn)
        return None if sem is None else sem.v

    def dma(self, eng, out, in_, sem):
        sem.v += 16
        self.emit(eng, lambda e, o=out, i=in_, h=sem: e.dma_start(out=o, in_=i).then_inc(h.h, 16))
        return sem.v



def swizzle_strips(W):
    K, M = W.shape
    return np.ascontiguousarray(
        W.reshape(K // 128, 128, M // 128, 128).transpose(2, 1, 0, 3)).astype(np.float32)


def build_rope_tables():
    inv_freq = 1.0 / (10000.0 ** (np.arange(0, HD, 2, dtype=np.float64) / HD))
    t = np.arange(T, dtype=np.float64)
    fr = t[:, None] * inv_freq[None, :]
    cos = np.cos(fr).T
    sin = np.sin(fr).T
    C2 = np.concatenate([cos, cos, cos, cos], axis=0).astype(np.float32)
    S2 = np.concatenate([-sin, sin, -sin, sin], axis=0).astype(np.float32)
    return np.ascontiguousarray(C2), np.ascontiguousarray(S2)


def build_masks(own):
    out = np.zeros((N_MSK, 128, 512), dtype=np.int8)
    for si in range(4):
        qg = np.concatenate([256 * own[2 * si] + np.arange(256),
                             256 * own[2 * si + 1] + np.arange(256)])[None, :]
        for d2 in range(8):
            k = 128 * (8 * si + d2) + np.arange(128)[:, None]
            out[8 * si + d2] = (k > qg).astype(np.int8)
    return np.ascontiguousarray(out.transpose(1, 0, 2).reshape(128, N_MSK * 512))


def build_tri():
    k = np.arange(128)[:, None]
    q = np.arange(128)[None, :]
    return np.ascontiguousarray(np.where(k <= q, 0.0, -1e30).astype(np.float32))



def build_program(stop_phase=99):
    nc = bass.Bass("TRN2", target_bir_lowering=False)
    g = Gen()
    ctx = ExitStack()

    din = {}

    def inp(name, shape, dtype=F32):
        din[name] = nc.dram_tensor(name, list(shape), dtype, kind="ExternalInput").ap()
        return din[name]

    x_fm = inp("x_fm", [FC, 128, T])
    wq_s = inp("wq_s", [FC, 128, FC, 128])
    wk_s = inp("wk_s", [FC, 128, FC, 128])
    wv_s = inp("wv_s", [FC, 128, FC, 128])
    wp_s = inp("wp_s", [FC, 128, FC, 128])
    w0t_s = inp("w0t_s", [FC, 128, FC, 128])
    wqkv_s = inp("wqkv_s", [3 * FC, 128, FC, 128])
    wo_s = inp("wo_s", [FC, 128, FC, 128])
    w1_s = inp("w1_s", [DMC, 128, FC, 128])
    w2_s = inp("w2_s", [FC, 128, DMC, 128])
    wg_nat = inp("wg_nat", [DIC, 128, D])
    wu_nat = inp("wu_nat", [DIC, 128, D])
    wd_nat = inp("wd_nat", [FC, 128, DI])
    vecs = inp("vecs", [128, 128])
    c2t = inp("c2t", [128, T])
    s2t = inp("s2t", [128, T])
    c2o = inp("c2o", [128, TO])
    s2o = inp("s2o", [128, TO])
    iden = inp("iden", [128, 128])
    mtri = inp("mtri", [128, 128])
    mask8 = inp("mask8", [128, N_MSK * 512], I8)

    y_out = nc.dram_tensor("y_out", [FC, 128, TO], F32, kind="ExternalOutput").ap()

    def scratch(name, shape, dtype=F32):
        return nc.dram_tensor(name, list(shape), dtype, kind="Internal").ap()

    qst = [scratch("qst_h", [NPAIR, 128, T], BF16), scratch("qst_l", [NPAIR, 128, T], BF16)]
    kst = [scratch("kst_h", [NPAIR, 128, T], BF16), scratch("kst_l", [NPAIR, 128, T], BF16)]
    vst = scratch("vst", [NPAIR, T, 128])
    ast = [scratch("ast_h", [NPAIR, 128, T], BF16), scratch("ast_l", [NPAIR, 128, T], BF16)]
    qst1 = [scratch("qst1_h", [NPAIR, 128, TO], BF16), scratch("qst1_l", [NPAIR, 128, TO], BF16)]
    ast1 = [scratch("ast1_h", [NPAIR, 128, TO], BF16), scratch("ast1_l", [NPAIR, 128, TO], BF16)]
    y1st = scratch("y1st", [FC, 128, T])
    y2st = scratch("y2st", [FC, 128, T])
    y2own = scratch("y2own", [FC, 128, TO])
    y3own = scratch("y3own", [FC, 128, TO])
    n1st = scratch("n1st", [FC, 128, T])
    z2dbg = scratch("z2dbg", [FC, 128, TO])
    w1hl_s = scratch("w1hl_s", [DMC, 128, 2 * FC * 128], BF16)
    w2hl_s = scratch("w2hl_s", [FC, 128, 2 * DMC * 128], BF16)
    wgq_s = [scratch("wgq_sh", [DIC, 128, FC, 128], BF16),
             scratch("wgq_sl", [DIC, 128, FC, 128], BF16)]
    wuq_s = [scratch("wuq_sh", [DIC, 128, FC, 128], BF16),
             scratch("wuq_sl", [DIC, 128, FC, 128], BF16)]
    wdq_s = [scratch("wdq_sh", [FC, 128, DIC, 128], BF16),
             scratch("wdq_sl", [FC, 128, DIC, 128], BF16)]

    A1t = ctx.enter_context(nc.sbuf_tensor("A1", [128, 32768], F32))
    A2t = ctx.enter_context(nc.sbuf_tensor("A2", [128, 16384], F32))
    sb_vecs = ctx.enter_context(nc.sbuf_tensor("sb_vecs", [128, 128], F32))
    sb_iden = ctx.enter_context(nc.sbuf_tensor("sb_iden", [128, 128], F32))
    sb_tri = ctx.enter_context(nc.sbuf_tensor("sb_tri", [128, 128], F32))
    sb_ones64 = ctx.enter_context(nc.sbuf_tensor("sb_ones64", [128, 64], F32))
    sb_ones64b_t = ctx.enter_context(nc.sbuf_tensor("sb_ones64b", [128, 64], BF16))
    sb_ones1 = ctx.enter_context(nc.sbuf_tensor("sb_ones1", [128, 1], F32))
    sb_onesk = ctx.enter_context(nc.sbuf_tensor("sb_onesk", [1, 128], F32))
    sb_rows = ctx.enter_context(nc.sbuf_tensor("sb_rows", [1, 3072], F32))
    PB = [ctx.enter_context(nc.psum_tensor(f"PB{i}", [128, 1024], F32)) for i in range(4)]

    sems = {}
    for name in ("TEN", "VEC", "SCA", "LDQ", "STQ", "GPC"):
        s = Sem(name)
        s.h = ctx.enter_context(nc.semaphore(name))
        sems[name] = s
    TEN, VEC, SCA, LDQ, STQ, GPC = (sems[n] for n in ("TEN", "VEC", "SCA", "LDQ", "STQ", "GPC"))
    CSEMS = (TEN, VEC, SCA)
    g.fin_STQ = STQ
    g.fin_LDQ = LDQ
    g.own_sems = {"vector": VEC, "scalar": SCA}
    _semctr = [0]

    def mk_sem():
        s = Sem(f"S{_semctr[0]}")
        _semctr[0] += 1
        s.h = ctx.enter_context(nc.semaphore(s.name))
        g.all_slot_sems.append(s)
        return s
    g.all_slot_sems = []

    bankrel = {}

    def bank_wait(eng, *keys):
        for k in keys:
            for s, v in bankrel.get(k, ()):
                g.wait(eng, s, v)

    def bank_rel(key, *svs):
        bankrel[key] = list(svs) if svs else [(VEC, VEC.v)]

    def phase_barrier():
        g.wait("tensor", VEC)
        g.wait("tensor", SCA)
        g.wait("tensor", STQ)
        g.wait("vector", TEN)
        g.wait("vector", SCA)
        g.wait("vector", STQ)
        g.wait("scalar", TEN)
        g.wait("scalar", VEC)
        g.wait("scalar", STQ)
        g.wait("sync", TEN)
        g.wait("sync", VEC)
        g.wait("sync", SCA)
        g.wait("sync", STQ)
        g.wait("sync", LDQ)
        for s_ in g.all_slot_sems:
            g.wait("sync", s_)
        g.wait("gpsimd", VEC)
        g.wait("gpsimd", SCA)

    arena_off = {"A1": 0, "A2": 0}

    def carve(which, cols):
        o = arena_off[which]
        lim = 32768 if which == "A1" else 16384
        assert o + cols <= lim, (which, o, cols)
        arena_off[which] = o + cols
        t_ = A1t if which == "A1" else A2t
        return t_[:][:, o:o + cols]

    def reset(which):
        arena_off[which] = 0

    VC = {}
    off = 0
    for nm, n in (("bp", FC), ("bqkv", 3 * FC), ("g1", FC), ("b1", FC), ("bo", FC),
                  ("g2", FC), ("b2", FC), ("b1m", DMC), ("b2m", FC), ("selA", 1), ("selB", 1)):
        VC[nm] = off
        off += n
    assert off <= 128

    def vcol(nm, i=0):
        return sb_vecs[:][:, VC[nm] + i:VC[nm] + i + 1]

    def tt(out, in0, in1, op, sem=None, eng="vector"):
        return g.op(eng, lambda e, o=out, a=in0, b=in1, p=op:
                    e.tensor_tensor(out=o, in0=a, in1=b, op=p), sem)

    def ts(out, in0, s1, op0, s2=None, op1=None, sem=None):
        def bld(e, o=out, a=in0, s1=s1, s2=s2, op0=op0, op1=op1):
            kw = dict(out=o, in0=a, scalar1=s1, scalar2=s2, op0=op0)
            if op1 is not None:
                kw["op1"] = op1
            return e.tensor_scalar(**kw)
        return g.op("vector", bld, sem)

    def stt(out, in0, scalar, in1, op0, op1, sem=None):
        return g.op("vector", lambda e, o=out, a=in0, s=scalar, b=in1, p0=op0, p1=op1:
                    e.scalar_tensor_tensor(out=o, in0=a, scalar=s, in1=b, op0=p0, op1=p1), sem)

    def act(out, in_, func, scale=None, bias=None, sem=None):
        def bld(e, o=out, i=in_, f=func, sc=scale, bi=bias):
            kw = dict(out=o, in_=i, func=f)
            if sc is not None:
                kw["scale"] = sc
            if bi is not None:
                kw["bias"] = bi
            return e.activation(**kw)
        return g.op("scalar", bld, sem)

    def vcopy(out, in_, sem=None):
        return g.op("vector", lambda e, o=out, i=in_: e.tensor_copy(out=o, in_=i), sem)

    def mm(out, lhsT, rhs, start, stop, tile_position=None, sem=None):
        def bld(e, o=out, l=lhsT, r=rhs, st=start, sp=stop, tp=tile_position):
            kw = dict(out=o, lhsT=l, rhs=r, start=st, stop=sp)
            if tp is not None:
                kw["tile_position"] = tp
            return e.matmul(**kw)
        return g.op("tensor", bld, sem)

    def split_hl(hi, lo, src):
        vcopy(hi, src)
        tt(lo, src, hi, ALU.subtract)

    def transpose(out, in_, sem=None):
        return g.op("tensor", lambda e, o=out, i=in_:
                    e.transpose(out=o, in_=i, identity=sb_iden[:]), sem)

    def waitall(eng):
        for s in CSEMS:
            g.wait(eng, s)

    class Reader:
        def __init__(self, which, cols, n=3):
            self.bufs = [carve(which, cols) for _ in range(n)]
            self.sems = [mk_sem() for _ in range(n)]
            self.n = n
            self.i = 0
            self.snaps = []

        def rd(self, src, *consumers):
            i = self.i
            self.snaps.append(tuple((s, s.v) for s in CSEMS))
            if i >= self.n:
                for s, v in self.snaps[i - self.n + 1]:
                    g.wait("sync", s, v)
            b = self.bufs[i % self.n]
            out = b
            if src.dtype == BF16:
                out = out.bitcast(BF16)
            if len(src.shape) == 3:
                out = out.rearrange("p (k c) -> p k c", c=src.shape[2])
                src = src.rearrange("k p c -> p k c")
            v = g.dma("sync", out, src, self.sems[i % self.n])
            for c in consumers:
                g.wait(c, self.sems[i % self.n], v)
            self.i = i + 1
            return b

    g.dma("sync", sb_vecs[:], vecs, LDQ)
    g.dma("sync", sb_iden[:], iden, LDQ)
    g.dma("sync", sb_tri[:], mtri, LDQ)
    v0 = LDQ.v
    g.op("gpsimd", lambda e: e.memset(sb_ones64[:], 1.0), GPC)
    g.op("gpsimd", lambda e: e.memset(sb_ones64b_t[:], 1.0), GPC)
    g.op("gpsimd", lambda e: e.memset(sb_ones1[:], 1.0), GPC)
    g.op("gpsimd", lambda e: e.memset(sb_onesk[:], 1.0), GPC)
    for eng in ("tensor", "vector", "scalar"):
        g.wait(eng, LDQ, v0)
        g.wait(eng, GPC)

    def ln_stats(read_chunk, ntt, dst_M, dst_R, sq_buf):
        phase_barrier()
        bank_wait("tensor", "P2a", "P3a")
        for t in range(ntt):
            for fc in range(FC):
                ch = read_chunk(fc, t)
                sq = sq_buf[:, 512 * (fc % 2):512 * (fc % 2) + 512]
                if fc >= 2:
                    g.wait("scalar", TEN)
                sv = act(sq, ch, AF.Square, sem=SCA)
                mm(PB[2][:][0:1, 0:512], sb_ones1[:], ch, start=(fc == 0),
                   stop=(fc == FC - 1), sem=TEN)
                g.wait("tensor", SCA, sv)
                mm(PB[3][:][0:1, 0:512], sb_ones1[:], sq, start=(fc == 0),
                   stop=(fc == FC - 1), sem=TEN)
            vmm = TEN.v
            g.wait("vector", TEN, vmm)
            r_sum = sb_rows[:][0:1, 0:512]
            r_sq = sb_rows[:][0:1, 512:1024]
            r_mean = sb_rows[:][0:1, 1024:1536]
            r_var = sb_rows[:][0:1, 1536:2048]
            r_rstd = sb_rows[:][0:1, 2048:2560]
            vcopy(r_sum, PB[2][:][0:1, 0:512])
            vcopy(r_sq, PB[3][:][0:1, 0:512])
            ts(r_mean, r_sum, 1.0 / D, ALU.mult)
            tt(r_var, r_mean, r_mean, ALU.mult)
            stt(r_var, r_sq, 1.0 / D, r_var, ALU.mult, ALU.subtract, sem=VEC)
            vv = VEC.v
            g.wait("scalar", VEC, vv)
            act(r_rstd, r_var, AF.Ln, scale=f32(1.0), bias=f32(EPS))
            act(r_rstd, r_rstd, AF.Exp, scale=f32(-0.5), sem=SCA)
            vs = SCA.v
            g.wait("tensor", SCA, vs)
            g.wait("tensor", VEC, vv)
            mm(PB[2][:][:, 0:512], sb_onesk[:], r_mean, start=True, stop=True)
            mm(PB[3][:][:, 0:512], sb_onesk[:], r_rstd, start=True, stop=True, sem=TEN)
            vb = TEN.v
            g.wait("vector", TEN, vb)
            vcopy(dst_M[:, 512 * t:512 * t + 512], PB[2][:][:, 0:512])
            vcopy(dst_R[:, 512 * t:512 * t + 512], PB[3][:][:, 0:512], sem=VEC)
            g.wait("tensor", VEC)
        bank_rel("P2a")
        bank_rel("P3a")

    def res_gemm(wsrc, n_k, rhs_src, res_src, bias_name, dst_st, ntt):
        phase_barrier()
        bank_wait("tensor", "P0a", "P1a")
        reset("A2")
        whl = carve("A2", FC * n_k * 128)
        wtmp = carve("A2", n_k * 128)
        wallsem = mk_sem()
        rhs_rh = Reader("A2", 256 * n_k, n=1)
        rhs_rl = Reader("A2", 256 * n_k, n=1)
        res_r = Reader("A2", 512, n=3)
        outb = [carve("A2", 512) for _ in range(2)]
        for oc in range(FC):
            g.wait("sync", VEC)
            v = g.dma("sync", wtmp.rearrange("p (k c) -> p k c", c=128),
                      wsrc[oc], wallsem)
            g.wait("vector", wallsem, v)
            wv = whl[:, n_k * 128 * oc:n_k * 128 * (oc + 1)].bitcast(BF16)
            split_hl(wv[:, 0:n_k * 128], wv[:, n_k * 128:2 * n_k * 128], wtmp)
        g.wait("tensor", VEC)
        idx = 0
        for t in range(ntt):
            rbh = rhs_rh.rd(rhs_src[0][:, :, 512 * t:512 * t + 512], "tensor").bitcast(BF16)
            rbl = rhs_rl.rd(rhs_src[1][:, :, 512 * t:512 * t + 512], "tensor").bitcast(BF16)
            for oc in range(FC):
                key = f"P{idx % 2}a"
                pv = PB[idx % 2][:][:, 0:512]
                bank_wait("tensor", key)
                wv = whl[:, n_k * 128 * oc:n_k * 128 * (oc + 1)].bitcast(BF16)
                n_mm = 3 * n_k
                i_mm = 0
                for k in range(n_k):
                    wh = wv[:, 128 * k:128 * k + 128]
                    wl = wv[:, n_k * 128 + 128 * k:n_k * 128 + 128 * (k + 1)]
                    ah = rbh[:, 512 * k:512 * k + 512]
                    al = rbl[:, 512 * k:512 * k + 512]
                    for (wop, aop) in ((wh, ah), (wh, al), (wl, ah)):
                        mm(pv, wop, aop, start=(i_mm == 0), stop=(i_mm == n_mm - 1),
                           sem=TEN if i_mm == n_mm - 1 else None)
                        i_mm += 1
                tv = TEN.v
                rb = res_r.rd(res_src[oc, :, 512 * t:512 * t + 512], "vector")
                ob = outb[idx % 2]
                if idx >= 2:
                    g.wait("vector", STQ)
                g.wait("vector", TEN, tv)
                if bias_name is not None:
                    ts(ob, pv, vcol(bias_name, oc), ALU.add)
                    bank_rel(key)
                    tt(ob, ob, rb, ALU.add)
                else:
                    tt(ob, pv, rb, ALU.add)
                    bank_rel(key)
                ev = VEC.v
                g.wait("gpsimd", VEC, ev)
                g.dma("gpsimd", dst_st[oc, :, 512 * t:512 * t + 512], ob, STQ)
                idx += 1

    reset("A1")
    reset("A2")
    XNh = carve("A1", FC * T // 2).bitcast(BF16)
    XNl = carve("A1", FC * T // 2).bitcast(BF16)
    M0 = carve("A2", T)
    R0 = carve("A2", T)
    sqb = carve("A2", 1024)
    rd1 = Reader("A2", 512, n=3)
    ln_stats(lambda fc, t_: rd1.rd(x_fm[fc, :, 512 * t_:512 * t_ + 512], "tensor", "scalar"),
             NT, M0, R0, sqb)

    rd2 = Reader("A2", 512, n=3)
    xtmp = carve("A2", 512)
    for t in range(NT):
        for fc in range(FC):
            ch = rd2.rd(x_fm[fc, :, 512 * t:512 * t + 512], "vector")
            c = slice(fc * T + 512 * t, fc * T + 512 * t + 512)
            tt(xtmp, ch, M0[:, 512 * t:512 * t + 512], ALU.subtract)
            tt(xtmp, xtmp, R0[:, 512 * t:512 * t + 512], ALU.mult)
            split_hl(XNh[:, c], XNl[:, c], xtmp)

    def xn_rhs(k, t):
        c = slice(k * T + 512 * t, k * T + 512 * t + 512)
        return (XNh[:, c], XNl[:, c])

    if stop_phase <= 1:
        return finish(nc, g, ctx)
    g.wait("sync", VEC)
    reset("A2")
    def stage_gemm(wsrc, n_oc, rhs_fn, ntt, evac_fn, wb=None, wsem=None, wtmp=None):
        if wb is None:
            wb = [carve("A2", FC * 128) for _ in range(2)]
        if wtmp is None:
            wtmp = carve("A2", FC * 128)
        if wsem is None:
            wsem = [mk_sem(), mk_sem()]
        phase_barrier()
        wsnap = []
        idx = 0
        for oc in range(n_oc):
            wsnap.append(TEN.v)
            if oc >= 2:
                g.wait("sync", TEN, wsnap[oc - 1])
            g.wait("sync", VEC)
            vldw = g.dma("sync", wtmp.rearrange("p (k c) -> p k c", c=128),
                         wsrc[oc], wsem[oc % 2])
            whl = wb[oc % 2].bitcast(BF16)
            whi = whl[:, 0:FC * 128]
            wlo = whl[:, FC * 128:2 * FC * 128]
            g.wait("vector", wsem[oc % 2], vldw)
            split_hl(whi, wlo, wtmp)
            vsp = VEC.v
            for t in range(ntt):
                key = f"P{idx % 2}a"
                pv = PB[idx % 2][:][:, 0:512]
                g.wait("tensor", VEC, vsp)
                bank_wait("tensor", key)
                n_mm = 3 * FC
                i_mm = 0
                for k in range(FC):
                    rh, rl = rhs_fn(k, t)
                    for (wop, aop) in ((whi, rh), (whi, rl), (wlo, rh)):
                        mm(pv, wop[:, 128 * k:128 * k + 128], aop,
                           start=(i_mm == 0), stop=(i_mm == n_mm - 1),
                           sem=TEN if i_mm == n_mm - 1 else None)
                        i_mm += 1
                tv = TEN.v
                g.wait("vector", TEN, tv)
                bank_rel(key, (VEC, evac_fn(oc, t, pv)))
                idx += 1

    def mk_copy_evac(dst_st):
        bufs = [carve("A2", 512) for _ in range(2)]
        state = {"i": 0}

        def evac(oc, t, pv):
            i = state["i"]
            b = bufs[i % 2].bitcast(BF16)
            if i >= 2:
                g.wait("vector", STQ)
            split_hl(b[:, 0:512], b[:, 512:1024], pv)
            ev = VEC.v
            g.wait("gpsimd", VEC, ev)
            g.dma("gpsimd", dst_st[0][oc, :, 512 * t:512 * t + 512], b[:, 0:512], STQ)
            g.dma("gpsimd", dst_st[1][oc, :, 512 * t:512 * t + 512], b[:, 512:1024], STQ)
            state["i"] = i + 1
            return ev
        return evac

    def mk_v_evac(dst_st, bias_name=None, bias_base=0):
        vb = carve("A2", 512)
        tb = carve("A2", 128)

        def evac(oc, t, pv):
            waitall("vector")
            if bias_name is None:
                vcopy(vb, pv, sem=VEC)
            else:
                ts(vb, pv, vcol(bias_name, bias_base + oc), ALU.add, sem=VEC)
            ev0 = VEC.v
            for i in range(4):
                g.wait("tensor", VEC)
                tv = transpose(PB[2][:][:, 0:128], vb[:, 128 * i:128 * i + 128], sem=TEN)
                g.wait("vector", TEN, tv)
                g.wait("vector", STQ)
                vcopy(tb, PB[2][:][:, 0:128], sem=VEC)
                g.wait("gpsimd", VEC)
                g.dma("gpsimd", dst_st[oc, 512 * t + 128 * i:512 * t + 128 * (i + 1), :],
                      tb, STQ)
            return ev0
        return evac

    wbA = [carve("A2", FC * 128) for _ in range(2)]
    wtmpA = carve("A2", FC * 128)
    wsemA = [mk_sem(), mk_sem()]
    stage_gemm(wq_s, FC, xn_rhs, NT, mk_copy_evac(qst), wb=wbA, wsem=wsemA, wtmp=wtmpA)
    stage_gemm(wk_s, FC, xn_rhs, NT, mk_copy_evac(kst), wb=wbA, wsem=wsemA, wtmp=wtmpA)
    stage_gemm(wv_s, FC, xn_rhs, NT, mk_v_evac(vst), wb=wbA, wsem=wsemA, wtmp=wtmpA)

    def att_slots(mode):
        slots = []
        if mode == "L0":
            for j in range(NSLOT):
                kts = [(kt, 0, None, None) for kt in range(4 * j)]
                kts += [(4 * j + d2, 128 * d2, 128 * d2, None) for d2 in range(4)]
                slots.append((512 * j, kts))
        else:
            for si in range(4):
                kts = [(kt, 0, None, None) for kt in range(8 * si)]
                kts += [(8 * si + d2, 0, None, 8 * si + d2) for d2 in range(8)]
                slots.append((512 * si, kts))
        return slots

    def attention(qsrc, ksrc, vsrc, astdst, mode):
        phase_barrier()
        bank_wait("tensor", "P0a", "P0b", "P1a", "P1b", "P2a", "P2b", "P3a", "P3b")
        reset("A1")
        reset("A2")
        QCOLS = T if mode == "L0" else TO
        slots = att_slots(mode)
        nslot = len(slots)
        cw = T // nslot
        Kbh = [carve("A1", T // 2).bitcast(BF16) for _ in range(2)]
        Kbl = [carve("A1", T // 2).bitcast(BF16) for _ in range(2)]
        Qbh = [carve("A1", QCOLS // 2).bitcast(BF16) for _ in range(2)]
        Qbl = [carve("A1", QCOLS // 2).bitcast(BF16) for _ in range(2)]
        Vb = [carve("A1", T) for _ in range(2)]
        Eb = [[carve("A2", 1024), carve("A2", 1024)] for _ in range(2)]
        accb = [carve("A2", 1024) for _ in range(2)]
        acchl = [carve("A2", 1024).bitcast(BF16) for _ in range(2)]
        rclb = [carve("A2", 512) for _ in range(2)]
        aob = [carve("A2", 512) for _ in range(2)]
        aosp = [carve("A2", 512).bitcast(BF16) for _ in range(2)]
        if not hasattr(attention, "KCS"):
            attention.KCS = [[mk_sem(), mk_sem()] for _ in range(2)]
        KCS = attention.KCS
        if mode == "L1":
            mskb = carve("A1", 4096).bitcast(I8)
            g.wait("sync", LDQ)
            vm = g.dma("sync", mskb, mask8, LDQ)
            g.wait("vector", LDQ, vm)

        def load_chunk(pp, s, ci):
            p = 2 * pp + s
            sem = KCS[s][ci % 2]
            g.wait("sync", sem)
            kc = slice(cw * ci, cw * (ci + 1))
            qc = slice(512 * ci, 512 * ci + 512)
            g.dma("sync", Kbh[s][:, kc], ksrc[0][p][:, kc], sem)
            g.dma("sync", Kbl[s][:, kc], ksrc[1][p][:, kc], sem)
            g.dma("sync", Qbh[s][:, qc], qsrc[0][p][:, qc], sem)
            g.dma("sync", Qbl[s][:, qc], qsrc[1][p][:, qc], sem)
            g.dma("sync", Vb[s][:, kc].rearrange("p (n c) -> p n c", c=128),
                  vsrc[p][kc, :].rearrange("(n p2) c -> p2 n c", p2=128), sem)

        prev_vav = [None, None]
        prev_ev = [None, None]
        prev_vr = [None, None]
        for pp in range(NPAIR // 2):
            waitall("sync")
            g.wait("sync", STQ)
            g.wait("sync", LDQ)
            snaps = []
            for ci in range(nslot):
                for s in (0, 1):
                    load_chunk(pp, s, ci)
                snaps.append([[(KCS[s][0], KCS[s][0].v), (KCS[s][1], KCS[s][1].v)]
                              for s in (0, 1)])
            for si, (q0, kts) in enumerate(slots):
                n_kt = len(kts)
                s_exp = [{}, {}]
                t_av = [{}, {}]
                a_v = [{}, {}]
                m_v = [{}, {}]

                def emit_av(s, ki, kts=kts, n_kt=n_kt, s_exp=s_exp, t_av=t_av,
                            m_v=m_v):
                    kt, off, _, _ = kts[ki]
                    E = Eb[s][ki % 2]
                    g.wait("tensor", SCA, s_exp[s][ki])
                    if off > 0:
                        g.wait("tensor", VEC, m_v[s][ki])
                    if ki == 0 and prev_ev[s] is not None:
                        g.wait("tensor", VEC, prev_ev[s])
                    vb = Vb[s][:, 128 * kt:128 * kt + 128]
                    mm(PB[s][:][0:64, 0:512], vb[:, 0:64], E[:, 0:512],
                       start=(ki == 0), stop=(ki == n_kt - 1), tile_position=(0, 0))
                    t_av[s][ki] = mm(PB[s][:][64:128, 512:1024], vb[:, 64:128],
                                     E[:, 512:1024], start=(ki == 0),
                                     stop=(ki == n_kt - 1), tile_position=(0, 64),
                                     sem=TEN)

                for s in (0, 1):
                    g.op("vector", lambda e, a=accb[s]: e.memset(a[:], 0.0), VEC)
                    for (sm, v) in snaps[si][s]:
                        g.wait("tensor", sm, v)
                for ki, (kt, off, trio, mki) in enumerate(kts):
                    for s in (0, 1):
                        if ki >= 1:
                            g.wait("tensor", SCA, s_exp[s][ki - 1])
                        elif prev_vr[s] is not None:
                            g.wait("tensor", SCA, prev_vr[s])
                        ks = slice(128 * kt, 128 * kt + 128)
                        qs = slice(q0 + off, q0 + 512)
                        pb = PB[2 + s][:]
                        for i_c, (kk, qq) in enumerate(
                                ((Kbh[s], Qbh[s]), (Kbh[s], Qbl[s]), (Kbl[s], Qbh[s]))):
                            mm(pb[:, off:512], kk[0:64, ks], qq[0:64, qs],
                               start=(i_c == 0), stop=(i_c == 2))
                            mm(pb[:, 512 + off:1024], kk[64:128, ks], qq[64:128, qs],
                               start=(i_c == 0), stop=(i_c == 2),
                               sem=TEN if i_c == 2 else None)
                        t_sc = TEN.v
                        pb3 = pb[:, 0:1024].rearrange("p (h q) -> p h q", q=512)
                        if mki is not None:
                            g.wait("vector", TEN, t_sc)
                            stt(pb3, mskb[:, 512 * mki:512 * mki + 512]
                                .unsqueeze(1).broadcast_to((128, 2, 512)),
                                -1e30, pb3, ALU.mult, ALU.add)
                            g.wait("scalar", VEC)
                        elif trio is not None:
                            g.wait("vector", TEN, t_sc)
                            tt(pb3[:, :, trio:trio + 128], pb3[:, :, trio:trio + 128],
                               sb_tri[:].unsqueeze(1).broadcast_to((128, 2, 128)),
                               ALU.add)
                            g.wait("scalar", VEC)
                        else:
                            g.wait("scalar", TEN, t_sc)
                        E = Eb[s][ki % 2]
                        if ki >= 2:
                            g.wait("scalar", TEN, t_av[s][ki - 2])
                            g.wait("scalar", VEC, a_v[s][ki - 2])
                        elif prev_vav[s] is not None:
                            g.wait("scalar", TEN, prev_vav[s])
                        E3 = E.rearrange("p (h q) -> p h q", q=512)
                        s_exp[s][ki] = act(E3[:, :, off:512], pb3[:, :, off:512],
                                           AF.Exp, scale=f32(0.125), sem=SCA)
                        if off > 0:
                            g.op("vector", lambda e, a=E3, o=off:
                                 e.memset(a[:, :, 0:o], 0.0), VEC)
                            m_v[s][ki] = VEC.v
                        g.wait("vector", SCA, s_exp[s][ki])
                        a3 = accb[s].rearrange("p (h q) -> p h q", q=512)
                        tt(a3, a3, E3, ALU.add)
                        a_v[s][ki] = VEC.v
                    for s in (0, 1):
                        if ki >= 1:
                            emit_av(s, ki - 1)
                for s in (0, 1):
                    emit_av(s, n_kt - 1)
                vav = [t_av[s][n_kt - 1] for s in (0, 1)]
                vsp = [0, 0]
                for s in (0, 1):
                    g.wait("vector", STQ)
                    ahl = acchl[s]
                    split_hl(ahl[:, 0:1024], ahl[:, 1024:2048], accb[s])
                    vsp[s] = VEC.v
                vsum = [0, 0]
                for s in (0, 1):
                    g.wait("tensor", VEC, vsp[s])
                    g.wait("tensor", SCA, s_exp[s][n_kt - 1])
                    ahl = acchl[s]
                    pbs = PB[2 + s][:]
                    mm(pbs[0:64, 0:512], sb_ones64b_t[:], ahl[:, 0:512],
                       start=True, stop=False, tile_position=(0, 0))
                    mm(pbs[0:64, 0:512], sb_ones64b_t[:], ahl[:, 1024:1536],
                       start=False, stop=True, tile_position=(0, 0))
                    mm(pbs[64:128, 0:512], sb_ones64b_t[:], ahl[:, 512:1024],
                       start=True, stop=False, tile_position=(0, 64))
                    mm(pbs[64:128, 0:512], sb_ones64b_t[:], ahl[:, 1536:2048],
                       start=False, stop=True, tile_position=(0, 64), sem=TEN)
                    vsum[s] = TEN.v
                vr = [0, 0]
                for s in (0, 1):
                    g.wait("scalar", TEN, vsum[s])
                    act(rclb[s], PB[2 + s][:][:, 0:512], AF.Ln)
                    act(rclb[s], rclb[s], AF.Exp, scale=f32(-1.0), sem=SCA)
                    vr[s] = SCA.v
                for s in (0, 1):
                    g.wait("vector", SCA, vr[s])
                    g.wait("vector", TEN, vav[s])
                    tt(aob[s][0:64, :], PB[s][:][0:64, 0:512], rclb[s][0:64, :],
                       ALU.mult)
                    tt(aob[s][64:128, :], PB[s][:][64:128, 512:1024],
                       rclb[s][64:128, :], ALU.mult)
                    av = aosp[s]
                    split_hl(av[:, 0:512], av[:, 512:1024], aob[s])
                    ev = VEC.v
                    g.wait("gpsimd", VEC, ev)
                    p = 2 * pp + s
                    g.dma("gpsimd", astdst[0][p, :, q0:q0 + 512], av[:, 0:512], STQ)
                    g.dma("gpsimd", astdst[1][p, :, q0:q0 + 512], av[:, 512:1024], STQ)
                    prev_vav[s] = vav[s]
                    prev_ev[s] = ev
                    prev_vr[s] = vr[s]
        for kk in ("P0a", "P0b", "P1a", "P1b", "P2a", "P2b", "P3a", "P3b"):
            bank_rel(kk, (VEC, VEC.v), (SCA, SCA.v))

    attention(qst, kst, vst, ast, "L0")

    if stop_phase <= 3:
        return finish(nc, g, ctx)
    res_gemm(wp_s, FC, ast, x_fm, "bp", y1st, NT)


    if stop_phase <= 4:
        return finish(nc, g, ctx)
    reset("A1")
    ZNh = carve("A1", FC * T // 2).bitcast(BF16)
    ZNl = carve("A1", FC * T // 2).bitcast(BF16)
    reset("A2")
    M1 = carve("A2", T)
    R1 = carve("A2", T)
    sqb = carve("A2", 1024)
    rd3 = Reader("A2", 512, n=3)
    ln_stats(lambda fc, t_: rd3.rd(y1st[fc, :, 512 * t_:512 * t_ + 512], "tensor", "scalar"),
             NT, M1, R1, sqb)
    rd4 = Reader("A2", 512, n=3)
    ztmp = carve("A2", 512)
    for t in range(NT):
        for fc in range(FC):
            ch = rd4.rd(y1st[fc, :, 512 * t:512 * t + 512], "vector")
            c = slice(fc * T + 512 * t, fc * T + 512 * t + 512)
            tt(ztmp, ch, M1[:, 512 * t:512 * t + 512], ALU.subtract)
            tt(ztmp, ztmp, R1[:, 512 * t:512 * t + 512], ALU.mult)
            split_hl(ZNh[:, c], ZNl[:, c], ztmp)

    def res_gemm_sbuf(wsrc, rhs_fn, res_src, bias_name, dst_st, ntt, sign_w=False):
        phase_barrier()
        bank_wait("tensor", "P0a", "P1a")
        reset("A2")
        res_r = Reader("A2", 512, n=3)
        outb = [carve("A2", 512) for _ in range(2)]
        wallsem = mk_sem()
        if sign_w:
            wsb = carve("A2", FC * FC * 64).bitcast(BF16)
            wtmp = carve("A2", FC * 128)
            for oc in range(FC):
                g.wait("sync", SCA)
                v = g.dma("sync", wtmp.rearrange("p (k c) -> p k c", c=128),
                          wsrc[oc], wallsem)
                g.wait("scalar", wallsem, v)
                act(wsb[:, 1024 * oc:1024 * (oc + 1)], wtmp, AF.Sign, sem=SCA)
            g.wait("tensor", SCA)
        else:
            wsb = carve("A2", FC * FC * 128)
            wtmp = carve("A2", FC * 128)
            for oc in range(FC):
                g.wait("sync", VEC)
                v = g.dma("sync", wtmp.rearrange("p (k c) -> p k c", c=128),
                          wsrc[oc], wallsem)
                g.wait("vector", wallsem, v)
                wv = wsb[:, 1024 * oc:1024 * (oc + 1)].bitcast(BF16)
                split_hl(wv[:, 0:1024], wv[:, 1024:2048], wtmp)
            g.wait("tensor", VEC)
        idx = 0
        for t in range(ntt):
            for oc in range(FC):
                key = f"P{idx % 2}a"
                pv = PB[idx % 2][:][:, 0:512]
                bank_wait("tensor", key)
                if sign_w:
                    n_mm = 2 * FC
                    i_mm = 0
                    for k in range(FC):
                        sw = wsb[:, 1024 * oc + 128 * k:1024 * oc + 128 * (k + 1)]
                        rh, rl = rhs_fn(k, t)
                        for aop in (rh, rl):
                            mm(pv, sw, aop, start=(i_mm == 0), stop=(i_mm == n_mm - 1),
                               sem=TEN if i_mm == n_mm - 1 else None)
                            i_mm += 1
                else:
                    wv = wsb[:, 1024 * oc:1024 * (oc + 1)].bitcast(BF16)
                    n_mm = 3 * FC
                    i_mm = 0
                    for k in range(FC):
                        wh = wv[:, 128 * k:128 * k + 128]
                        wl = wv[:, 1024 + 128 * k:1024 + 128 * (k + 1)]
                        rh, rl = rhs_fn(k, t)
                        for (wop, aop) in ((wh, rh), (wh, rl), (wl, rh)):
                            mm(pv, wop, aop, start=(i_mm == 0), stop=(i_mm == n_mm - 1),
                               sem=TEN if i_mm == n_mm - 1 else None)
                            i_mm += 1
                tv = TEN.v
                rb = res_r.rd(res_src[oc, :, 512 * t:512 * t + 512], "vector")
                ob = outb[idx % 2]
                if idx >= 2:
                    g.wait("vector", STQ)
                g.wait("vector", TEN, tv)
                if bias_name is not None:
                    ts(ob, pv, vcol(bias_name, oc), ALU.add)
                    bank_rel(key)
                    tt(ob, ob, rb, ALU.add)
                else:
                    tt(ob, pv, rb, ALU.add)
                    bank_rel(key)
                ev = VEC.v
                g.wait("gpsimd", VEC, ev)
                g.dma("gpsimd", dst_st[oc, :, 512 * t:512 * t + 512], ob, STQ)
                idx += 1

    def zn_rhs(k, t):
        c = slice(k * T + 512 * t, k * T + 512 * t + 512)
        return (ZNh[:, c], ZNl[:, c])

    res_gemm_sbuf(w0t_s, zn_rhs, y1st, None, y2st, NT, sign_w=True)

    if stop_phase <= 5:
        return finish(nc, g, ctx)
    phase_barrier()
    reset("A2")
    rda = Reader("A2", 256, n=3)
    rdb = Reader("A2", 256, n=3)
    bo_ = [carve("A2", 256) for _ in range(2)]
    st_hist = []
    ii = 0
    for oc in range(FC):
        for j in range(NSLOT):
            ba = rda.rd(y2st[oc, :, 256 * OWN_A[j]:256 * OWN_A[j] + 256], "vector")
            bb = rdb.rd(y2st[oc, :, 256 * OWN_B[j]:256 * OWN_B[j] + 256], "vector")
            ob = bo_[ii % 2]
            if ii >= 2:
                g.wait("vector", STQ)
            ts(ob, ba, vcol("selA"), ALU.mult)
            stt(ob, bb, vcol("selB"), ob, ALU.mult, ALU.add, sem=VEC)
            g.wait("gpsimd", VEC)
            st_hist.append(g.dma("gpsimd", y2own[oc, :, 256 * j:256 * j + 256], ob, STQ))
            ii += 1

    if stop_phase <= 6:
        return finish(nc, g, ctx)
    reset("A1")
    M2 = carve("A1", T)
    R2 = carve("A1", T)
    P2 = R2
    reset("A2")
    sqb = carve("A2", 1024)
    rd5 = Reader("A2", 512, n=3)
    ln_stats(lambda fc, t_: rd5.rd(y2st[fc, :, 512 * t_:512 * t_ + 512], "tensor", "scalar"),
             NT, M2, R2, sqb)
    tmpc = carve("A2", 512)
    for t in range(NT):
        c = slice(512 * t, 512 * t + 512)
        g.wait("vector", SCA)
        tt(tmpc, R2[:, c], R2[:, c], ALU.mult)
        ts(tmpc, tmpc, -float(EPS), ALU.mult, float(1.0 + EPS), ALU.add, sem=VEC)
        g.wait("scalar", VEC)
        act(tmpc, tmpc, AF.Ln)
        act(tmpc, tmpc, AF.Exp, scale=f32(-0.5), sem=SCA)
        g.wait("vector", SCA)
        tt(P2[:, c], R2[:, c], tmpc, ALU.mult, sem=VEC)
    g.wait("scalar", VEC)

    rd6 = Reader("A2", 512, n=3)
    n1b = [carve("A2", 512) for _ in range(2)]
    st_hist = []
    ii = 0
    for t in range(NT):
        for fc in range(FC):
            ch = rd6.rd(y2st[fc, :, 512 * t:512 * t + 512], "vector")
            nb = n1b[ii % 2]
            if ii >= 2:
                g.wait("vector", STQ)
            tt(nb, ch, M2[:, 512 * t:512 * t + 512], ALU.subtract)
            tt(nb, nb, P2[:, 512 * t:512 * t + 512], ALU.mult)
            ts(nb, nb, vcol("g1", fc), ALU.mult, vcol("b1", fc), ALU.add, sem=VEC)
            g.wait("gpsimd", VEC)
            st_hist.append(g.dma("gpsimd", n1st[fc, :, 512 * t:512 * t + 512], nb, STQ))
            ii += 1

    if stop_phase <= 7:
        return finish(nc, g, ctx)
    reset("A1")
    N1h = carve("A1", FC * T // 2).bitcast(BF16)
    N1l = carve("A1", FC * T // 2).bitcast(BF16)
    reset("A2")
    phase_barrier()
    rd7 = Reader("A2", 512, n=1)
    g.wait("sync", STQ)
    for t in range(NT):
        for fc in range(FC):
            ch = rd7.rd(n1st[fc, :, 512 * t:512 * t + 512], "vector")
            c = slice(fc * T + 512 * t, fc * T + 512 * t + 512)
            split_hl(N1h[:, c], N1l[:, c], ch)

    def n1_rhs(k, t):
        c = slice(k * T + 512 * t, k * T + 512 * t + 512)
        return (N1h[:, c], N1l[:, c])

    C2b_ = carve("A2", T)
    S2b_ = carve("A2", T)
    g.wait("sync", LDQ)
    v1_ = g.dma("sync", C2b_, c2t, LDQ)
    v2_ = g.dma("sync", S2b_, s2t, LDQ)
    g.wait("vector", LDQ)

    def mk_rope_evac(dst_st, ctab, stab, bias_base):
        kpre = [carve("A2", 512) for _ in range(2)]
        krot = [carve("A2", 512) for _ in range(2)]
        kout1 = carve("A2", 512)
        kout = [kout1, kout1]
        ksp = [carve("A2", 512) for _ in range(2)]
        rsem = mk_sem()
        state = {"i": 0}

        def evac(oc, t, pv):
            i = state["i"]
            b0, b1, b2 = kpre[i % 2], krot[i % 2], kout[i % 2]
            if i >= 2:
                g.wait("vector", rsem, 64 * (i - 1))
                g.wait("vector", STQ)
            ts(b0, pv, vcol("bqkv", bias_base + oc), ALU.add)
            ev0 = VEC.v
            g.wait("gpsimd", VEC, ev0)
            for (d0, s0) in ((0, 32), (32, 0), (64, 96), (96, 64)):
                g.dma("gpsimd", b1[d0:d0 + 32, :], b0[s0:s0 + 32, :], rsem)
            c = slice(512 * t, 512 * t + 512)
            g.wait("vector", rsem, 64 * (i + 1))
            tt(b2, b0, ctab[:, c], ALU.mult)
            tt(b1, b1, stab[:, c], ALU.mult)
            tt(b2, b2, b1, ALU.add)
            bh = ksp[i % 2].bitcast(BF16)
            split_hl(bh[:, 0:512], bh[:, 512:1024], b2)
            ev = VEC.v
            g.wait("gpsimd", VEC, ev)
            g.dma("gpsimd", dst_st[0][oc, :, c], bh[:, 0:512], STQ)
            g.dma("gpsimd", dst_st[1][oc, :, c], bh[:, 512:1024], STQ)
            state["i"] = i + 1
            return ev0
        return evac

    class WView:
        def __init__(self, base):
            self.base = base

        def __getitem__(self, i):
            return wqkv_s[self.base + i]

    wb_c2 = [carve("A2", FC * 128) for _ in range(2)]
    wtmp_c2 = carve("A2", FC * 128)
    wsem_c2 = [mk_sem(), mk_sem()]
    stage_gemm(WView(FC), FC, n1_rhs, NT, mk_rope_evac(kst, C2b_, S2b_, FC),
               wb=wb_c2, wsem=wsem_c2, wtmp=wtmp_c2)
    stage_gemm(WView(2 * FC), FC, n1_rhs, NT, mk_v_evac(vst, "bqkv", 2 * FC),
               wb=wb_c2, wsem=wsem_c2, wtmp=wtmp_c2)

    if stop_phase <= 8:
        return finish(nc, g, ctx)
    reset("A1")
    M2o = carve("A1", TO)
    R2o = carve("A1", TO)
    P2o = R2o
    N1oh = carve("A1", FC * TO // 2).bitcast(BF16)
    N1ol = carve("A1", FC * TO // 2).bitcast(BF16)
    reset("A2")
    sqb = carve("A2", 1024)
    g.wait("sync", STQ)
    rd8 = Reader("A2", 512, n=3)
    ln_stats(lambda fc, t_: rd8.rd(y2own[fc, :, 512 * t_:512 * t_ + 512], "tensor", "scalar"),
             NTO, M2o, R2o, sqb)
    tmpo = carve("A2", 512)
    for t in range(NTO):
        c = slice(512 * t, 512 * t + 512)
        g.wait("vector", SCA)
        tt(tmpo, R2o[:, c], R2o[:, c], ALU.mult)
        ts(tmpo, tmpo, -float(EPS), ALU.mult, float(1.0 + EPS), ALU.add, sem=VEC)
        g.wait("scalar", VEC)
        act(tmpo, tmpo, AF.Ln)
        act(tmpo, tmpo, AF.Exp, scale=f32(-0.5), sem=SCA)
        g.wait("vector", SCA)
        tt(P2o[:, c], R2o[:, c], tmpo, ALU.mult, sem=VEC)
    g.wait("scalar", VEC)
    rd9 = Reader("A2", 512, n=3)
    ntmp = carve("A2", 512)
    for t in range(NTO):
        for fc in range(FC):
            ch = rd9.rd(y2own[fc, :, 512 * t:512 * t + 512], "vector")
            c = slice(fc * TO + 512 * t, fc * TO + 512 * t + 512)
            tt(ntmp, ch, M2o[:, 512 * t:512 * t + 512], ALU.subtract)
            tt(ntmp, ntmp, P2o[:, 512 * t:512 * t + 512], ALU.mult)
            ts(ntmp, ntmp, vcol("g1", fc), ALU.mult, vcol("b1", fc), ALU.add)
            split_hl(N1oh[:, c], N1ol[:, c], ntmp)

    def n1o_rhs(k, t):
        c = slice(k * TO + 512 * t, k * TO + 512 * t + 512)
        return (N1oh[:, c], N1ol[:, c])

    C2ob = carve("A2", TO)
    S2ob = carve("A2", TO)
    g.wait("sync", LDQ)
    v1_ = g.dma("sync", C2ob, c2o, LDQ)
    v2_ = g.dma("sync", S2ob, s2o, LDQ)
    g.wait("vector", LDQ)
    stage_gemm(WView(0), FC, n1o_rhs, NTO, mk_rope_evac(qst1, C2ob, S2ob, 0))

    if stop_phase <= 9:
        return finish(nc, g, ctx)
    reset("A1")
    attention(qst1, kst, vst, ast1, "L1")

    if stop_phase <= 10:
        return finish(nc, g, ctx)
    reset("A1")
    Z2 = carve("A1", FC * TO)
    M2r = carve("A1", TO)
    R2r = carve("A1", TO)
    reset("A2")
    sqb = carve("A2", 1024)
    rd10 = Reader("A2", 512, n=3)
    ln_stats(lambda fc, t_: rd10.rd(y2own[fc, :, 512 * t_:512 * t_ + 512], "tensor", "scalar"),
             NTO, M2r, R2r, sqb)

    def wo_gemm():
        phase_barrier()
        bank_wait("tensor", "P0a", "P1a")
        whl = carve("A1", FC * FC * 128)
        wtmp = carve("A2", FC * 128)
        wallsem = mk_sem()
        rhs_rh = Reader("A2", FC * 256, n=1)
        rhs_rl = Reader("A2", FC * 256, n=1)
        res_r = Reader("A2", 512, n=3)
        zb = [carve("A2", 512) for _ in range(2)]
        for oc in range(FC):
            g.wait("sync", VEC)
            v = g.dma("sync", wtmp.rearrange("p (k c) -> p k c", c=128),
                      wo_s[oc], wallsem)
            g.wait("vector", wallsem, v)
            wv = whl[:, 1024 * oc:1024 * (oc + 1)].bitcast(BF16)
            split_hl(wv[:, 0:1024], wv[:, 1024:2048], wtmp)
        g.wait("tensor", VEC)
        idx = 0
        for t in range(NTO):
            rbh = rhs_rh.rd(ast1[0][:, :, 512 * t:512 * t + 512], "tensor").bitcast(BF16)
            rbl = rhs_rl.rd(ast1[1][:, :, 512 * t:512 * t + 512], "tensor").bitcast(BF16)
            for oc in range(FC):
                key = f"P{idx % 2}a"
                pv = PB[idx % 2][:][:, 0:512]
                bank_wait("tensor", key)
                wv = whl[:, 1024 * oc:1024 * (oc + 1)].bitcast(BF16)
                i_mm = 0
                for k in range(FC):
                    wh = wv[:, 128 * k:128 * k + 128]
                    wl = wv[:, 1024 + 128 * k:1024 + 128 * (k + 1)]
                    ah = rbh[:, 512 * k:512 * k + 512]
                    al = rbl[:, 512 * k:512 * k + 512]
                    for (wop, aop) in ((wh, ah), (wh, al), (wl, ah)):
                        mm(pv, wop, aop, start=(i_mm == 0), stop=(i_mm == 3 * FC - 1),
                           sem=TEN if i_mm == 3 * FC - 1 else None)
                        i_mm += 1
                tv = TEN.v
                rb = res_r.rd(y2own[oc, :, 512 * t:512 * t + 512], "vector")
                zbb = zb[idx % 2]
                c = slice(512 * t, 512 * t + 512)
                g.wait("vector", TEN, tv)
                tt(zbb, rb, M2r[:, c], ALU.subtract)
                tt(zbb, zbb, R2r[:, c], ALU.mult)
                dst = Z2[:, oc * TO + 512 * t:oc * TO + 512 * t + 512]
                ts(dst, pv, vcol("bo", oc), ALU.add)
                bank_rel(key)
                tt(dst, dst, zbb, ALU.add)
                idx += 1

    wo_gemm()
    phase_barrier()
    for fc in range(FC):
        g.wait("gpsimd", VEC)
        g.dma("gpsimd", z2dbg[fc], Z2[:, fc * TO:(fc + 1) * TO], STQ)

    if stop_phase <= 11:
        return finish(nc, g, ctx)
    w1sem = [mk_sem(), mk_sem()]
    w2sem = [mk_sem(), mk_sem()]
    wstg_sem = mk_sem()

    def stage_whl():
        phase_barrier()
        reset("A2")
        wtmp = carve("A2", DMC * 128)
        whl = carve("A2", DMC * 128)
        for (wsrc, dst, n_oc, n_k) in ((w1_s, w1hl_s, DMC, FC),
                                       (w2_s, w2hl_s, FC, DMC)):
            for oc in range(n_oc):
                waitall("sync")
                g.wait("sync", STQ)
                v = g.dma("sync", wtmp[:, 0:n_k * 128].rearrange("p (k c) -> p k c", c=128),
                          wsrc[oc], wstg_sem)
                g.wait("vector", wstg_sem, v)
                g.wait("vector", STQ)
                wv = whl[:, 0:n_k * 128].bitcast(BF16)
                split_hl(wv[:, 0:n_k * 128], wv[:, n_k * 128:2 * n_k * 128], wtmp[:, 0:n_k * 128])
                g.wait("gpsimd", VEC)
                g.dma("gpsimd", dst[oc], wv[:, 0:2 * n_k * 128], STQ)

    stage_whl()

    reset("A2")
    M3 = carve("A2", TO)
    R3 = carve("A2", TO)
    sqb = carve("A2", 1024)

    def z2_chunk(fc, t):
        return Z2[:, fc * TO + 512 * t:fc * TO + 512 * t + 512]

    ln_stats(z2_chunk, NTO, M3, R3, sqb)

    n2b = carve("A2", FC * 512)
    w1b = [carve("A2", FC * 128) for _ in range(2)]
    gtmp = carve("A2", 512)
    ob2 = carve("A2", 256)
    res_r2 = Reader("A2", 256, n=3)

    def mlp():
        phase_barrier()
        bank_wait("tensor", "P0a", "P1a", "P2a")
        w2b = [A1t[:][:, 16384:16384 + 4096], A1t[:][:, 20480:20480 + 4096]]
        hb = A1t[:][:, 24576:24576 + DMC * 256].bitcast(BF16)
        hh = hb[:, 0:DMC * 256]
        hl = hb[:, DMC * 256:2 * DMC * 256]
        g.wait("sync", STQ)
        for t in range(NTO):
            g.wait("vector", TEN)
            for fc in range(FC):
                c = slice(512 * t, 512 * t + 512)
                tt(gtmp, z2_chunk(fc, t), M3[:, c], ALU.subtract)
                tt(gtmp, gtmp, R3[:, c], ALU.mult)
                ts(gtmp, gtmp, vcol("g2", fc), ALU.mult, vcol("b2", fc), ALU.add)
                nv = n2b.bitcast(BF16)
                split_hl(nv[:, 512 * fc:512 * fc + 512],
                         nv[:, FC * 512 + 512 * fc:FC * 512 + 512 * (fc + 1)], gtmp)
            vn2 = VEC.v
            g.wait("tensor", VEC, vn2)
            nv = n2b.bitcast(BF16)
            for half in range(2):
                hrel = []
                wsnap = []
                for oc in range(DMC):
                    wsnap.append(TEN.v)
                    if oc >= 2:
                        g.wait("sync", TEN, wsnap[oc - 1])
                    vldw = g.dma("sync", w1b[oc % 2].bitcast(BF16), w1hl_s[oc],
                                 w1sem[oc % 2])
                    pv = PB[oc % 2][:][:, 0:256]
                    g.wait("tensor", w1sem[oc % 2], vldw)
                    if oc >= 2:
                        g.wait("tensor", SCA, hrel[oc - 2])
                    wv = w1b[oc % 2].bitcast(BF16)
                    i_mm = 0
                    for k in range(FC):
                        wh = wv[:, 128 * k:128 * k + 128]
                        wl = wv[:, FC * 128 + 128 * k:FC * 128 + 128 * (k + 1)]
                        cq = slice(512 * k + 256 * half, 512 * k + 256 * half + 256)
                        ah = nv[:, cq]
                        al = nv[:, FC * 512:2 * FC * 512][:, cq]
                        for (wop, aop) in ((wh, ah), (wh, al), (wl, ah)):
                            mm(pv, wop, aop, start=(i_mm == 0), stop=(i_mm == 3 * FC - 1),
                               sem=TEN if i_mm == 3 * FC - 1 else None)
                            i_mm += 1
                    g.wait("scalar", TEN)
                    g.wait("scalar", VEC)
                    act(gtmp[:, 0:256], pv, AF.Gelu, bias=vcol("b1m", oc), sem=SCA)
                    hrel.append(SCA.v)
                    g.wait("vector", SCA)
                    split_hl(hh[:, 256 * oc:256 * oc + 256],
                             hl[:, 256 * oc:256 * oc + 256], gtmp[:, 0:256])
                g.wait("tensor", VEC)
                wsnap2 = []
                for oc in range(FC):
                    wsnap2.append(TEN.v)
                    if oc >= 2:
                        g.wait("sync", TEN, wsnap2[oc - 1])
                    elif oc == 0:
                        g.wait("sync", TEN)
                    vldw = g.dma("sync", w2b[oc % 2].bitcast(BF16), w2hl_s[oc],
                                 w2sem[oc % 2])
                    pv = PB[2][:][:, 0:256]
                    g.wait("tensor", w2sem[oc % 2], vldw)
                    g.wait("tensor", VEC)
                    wv = w2b[oc % 2].bitcast(BF16)
                    i_mm = 0
                    for k in range(DMC):
                        wh = wv[:, 128 * k:128 * k + 128]
                        wl = wv[:, DMC * 128 + 128 * k:DMC * 128 + 128 * (k + 1)]
                        ah = hh[:, 256 * k:256 * k + 256]
                        al = hl[:, 256 * k:256 * k + 256]
                        for (wop, aop) in ((wh, ah), (wh, al), (wl, ah)):
                            mm(pv, wop, aop, start=(i_mm == 0), stop=(i_mm == 3 * DMC - 1),
                               sem=TEN if i_mm == 3 * DMC - 1 else None)
                            i_mm += 1
                    tv = TEN.v
                    cq = slice(512 * t + 256 * half, 512 * t + 256 * half + 256)
                    rb = res_r2.rd(y2own[oc, :, cq], "vector")
                    g.wait("vector", TEN, tv)
                    g.wait("vector", STQ)
                    ts(ob2, pv, vcol("b2m", oc), ALU.add)
                    tt(ob2, ob2, Z2[:, oc * TO + 256 * (2 * t + half):
                                    oc * TO + 256 * (2 * t + half) + 256], ALU.add)
                    tt(ob2, ob2, rb, ALU.add)
                    ev = VEC.v
                    g.wait("gpsimd", VEC, ev)
                    g.dma("gpsimd", y3own[oc, :, cq], ob2, STQ)
                    g.wait("tensor", VEC, ev)

    mlp()

    if stop_phase <= 12:
        return finish(nc, g, ctx)
    reset("A1")
    QA = carve("A1", 1024)
    QV = carve("A1", 1024)
    QS1 = carve("A1", 1024)
    QS2 = carve("A1", 1024)
    QG = carve("A1", 1024)
    QM = carve("A1", 64)
    QSS = carve("A1", 64)
    QR = carve("A1", 64)
    trb = carve("A1", 128)

    qsl_state = [None]

    def quantize_slice(X, Fcols, out, xv=None):
        nb = Fcols // 16
        assert Fcols <= 1024
        a = QA[:, 0:Fcols]
        vv = QV[:, 0:Fcols]
        s1 = QS1[:, 0:Fcols]
        s2 = QS2[:, 0:Fcols]
        gsn = QG[:, 0:Fcols]
        m = QM[:, 0:nb]
        ss = QSS[:, 0:nb]
        r = QR[:, 0:nb]
        if qsl_state[0] is None:
            g.wait("scalar", VEC)
        else:
            g.wait("scalar", VEC, qsl_state[0])
        if xv is not None:
            g.wait("scalar", VEC, xv)
        act(a, X, AF.Abs)
        act(gsn, X, AF.Sign, sem=SCA)
        g.wait("vector", SCA)
        g.op("vector", lambda e, o=m, i=a: e.tensor_reduce(
            out=o, in_=i.rearrange("p (b k) -> p b k", k=16), axis=AX.X, op=ALU.max))
        ts(ss, m, C6_1, ALU.mult)
        stt(ss, m, C6_2, ss, ALU.mult, ALU.add)
        ts(ss, ss, float(f32(1e-12)), ALU.max, sem=VEC)
        g.wait("scalar", VEC)
        act(r, ss, AF.Ln)
        act(r, r, AF.Exp, scale=f32(-1.0), sem=SCA)
        g.wait("vector", SCA)
        rbc = r.unsqueeze(-1).broadcast_to((128, nb, 16))
        sbc = ss.unsqueeze(-1).broadcast_to((128, nb, 16))
        tt(vv.rearrange("p (b k) -> p b k", k=16),
           a.rearrange("p (b k) -> p b k", k=16), rbc, ALU.mult)
        ts(s1, vv, 0.25, ALU.is_gt)
        stt(s1, vv, 0.75, s1, ALU.is_gt, ALU.add)
        stt(s1, vv, 1.25, s1, ALU.is_gt, ALU.add)
        stt(s1, vv, 1.75, s1, ALU.is_gt, ALU.add)
        ts(s2, vv, 2.5, ALU.is_gt)
        stt(s2, vv, 3.5, s2, ALU.is_gt, ALU.add)
        stt(s2, vv, 5.0, s2, ALU.is_gt, ALU.add)
        stt(s2, vv, 5.0, s2, ALU.is_gt, ALU.add)
        stt(s2, s1, 0.5, s2, ALU.mult, ALU.add)
        tt(s2.rearrange("p (b k) -> p b k", k=16),
           s2.rearrange("p (b k) -> p b k", k=16), sbc, ALU.mult)
        tt(out, s2, gsn, ALU.mult, sem=VEC)
        qsl_state[0] = VEC.v
        return VEC.v

    def quantize(X, Fcols, out, xv=None):
        v = None
        o = 0
        while o < Fcols:
            w = min(1024, Fcols - o)
            v = quantize_slice(X[:, o:o + w], w, out[:, o:o + w], xv=xv)
            o += w
        return v

    reset("A2")
    natb = [carve("A2", DI) for _ in range(2)]
    trbig = [carve("A2", 1024) for _ in range(2)]
    natsem = [mk_sem(), mk_sem()]
    qw_stsem = [mk_sem(), mk_sem()]
    qw_state = {"lastten": [None, None], "split": {}, "store": {}, "gi": 0}

    def quant_weight(nat_src, n_och, icols, dst_strips, barrier=True):
        if barrier:
            phase_barrier()
        bank_wait("tensor", "P2a", "P2b", "P3a", "P3b")
        nic = icols // 128
        st = qw_state
        for oc in range(n_och):
            nb_ = natb[oc % 2]
            sem = natsem[oc % 2]
            g.wait("sync", sem)
            if st["lastten"][oc % 2] is not None:
                g.wait("sync", TEN, st["lastten"][oc % 2])
            v = g.dma("sync", nb_[:, 0:icols], nat_src[oc], sem)
            g.wait("vector", sem, v)
            g.wait("scalar", sem, v)
            qv = quantize(nb_[:, 0:icols], icols, nb_[:, 0:icols])
            g.wait("tensor", VEC, qv)
            for k0 in range(0, nic, 8):
                kn = min(8, nic - k0)
                gp = st["gi"] % 2
                st["gi"] += 1
                pb = PB[2 + gp][:]
                trv = trbig[gp].bitcast(BF16)
                ssem = qw_stsem[gp]
                pv = st["split"].get(gp)
                if pv is not None:
                    g.wait("tensor", VEC, pv)
                tv = 0
                for k in range(kn):
                    tv = transpose(pb[:, 128 * k:128 * k + 128],
                                   nb_[:, 128 * (k0 + k):128 * (k0 + k) + 128],
                                   sem=TEN)
                g.wait("vector", TEN, tv)
                sv_ = st["store"].get(gp)
                if sv_ is not None:
                    g.wait("vector", ssem, sv_)
                split_hl(trv[:, 0:128 * kn], trv[:, 128 * kn:256 * kn],
                         pb[:, 0:128 * kn])
                st["split"][gp] = VEC.v
                g.wait("gpsimd", VEC, VEC.v)
                g.dma("gpsimd", dst_strips[0][oc, :, k0:k0 + kn, :]
                      .rearrange("p k c -> p (k c)"), trv[:, 0:128 * kn], ssem)
                g.dma("gpsimd", dst_strips[1][oc, :, k0:k0 + kn, :]
                      .rearrange("p k c -> p (k c)"), trv[:, 128 * kn:256 * kn], ssem)
                st["store"][gp] = ssem.v
            st["lastten"][oc % 2] = TEN.v

    quant_weight(wg_nat, DIC, D, wgq_s)
    quant_weight(wu_nat, DIC, D, wuq_s, barrier=False)
    quant_weight(wd_nat, FC, DI, wdq_s, barrier=False)

    if stop_phase <= 13:
        return finish(nc, g, ctx)
    reset("A2")
    M4 = carve("A2", TO)
    R4 = carve("A2", TO)
    a2_mark = arena_off["A2"]
    sqb = carve("A2", 1024)
    g.wait("sync", STQ)
    rd11 = Reader("A2", 512, n=3)
    ln_stats(lambda fc, t_: rd11.rd(y3own[fc, :, 512 * t_:512 * t_ + 512], "tensor", "scalar"),
             NTO, M4, R4, sqb)
    arena_off["A2"] = a2_mark

    XQb2 = [A1t[:][:, 5440:5440 + 4096].bitcast(BF16),
            A1t[:][:, 9536:9536 + 4096].bitcast(BF16)]
    XTM = A1t[:][:, 13632:13632 + 4096]
    htmp = [A1t[:][:, 17728:17728 + 512], A1t[:][:, 18240:18240 + 512]]
    HQFb = A1t[:][:, 18752:18752 + DIC * 512].bitcast(BF16)
    xb = A1t[:][:, 30016:30016 + 512]
    xbd = A1t[:][:, 30528:30528 + 512]
    wgb = [carve("A2", FC * 128) for _ in range(2)]
    wub = [carve("A2", FC * 128) for _ in range(2)]
    wdb = [carve("A2", DIC * 128) for _ in range(2)]
    ub = carve("A2", 512)
    rd12 = Reader("A2", 512, n=2)
    rd13 = Reader("A2", 512, n=2)
    gusem = [mk_sem(), mk_sem()]
    dsem2 = [mk_sem(), mk_sem()]

    def ffn():
        phase_barrier()
        bank_wait("tensor", "P0a", "P0b", "P1a", "P1b", "P2a", "P2b")
        gu_guard = {}
        d_guard = {}
        tp_guard = {}
        pg_guard = {}
        xb_ten = [None]
        HQH = HQFb[:, 0:DIC * 512]
        HQL = HQFb[:, DIC * 512:2 * DIC * 512]

        def tp_slot(j):
            return PB[2][:][:, 512 * (j % 2):512 * (j % 2) + 128]

        def x_stage(t):
            xq = XQb2[t % 2]
            XQH_ = xq[:, 0:FC * 512]
            XQL_ = xq[:, FC * 512:2 * FC * 512]
            for fc in range(FC):
                ch = rd12.rd(y3own[fc, :, 512 * t:512 * t + 512], "vector")
                c = slice(512 * t, 512 * t + 512)
                if xb_ten[0] is not None:
                    g.wait("vector", TEN, xb_ten[0])
                tt(xb, ch, M4[:, c], ALU.subtract)
                tt(xb, xb, R4[:, c], ALU.mult, sem=VEC)
                vx = VEC.v
                g.wait("tensor", VEC, vx)
                for i in range(4):
                    pv = tp_guard.get(i % 2)
                    if pv is not None:
                        g.wait("tensor", VEC, pv)
                    tv = transpose(tp_slot(i), xb[:, 128 * i:128 * i + 128], sem=TEN)
                    g.wait("vector", TEN, tv)
                    vcopy(XTM[:, 1024 * i + 128 * fc:1024 * i + 128 * fc + 128],
                          tp_slot(i), sem=VEC)
                    tp_guard[i % 2] = VEC.v
                xb_ten[0] = TEN.v
            for i in range(4):
                qv = quantize(XTM[:, 1024 * i:1024 * (i + 1)], 1024,
                              XTM[:, 1024 * i:1024 * (i + 1)], xv=VEC.v)
                g.wait("tensor", VEC, qv)
                for fc in range(FC):
                    j = 4 * i + fc
                    pv = tp_guard.get(j % 2)
                    if pv is not None:
                        g.wait("tensor", VEC, pv)
                    tv = transpose(tp_slot(j),
                                   XTM[:, 1024 * i + 128 * fc:1024 * i + 128 * fc + 128],
                                   sem=TEN)
                    g.wait("vector", TEN, tv)
                    cc = slice(512 * fc + 128 * i, 512 * fc + 128 * i + 128)
                    split_hl(XQH_[:, cc], XQL_[:, cc], tp_slot(j))
                    tp_guard[j % 2] = VEC.v

        def g_stage(t):
            xq = XQb2[t % 2]
            XQH_ = xq[:, 0:FC * 512]
            XQL_ = xq[:, FC * 512:2 * FC * 512]
            for oc in range(DIC):
                bi = oc % 2
                wgv = wgb[bi].bitcast(BF16)
                wuv = wub[bi].bitcast(BF16)
                g.wait("sync", gusem[bi])
                if gu_guard.get(bi) is not None:
                    g.wait("sync", TEN, gu_guard[bi])
                g.dma("sync", wgv[:, 0:FC * 128].rearrange("p (k c) -> p k c", c=128),
                      wgq_s[0][oc], gusem[bi])
                g.dma("sync", wgv[:, FC * 128:2 * FC * 128].rearrange("p (k c) -> p k c", c=128),
                      wgq_s[1][oc], gusem[bi])
                g.dma("sync", wuv[:, 0:FC * 128].rearrange("p (k c) -> p k c", c=128),
                      wuq_s[0][oc], gusem[bi])
                v2 = g.dma("sync", wuv[:, FC * 128:2 * FC * 128].rearrange("p (k c) -> p k c", c=128),
                           wuq_s[1][oc], gusem[bi])
                g.wait("tensor", gusem[bi], v2)
                pg = PB[0][:][:, 512 * bi:512 * bi + 512]
                pu = PB[1][:][:, 512 * bi:512 * bi + 512]
                if oc == 0:
                    g.wait("tensor", VEC)
                    g.wait("tensor", SCA)
                pgv = pg_guard.get(bi)
                if pgv is not None:
                    g.wait("tensor", SCA, pgv[0])
                    g.wait("tensor", VEC, pgv[1])
                for (pp, wv) in ((pg, wgv), (pu, wuv)):
                    i_mm = 0
                    for k in range(FC):
                        wh = wv[:, 128 * k:128 * k + 128]
                        wl = wv[:, FC * 128 + 128 * k:FC * 128 + 128 * (k + 1)]
                        ah = XQH_[:, 512 * k:512 * k + 512]
                        al = XQL_[:, 512 * k:512 * k + 512]
                        for (wop, aop) in ((wh, ah), (wh, al), (wl, ah)):
                            mm(pp, wop, aop, start=(i_mm == 0), stop=(i_mm == 3 * FC - 1),
                               sem=TEN if i_mm == 3 * FC - 1 else None)
                            i_mm += 1
                tv = TEN.v
                gu_guard[bi] = tv
                g.wait("scalar", TEN, tv)
                act(ub, pg, AF.Silu, sem=SCA)
                pg_sca = SCA.v
                g.wait("vector", SCA, pg_sca)
                tt(ub, ub, pu, ALU.mult, sem=VEC)
                vv = VEC.v
                pg_guard[bi] = (pg_sca, vv)
                g.wait("tensor", VEC, vv)
                ht = htmp[oc % 2]
                for i in range(4):
                    pv = tp_guard.get(i % 2)
                    if pv is not None:
                        g.wait("tensor", VEC, pv)
                    tv2 = transpose(tp_slot(i), ub[:, 128 * i:128 * i + 128], sem=TEN)
                    g.wait("vector", TEN, tv2)
                    vcopy(ht[:, 128 * i:128 * i + 128], tp_slot(i), sem=VEC)
                    tp_guard[i % 2] = VEC.v
                qv = quantize(ht, 512, ht, xv=VEC.v)
                g.wait("tensor", VEC, qv)
                for i in range(4):
                    pv = tp_guard.get(i % 2)
                    if pv is not None:
                        g.wait("tensor", VEC, pv)
                    tv3 = transpose(tp_slot(i), ht[:, 128 * i:128 * i + 128], sem=TEN)
                    g.wait("vector", TEN, tv3)
                    cc = slice(512 * oc + 128 * i, 512 * oc + 128 * i + 128)
                    split_hl(HQH[:, cc], HQL[:, cc], tp_slot(i))
                    tp_guard[i % 2] = VEC.v
            return (SCA.v, VEC.v)

        def d_stage(t, g_end):
            for oc in range(FC):
                bi = oc % 2
                wdv = wdb[bi].bitcast(BF16)
                g.wait("sync", dsem2[bi])
                if d_guard.get(bi) is not None:
                    g.wait("sync", TEN, d_guard[bi])
                g.dma("sync", wdv[:, 0:DIC * 128].rearrange("p (k c) -> p k c", c=128),
                      wdq_s[0][oc], dsem2[bi])
                v1 = g.dma("sync", wdv[:, DIC * 128:2 * DIC * 128].rearrange("p (k c) -> p k c", c=128),
                           wdq_s[1][oc], dsem2[bi])
                pv = PB[0][:][:, 512 * (oc % 2):512 * (oc % 2) + 512]
                g.wait("tensor", dsem2[bi], v1)
                if oc == 0:
                    g.wait("tensor", SCA, g_end[0])
                    g.wait("tensor", VEC, g_end[1])
                pgv = pg_guard.get(2 + oc % 2)
                if pgv is not None:
                    g.wait("tensor", VEC, pgv)
                i_mm = 0
                for k in range(DIC):
                    wh = wdv[:, 128 * k:128 * k + 128]
                    wl = wdv[:, DIC * 128 + 128 * k:DIC * 128 + 128 * (k + 1)]
                    ah = HQH[:, 512 * k:512 * k + 512]
                    al = HQL[:, 512 * k:512 * k + 512]
                    for (wop, aop) in ((wh, ah), (wh, al), (wl, ah)):
                        mm(pv, wop, aop, start=(i_mm == 0), stop=(i_mm == 3 * DIC - 1),
                           sem=TEN if i_mm == 3 * DIC - 1 else None)
                        i_mm += 1
                tv = TEN.v
                d_guard[bi] = tv
                ch = rd13.rd(y3own[oc, :, 512 * t:512 * t + 512], "vector")
                g.wait("vector", TEN, tv)
                g.wait("vector", STQ)
                tt(xbd, pv, ch, ALU.add, sem=VEC)
                pg_guard[2 + oc % 2] = VEC.v
                g.wait("gpsimd", VEC)
                g.dma("gpsimd", y_out[oc, :, 512 * t:512 * t + 512], xbd, STQ)

        x_stage(0)
        for t in range(NTO):
            g_end = g_stage(t)
            if t + 1 < NTO:
                x_stage(t + 1)
            d_stage(t, g_end)

    ffn()

    return finish(nc, g, ctx)


def finish(nc, g, ctx):
    g.wait("sync", g.fin_STQ)
    g.wait("sync", g.fin_LDQ)
    with ctx:
        blk = ctx.enter_context(nc.Block())
        for eng in ENGS:
            def make_body(eng):
                def body(e):
                    for fn in g.ops[eng]:
                        fn(e)
                return body
            getattr(blk, eng)(make_body(eng))
    return nc



def pack_vecs(ins, selA, selB):
    out = np.zeros((128, 128), dtype=np.float32)
    col = 0
    for key, n in (("L0_attn_bp", FC), ("L1_attn_bqkv", 3 * FC), ("L1_attn_ln1_g", FC),
                   ("L1_attn_ln1_b", FC), ("L1_attn_bo", FC), ("L1_attn_ln2_g", FC),
                   ("L1_attn_ln2_b", FC), ("L1_attn_b1", DMC), ("L1_attn_b2", FC)):
        arr = np.asarray(ins[key], dtype=np.float32)
        out[:, col:col + n] = arr.reshape(n, 128).T
        col += n
    out[:, col] = selA
    out[:, col + 1] = selB
    return out


def kernel(**inputs):
    ins = {k: np.asarray(v) for k, v in inputs.items()}
    x = np.asarray(ins["L0_attn_x"], dtype=np.float32)

    C2, S2 = build_rope_tables()
    tri = build_tri()
    identity = np.ascontiguousarray(np.eye(128, dtype=np.float32))
    maskA = build_masks(OWN_A)
    maskB = build_masks(OWN_B)

    wq_s = swizzle_strips(ins["L0_attn_Wq"])
    wk_s = swizzle_strips(ins["L0_attn_Wk"])
    wv_s = swizzle_strips(ins["L0_attn_Wv"])
    wp_s = swizzle_strips(ins["L0_attn_Wp"])
    w0t_s = swizzle_strips(np.ascontiguousarray(np.asarray(ins["L0_ffn_weight"]).T))
    wqkv_s = swizzle_strips(ins["L1_attn_Wqkv"])
    wo_s = swizzle_strips(ins["L1_attn_Wo"])
    w1_s = swizzle_strips(ins["L1_attn_W1"])
    w2_s = swizzle_strips(ins["L1_attn_W2"])
    wg_nat = np.ascontiguousarray(
        np.asarray(ins["L1_ffn_w_gate"], dtype=np.float32).reshape(DIC, 128, D))
    wu_nat = np.ascontiguousarray(
        np.asarray(ins["L1_ffn_w_up"], dtype=np.float32).reshape(DIC, 128, D))
    wd_nat = np.ascontiguousarray(
        np.asarray(ins["L1_ffn_w_down"], dtype=np.float32).reshape(FC, 128, DI))

    nc = build_program()

    in_maps = []
    for core in range(8):
        b, s = core // 2, core % 2
        own = OWN_A if s == 0 else OWN_B
        own_tok = np.concatenate([np.arange(256 * a, 256 * a + 256) for a in own])
        xb = np.ascontiguousarray(x[b].T)
        im = dict(
            x_fm=np.ascontiguousarray(xb.reshape(FC, 128, T)),
            wq_s=wq_s, wk_s=wk_s, wv_s=wv_s, wp_s=wp_s, w0t_s=w0t_s,
            wqkv_s=wqkv_s, wo_s=wo_s, w1_s=w1_s, w2_s=w2_s,
            wg_nat=wg_nat, wu_nat=wu_nat, wd_nat=wd_nat,
            vecs=pack_vecs(ins, 1.0 if s == 0 else 0.0, 0.0 if s == 0 else 1.0),
            c2t=C2, s2t=S2,
            c2o=np.ascontiguousarray(C2[:, own_tok]),
            s2o=np.ascontiguousarray(S2[:, own_tok]),
            iden=identity, mtri=tri,
            mask8=maskA if s == 0 else maskB,
        )
        in_maps.append(im)

    res = run_bass_kernel_spmd(nc, in_maps, core_ids=list(range(8)))

    out = np.empty((B, T, D), dtype=np.float32)
    for core in range(8):
        b, s = core // 2, core % 2
        own = OWN_A if s == 0 else OWN_B
        y = np.asarray(res.results[core]["y_out"]).reshape(D, TO).T
        for j, a in enumerate(own):
            out[b, 256 * a:256 * a + 256, :] = y[256 * j:256 * j + 256, :]
    return out


# version 10
# speedup vs baseline: 1.0328x; 1.0328x over previous
import numpy as np
from contextlib import ExitStack

import concourse.bass as bass
import concourse.mybir as mybir
from concourse.bass_utils import run_bass_kernel_spmd

F32 = mybir.dt.float32
BF16 = mybir.dt.bfloat16
I8 = mybir.dt.int8
AF = mybir.ActivationFunctionType
ALU = mybir.AluOpType
AX = mybir.AxisListType
f32 = np.float32

B, T, D, H, HD = 4, 4096, 1024, 16, 64
NPAIR = 8
TO = T // 2
OWN_A = [0, 3, 4, 7, 8, 11, 12, 15]
OWN_B = [1, 2, 5, 6, 9, 10, 13, 14]
NSLOT = 8
N_MSK = 32
FC = D // 128
NT = T // 512
NTO = TO // 512
EPS = 1e-5
DI = 2816
DIC = DI // 128
DM = 4096
DMC = DM // 128
C6_1 = float(f32(1.0 / 6.0))
C6_2 = float(f32(1.0 / 6.0 - float(f32(1.0 / 6.0))))

ENGS = ("sync", "tensor", "vector", "scalar", "gpsimd")
DBG = {}


class Sem:
    __slots__ = ("name", "h", "v")

    def __init__(self, name):
        self.name, self.h, self.v = name, None, 0


class Gen:
    def __init__(self):
        self.ops = {e: [] for e in ENGS}

    def emit(self, eng, fn):
        self.ops[eng].append(fn)

    def wait(self, eng, sem, val=None):
        v = sem.v if val is None else val
        assert v <= sem.v, (sem.name, v, sem.v)
        if v > 0:
            self.emit(eng, lambda e, h=sem, vv=v: e.wait_ge(h.h, vv))

    def op(self, eng, build, sem=None, inc=1):
        own = getattr(self, "own_sems", {}).get(eng)
        if own is not None:
            self.wait(eng, own)
            sem, inc = own, 1
        if sem is not None:
            sem.v += inc

        def fn(e, build=build, sem=sem, inc=inc):
            r = build(e)
            if sem is not None:
                r.then_inc(sem.h, inc)
        self.emit(eng, fn)
        return None if sem is None else sem.v

    def dma(self, eng, out, in_, sem):
        sem.v += 16
        self.emit(eng, lambda e, o=out, i=in_, h=sem: e.dma_start(out=o, in_=i).then_inc(h.h, 16))
        return sem.v



def swizzle_strips(W):
    K, M = W.shape
    return np.ascontiguousarray(
        W.reshape(K // 128, 128, M // 128, 128).transpose(2, 1, 0, 3)).astype(np.float32)


def build_rope_tables():
    inv_freq = 1.0 / (10000.0 ** (np.arange(0, HD, 2, dtype=np.float64) / HD))
    t = np.arange(T, dtype=np.float64)
    fr = t[:, None] * inv_freq[None, :]
    cos = np.cos(fr).T
    sin = np.sin(fr).T
    C2 = np.concatenate([cos, cos, cos, cos], axis=0).astype(np.float32)
    S2 = np.concatenate([-sin, sin, -sin, sin], axis=0).astype(np.float32)
    return np.ascontiguousarray(C2), np.ascontiguousarray(S2)


def build_masks(own):
    out = np.zeros((N_MSK, 128, 512), dtype=np.int8)
    for si in range(4):
        qg = np.concatenate([256 * own[2 * si] + np.arange(256),
                             256 * own[2 * si + 1] + np.arange(256)])[None, :]
        for d2 in range(8):
            k = 128 * (8 * si + d2) + np.arange(128)[:, None]
            out[8 * si + d2] = (k > qg).astype(np.int8)
    return np.ascontiguousarray(out.transpose(1, 0, 2).reshape(128, N_MSK * 512))


def build_tri():
    k = np.arange(128)[:, None]
    q = np.arange(128)[None, :]
    return np.ascontiguousarray(np.where(k <= q, 0.0, -1e30).astype(np.float32))



def build_program(stop_phase=99):
    nc = bass.Bass("TRN2", target_bir_lowering=False)
    g = Gen()
    ctx = ExitStack()

    din = {}

    def inp(name, shape, dtype=F32):
        din[name] = nc.dram_tensor(name, list(shape), dtype, kind="ExternalInput").ap()
        return din[name]

    x_fm = inp("x_fm", [FC, 128, T])
    wq_s = inp("wq_s", [FC, 128, FC, 128])
    wk_s = inp("wk_s", [FC, 128, FC, 128])
    wv_s = inp("wv_s", [FC, 128, FC, 128])
    wp_s = inp("wp_s", [FC, 128, FC, 128])
    w0t_s = inp("w0t_s", [FC, 128, FC, 128])
    wqkv_s = inp("wqkv_s", [3 * FC, 128, FC, 128])
    wo_s = inp("wo_s", [FC, 128, FC, 128])
    w1_s = inp("w1_s", [DMC, 128, FC, 128])
    w2_s = inp("w2_s", [FC, 128, DMC, 128])
    wg_nat = inp("wg_nat", [DIC, 128, D])
    wu_nat = inp("wu_nat", [DIC, 128, D])
    wd_nat = inp("wd_nat", [FC, 128, DI])
    vecs = inp("vecs", [128, 128])
    c2t = inp("c2t", [128, T])
    s2t = inp("s2t", [128, T])
    c2o = inp("c2o", [128, TO])
    s2o = inp("s2o", [128, TO])
    iden = inp("iden", [128, 128])
    mtri = inp("mtri", [128, 128])
    mask8 = inp("mask8", [128, N_MSK * 512], I8)

    y_out = nc.dram_tensor("y_out", [FC, 128, TO], F32, kind="ExternalOutput").ap()

    def scratch(name, shape, dtype=F32):
        return nc.dram_tensor(name, list(shape), dtype, kind="Internal").ap()

    qst = [scratch("qst_h", [NPAIR, 128, T], BF16), scratch("qst_l", [NPAIR, 128, T], BF16)]
    kst = [scratch("kst_h", [NPAIR, 128, T], BF16), scratch("kst_l", [NPAIR, 128, T], BF16)]
    vst = scratch("vst", [NPAIR, T, 128])
    ast = [scratch("ast_h", [NPAIR, 128, T], BF16), scratch("ast_l", [NPAIR, 128, T], BF16)]
    qst1 = [scratch("qst1_h", [NPAIR, 128, TO], BF16), scratch("qst1_l", [NPAIR, 128, TO], BF16)]
    ast1 = [scratch("ast1_h", [NPAIR, 128, TO], BF16), scratch("ast1_l", [NPAIR, 128, TO], BF16)]
    y1st = scratch("y1st", [FC, 128, T])
    y2st = scratch("y2st", [FC, 128, T])
    y2own = scratch("y2own", [FC, 128, TO])
    y3own = scratch("y3own", [FC, 128, TO])
    n1st = scratch("n1st", [FC, 128, T])
    z2dbg = scratch("z2dbg", [FC, 128, TO])
    w1hl_s = scratch("w1hl_s", [DMC, 128, 2 * FC * 128], BF16)
    w2hl_s = scratch("w2hl_s", [FC, 128, 2 * DMC * 128], BF16)
    wgq_s = [scratch("wgq_sh", [DIC, 128, FC, 128], BF16),
             scratch("wgq_sl", [DIC, 128, FC, 128], BF16)]
    wuq_s = [scratch("wuq_sh", [DIC, 128, FC, 128], BF16),
             scratch("wuq_sl", [DIC, 128, FC, 128], BF16)]
    wdq_s = [scratch("wdq_sh", [FC, 128, DIC, 128], BF16),
             scratch("wdq_sl", [FC, 128, DIC, 128], BF16)]

    A1t = ctx.enter_context(nc.sbuf_tensor("A1", [128, 32768], F32))
    A2t = ctx.enter_context(nc.sbuf_tensor("A2", [128, 16384], F32))
    sb_vecs = ctx.enter_context(nc.sbuf_tensor("sb_vecs", [128, 128], F32))
    sb_iden = ctx.enter_context(nc.sbuf_tensor("sb_iden", [128, 128], F32))
    sb_tri = ctx.enter_context(nc.sbuf_tensor("sb_tri", [128, 128], F32))
    sb_ones64 = ctx.enter_context(nc.sbuf_tensor("sb_ones64", [128, 64], F32))
    sb_ones64b_t = ctx.enter_context(nc.sbuf_tensor("sb_ones64b", [128, 64], BF16))
    sb_ones1 = ctx.enter_context(nc.sbuf_tensor("sb_ones1", [128, 1], F32))
    sb_onesk = ctx.enter_context(nc.sbuf_tensor("sb_onesk", [1, 128], F32))
    sb_rows = ctx.enter_context(nc.sbuf_tensor("sb_rows", [1, 3072], F32))
    PB = [ctx.enter_context(nc.psum_tensor(f"PB{i}", [128, 1024], F32)) for i in range(4)]

    sems = {}
    for name in ("TEN", "VEC", "SCA", "LDQ", "STQ", "GPC"):
        s = Sem(name)
        s.h = ctx.enter_context(nc.semaphore(name))
        sems[name] = s
    TEN, VEC, SCA, LDQ, STQ, GPC = (sems[n] for n in ("TEN", "VEC", "SCA", "LDQ", "STQ", "GPC"))
    CSEMS = (TEN, VEC, SCA)
    g.fin_STQ = STQ
    g.fin_LDQ = LDQ
    g.own_sems = {"vector": VEC, "scalar": SCA}
    _semctr = [0]

    def mk_sem():
        s = Sem(f"S{_semctr[0]}")
        _semctr[0] += 1
        s.h = ctx.enter_context(nc.semaphore(s.name))
        g.all_slot_sems.append(s)
        return s
    g.all_slot_sems = []

    bankrel = {}

    def bank_wait(eng, *keys):
        for k in keys:
            for s, v in bankrel.get(k, ()):
                g.wait(eng, s, v)

    def bank_rel(key, *svs):
        bankrel[key] = list(svs) if svs else [(VEC, VEC.v)]

    def phase_barrier():
        g.wait("tensor", VEC)
        g.wait("tensor", SCA)
        g.wait("tensor", STQ)
        g.wait("vector", TEN)
        g.wait("vector", SCA)
        g.wait("vector", STQ)
        g.wait("scalar", TEN)
        g.wait("scalar", VEC)
        g.wait("scalar", STQ)
        g.wait("sync", TEN)
        g.wait("sync", VEC)
        g.wait("sync", SCA)
        g.wait("sync", STQ)
        g.wait("sync", LDQ)
        for s_ in g.all_slot_sems:
            g.wait("sync", s_)
        g.wait("gpsimd", VEC)
        g.wait("gpsimd", SCA)

    arena_off = {"A1": 0, "A2": 0}

    def carve(which, cols):
        o = arena_off[which]
        lim = 32768 if which == "A1" else 16384
        assert o + cols <= lim, (which, o, cols)
        arena_off[which] = o + cols
        t_ = A1t if which == "A1" else A2t
        return t_[:][:, o:o + cols]

    def reset(which):
        arena_off[which] = 0

    VC = {}
    off = 0
    for nm, n in (("bp", FC), ("bqkv", 3 * FC), ("g1", FC), ("b1", FC), ("bo", FC),
                  ("g2", FC), ("b2", FC), ("b1m", DMC), ("b2m", FC), ("selA", 1), ("selB", 1)):
        VC[nm] = off
        off += n
    assert off <= 128

    def vcol(nm, i=0):
        return sb_vecs[:][:, VC[nm] + i:VC[nm] + i + 1]

    def tt(out, in0, in1, op, sem=None, eng="vector"):
        return g.op(eng, lambda e, o=out, a=in0, b=in1, p=op:
                    e.tensor_tensor(out=o, in0=a, in1=b, op=p), sem)

    def ts(out, in0, s1, op0, s2=None, op1=None, sem=None):
        def bld(e, o=out, a=in0, s1=s1, s2=s2, op0=op0, op1=op1):
            kw = dict(out=o, in0=a, scalar1=s1, scalar2=s2, op0=op0)
            if op1 is not None:
                kw["op1"] = op1
            return e.tensor_scalar(**kw)
        return g.op("vector", bld, sem)

    def stt(out, in0, scalar, in1, op0, op1, sem=None):
        return g.op("vector", lambda e, o=out, a=in0, s=scalar, b=in1, p0=op0, p1=op1:
                    e.scalar_tensor_tensor(out=o, in0=a, scalar=s, in1=b, op0=p0, op1=p1), sem)

    def act(out, in_, func, scale=None, bias=None, sem=None):
        def bld(e, o=out, i=in_, f=func, sc=scale, bi=bias):
            kw = dict(out=o, in_=i, func=f)
            if sc is not None:
                kw["scale"] = sc
            if bi is not None:
                kw["bias"] = bi
            return e.activation(**kw)
        return g.op("scalar", bld, sem)

    def vcopy(out, in_, sem=None):
        return g.op("vector", lambda e, o=out, i=in_: e.tensor_copy(out=o, in_=i), sem)

    def mm(out, lhsT, rhs, start, stop, tile_position=None, sem=None):
        def bld(e, o=out, l=lhsT, r=rhs, st=start, sp=stop, tp=tile_position):
            kw = dict(out=o, lhsT=l, rhs=r, start=st, stop=sp)
            if tp is not None:
                kw["tile_position"] = tp
            return e.matmul(**kw)
        return g.op("tensor", bld, sem)

    def split_hl(hi, lo, src):
        vcopy(hi, src)
        tt(lo, src, hi, ALU.subtract)

    def transpose(out, in_, sem=None):
        return g.op("tensor", lambda e, o=out, i=in_:
                    e.transpose(out=o, in_=i, identity=sb_iden[:]), sem)

    def waitall(eng):
        for s in CSEMS:
            g.wait(eng, s)

    class Reader:
        def __init__(self, which, cols, n=3):
            self.bufs = [carve(which, cols) for _ in range(n)]
            self.sems = [mk_sem() for _ in range(n)]
            self.n = n
            self.i = 0
            self.snaps = []

        def rd(self, src, *consumers):
            i = self.i
            self.snaps.append(tuple((s, s.v) for s in CSEMS))
            if i >= self.n:
                for s, v in self.snaps[i - self.n + 1]:
                    g.wait("sync", s, v)
            b = self.bufs[i % self.n]
            out = b
            if src.dtype == BF16:
                out = out.bitcast(BF16)
            if len(src.shape) == 3:
                out = out.rearrange("p (k c) -> p k c", c=src.shape[2])
                src = src.rearrange("k p c -> p k c")
            v = g.dma("sync", out, src, self.sems[i % self.n])
            for c in consumers:
                g.wait(c, self.sems[i % self.n], v)
            self.i = i + 1
            return b

    qsl_state = [None]
    QSCR = {}

    def quantize_slice(X, Fcols, out, xv=None):
        nb = Fcols // 16
        assert Fcols <= 1024
        a = QSCR["A"][:, 0:Fcols]
        s1 = QSCR["S1"][:, 0:Fcols]
        s2 = QSCR["S2"][:, 0:Fcols]
        gsn = QSCR["G"][:, 0:Fcols]
        m = QSCR["M"][:, 0:nb]
        ss = QSCR["SS"][:, 0:nb]
        r = QSCR["R"][:, 0:nb]
        if qsl_state[0] is None:
            g.wait("scalar", VEC)
        else:
            g.wait("scalar", VEC, qsl_state[0])
        if xv is not None:
            g.wait("scalar", VEC, xv)
        act(a, X, AF.Abs)
        act(gsn, X, AF.Sign, sem=SCA)
        g.wait("vector", SCA)
        g.op("vector", lambda e, o=m, i=a: e.tensor_reduce(
            out=o, in_=i.rearrange("p (b k) -> p b k", k=16), axis=AX.X, op=ALU.max))
        ts(ss, m, C6_1, ALU.mult)
        stt(ss, m, C6_2, ss, ALU.mult, ALU.add)
        ts(ss, ss, float(f32(1e-12)), ALU.max, sem=VEC)
        g.wait("scalar", VEC)
        act(r, ss, AF.Ln)
        act(r, r, AF.Exp, scale=f32(-1.0), sem=SCA)
        g.wait("vector", SCA)
        rbc = r.unsqueeze(-1).broadcast_to((128, nb, 16))
        sbc = ss.unsqueeze(-1).broadcast_to((128, nb, 16))
        a3d = a.rearrange("p (b k) -> p b k", k=16)
        tt(a3d, a3d, rbc, ALU.mult)
        ts(s1, a, 0.25, ALU.is_gt)
        stt(s1, a, 0.75, s1, ALU.is_gt, ALU.add)
        stt(s1, a, 1.25, s1, ALU.is_gt, ALU.add)
        stt(s1, a, 1.75, s1, ALU.is_gt, ALU.add)
        ts(s2, a, 2.5, ALU.is_gt)
        stt(s2, a, 3.5, s2, ALU.is_gt, ALU.add)
        stt(s2, a, 5.0, s2, ALU.is_gt, ALU.add)
        stt(s2, a, 5.0, s2, ALU.is_gt, ALU.add)
        stt(s2, s1, 0.5, s2, ALU.mult, ALU.add)
        tt(s2.rearrange("p (b k) -> p b k", k=16),
           s2.rearrange("p (b k) -> p b k", k=16), sbc, ALU.mult)
        tt(out, s2, gsn, ALU.mult, sem=VEC)
        qsl_state[0] = VEC.v
        return VEC.v

    def quantize(X, Fcols, out, xv=None):
        v = None
        o = 0
        while o < Fcols:
            w = min(1024, Fcols - o)
            v = quantize_slice(X[:, o:o + w], w, out[:, o:o + w], xv=xv)
            o += w
        return v

    qw_steps = []

    def tick(n=1):
        for _ in range(n):
            if qw_steps:
                qw_steps.pop(0)()

    g.dma("sync", sb_vecs[:], vecs, LDQ)
    g.dma("sync", sb_iden[:], iden, LDQ)
    g.dma("sync", sb_tri[:], mtri, LDQ)
    v0 = LDQ.v
    g.op("gpsimd", lambda e: e.memset(sb_ones64[:], 1.0), GPC)
    g.op("gpsimd", lambda e: e.memset(sb_ones64b_t[:], 1.0), GPC)
    g.op("gpsimd", lambda e: e.memset(sb_ones1[:], 1.0), GPC)
    g.op("gpsimd", lambda e: e.memset(sb_onesk[:], 1.0), GPC)
    for eng in ("tensor", "vector", "scalar"):
        g.wait(eng, LDQ, v0)
        g.wait(eng, GPC)

    def ln_stats(read_chunk, ntt, dst_M, dst_R, sq_buf, barrier=True):
        if barrier:
            phase_barrier()
        bank_wait("tensor", "P2a", "P3a")
        for t in range(ntt):
            for fc in range(FC):
                ch = read_chunk(fc, t)
                sq = sq_buf[:, 512 * (fc % 2):512 * (fc % 2) + 512]
                if fc >= 2:
                    g.wait("scalar", TEN)
                sv = act(sq, ch, AF.Square, sem=SCA)
                mm(PB[2][:][0:1, 0:512], sb_ones1[:], ch, start=(fc == 0),
                   stop=(fc == FC - 1), sem=TEN)
                g.wait("tensor", SCA, sv)
                mm(PB[3][:][0:1, 0:512], sb_ones1[:], sq, start=(fc == 0),
                   stop=(fc == FC - 1), sem=TEN)
            vmm = TEN.v
            g.wait("vector", TEN, vmm)
            r_sum = sb_rows[:][0:1, 0:512]
            r_sq = sb_rows[:][0:1, 512:1024]
            r_mean = sb_rows[:][0:1, 1024:1536]
            r_var = sb_rows[:][0:1, 1536:2048]
            r_rstd = sb_rows[:][0:1, 2048:2560]
            vcopy(r_sum, PB[2][:][0:1, 0:512])
            vcopy(r_sq, PB[3][:][0:1, 0:512])
            ts(r_mean, r_sum, 1.0 / D, ALU.mult)
            tt(r_var, r_mean, r_mean, ALU.mult)
            stt(r_var, r_sq, 1.0 / D, r_var, ALU.mult, ALU.subtract, sem=VEC)
            vv = VEC.v
            g.wait("scalar", VEC, vv)
            act(r_rstd, r_var, AF.Ln, scale=f32(1.0), bias=f32(EPS))
            act(r_rstd, r_rstd, AF.Exp, scale=f32(-0.5), sem=SCA)
            vs = SCA.v
            g.wait("tensor", SCA, vs)
            g.wait("tensor", VEC, vv)
            mm(PB[2][:][:, 0:512], sb_onesk[:], r_mean, start=True, stop=True)
            mm(PB[3][:][:, 0:512], sb_onesk[:], r_rstd, start=True, stop=True, sem=TEN)
            vb = TEN.v
            g.wait("vector", TEN, vb)
            vcopy(dst_M[:, 512 * t:512 * t + 512], PB[2][:][:, 0:512])
            vcopy(dst_R[:, 512 * t:512 * t + 512], PB[3][:][:, 0:512], sem=VEC)
            g.wait("tensor", VEC)
            tick()
        bank_rel("P2a")
        bank_rel("P3a")

    def res_gemm(wsrc, n_k, rhs_src, res_src, bias_name, dst_st, ntt):
        phase_barrier()
        bank_wait("tensor", "P0a", "P1a")
        reset("A2")
        whl = carve("A2", FC * n_k * 128)
        wtmp = carve("A2", n_k * 128)
        wallsem = mk_sem()
        rhs_rh = Reader("A2", 256 * n_k, n=1)
        rhs_rl = Reader("A2", 256 * n_k, n=1)
        res_r = Reader("A2", 512, n=3)
        outb = [carve("A2", 512) for _ in range(2)]
        for oc in range(FC):
            g.wait("sync", VEC)
            v = g.dma("sync", wtmp.rearrange("p (k c) -> p k c", c=128),
                      wsrc[oc], wallsem)
            g.wait("vector", wallsem, v)
            wv = whl[:, n_k * 128 * oc:n_k * 128 * (oc + 1)].bitcast(BF16)
            split_hl(wv[:, 0:n_k * 128], wv[:, n_k * 128:2 * n_k * 128], wtmp)
        g.wait("tensor", VEC)
        idx = 0
        for t in range(ntt):
            rbh = rhs_rh.rd(rhs_src[0][:, :, 512 * t:512 * t + 512], "tensor").bitcast(BF16)
            rbl = rhs_rl.rd(rhs_src[1][:, :, 512 * t:512 * t + 512], "tensor").bitcast(BF16)
            for oc in range(FC):
                key = f"P{idx % 2}a"
                pv = PB[idx % 2][:][:, 0:512]
                bank_wait("tensor", key)
                wv = whl[:, n_k * 128 * oc:n_k * 128 * (oc + 1)].bitcast(BF16)
                n_mm = 3 * n_k
                i_mm = 0
                for k in range(n_k):
                    wh = wv[:, 128 * k:128 * k + 128]
                    wl = wv[:, n_k * 128 + 128 * k:n_k * 128 + 128 * (k + 1)]
                    ah = rbh[:, 512 * k:512 * k + 512]
                    al = rbl[:, 512 * k:512 * k + 512]
                    for (wop, aop) in ((wh, ah), (wh, al), (wl, ah)):
                        mm(pv, wop, aop, start=(i_mm == 0), stop=(i_mm == n_mm - 1),
                           sem=TEN if i_mm == n_mm - 1 else None)
                        i_mm += 1
                tv = TEN.v
                rb = res_r.rd(res_src[oc, :, 512 * t:512 * t + 512], "vector")
                ob = outb[idx % 2]
                if idx >= 2:
                    g.wait("vector", STQ)
                g.wait("vector", TEN, tv)
                if bias_name is not None:
                    ts(ob, pv, vcol(bias_name, oc), ALU.add)
                    bank_rel(key)
                    tt(ob, ob, rb, ALU.add)
                else:
                    tt(ob, pv, rb, ALU.add)
                    bank_rel(key)
                ev = VEC.v
                g.wait("gpsimd", VEC, ev)
                g.dma("gpsimd", dst_st[oc, :, 512 * t:512 * t + 512], ob, STQ)
                idx += 1

    reset("A1")
    reset("A2")
    XNh = carve("A1", FC * T // 2).bitcast(BF16)
    XNl = carve("A1", FC * T // 2).bitcast(BF16)
    M0 = carve("A2", T)
    R0 = carve("A2", T)
    sqb = carve("A2", 1024)
    rd1 = Reader("A2", 512, n=3)
    ln_stats(lambda fc, t_: rd1.rd(x_fm[fc, :, 512 * t_:512 * t_ + 512], "tensor", "scalar"),
             NT, M0, R0, sqb)

    rd2 = Reader("A2", 512, n=3)
    xtmp = carve("A2", 512)
    for t in range(NT):
        for fc in range(FC):
            ch = rd2.rd(x_fm[fc, :, 512 * t:512 * t + 512], "vector")
            c = slice(fc * T + 512 * t, fc * T + 512 * t + 512)
            tt(xtmp, ch, M0[:, 512 * t:512 * t + 512], ALU.subtract)
            tt(xtmp, xtmp, R0[:, 512 * t:512 * t + 512], ALU.mult)
            split_hl(XNh[:, c], XNl[:, c], xtmp)

    def xn_rhs(k, t):
        c = slice(k * T + 512 * t, k * T + 512 * t + 512)
        return (XNh[:, c], XNl[:, c])

    if stop_phase <= 1:
        return finish(nc, g, ctx)
    g.wait("sync", VEC)
    reset("A2")
    def stage_gemm(wsrc, n_oc, rhs_fn, ntt, evac_fn, wb=None, wsem=None, wtmp=None):
        if wb is None:
            wb = [carve("A2", FC * 128) for _ in range(2)]
        if wtmp is None:
            wtmp = carve("A2", FC * 128)
        if wsem is None:
            wsem = [mk_sem(), mk_sem()]
        phase_barrier()
        wsnap = []
        idx = 0
        for oc in range(n_oc):
            wsnap.append(TEN.v)
            if oc >= 2:
                g.wait("sync", TEN, wsnap[oc - 1])
            g.wait("sync", VEC)
            vldw = g.dma("sync", wtmp.rearrange("p (k c) -> p k c", c=128),
                         wsrc[oc], wsem[oc % 2])
            whl = wb[oc % 2].bitcast(BF16)
            whi = whl[:, 0:FC * 128]
            wlo = whl[:, FC * 128:2 * FC * 128]
            g.wait("vector", wsem[oc % 2], vldw)
            split_hl(whi, wlo, wtmp)
            vsp = VEC.v
            for t in range(ntt):
                key = f"P{idx % 2}a"
                pv = PB[idx % 2][:][:, 0:512]
                g.wait("tensor", VEC, vsp)
                bank_wait("tensor", key)
                n_mm = 3 * FC
                i_mm = 0
                for k in range(FC):
                    rh, rl = rhs_fn(k, t)
                    for (wop, aop) in ((whi, rh), (whi, rl), (wlo, rh)):
                        mm(pv, wop[:, 128 * k:128 * k + 128], aop,
                           start=(i_mm == 0), stop=(i_mm == n_mm - 1),
                           sem=TEN if i_mm == n_mm - 1 else None)
                        i_mm += 1
                tv = TEN.v
                g.wait("vector", TEN, tv)
                bank_rel(key, (VEC, evac_fn(oc, t, pv)))
                idx += 1

    def mk_copy_evac(dst_st):
        bufs = [carve("A2", 512) for _ in range(2)]
        state = {"i": 0}

        def evac(oc, t, pv):
            i = state["i"]
            b = bufs[i % 2].bitcast(BF16)
            if i >= 2:
                g.wait("vector", STQ)
            split_hl(b[:, 0:512], b[:, 512:1024], pv)
            ev = VEC.v
            g.wait("gpsimd", VEC, ev)
            g.dma("gpsimd", dst_st[0][oc, :, 512 * t:512 * t + 512], b[:, 0:512], STQ)
            g.dma("gpsimd", dst_st[1][oc, :, 512 * t:512 * t + 512], b[:, 512:1024], STQ)
            state["i"] = i + 1
            return ev
        return evac

    def mk_v_evac(dst_st, bias_name=None, bias_base=0):
        vb = carve("A2", 512)
        tb = carve("A2", 128)

        def evac(oc, t, pv):
            waitall("vector")
            if bias_name is None:
                vcopy(vb, pv, sem=VEC)
            else:
                ts(vb, pv, vcol(bias_name, bias_base + oc), ALU.add, sem=VEC)
            ev0 = VEC.v
            for i in range(4):
                g.wait("tensor", VEC)
                tv = transpose(PB[2][:][:, 0:128], vb[:, 128 * i:128 * i + 128], sem=TEN)
                g.wait("vector", TEN, tv)
                g.wait("vector", STQ)
                vcopy(tb, PB[2][:][:, 0:128], sem=VEC)
                g.wait("gpsimd", VEC)
                g.dma("gpsimd", dst_st[oc, 512 * t + 128 * i:512 * t + 128 * (i + 1), :],
                      tb, STQ)
            return ev0
        return evac

    wbA = [carve("A2", FC * 128) for _ in range(2)]
    wtmpA = carve("A2", FC * 128)
    wsemA = [mk_sem(), mk_sem()]
    stage_gemm(wq_s, FC, xn_rhs, NT, mk_copy_evac(qst), wb=wbA, wsem=wsemA, wtmp=wtmpA)
    stage_gemm(wk_s, FC, xn_rhs, NT, mk_copy_evac(kst), wb=wbA, wsem=wsemA, wtmp=wtmpA)
    stage_gemm(wv_s, FC, xn_rhs, NT, mk_v_evac(vst), wb=wbA, wsem=wsemA, wtmp=wtmpA)

    def att_slots(mode):
        slots = []
        if mode == "L0":
            for j in range(NSLOT):
                kts = [(kt, 0, None, None) for kt in range(4 * j)]
                kts += [(4 * j + d2, 128 * d2, 128 * d2, None) for d2 in range(4)]
                slots.append((512 * j, kts))
        else:
            for si in range(4):
                kts = [(kt, 0, None, None) for kt in range(8 * si)]
                kts += [(8 * si + d2, 0, None, 8 * si + d2) for d2 in range(8)]
                slots.append((512 * si, kts))
        return slots

    def attention(qsrc, ksrc, vsrc, astdst, mode):
        phase_barrier()
        bank_wait("tensor", "P0a", "P0b", "P1a", "P1b", "P2a", "P2b", "P3a", "P3b")
        reset("A1")
        reset("A2")
        QCOLS = T if mode == "L0" else TO
        slots = att_slots(mode)
        nslot = len(slots)
        cw = T // nslot
        Kbh = [carve("A1", T // 2).bitcast(BF16) for _ in range(2)]
        Kbl = [carve("A1", T // 2).bitcast(BF16) for _ in range(2)]
        Qbh = [carve("A1", QCOLS // 2).bitcast(BF16) for _ in range(2)]
        Qbl = [carve("A1", QCOLS // 2).bitcast(BF16) for _ in range(2)]
        Vb = [carve("A1", T) for _ in range(2)]
        Eb = [[carve("A2", 1024), carve("A2", 1024)] for _ in range(2)]
        accb = [carve("A2", 1024) for _ in range(2)]
        acchl = [carve("A2", 1024).bitcast(BF16) for _ in range(2)]
        rclb = [carve("A2", 512) for _ in range(2)]
        aob = [carve("A2", 512) for _ in range(2)]
        aosp = [carve("A2", 512).bitcast(BF16) for _ in range(2)]
        if not hasattr(attention, "KCS"):
            attention.KCS = [[mk_sem(), mk_sem()] for _ in range(2)]
        KCS = attention.KCS
        if mode == "L1":
            mskb = carve("A1", 4096).bitcast(I8)
            g.wait("sync", LDQ)
            vm = g.dma("sync", mskb, mask8, LDQ)
            g.wait("vector", LDQ, vm)

        def load_chunk(pp, s, ci):
            p = 2 * pp + s
            sem = KCS[s][ci % 2]
            g.wait("sync", sem)
            kc = slice(cw * ci, cw * (ci + 1))
            qc = slice(512 * ci, 512 * ci + 512)
            g.dma("sync", Kbh[s][:, kc], ksrc[0][p][:, kc], sem)
            g.dma("sync", Kbl[s][:, kc], ksrc[1][p][:, kc], sem)
            g.dma("sync", Qbh[s][:, qc], qsrc[0][p][:, qc], sem)
            g.dma("sync", Qbl[s][:, qc], qsrc[1][p][:, qc], sem)
            g.dma("sync", Vb[s][:, kc].rearrange("p (n c) -> p n c", c=128),
                  vsrc[p][kc, :].rearrange("(n p2) c -> p2 n c", p2=128), sem)

        prev_vav = [None, None]
        prev_ev = [None, None]
        prev_vr = [None, None]
        for pp in range(NPAIR // 2):
            waitall("sync")
            g.wait("sync", STQ)
            g.wait("sync", LDQ)
            snaps = []
            for ci in range(nslot):
                for s in (0, 1):
                    load_chunk(pp, s, ci)
                snaps.append([[(KCS[s][0], KCS[s][0].v), (KCS[s][1], KCS[s][1].v)]
                              for s in (0, 1)])
            for si, (q0, kts) in enumerate(slots):
                n_kt = len(kts)
                s_exp = [{}, {}]
                t_av = [{}, {}]
                a_v = [{}, {}]
                m_v = [{}, {}]

                def emit_av(s, ki, kts=kts, n_kt=n_kt, s_exp=s_exp, t_av=t_av,
                            m_v=m_v):
                    kt, off, _, _ = kts[ki]
                    E = Eb[s][ki % 2]
                    g.wait("tensor", SCA, s_exp[s][ki])
                    if off > 0:
                        g.wait("tensor", VEC, m_v[s][ki])
                    if ki == 0 and prev_ev[s] is not None:
                        g.wait("tensor", VEC, prev_ev[s])
                    vb = Vb[s][:, 128 * kt:128 * kt + 128]
                    mm(PB[s][:][0:64, 0:512], vb[:, 0:64], E[:, 0:512],
                       start=(ki == 0), stop=(ki == n_kt - 1), tile_position=(0, 0))
                    t_av[s][ki] = mm(PB[s][:][64:128, 512:1024], vb[:, 64:128],
                                     E[:, 512:1024], start=(ki == 0),
                                     stop=(ki == n_kt - 1), tile_position=(0, 64),
                                     sem=TEN)

                for s in (0, 1):
                    g.op("vector", lambda e, a=accb[s]: e.memset(a[:], 0.0), VEC)
                    for (sm, v) in snaps[si][s]:
                        g.wait("tensor", sm, v)
                for ki, (kt, off, trio, mki) in enumerate(kts):
                    for s in (0, 1):
                        if ki >= 1:
                            g.wait("tensor", SCA, s_exp[s][ki - 1])
                        elif prev_vr[s] is not None:
                            g.wait("tensor", SCA, prev_vr[s])
                        ks = slice(128 * kt, 128 * kt + 128)
                        qs = slice(q0 + off, q0 + 512)
                        pb = PB[2 + s][:]
                        for i_c, (kk, qq) in enumerate(
                                ((Kbh[s], Qbh[s]), (Kbh[s], Qbl[s]), (Kbl[s], Qbh[s]))):
                            mm(pb[:, off:512], kk[0:64, ks], qq[0:64, qs],
                               start=(i_c == 0), stop=(i_c == 2))
                            mm(pb[:, 512 + off:1024], kk[64:128, ks], qq[64:128, qs],
                               start=(i_c == 0), stop=(i_c == 2),
                               sem=TEN if i_c == 2 else None)
                        t_sc = TEN.v
                        pb3 = pb[:, 0:1024].rearrange("p (h q) -> p h q", q=512)
                        if mki is not None:
                            g.wait("vector", TEN, t_sc)
                            stt(pb3, mskb[:, 512 * mki:512 * mki + 512]
                                .unsqueeze(1).broadcast_to((128, 2, 512)),
                                -1e30, pb3, ALU.mult, ALU.add)
                            g.wait("scalar", VEC)
                        elif trio is not None:
                            g.wait("vector", TEN, t_sc)
                            tt(pb3[:, :, trio:trio + 128], pb3[:, :, trio:trio + 128],
                               sb_tri[:].unsqueeze(1).broadcast_to((128, 2, 128)),
                               ALU.add)
                            g.wait("scalar", VEC)
                        else:
                            g.wait("scalar", TEN, t_sc)
                        E = Eb[s][ki % 2]
                        if ki >= 2:
                            g.wait("scalar", TEN, t_av[s][ki - 2])
                            g.wait("scalar", VEC, a_v[s][ki - 2])
                        elif prev_vav[s] is not None:
                            g.wait("scalar", TEN, prev_vav[s])
                        E3 = E.rearrange("p (h q) -> p h q", q=512)
                        s_exp[s][ki] = act(E3[:, :, off:512], pb3[:, :, off:512],
                                           AF.Exp, scale=f32(0.125), sem=SCA)
                        if off > 0:
                            g.op("vector", lambda e, a=E3, o=off:
                                 e.memset(a[:, :, 0:o], 0.0), VEC)
                            m_v[s][ki] = VEC.v
                        g.wait("vector", SCA, s_exp[s][ki])
                        a3 = accb[s].rearrange("p (h q) -> p h q", q=512)
                        tt(a3, a3, E3, ALU.add)
                        a_v[s][ki] = VEC.v
                    for s in (0, 1):
                        if ki >= 1:
                            emit_av(s, ki - 1)
                for s in (0, 1):
                    emit_av(s, n_kt - 1)
                vav = [t_av[s][n_kt - 1] for s in (0, 1)]
                vsp = [0, 0]
                for s in (0, 1):
                    g.wait("vector", STQ)
                    ahl = acchl[s]
                    split_hl(ahl[:, 0:1024], ahl[:, 1024:2048], accb[s])
                    vsp[s] = VEC.v
                vsum = [0, 0]
                for s in (0, 1):
                    g.wait("tensor", VEC, vsp[s])
                    g.wait("tensor", SCA, s_exp[s][n_kt - 1])
                    ahl = acchl[s]
                    pbs = PB[2 + s][:]
                    mm(pbs[0:64, 0:512], sb_ones64b_t[:], ahl[:, 0:512],
                       start=True, stop=False, tile_position=(0, 0))
                    mm(pbs[0:64, 0:512], sb_ones64b_t[:], ahl[:, 1024:1536],
                       start=False, stop=True, tile_position=(0, 0))
                    mm(pbs[64:128, 0:512], sb_ones64b_t[:], ahl[:, 512:1024],
                       start=True, stop=False, tile_position=(0, 64))
                    mm(pbs[64:128, 0:512], sb_ones64b_t[:], ahl[:, 1536:2048],
                       start=False, stop=True, tile_position=(0, 64), sem=TEN)
                    vsum[s] = TEN.v
                vr = [0, 0]
                for s in (0, 1):
                    g.wait("scalar", TEN, vsum[s])
                    act(rclb[s], PB[2 + s][:][:, 0:512], AF.Ln)
                    act(rclb[s], rclb[s], AF.Exp, scale=f32(-1.0), sem=SCA)
                    vr[s] = SCA.v
                for s in (0, 1):
                    g.wait("vector", SCA, vr[s])
                    g.wait("vector", TEN, vav[s])
                    tt(aob[s][0:64, :], PB[s][:][0:64, 0:512], rclb[s][0:64, :],
                       ALU.mult)
                    tt(aob[s][64:128, :], PB[s][:][64:128, 512:1024],
                       rclb[s][64:128, :], ALU.mult)
                    av = aosp[s]
                    split_hl(av[:, 0:512], av[:, 512:1024], aob[s])
                    ev = VEC.v
                    g.wait("gpsimd", VEC, ev)
                    p = 2 * pp + s
                    g.dma("gpsimd", astdst[0][p, :, q0:q0 + 512], av[:, 0:512], STQ)
                    g.dma("gpsimd", astdst[1][p, :, q0:q0 + 512], av[:, 512:1024], STQ)
                    prev_vav[s] = vav[s]
                    prev_ev[s] = ev
                    prev_vr[s] = vr[s]
        for kk in ("P0a", "P0b", "P1a", "P1b", "P2a", "P2b", "P3a", "P3b"):
            bank_rel(kk, (VEC, VEC.v), (SCA, SCA.v))

    attention(qst, kst, vst, ast, "L0")

    if stop_phase <= 3:
        return finish(nc, g, ctx)
    res_gemm(wp_s, FC, ast, x_fm, "bp", y1st, NT)

    natsem = [mk_sem(), mk_sem()]
    qw_stsem = [mk_sem(), mk_sem()]

    def build_qw_steps():
        natb_w = [A2t[:][:, 8704:9728], A2t[:][:, 9728:10752]]
        QSCR.update(A=A2t[:][:, 10752:11776], S1=A2t[:][:, 11776:12800],
                    S2=A2t[:][:, 12800:13824], G=A2t[:][:, 13824:14848],
                    M=A2t[:][:, 14848:14912], SS=A2t[:][:, 14912:14976],
                    R=A2t[:][:, 14976:15040])
        trbig_w = [A2t[:][:, 15040:15552], A2t[:][:, 15552:16064]]
        st = {"lastten": {}, "split": {}, "store": {}, "gi": [0], "first": [True]}

        def mk_load(src, si, w):
            def f():
                sem = natsem[si % 2]
                g.wait("sync", sem)
                lt = st["lastten"].get(si % 2)
                if lt is not None:
                    g.wait("sync", TEN, lt)
                st[("ldv", si % 2)] = g.dma("sync", natb_w[si % 2][:, 0:w], src, sem)
            return f

        def mk_quant(si, w):
            def f():
                sem = natsem[si % 2]
                v = st[("ldv", si % 2)]
                g.wait("vector", sem, v)
                g.wait("scalar", sem, v)
                st[("qv", si)] = quantize(natb_w[si % 2][:, 0:w], w,
                                          natb_w[si % 2][:, 0:w])
            return f

        def mk_group(si, dst, oc, k0g, kloc, kn):
            def f():
                if st["first"][0]:
                    bank_wait("tensor", "P2b", "P3b")
                    st["first"][0] = False
                gp = st["gi"][0] % 2
                st["gi"][0] += 1
                pb = PB[2 + gp][:][:, 512:1024]
                trv = trbig_w[gp].bitcast(BF16)
                ssem = qw_stsem[gp]
                pv = st["split"].get(gp)
                if pv is not None:
                    g.wait("tensor", VEC, pv)
                g.wait("tensor", VEC, st[("qv", si)])
                tv = 0
                for k in range(kn):
                    tv = transpose(pb[:, 128 * k:128 * k + 128],
                                   natb_w[si % 2][:, 128 * (kloc + k):
                                                  128 * (kloc + k) + 128], sem=TEN)
                g.wait("vector", TEN, tv)
                sv_ = st["store"].get(gp)
                if sv_ is not None:
                    g.wait("vector", ssem, sv_)
                split_hl(trv[:, 0:128 * kn], trv[:, 128 * kn:256 * kn],
                         pb[:, 0:128 * kn])
                st["split"][gp] = VEC.v
                g.wait("gpsimd", VEC, VEC.v)
                g.dma("gpsimd", dst[0][oc, :, k0g:k0g + kn, :]
                      .rearrange("p k c -> p (k c)"), trv[:, 0:128 * kn], ssem)
                g.dma("gpsimd", dst[1][oc, :, k0g:k0g + kn, :]
                      .rearrange("p k c -> p (k c)"), trv[:, 128 * kn:256 * kn], ssem)
                st["store"][gp] = ssem.v
                st["lastten"][si % 2] = TEN.v
            return f

        si = 0
        for (nat, dst, n_och, icols) in ((wg_nat, wgq_s, DIC, D),
                                         (wu_nat, wuq_s, DIC, D),
                                         (wd_nat, wdq_s, FC, DI)):
            for oc in range(n_och):
                for off in range(0, icols, 1024):
                    w = min(1024, icols - off)
                    qw_steps.append(mk_load(nat[oc][:, off:off + w], si, w))
                    qw_steps.append(mk_quant(si, w))
                    for kloc in range(0, w // 128, 4):
                        kn = min(4, w // 128 - kloc)
                        qw_steps.append(mk_group(si, dst, oc, off // 128 + kloc,
                                                 kloc, kn))
                    si += 1

    build_qw_steps()

    if stop_phase <= 4:
        return finish(nc, g, ctx)
    reset("A1")
    ZNh = carve("A1", FC * T // 2).bitcast(BF16)
    ZNl = carve("A1", FC * T // 2).bitcast(BF16)
    reset("A2")
    M1 = carve("A2", 512)
    R1 = carve("A2", 512)
    sqb = carve("A2", 1024)
    rd3 = Reader("A2", 512, n=3)
    rd4 = Reader("A2", 512, n=3)
    ztmp = carve("A2", 512)
    for t in range(NT):
        ln_stats(lambda fc, t_, t=t: rd3.rd(y1st[fc, :, 512 * t:512 * t + 512],
                                            "tensor", "scalar"),
                 1, M1, R1, sqb, barrier=(t == 0))
        for fc in range(FC):
            ch = rd4.rd(y1st[fc, :, 512 * t:512 * t + 512], "vector")
            c = slice(fc * T + 512 * t, fc * T + 512 * t + 512)
            tt(ztmp, ch, M1, ALU.subtract)
            tt(ztmp, ztmp, R1, ALU.mult)
            split_hl(ZNh[:, c], ZNl[:, c], ztmp)
            tick(2)

    def res_gemm_sbuf(wsrc, rhs_fn, res_src, bias_name, dst_st, ntt, sign_w=False):
        phase_barrier()
        bank_wait("tensor", "P0a", "P1a")
        reset("A2")
        res_r = Reader("A2", 512, n=3)
        outb = [carve("A2", 512) for _ in range(2)]
        wallsem = mk_sem()
        if sign_w:
            wsb = carve("A2", FC * FC * 64).bitcast(BF16)
            wtmp = carve("A2", FC * 128)
            for oc in range(FC):
                g.wait("sync", SCA)
                v = g.dma("sync", wtmp.rearrange("p (k c) -> p k c", c=128),
                          wsrc[oc], wallsem)
                g.wait("scalar", wallsem, v)
                act(wsb[:, 1024 * oc:1024 * (oc + 1)], wtmp, AF.Sign, sem=SCA)
            g.wait("tensor", SCA)
        else:
            wsb = carve("A2", FC * FC * 128)
            wtmp = carve("A2", FC * 128)
            for oc in range(FC):
                g.wait("sync", VEC)
                v = g.dma("sync", wtmp.rearrange("p (k c) -> p k c", c=128),
                          wsrc[oc], wallsem)
                g.wait("vector", wallsem, v)
                wv = wsb[:, 1024 * oc:1024 * (oc + 1)].bitcast(BF16)
                split_hl(wv[:, 0:1024], wv[:, 1024:2048], wtmp)
            g.wait("tensor", VEC)
        idx = 0
        for t in range(ntt):
            for oc in range(FC):
                key = f"P{idx % 2}a"
                pv = PB[idx % 2][:][:, 0:512]
                bank_wait("tensor", key)
                if sign_w:
                    n_mm = 2 * FC
                    i_mm = 0
                    for k in range(FC):
                        sw = wsb[:, 1024 * oc + 128 * k:1024 * oc + 128 * (k + 1)]
                        rh, rl = rhs_fn(k, t)
                        for aop in (rh, rl):
                            mm(pv, sw, aop, start=(i_mm == 0), stop=(i_mm == n_mm - 1),
                               sem=TEN if i_mm == n_mm - 1 else None)
                            i_mm += 1
                else:
                    wv = wsb[:, 1024 * oc:1024 * (oc + 1)].bitcast(BF16)
                    n_mm = 3 * FC
                    i_mm = 0
                    for k in range(FC):
                        wh = wv[:, 128 * k:128 * k + 128]
                        wl = wv[:, 1024 + 128 * k:1024 + 128 * (k + 1)]
                        rh, rl = rhs_fn(k, t)
                        for (wop, aop) in ((wh, rh), (wh, rl), (wl, rh)):
                            mm(pv, wop, aop, start=(i_mm == 0), stop=(i_mm == n_mm - 1),
                               sem=TEN if i_mm == n_mm - 1 else None)
                            i_mm += 1
                tv = TEN.v
                rb = res_r.rd(res_src[oc, :, 512 * t:512 * t + 512], "vector")
                ob = outb[idx % 2]
                if idx >= 2:
                    g.wait("vector", STQ)
                g.wait("vector", TEN, tv)
                if bias_name is not None:
                    ts(ob, pv, vcol(bias_name, oc), ALU.add)
                    bank_rel(key)
                    tt(ob, ob, rb, ALU.add)
                else:
                    tt(ob, pv, rb, ALU.add)
                    bank_rel(key)
                ev = VEC.v
                g.wait("gpsimd", VEC, ev)
                g.dma("gpsimd", dst_st[oc, :, 512 * t:512 * t + 512], ob, STQ)
                idx += 1
                tick(2)

    def zn_rhs(k, t):
        c = slice(k * T + 512 * t, k * T + 512 * t + 512)
        return (ZNh[:, c], ZNl[:, c])

    res_gemm_sbuf(w0t_s, zn_rhs, y1st, None, y2st, NT, sign_w=True)

    if stop_phase <= 5:
        return finish(nc, g, ctx)
    phase_barrier()
    reset("A2")
    rda = Reader("A2", 256, n=3)
    rdb = Reader("A2", 256, n=3)
    bo_ = [carve("A2", 256) for _ in range(2)]
    st_hist = []
    ii = 0
    for oc in range(FC):
        for j in range(NSLOT):
            ba = rda.rd(y2st[oc, :, 256 * OWN_A[j]:256 * OWN_A[j] + 256], "vector")
            bb = rdb.rd(y2st[oc, :, 256 * OWN_B[j]:256 * OWN_B[j] + 256], "vector")
            ob = bo_[ii % 2]
            if ii >= 2:
                g.wait("vector", STQ)
            ts(ob, ba, vcol("selA"), ALU.mult)
            stt(ob, bb, vcol("selB"), ob, ALU.mult, ALU.add, sem=VEC)
            g.wait("gpsimd", VEC)
            st_hist.append(g.dma("gpsimd", y2own[oc, :, 256 * j:256 * j + 256], ob, STQ))
            ii += 1
            tick(2)

    if stop_phase <= 6:
        return finish(nc, g, ctx)
    reset("A1")
    M2 = carve("A1", T)
    R2 = carve("A1", T)
    P2 = R2
    reset("A2")
    sqb = carve("A2", 1024)
    rd5 = Reader("A2", 512, n=3)
    ln_stats(lambda fc, t_: rd5.rd(y2st[fc, :, 512 * t_:512 * t_ + 512], "tensor", "scalar"),
             NT, M2, R2, sqb)
    tmpc = carve("A2", 512)
    for t in range(NT):
        c = slice(512 * t, 512 * t + 512)
        g.wait("vector", SCA)
        tt(tmpc, R2[:, c], R2[:, c], ALU.mult)
        ts(tmpc, tmpc, -float(EPS), ALU.mult, float(1.0 + EPS), ALU.add, sem=VEC)
        g.wait("scalar", VEC)
        act(tmpc, tmpc, AF.Ln)
        act(tmpc, tmpc, AF.Exp, scale=f32(-0.5), sem=SCA)
        g.wait("vector", SCA)
        tt(P2[:, c], R2[:, c], tmpc, ALU.mult, sem=VEC)
    g.wait("scalar", VEC)

    rd6 = Reader("A2", 512, n=3)
    n1b = [carve("A2", 512) for _ in range(2)]
    st_hist = []
    ii = 0
    for t in range(NT):
        for fc in range(FC):
            ch = rd6.rd(y2st[fc, :, 512 * t:512 * t + 512], "vector")
            nb = n1b[ii % 2]
            if ii >= 2:
                g.wait("vector", STQ)
            tt(nb, ch, M2[:, 512 * t:512 * t + 512], ALU.subtract)
            tt(nb, nb, P2[:, 512 * t:512 * t + 512], ALU.mult)
            ts(nb, nb, vcol("g1", fc), ALU.mult, vcol("b1", fc), ALU.add, sem=VEC)
            g.wait("gpsimd", VEC)
            st_hist.append(g.dma("gpsimd", n1st[fc, :, 512 * t:512 * t + 512], nb, STQ))
            ii += 1
            tick(2)

    while qw_steps:
        qw_steps.pop(0)()

    if stop_phase <= 7:
        return finish(nc, g, ctx)
    reset("A1")
    N1h = carve("A1", FC * T // 2).bitcast(BF16)
    N1l = carve("A1", FC * T // 2).bitcast(BF16)
    reset("A2")
    phase_barrier()
    rd7 = Reader("A2", 512, n=1)
    g.wait("sync", STQ)
    for t in range(NT):
        for fc in range(FC):
            ch = rd7.rd(n1st[fc, :, 512 * t:512 * t + 512], "vector")
            c = slice(fc * T + 512 * t, fc * T + 512 * t + 512)
            split_hl(N1h[:, c], N1l[:, c], ch)

    def n1_rhs(k, t):
        c = slice(k * T + 512 * t, k * T + 512 * t + 512)
        return (N1h[:, c], N1l[:, c])

    C2b_ = carve("A2", T)
    S2b_ = carve("A2", T)
    g.wait("sync", LDQ)
    v1_ = g.dma("sync", C2b_, c2t, LDQ)
    v2_ = g.dma("sync", S2b_, s2t, LDQ)
    g.wait("vector", LDQ)

    def mk_rope_evac(dst_st, ctab, stab, bias_base):
        kpre = [carve("A2", 512) for _ in range(2)]
        krot = [carve("A2", 512) for _ in range(2)]
        kout1 = carve("A2", 512)
        kout = [kout1, kout1]
        ksp = [carve("A2", 512) for _ in range(2)]
        rsem = mk_sem()
        state = {"i": 0}

        def evac(oc, t, pv):
            i = state["i"]
            b0, b1, b2 = kpre[i % 2], krot[i % 2], kout[i % 2]
            if i >= 2:
                g.wait("vector", rsem, 64 * (i - 1))
                g.wait("vector", STQ)
            ts(b0, pv, vcol("bqkv", bias_base + oc), ALU.add)
            ev0 = VEC.v
            g.wait("gpsimd", VEC, ev0)
            for (d0, s0) in ((0, 32), (32, 0), (64, 96), (96, 64)):
                g.dma("gpsimd", b1[d0:d0 + 32, :], b0[s0:s0 + 32, :], rsem)
            c = slice(512 * t, 512 * t + 512)
            g.wait("vector", rsem, 64 * (i + 1))
            tt(b2, b0, ctab[:, c], ALU.mult)
            tt(b1, b1, stab[:, c], ALU.mult)
            tt(b2, b2, b1, ALU.add)
            bh = ksp[i % 2].bitcast(BF16)
            split_hl(bh[:, 0:512], bh[:, 512:1024], b2)
            ev = VEC.v
            g.wait("gpsimd", VEC, ev)
            g.dma("gpsimd", dst_st[0][oc, :, c], bh[:, 0:512], STQ)
            g.dma("gpsimd", dst_st[1][oc, :, c], bh[:, 512:1024], STQ)
            state["i"] = i + 1
            return ev0
        return evac

    class WView:
        def __init__(self, base):
            self.base = base

        def __getitem__(self, i):
            return wqkv_s[self.base + i]

    wb_c2 = [carve("A2", FC * 128) for _ in range(2)]
    wtmp_c2 = carve("A2", FC * 128)
    wsem_c2 = [mk_sem(), mk_sem()]
    stage_gemm(WView(FC), FC, n1_rhs, NT, mk_rope_evac(kst, C2b_, S2b_, FC),
               wb=wb_c2, wsem=wsem_c2, wtmp=wtmp_c2)
    stage_gemm(WView(2 * FC), FC, n1_rhs, NT, mk_v_evac(vst, "bqkv", 2 * FC),
               wb=wb_c2, wsem=wsem_c2, wtmp=wtmp_c2)

    if stop_phase <= 8:
        return finish(nc, g, ctx)
    reset("A1")
    M2o = carve("A1", TO)
    R2o = carve("A1", TO)
    P2o = R2o
    N1oh = carve("A1", FC * TO // 2).bitcast(BF16)
    N1ol = carve("A1", FC * TO // 2).bitcast(BF16)
    reset("A2")
    sqb = carve("A2", 1024)
    g.wait("sync", STQ)
    rd8 = Reader("A2", 512, n=3)
    ln_stats(lambda fc, t_: rd8.rd(y2own[fc, :, 512 * t_:512 * t_ + 512], "tensor", "scalar"),
             NTO, M2o, R2o, sqb)
    tmpo = carve("A2", 512)
    for t in range(NTO):
        c = slice(512 * t, 512 * t + 512)
        g.wait("vector", SCA)
        tt(tmpo, R2o[:, c], R2o[:, c], ALU.mult)
        ts(tmpo, tmpo, -float(EPS), ALU.mult, float(1.0 + EPS), ALU.add, sem=VEC)
        g.wait("scalar", VEC)
        act(tmpo, tmpo, AF.Ln)
        act(tmpo, tmpo, AF.Exp, scale=f32(-0.5), sem=SCA)
        g.wait("vector", SCA)
        tt(P2o[:, c], R2o[:, c], tmpo, ALU.mult, sem=VEC)
    g.wait("scalar", VEC)
    rd9 = Reader("A2", 512, n=3)
    ntmp = carve("A2", 512)
    for t in range(NTO):
        for fc in range(FC):
            ch = rd9.rd(y2own[fc, :, 512 * t:512 * t + 512], "vector")
            c = slice(fc * TO + 512 * t, fc * TO + 512 * t + 512)
            tt(ntmp, ch, M2o[:, 512 * t:512 * t + 512], ALU.subtract)
            tt(ntmp, ntmp, P2o[:, 512 * t:512 * t + 512], ALU.mult)
            ts(ntmp, ntmp, vcol("g1", fc), ALU.mult, vcol("b1", fc), ALU.add)
            split_hl(N1oh[:, c], N1ol[:, c], ntmp)

    def n1o_rhs(k, t):
        c = slice(k * TO + 512 * t, k * TO + 512 * t + 512)
        return (N1oh[:, c], N1ol[:, c])

    C2ob = carve("A2", TO)
    S2ob = carve("A2", TO)
    g.wait("sync", LDQ)
    v1_ = g.dma("sync", C2ob, c2o, LDQ)
    v2_ = g.dma("sync", S2ob, s2o, LDQ)
    g.wait("vector", LDQ)
    stage_gemm(WView(0), FC, n1o_rhs, NTO, mk_rope_evac(qst1, C2ob, S2ob, 0))

    if stop_phase <= 9:
        return finish(nc, g, ctx)
    reset("A1")
    attention(qst1, kst, vst, ast1, "L1")

    if stop_phase <= 10:
        return finish(nc, g, ctx)
    reset("A1")
    Z2 = carve("A1", FC * TO)
    M2r = carve("A1", TO)
    R2r = carve("A1", TO)
    reset("A2")
    sqb = carve("A2", 1024)
    rd10 = Reader("A2", 512, n=3)
    ln_stats(lambda fc, t_: rd10.rd(y2own[fc, :, 512 * t_:512 * t_ + 512], "tensor", "scalar"),
             NTO, M2r, R2r, sqb)

    def wo_gemm():
        phase_barrier()
        bank_wait("tensor", "P0a", "P1a")
        whl = carve("A1", FC * FC * 128)
        wtmp = carve("A2", FC * 128)
        wallsem = mk_sem()
        rhs_rh = Reader("A2", FC * 256, n=1)
        rhs_rl = Reader("A2", FC * 256, n=1)
        res_r = Reader("A2", 512, n=3)
        zb = [carve("A2", 512) for _ in range(2)]
        for oc in range(FC):
            g.wait("sync", VEC)
            v = g.dma("sync", wtmp.rearrange("p (k c) -> p k c", c=128),
                      wo_s[oc], wallsem)
            g.wait("vector", wallsem, v)
            wv = whl[:, 1024 * oc:1024 * (oc + 1)].bitcast(BF16)
            split_hl(wv[:, 0:1024], wv[:, 1024:2048], wtmp)
        g.wait("tensor", VEC)
        idx = 0
        for t in range(NTO):
            rbh = rhs_rh.rd(ast1[0][:, :, 512 * t:512 * t + 512], "tensor").bitcast(BF16)
            rbl = rhs_rl.rd(ast1[1][:, :, 512 * t:512 * t + 512], "tensor").bitcast(BF16)
            for oc in range(FC):
                key = f"P{idx % 2}a"
                pv = PB[idx % 2][:][:, 0:512]
                bank_wait("tensor", key)
                wv = whl[:, 1024 * oc:1024 * (oc + 1)].bitcast(BF16)
                i_mm = 0
                for k in range(FC):
                    wh = wv[:, 128 * k:128 * k + 128]
                    wl = wv[:, 1024 + 128 * k:1024 + 128 * (k + 1)]
                    ah = rbh[:, 512 * k:512 * k + 512]
                    al = rbl[:, 512 * k:512 * k + 512]
                    for (wop, aop) in ((wh, ah), (wh, al), (wl, ah)):
                        mm(pv, wop, aop, start=(i_mm == 0), stop=(i_mm == 3 * FC - 1),
                           sem=TEN if i_mm == 3 * FC - 1 else None)
                        i_mm += 1
                tv = TEN.v
                rb = res_r.rd(y2own[oc, :, 512 * t:512 * t + 512], "vector")
                zbb = zb[idx % 2]
                c = slice(512 * t, 512 * t + 512)
                g.wait("vector", TEN, tv)
                tt(zbb, rb, M2r[:, c], ALU.subtract)
                tt(zbb, zbb, R2r[:, c], ALU.mult)
                dst = Z2[:, oc * TO + 512 * t:oc * TO + 512 * t + 512]
                ts(dst, pv, vcol("bo", oc), ALU.add)
                bank_rel(key)
                tt(dst, dst, zbb, ALU.add)
                idx += 1

    wo_gemm()
    phase_barrier()
    for fc in range(FC):
        g.wait("gpsimd", VEC)
        g.dma("gpsimd", z2dbg[fc], Z2[:, fc * TO:(fc + 1) * TO], STQ)

    if stop_phase <= 11:
        return finish(nc, g, ctx)
    w1sem = [mk_sem(), mk_sem()]
    w2sem = [mk_sem(), mk_sem()]
    wstg_sem = mk_sem()

    def stage_whl():
        phase_barrier()
        reset("A2")
        wtmp = carve("A2", DMC * 128)
        whl = carve("A2", DMC * 128)
        for (wsrc, dst, n_oc, n_k) in ((w1_s, w1hl_s, DMC, FC),
                                       (w2_s, w2hl_s, FC, DMC)):
            for oc in range(n_oc):
                waitall("sync")
                g.wait("sync", STQ)
                v = g.dma("sync", wtmp[:, 0:n_k * 128].rearrange("p (k c) -> p k c", c=128),
                          wsrc[oc], wstg_sem)
                g.wait("vector", wstg_sem, v)
                g.wait("vector", STQ)
                wv = whl[:, 0:n_k * 128].bitcast(BF16)
                split_hl(wv[:, 0:n_k * 128], wv[:, n_k * 128:2 * n_k * 128], wtmp[:, 0:n_k * 128])
                g.wait("gpsimd", VEC)
                g.dma("gpsimd", dst[oc], wv[:, 0:2 * n_k * 128], STQ)

    stage_whl()

    reset("A2")
    M3 = carve("A2", TO)
    R3 = carve("A2", TO)
    sqb = carve("A2", 1024)

    def z2_chunk(fc, t):
        return Z2[:, fc * TO + 512 * t:fc * TO + 512 * t + 512]

    ln_stats(z2_chunk, NTO, M3, R3, sqb)

    n2b = carve("A2", FC * 512)
    w1b = [carve("A2", FC * 128) for _ in range(2)]
    gtmp = carve("A2", 512)
    ob2 = carve("A2", 256)
    res_r2 = Reader("A2", 256, n=3)

    def mlp():
        phase_barrier()
        bank_wait("tensor", "P0a", "P1a", "P2a")
        w2b = [A1t[:][:, 16384:16384 + 4096], A1t[:][:, 20480:20480 + 4096]]
        hb = A1t[:][:, 24576:24576 + DMC * 256].bitcast(BF16)
        hh = hb[:, 0:DMC * 256]
        hl = hb[:, DMC * 256:2 * DMC * 256]
        g.wait("sync", STQ)
        for t in range(NTO):
            g.wait("vector", TEN)
            for fc in range(FC):
                c = slice(512 * t, 512 * t + 512)
                tt(gtmp, z2_chunk(fc, t), M3[:, c], ALU.subtract)
                tt(gtmp, gtmp, R3[:, c], ALU.mult)
                ts(gtmp, gtmp, vcol("g2", fc), ALU.mult, vcol("b2", fc), ALU.add)
                nv = n2b.bitcast(BF16)
                split_hl(nv[:, 512 * fc:512 * fc + 512],
                         nv[:, FC * 512 + 512 * fc:FC * 512 + 512 * (fc + 1)], gtmp)
            vn2 = VEC.v
            g.wait("tensor", VEC, vn2)
            nv = n2b.bitcast(BF16)
            for half in range(2):
                hrel = []
                wsnap = []
                for oc in range(DMC):
                    wsnap.append(TEN.v)
                    if oc >= 2:
                        g.wait("sync", TEN, wsnap[oc - 1])
                    vldw = g.dma("sync", w1b[oc % 2].bitcast(BF16), w1hl_s[oc],
                                 w1sem[oc % 2])
                    pv = PB[oc % 2][:][:, 0:256]
                    g.wait("tensor", w1sem[oc % 2], vldw)
                    if oc >= 2:
                        g.wait("tensor", SCA, hrel[oc - 2])
                    wv = w1b[oc % 2].bitcast(BF16)
                    i_mm = 0
                    for k in range(FC):
                        wh = wv[:, 128 * k:128 * k + 128]
                        wl = wv[:, FC * 128 + 128 * k:FC * 128 + 128 * (k + 1)]
                        cq = slice(512 * k + 256 * half, 512 * k + 256 * half + 256)
                        ah = nv[:, cq]
                        al = nv[:, FC * 512:2 * FC * 512][:, cq]
                        for (wop, aop) in ((wh, ah), (wh, al), (wl, ah)):
                            mm(pv, wop, aop, start=(i_mm == 0), stop=(i_mm == 3 * FC - 1),
                               sem=TEN if i_mm == 3 * FC - 1 else None)
                            i_mm += 1
                    g.wait("scalar", TEN)
                    g.wait("scalar", VEC)
                    act(gtmp[:, 0:256], pv, AF.Gelu, bias=vcol("b1m", oc), sem=SCA)
                    hrel.append(SCA.v)
                    g.wait("vector", SCA)
                    split_hl(hh[:, 256 * oc:256 * oc + 256],
                             hl[:, 256 * oc:256 * oc + 256], gtmp[:, 0:256])
                g.wait("tensor", VEC)
                wsnap2 = []
                for oc in range(FC):
                    wsnap2.append(TEN.v)
                    if oc >= 2:
                        g.wait("sync", TEN, wsnap2[oc - 1])
                    elif oc == 0:
                        g.wait("sync", TEN)
                    vldw = g.dma("sync", w2b[oc % 2].bitcast(BF16), w2hl_s[oc],
                                 w2sem[oc % 2])
                    pv = PB[2][:][:, 0:256]
                    g.wait("tensor", w2sem[oc % 2], vldw)
                    g.wait("tensor", VEC)
                    wv = w2b[oc % 2].bitcast(BF16)
                    i_mm = 0
                    for k in range(DMC):
                        wh = wv[:, 128 * k:128 * k + 128]
                        wl = wv[:, DMC * 128 + 128 * k:DMC * 128 + 128 * (k + 1)]
                        ah = hh[:, 256 * k:256 * k + 256]
                        al = hl[:, 256 * k:256 * k + 256]
                        for (wop, aop) in ((wh, ah), (wh, al), (wl, ah)):
                            mm(pv, wop, aop, start=(i_mm == 0), stop=(i_mm == 3 * DMC - 1),
                               sem=TEN if i_mm == 3 * DMC - 1 else None)
                            i_mm += 1
                    tv = TEN.v
                    cq = slice(512 * t + 256 * half, 512 * t + 256 * half + 256)
                    rb = res_r2.rd(y2own[oc, :, cq], "vector")
                    g.wait("vector", TEN, tv)
                    g.wait("vector", STQ)
                    ts(ob2, pv, vcol("b2m", oc), ALU.add)
                    tt(ob2, ob2, Z2[:, oc * TO + 256 * (2 * t + half):
                                    oc * TO + 256 * (2 * t + half) + 256], ALU.add)
                    tt(ob2, ob2, rb, ALU.add)
                    ev = VEC.v
                    g.wait("gpsimd", VEC, ev)
                    g.dma("gpsimd", y3own[oc, :, cq], ob2, STQ)
                    g.wait("tensor", VEC, ev)

    mlp()

    if stop_phase <= 12:
        return finish(nc, g, ctx)
    reset("A1")
    QSCR.update(A=carve("A1", 1024), S1=carve("A1", 1024), S2=carve("A1", 1024),
                G=carve("A1", 1024), M=carve("A1", 64), SS=carve("A1", 64),
                R=carve("A1", 64))
    carve("A1", 1216)

    if stop_phase <= 13:
        return finish(nc, g, ctx)
    reset("A2")
    M4 = carve("A2", TO)
    R4 = carve("A2", TO)
    a2_mark = arena_off["A2"]
    sqb = carve("A2", 1024)
    g.wait("sync", STQ)
    rd11 = Reader("A2", 512, n=3)
    ln_stats(lambda fc, t_: rd11.rd(y3own[fc, :, 512 * t_:512 * t_ + 512], "tensor", "scalar"),
             NTO, M4, R4, sqb)
    arena_off["A2"] = a2_mark

    XQb = A1t[:][:, 5440:5440 + 4096].bitcast(BF16)
    XTM = A1t[:][:, 9536:9536 + 4096]
    HTM = A1t[:][:, 9536:9536 + 4 * DI]
    HQFb = A1t[:][:, 20800:20800 + DIC * 512].bitcast(BF16)
    xb = A1t[:][:, 32064:32064 + 512]
    wgb = [carve("A2", FC * 128) for _ in range(2)]
    wub = [carve("A2", FC * 128) for _ in range(2)]
    wdb = [carve("A2", DIC * 128) for _ in range(2)]
    ub = carve("A2", 512)
    rd12 = Reader("A2", 512, n=2)
    rd13 = Reader("A2", 512, n=2)
    gusem = [mk_sem(), mk_sem()]
    dsem2 = [mk_sem(), mk_sem()]

    def ffn():
        phase_barrier()
        bank_wait("tensor", "P0a", "P0b", "P1a", "P1b", "P2a", "P2b")
        gu_guard = {}
        d_guard = {}
        tp_guard = {}
        pg_guard = {}
        def tp_slot(j):
            return PB[2][:][:, 512 * (j % 2):512 * (j % 2) + 128]
        XQH = XQb[:, 0:FC * 512]
        XQL = XQb[:, FC * 512:2 * FC * 512]
        HQH = HQFb[:, 0:DIC * 512]
        HQL = HQFb[:, DIC * 512:2 * DIC * 512]

        def x_stage(t):
            g.wait("vector", TEN)
            g.wait("scalar", TEN)
            g.wait("vector", STQ)
            for fc in range(FC):
                ch = rd12.rd(y3own[fc, :, 512 * t:512 * t + 512], "vector")
                c = slice(512 * t, 512 * t + 512)
                g.wait("vector", TEN)
                tt(xb, ch, M4[:, c], ALU.subtract)
                tt(xb, xb, R4[:, c], ALU.mult, sem=VEC)
                vx = VEC.v
                g.wait("tensor", VEC, vx)
                for i in range(4):
                    pv = tp_guard.get(i % 2)
                    if pv is not None:
                        g.wait("tensor", VEC, pv)
                    tv = transpose(tp_slot(i), xb[:, 128 * i:128 * i + 128], sem=TEN)
                    g.wait("vector", TEN, tv)
                    vcopy(XTM[:, 1024 * i + 128 * fc:1024 * i + 128 * fc + 128],
                          tp_slot(i), sem=VEC)
                    tp_guard[i % 2] = VEC.v
            for i in range(4):
                qv = quantize(XTM[:, 1024 * i:1024 * (i + 1)], 1024,
                              XTM[:, 1024 * i:1024 * (i + 1)], xv=VEC.v)
                g.wait("tensor", VEC, qv)
                for fc in range(FC):
                    j = 4 * i + fc
                    pv = tp_guard.get(j % 2)
                    if pv is not None:
                        g.wait("tensor", VEC, pv)
                    tv = transpose(tp_slot(j),
                                   XTM[:, 1024 * i + 128 * fc:1024 * i + 128 * fc + 128],
                                   sem=TEN)
                    g.wait("vector", TEN, tv)
                    cc = slice(512 * fc + 128 * i, 512 * fc + 128 * i + 128)
                    split_hl(XQH[:, cc], XQL[:, cc], tp_slot(j))
                    tp_guard[j % 2] = VEC.v

        def gh_stage(t):
            for g0 in range(0, DIC, 8):
                gn = min(8, DIC - g0)
                for oc in range(g0, g0 + gn):
                    bi = oc % 2
                    wgv = wgb[bi].bitcast(BF16)
                    wuv = wub[bi].bitcast(BF16)
                    g.wait("sync", gusem[bi])
                    if gu_guard.get(bi) is not None:
                        g.wait("sync", TEN, gu_guard[bi])
                    g.dma("sync", wgv[:, 0:FC * 128].rearrange("p (k c) -> p k c", c=128),
                          wgq_s[0][oc], gusem[bi])
                    g.dma("sync", wgv[:, FC * 128:2 * FC * 128].rearrange("p (k c) -> p k c", c=128),
                          wgq_s[1][oc], gusem[bi])
                    g.dma("sync", wuv[:, 0:FC * 128].rearrange("p (k c) -> p k c", c=128),
                          wuq_s[0][oc], gusem[bi])
                    v2 = g.dma("sync", wuv[:, FC * 128:2 * FC * 128].rearrange("p (k c) -> p k c", c=128),
                               wuq_s[1][oc], gusem[bi])
                    g.wait("tensor", gusem[bi], v2)
                    pg = PB[0][:][:, 512 * bi:512 * bi + 512]
                    pu = PB[1][:][:, 512 * bi:512 * bi + 512]
                    if oc == 0:
                        g.wait("tensor", VEC)
                        g.wait("tensor", SCA)
                    pgv = pg_guard.get(bi)
                    if pgv is not None:
                        g.wait("tensor", SCA, pgv[0])
                        g.wait("tensor", VEC, pgv[1])
                    for (pp, wv) in ((pg, wgv), (pu, wuv)):
                        i_mm = 0
                        for k in range(FC):
                            wh = wv[:, 128 * k:128 * k + 128]
                            wl = wv[:, FC * 128 + 128 * k:FC * 128 + 128 * (k + 1)]
                            ah = XQH[:, 512 * k:512 * k + 512]
                            al = XQL[:, 512 * k:512 * k + 512]
                            for (wop, aop) in ((wh, ah), (wh, al), (wl, ah)):
                                mm(pp, wop, aop, start=(i_mm == 0), stop=(i_mm == 3 * FC - 1),
                                   sem=TEN if i_mm == 3 * FC - 1 else None)
                                i_mm += 1
                    tv = TEN.v
                    gu_guard[bi] = tv
                    g.wait("scalar", TEN, tv)
                    act(ub, pg, AF.Silu, sem=SCA)
                    pg_sca = SCA.v
                    g.wait("vector", SCA, pg_sca)
                    tt(ub, ub, pu, ALU.mult, sem=VEC)
                    vv = VEC.v
                    pg_guard[bi] = (pg_sca, vv)
                    g.wait("tensor", VEC, vv)
                    for i in range(4):
                        pv = tp_guard.get(i % 2)
                        if pv is not None:
                            g.wait("tensor", VEC, pv)
                        tv2 = transpose(tp_slot(i), ub[:, 128 * i:128 * i + 128], sem=TEN)
                        g.wait("vector", TEN, tv2)
                        vcopy(HTM[:, DI * i + 128 * oc:DI * i + 128 * oc + 128],
                              tp_slot(i), sem=VEC)
                        tp_guard[i % 2] = VEC.v
                for i in range(4):
                    hs = slice(DI * i + 128 * g0, DI * i + 128 * (g0 + gn))
                    qv = quantize(HTM[:, hs], 128 * gn, HTM[:, hs], xv=VEC.v)
                    g.wait("tensor", VEC, qv)
                    for kc in range(g0, g0 + gn):
                        pv = tp_guard.get(kc % 2)
                        if pv is not None:
                            g.wait("tensor", VEC, pv)
                        tv = transpose(tp_slot(kc),
                                       HTM[:, DI * i + 128 * kc:DI * i + 128 * kc + 128],
                                       sem=TEN)
                        g.wait("vector", TEN, tv)
                        cc = slice(512 * kc + 128 * i, 512 * kc + 128 * i + 128)
                        split_hl(HQH[:, cc], HQL[:, cc], tp_slot(kc))
                        tp_guard[kc % 2] = VEC.v
            return (SCA.v, VEC.v)

        def d_stage(t, g_end):
            for oc in range(FC):
                bi = oc % 2
                wdv = wdb[bi].bitcast(BF16)
                g.wait("sync", dsem2[bi])
                if d_guard.get(bi) is not None:
                    g.wait("sync", TEN, d_guard[bi])
                g.dma("sync", wdv[:, 0:DIC * 128].rearrange("p (k c) -> p k c", c=128),
                      wdq_s[0][oc], dsem2[bi])
                v1 = g.dma("sync", wdv[:, DIC * 128:2 * DIC * 128].rearrange("p (k c) -> p k c", c=128),
                           wdq_s[1][oc], dsem2[bi])
                pv = PB[0][:][:, 512 * (oc % 2):512 * (oc % 2) + 512]
                g.wait("tensor", dsem2[bi], v1)
                if oc == 0:
                    g.wait("tensor", SCA, g_end[0])
                    g.wait("tensor", VEC, g_end[1])
                pgv = pg_guard.get(2 + oc % 2)
                if pgv is not None:
                    g.wait("tensor", VEC, pgv)
                i_mm = 0
                for k in range(DIC):
                    wh = wdv[:, 128 * k:128 * k + 128]
                    wl = wdv[:, DIC * 128 + 128 * k:DIC * 128 + 128 * (k + 1)]
                    ah = HQH[:, 512 * k:512 * k + 512]
                    al = HQL[:, 512 * k:512 * k + 512]
                    for (wop, aop) in ((wh, ah), (wh, al), (wl, ah)):
                        mm(pv, wop, aop, start=(i_mm == 0), stop=(i_mm == 3 * DIC - 1),
                           sem=TEN if i_mm == 3 * DIC - 1 else None)
                        i_mm += 1
                tv = TEN.v
                d_guard[bi] = tv
                ch = rd13.rd(y3own[oc, :, 512 * t:512 * t + 512], "vector")
                g.wait("vector", TEN, tv)
                g.wait("vector", STQ)
                tt(xb, pv, ch, ALU.add, sem=VEC)
                pg_guard[2 + oc % 2] = VEC.v
                g.wait("gpsimd", VEC)
                g.dma("gpsimd", y_out[oc, :, 512 * t:512 * t + 512], xb, STQ)

        x_stage(0)
        for t in range(NTO):
            g_end = gh_stage(t)
            if t + 1 < NTO:
                x_stage(t + 1)
            d_stage(t, g_end)

    ffn()

    return finish(nc, g, ctx)


def finish(nc, g, ctx):
    g.wait("sync", g.fin_STQ)
    g.wait("sync", g.fin_LDQ)
    with ctx:
        blk = ctx.enter_context(nc.Block())
        for eng in ENGS:
            def make_body(eng):
                def body(e):
                    for fn in g.ops[eng]:
                        fn(e)
                return body
            getattr(blk, eng)(make_body(eng))
    return nc



def pack_vecs(ins, selA, selB):
    out = np.zeros((128, 128), dtype=np.float32)
    col = 0
    for key, n in (("L0_attn_bp", FC), ("L1_attn_bqkv", 3 * FC), ("L1_attn_ln1_g", FC),
                   ("L1_attn_ln1_b", FC), ("L1_attn_bo", FC), ("L1_attn_ln2_g", FC),
                   ("L1_attn_ln2_b", FC), ("L1_attn_b1", DMC), ("L1_attn_b2", FC)):
        arr = np.asarray(ins[key], dtype=np.float32)
        out[:, col:col + n] = arr.reshape(n, 128).T
        col += n
    out[:, col] = selA
    out[:, col + 1] = selB
    return out


def kernel(**inputs):
    ins = {k: np.asarray(v) for k, v in inputs.items()}
    x = np.asarray(ins["L0_attn_x"], dtype=np.float32)

    C2, S2 = build_rope_tables()
    tri = build_tri()
    identity = np.ascontiguousarray(np.eye(128, dtype=np.float32))
    maskA = build_masks(OWN_A)
    maskB = build_masks(OWN_B)

    wq_s = swizzle_strips(ins["L0_attn_Wq"])
    wk_s = swizzle_strips(ins["L0_attn_Wk"])
    wv_s = swizzle_strips(ins["L0_attn_Wv"])
    wp_s = swizzle_strips(ins["L0_attn_Wp"])
    w0t_s = swizzle_strips(np.ascontiguousarray(np.asarray(ins["L0_ffn_weight"]).T))
    wqkv_s = swizzle_strips(ins["L1_attn_Wqkv"])
    wo_s = swizzle_strips(ins["L1_attn_Wo"])
    w1_s = swizzle_strips(ins["L1_attn_W1"])
    w2_s = swizzle_strips(ins["L1_attn_W2"])
    wg_nat = np.ascontiguousarray(
        np.asarray(ins["L1_ffn_w_gate"], dtype=np.float32).reshape(DIC, 128, D))
    wu_nat = np.ascontiguousarray(
        np.asarray(ins["L1_ffn_w_up"], dtype=np.float32).reshape(DIC, 128, D))
    wd_nat = np.ascontiguousarray(
        np.asarray(ins["L1_ffn_w_down"], dtype=np.float32).reshape(FC, 128, DI))

    nc = build_program()

    in_maps = []
    for core in range(8):
        b, s = core // 2, core % 2
        own = OWN_A if s == 0 else OWN_B
        own_tok = np.concatenate([np.arange(256 * a, 256 * a + 256) for a in own])
        xb = np.ascontiguousarray(x[b].T)
        im = dict(
            x_fm=np.ascontiguousarray(xb.reshape(FC, 128, T)),
            wq_s=wq_s, wk_s=wk_s, wv_s=wv_s, wp_s=wp_s, w0t_s=w0t_s,
            wqkv_s=wqkv_s, wo_s=wo_s, w1_s=w1_s, w2_s=w2_s,
            wg_nat=wg_nat, wu_nat=wu_nat, wd_nat=wd_nat,
            vecs=pack_vecs(ins, 1.0 if s == 0 else 0.0, 0.0 if s == 0 else 1.0),
            c2t=C2, s2t=S2,
            c2o=np.ascontiguousarray(C2[:, own_tok]),
            s2o=np.ascontiguousarray(S2[:, own_tok]),
            iden=identity, mtri=tri,
            mask8=maskA if s == 0 else maskB,
        )
        in_maps.append(im)

    res = run_bass_kernel_spmd(nc, in_maps, core_ids=list(range(8)))

    out = np.empty((B, T, D), dtype=np.float32)
    for core in range(8):
        b, s = core // 2, core % 2
        own = OWN_A if s == 0 else OWN_B
        y = np.asarray(res.results[core]["y_out"]).reshape(D, TO).T
        for j, a in enumerate(own):
            out[b, 256 * a:256 * a + 256, :] = y[256 * j:256 * j + 256, :]
    return out


# version 12
# speedup vs baseline: 1.0485x; 1.0152x over previous
import numpy as np
from contextlib import ExitStack

import concourse.bass as bass
import concourse.mybir as mybir
from concourse.bass_utils import run_bass_kernel_spmd

F32 = mybir.dt.float32
BF16 = mybir.dt.bfloat16
I8 = mybir.dt.int8
AF = mybir.ActivationFunctionType
ALU = mybir.AluOpType
AX = mybir.AxisListType
f32 = np.float32

B, T, D, H, HD = 4, 4096, 1024, 16, 64
NPAIR = 8
TO = T // 2
OWN_A = [0, 3, 4, 7, 8, 11, 12, 15]
OWN_B = [1, 2, 5, 6, 9, 10, 13, 14]
NSLOT = 8
N_MSK = 32
FC = D // 128
NT = T // 512
NTO = TO // 512
EPS = 1e-5
DI = 2816
DIC = DI // 128
DM = 4096
DMC = DM // 128
C6_1 = float(f32(1.0 / 6.0))
C6_2 = float(f32(1.0 / 6.0 - float(f32(1.0 / 6.0))))

ENGS = ("sync", "tensor", "vector", "scalar", "gpsimd")
DBG = {}


class Sem:
    __slots__ = ("name", "h", "v")

    def __init__(self, name):
        self.name, self.h, self.v = name, None, 0


class Gen:
    def __init__(self):
        self.ops = {e: [] for e in ENGS}

    def emit(self, eng, fn):
        self.ops[eng].append(fn)

    def wait(self, eng, sem, val=None):
        v = sem.v if val is None else val
        assert v <= sem.v, (sem.name, v, sem.v)
        if v > 0:
            self.emit(eng, lambda e, h=sem, vv=v: e.wait_ge(h.h, vv))

    def op(self, eng, build, sem=None, inc=1):
        own = getattr(self, "own_sems", {}).get(eng)
        if own is not None:
            self.wait(eng, own)
            sem, inc = own, 1
        if sem is not None:
            sem.v += inc

        def fn(e, build=build, sem=sem, inc=inc):
            r = build(e)
            if sem is not None:
                r.then_inc(sem.h, inc)
        self.emit(eng, fn)
        return None if sem is None else sem.v

    def dma(self, eng, out, in_, sem):
        sem.v += 16
        self.emit(eng, lambda e, o=out, i=in_, h=sem: e.dma_start(out=o, in_=i).then_inc(h.h, 16))
        return sem.v



def swizzle_strips(W):
    K, M = W.shape
    return np.ascontiguousarray(
        W.reshape(K // 128, 128, M // 128, 128).transpose(2, 1, 0, 3)).astype(np.float32)


def build_rope_tables():
    inv_freq = 1.0 / (10000.0 ** (np.arange(0, HD, 2, dtype=np.float64) / HD))
    t = np.arange(T, dtype=np.float64)
    fr = t[:, None] * inv_freq[None, :]
    cos = np.cos(fr).T
    sin = np.sin(fr).T
    C2 = np.concatenate([cos, cos, cos, cos], axis=0).astype(np.float32)
    S2 = np.concatenate([-sin, sin, -sin, sin], axis=0).astype(np.float32)
    return np.ascontiguousarray(C2), np.ascontiguousarray(S2)


def build_masks(own):
    out = np.zeros((N_MSK, 128, 512), dtype=np.int8)
    for si in range(4):
        qg = np.concatenate([256 * own[2 * si] + np.arange(256),
                             256 * own[2 * si + 1] + np.arange(256)])[None, :]
        for d2 in range(8):
            k = 128 * (8 * si + d2) + np.arange(128)[:, None]
            out[8 * si + d2] = (k > qg).astype(np.int8)
    return np.ascontiguousarray(out.transpose(1, 0, 2).reshape(128, N_MSK * 512))


def build_tri():
    k = np.arange(128)[:, None]
    q = np.arange(128)[None, :]
    return np.ascontiguousarray(np.where(k <= q, 0.0, -1e30).astype(np.float32))



def build_program(stop_phase=99):
    nc = bass.Bass("TRN2", target_bir_lowering=False)
    g = Gen()
    ctx = ExitStack()

    din = {}

    def inp(name, shape, dtype=F32):
        din[name] = nc.dram_tensor(name, list(shape), dtype, kind="ExternalInput").ap()
        return din[name]

    x_fm = inp("x_fm", [FC, 128, T])
    wq_s = inp("wq_s", [FC, 128, FC, 128])
    wk_s = inp("wk_s", [FC, 128, FC, 128])
    wv_s = inp("wv_s", [FC, 128, FC, 128])
    wp_s = inp("wp_s", [FC, 128, FC, 128])
    w0t_s = inp("w0t_s", [FC, 128, FC, 128])
    wqkv_s = inp("wqkv_s", [3 * FC, 128, FC, 128])
    wo_s = inp("wo_s", [FC, 128, FC, 128])
    w1_s = inp("w1_s", [DMC, 128, FC, 128])
    w2_s = inp("w2_s", [FC, 128, DMC, 128])
    wg_nat = inp("wg_nat", [DIC, 128, D])
    wu_nat = inp("wu_nat", [DIC, 128, D])
    wd_nat = inp("wd_nat", [FC, 128, DI])
    vecs = inp("vecs", [128, 128])
    c2t = inp("c2t", [128, T])
    s2t = inp("s2t", [128, T])
    c2o = inp("c2o", [128, TO])
    s2o = inp("s2o", [128, TO])
    iden = inp("iden", [128, 128])
    mtri = inp("mtri", [128, 128])
    mask8 = inp("mask8", [128, N_MSK * 512], I8)

    y_out = nc.dram_tensor("y_out", [FC, 128, TO], F32, kind="ExternalOutput").ap()

    def scratch(name, shape, dtype=F32):
        return nc.dram_tensor(name, list(shape), dtype, kind="Internal").ap()

    qst = [scratch("qst_h", [NPAIR, 128, T], BF16), scratch("qst_l", [NPAIR, 128, T], BF16)]
    kst = [scratch("kst_h", [NPAIR, 128, T], BF16), scratch("kst_l", [NPAIR, 128, T], BF16)]
    vst = scratch("vst", [NPAIR, T, 128])
    ast = [scratch("ast_h", [NPAIR, 128, T], BF16), scratch("ast_l", [NPAIR, 128, T], BF16)]
    qst1 = [scratch("qst1_h", [NPAIR, 128, TO], BF16), scratch("qst1_l", [NPAIR, 128, TO], BF16)]
    ast1 = [scratch("ast1_h", [NPAIR, 128, TO], BF16), scratch("ast1_l", [NPAIR, 128, TO], BF16)]
    y1st = scratch("y1st", [FC, 128, T])
    y2st = scratch("y2st", [FC, 128, T])
    y2own = scratch("y2own", [FC, 128, TO])
    y3own = scratch("y3own", [FC, 128, TO])
    n1st = scratch("n1st", [FC, 128, T])
    z2dbg = scratch("z2dbg", [FC, 128, TO])
    w1hl_s = scratch("w1hl_s", [DMC, 128, 2 * FC * 128], BF16)
    w2hl_s = scratch("w2hl_s", [FC, 128, 2 * DMC * 128], BF16)
    wgq_s = [scratch("wgq_sh", [DIC, 128, FC, 128], BF16),
             scratch("wgq_sl", [DIC, 128, FC, 128], BF16)]
    wuq_s = [scratch("wuq_sh", [DIC, 128, FC, 128], BF16),
             scratch("wuq_sl", [DIC, 128, FC, 128], BF16)]
    wdq_s = [scratch("wdq_sh", [FC, 128, DIC, 128], BF16),
             scratch("wdq_sl", [FC, 128, DIC, 128], BF16)]

    A1t = ctx.enter_context(nc.sbuf_tensor("A1", [128, 32768], F32))
    A2t = ctx.enter_context(nc.sbuf_tensor("A2", [128, 16384], F32))
    sb_vecs = ctx.enter_context(nc.sbuf_tensor("sb_vecs", [128, 128], F32))
    sb_iden = ctx.enter_context(nc.sbuf_tensor("sb_iden", [128, 128], F32))
    sb_tri = ctx.enter_context(nc.sbuf_tensor("sb_tri", [128, 128], F32))
    sb_ones64 = ctx.enter_context(nc.sbuf_tensor("sb_ones64", [128, 64], F32))
    sb_ones64b_t = ctx.enter_context(nc.sbuf_tensor("sb_ones64b", [128, 64], BF16))
    sb_ones1 = ctx.enter_context(nc.sbuf_tensor("sb_ones1", [128, 1], F32))
    sb_onesk = ctx.enter_context(nc.sbuf_tensor("sb_onesk", [1, 128], F32))
    sb_rows = ctx.enter_context(nc.sbuf_tensor("sb_rows", [1, 3072], F32))
    PB = [ctx.enter_context(nc.psum_tensor(f"PB{i}", [128, 1024], F32)) for i in range(4)]

    sems = {}
    for name in ("TEN", "VEC", "SCA", "LDQ", "STQ", "GPC"):
        s = Sem(name)
        s.h = ctx.enter_context(nc.semaphore(name))
        sems[name] = s
    TEN, VEC, SCA, LDQ, STQ, GPC = (sems[n] for n in ("TEN", "VEC", "SCA", "LDQ", "STQ", "GPC"))
    CSEMS = (TEN, VEC, SCA)
    g.fin_STQ = STQ
    g.fin_LDQ = LDQ
    g.own_sems = {"vector": VEC, "scalar": SCA}
    _semctr = [0]

    def mk_sem():
        s = Sem(f"S{_semctr[0]}")
        _semctr[0] += 1
        s.h = ctx.enter_context(nc.semaphore(s.name))
        g.all_slot_sems.append(s)
        return s
    g.all_slot_sems = []

    bankrel = {}

    def bank_wait(eng, *keys):
        for k in keys:
            for s, v in bankrel.get(k, ()):
                g.wait(eng, s, v)

    def bank_rel(key, *svs):
        bankrel[key] = list(svs) if svs else [(VEC, VEC.v)]

    def phase_barrier():
        g.wait("tensor", VEC)
        g.wait("tensor", SCA)
        g.wait("tensor", STQ)
        g.wait("vector", TEN)
        g.wait("vector", SCA)
        g.wait("vector", STQ)
        g.wait("scalar", TEN)
        g.wait("scalar", VEC)
        g.wait("scalar", STQ)
        g.wait("sync", TEN)
        g.wait("sync", VEC)
        g.wait("sync", SCA)
        g.wait("sync", STQ)
        g.wait("sync", LDQ)
        for s_ in g.all_slot_sems:
            g.wait("sync", s_)
        g.wait("gpsimd", VEC)
        g.wait("gpsimd", SCA)

    arena_off = {"A1": 0, "A2": 0}

    def carve(which, cols):
        o = arena_off[which]
        lim = 32768 if which == "A1" else 16384
        assert o + cols <= lim, (which, o, cols)
        arena_off[which] = o + cols
        t_ = A1t if which == "A1" else A2t
        return t_[:][:, o:o + cols]

    def reset(which):
        arena_off[which] = 0

    VC = {}
    off = 0
    for nm, n in (("bp", FC), ("bqkv", 3 * FC), ("g1", FC), ("b1", FC), ("bo", FC),
                  ("g2", FC), ("b2", FC), ("b1m", DMC), ("b2m", FC), ("selA", 1), ("selB", 1)):
        VC[nm] = off
        off += n
    assert off <= 128

    def vcol(nm, i=0):
        return sb_vecs[:][:, VC[nm] + i:VC[nm] + i + 1]

    def tt(out, in0, in1, op, sem=None, eng="vector"):
        return g.op(eng, lambda e, o=out, a=in0, b=in1, p=op:
                    e.tensor_tensor(out=o, in0=a, in1=b, op=p), sem)

    def ts(out, in0, s1, op0, s2=None, op1=None, sem=None):
        def bld(e, o=out, a=in0, s1=s1, s2=s2, op0=op0, op1=op1):
            kw = dict(out=o, in0=a, scalar1=s1, scalar2=s2, op0=op0)
            if op1 is not None:
                kw["op1"] = op1
            return e.tensor_scalar(**kw)
        return g.op("vector", bld, sem)

    def stt(out, in0, scalar, in1, op0, op1, sem=None):
        return g.op("vector", lambda e, o=out, a=in0, s=scalar, b=in1, p0=op0, p1=op1:
                    e.scalar_tensor_tensor(out=o, in0=a, scalar=s, in1=b, op0=p0, op1=p1), sem)

    def act(out, in_, func, scale=None, bias=None, sem=None):
        def bld(e, o=out, i=in_, f=func, sc=scale, bi=bias):
            kw = dict(out=o, in_=i, func=f)
            if sc is not None:
                kw["scale"] = sc
            if bi is not None:
                kw["bias"] = bi
            return e.activation(**kw)
        return g.op("scalar", bld, sem)

    def vcopy(out, in_, sem=None):
        return g.op("vector", lambda e, o=out, i=in_: e.tensor_copy(out=o, in_=i), sem)

    def mm(out, lhsT, rhs, start, stop, tile_position=None, sem=None):
        def bld(e, o=out, l=lhsT, r=rhs, st=start, sp=stop, tp=tile_position):
            kw = dict(out=o, lhsT=l, rhs=r, start=st, stop=sp)
            if tp is not None:
                kw["tile_position"] = tp
            return e.matmul(**kw)
        return g.op("tensor", bld, sem)

    def split_hl(hi, lo, src):
        vcopy(hi, src)
        tt(lo, src, hi, ALU.subtract)

    def transpose(out, in_, sem=None):
        return g.op("tensor", lambda e, o=out, i=in_:
                    e.transpose(out=o, in_=i, identity=sb_iden[:]), sem)

    def waitall(eng):
        for s in CSEMS:
            g.wait(eng, s)

    class Reader:
        def __init__(self, which, cols, n=3):
            self.bufs = [carve(which, cols) for _ in range(n)]
            self.sems = [mk_sem() for _ in range(n)]
            self.n = n
            self.i = 0
            self.snaps = []

        def rd(self, src, *consumers):
            i = self.i
            self.snaps.append(tuple((s, s.v) for s in CSEMS))
            if i >= self.n:
                for s, v in self.snaps[i - self.n + 1]:
                    g.wait("sync", s, v)
            b = self.bufs[i % self.n]
            out = b
            if src.dtype == BF16:
                out = out.bitcast(BF16)
            if len(src.shape) == 3:
                out = out.rearrange("p (k c) -> p k c", c=src.shape[2])
                src = src.rearrange("k p c -> p k c")
            v = g.dma("sync", out, src, self.sems[i % self.n])
            for c in consumers:
                g.wait(c, self.sems[i % self.n], v)
            self.i = i + 1
            return b

    qsl_state = [None]
    QSCR = {}

    def quantize_slice(X, Fcols, out, xv=None):
        nb = Fcols // 16
        assert Fcols <= 1024
        a = QSCR["A"][:, 0:Fcols]
        s1 = QSCR["S1"][:, 0:Fcols]
        s2 = QSCR["S2"][:, 0:Fcols]
        gsn = QSCR["G"][:, 0:Fcols]
        m = QSCR["M"][:, 0:nb]
        ss = QSCR["SS"][:, 0:nb]
        r = QSCR["R"][:, 0:nb]
        if qsl_state[0] is None:
            g.wait("scalar", VEC)
        else:
            g.wait("scalar", VEC, qsl_state[0])
        if xv is not None:
            g.wait("scalar", VEC, xv)
        act(a, X, AF.Abs)
        act(gsn, X, AF.Sign, sem=SCA)
        g.wait("vector", SCA)
        g.op("vector", lambda e, o=m, i=a: e.tensor_reduce(
            out=o, in_=i.rearrange("p (b k) -> p b k", k=16), axis=AX.X, op=ALU.max))
        ts(ss, m, C6_1, ALU.mult)
        stt(ss, m, C6_2, ss, ALU.mult, ALU.add)
        ts(ss, ss, float(f32(1e-12)), ALU.max, sem=VEC)
        g.wait("scalar", VEC)
        act(r, ss, AF.Ln)
        act(r, r, AF.Exp, scale=f32(-1.0), sem=SCA)
        g.wait("vector", SCA)
        rbc = r.unsqueeze(-1).broadcast_to((128, nb, 16))
        sbc = ss.unsqueeze(-1).broadcast_to((128, nb, 16))
        a3d = a.rearrange("p (b k) -> p b k", k=16)
        tt(a3d, a3d, rbc, ALU.mult)
        ts(s1, a, 0.25, ALU.is_gt)
        stt(s1, a, 0.75, s1, ALU.is_gt, ALU.add)
        stt(s1, a, 1.25, s1, ALU.is_gt, ALU.add)
        stt(s1, a, 1.75, s1, ALU.is_gt, ALU.add)
        ts(s2, a, 2.5, ALU.is_gt)
        stt(s2, a, 3.5, s2, ALU.is_gt, ALU.add)
        stt(s2, a, 5.0, s2, ALU.is_gt, ALU.add)
        stt(s2, a, 5.0, s2, ALU.is_gt, ALU.add)
        stt(s2, s1, 0.5, s2, ALU.mult, ALU.add)
        tt(s2.rearrange("p (b k) -> p b k", k=16),
           s2.rearrange("p (b k) -> p b k", k=16), sbc, ALU.mult)
        tt(out, s2, gsn, ALU.mult, sem=VEC)
        qsl_state[0] = VEC.v
        return VEC.v

    def quantize(X, Fcols, out, xv=None):
        v = None
        o = 0
        while o < Fcols:
            w = min(1024, Fcols - o)
            v = quantize_slice(X[:, o:o + w], w, out[:, o:o + w], xv=xv)
            o += w
        return v

    qw_steps = []

    def tick(n=1):
        for _ in range(n):
            if qw_steps:
                qw_steps.pop(0)()

    g.dma("sync", sb_vecs[:], vecs, LDQ)
    g.dma("sync", sb_iden[:], iden, LDQ)
    g.dma("sync", sb_tri[:], mtri, LDQ)
    v0 = LDQ.v
    g.op("gpsimd", lambda e: e.memset(sb_ones64[:], 1.0), GPC)
    g.op("gpsimd", lambda e: e.memset(sb_ones64b_t[:], 1.0), GPC)
    g.op("gpsimd", lambda e: e.memset(sb_ones1[:], 1.0), GPC)
    g.op("gpsimd", lambda e: e.memset(sb_onesk[:], 1.0), GPC)
    for eng in ("tensor", "vector", "scalar"):
        g.wait(eng, LDQ, v0)
        g.wait(eng, GPC)

    def ln_stats(read_chunk, ntt, dst_M, dst_R, sq_buf, barrier=True):
        if barrier:
            phase_barrier()
        bank_wait("tensor", "P2a", "P3a")
        for t in range(ntt):
            for fc in range(FC):
                ch = read_chunk(fc, t)
                sq = sq_buf[:, 512 * (fc % 2):512 * (fc % 2) + 512]
                if fc >= 2:
                    g.wait("scalar", TEN)
                sv = act(sq, ch, AF.Square, sem=SCA)
                mm(PB[2][:][0:1, 0:512], sb_ones1[:], ch, start=(fc == 0),
                   stop=(fc == FC - 1), sem=TEN)
                g.wait("tensor", SCA, sv)
                mm(PB[3][:][0:1, 0:512], sb_ones1[:], sq, start=(fc == 0),
                   stop=(fc == FC - 1), sem=TEN)
            vmm = TEN.v
            g.wait("vector", TEN, vmm)
            r_sum = sb_rows[:][0:1, 0:512]
            r_sq = sb_rows[:][0:1, 512:1024]
            r_mean = sb_rows[:][0:1, 1024:1536]
            r_var = sb_rows[:][0:1, 1536:2048]
            r_rstd = sb_rows[:][0:1, 2048:2560]
            vcopy(r_sum, PB[2][:][0:1, 0:512])
            vcopy(r_sq, PB[3][:][0:1, 0:512])
            ts(r_mean, r_sum, 1.0 / D, ALU.mult)
            tt(r_var, r_mean, r_mean, ALU.mult)
            stt(r_var, r_sq, 1.0 / D, r_var, ALU.mult, ALU.subtract, sem=VEC)
            vv = VEC.v
            g.wait("scalar", VEC, vv)
            act(r_rstd, r_var, AF.Ln, scale=f32(1.0), bias=f32(EPS))
            act(r_rstd, r_rstd, AF.Exp, scale=f32(-0.5), sem=SCA)
            vs = SCA.v
            g.wait("tensor", SCA, vs)
            g.wait("tensor", VEC, vv)
            mm(PB[2][:][:, 0:512], sb_onesk[:], r_mean, start=True, stop=True)
            mm(PB[3][:][:, 0:512], sb_onesk[:], r_rstd, start=True, stop=True, sem=TEN)
            vb = TEN.v
            g.wait("vector", TEN, vb)
            vcopy(dst_M[:, 512 * t:512 * t + 512], PB[2][:][:, 0:512])
            vcopy(dst_R[:, 512 * t:512 * t + 512], PB[3][:][:, 0:512], sem=VEC)
            g.wait("tensor", VEC)
            tick()
        bank_rel("P2a")
        bank_rel("P3a")

    def res_gemm(wsrc, n_k, rhs_src, res_src, bias_name, dst_st, ntt):
        phase_barrier()
        bank_wait("tensor", "P0a", "P1a")
        reset("A2")
        whl = carve("A2", FC * n_k * 128)
        wtmp = carve("A2", n_k * 128)
        wallsem = mk_sem()
        rhs_rh = Reader("A2", 256 * n_k, n=1)
        rhs_rl = Reader("A2", 256 * n_k, n=1)
        res_r = Reader("A2", 512, n=3)
        outb = [carve("A2", 512) for _ in range(2)]
        for oc in range(FC):
            g.wait("sync", VEC)
            v = g.dma("sync", wtmp.rearrange("p (k c) -> p k c", c=128),
                      wsrc[oc], wallsem)
            g.wait("vector", wallsem, v)
            wv = whl[:, n_k * 128 * oc:n_k * 128 * (oc + 1)].bitcast(BF16)
            split_hl(wv[:, 0:n_k * 128], wv[:, n_k * 128:2 * n_k * 128], wtmp)
        g.wait("tensor", VEC)
        idx = 0
        for t in range(ntt):
            rbh = rhs_rh.rd(rhs_src[0][:, :, 512 * t:512 * t + 512], "tensor").bitcast(BF16)
            rbl = rhs_rl.rd(rhs_src[1][:, :, 512 * t:512 * t + 512], "tensor").bitcast(BF16)
            for oc in range(FC):
                key = f"P{idx % 2}a"
                pv = PB[idx % 2][:][:, 0:512]
                bank_wait("tensor", key)
                wv = whl[:, n_k * 128 * oc:n_k * 128 * (oc + 1)].bitcast(BF16)
                n_mm = 3 * n_k
                i_mm = 0
                for k in range(n_k):
                    wh = wv[:, 128 * k:128 * k + 128]
                    wl = wv[:, n_k * 128 + 128 * k:n_k * 128 + 128 * (k + 1)]
                    ah = rbh[:, 512 * k:512 * k + 512]
                    al = rbl[:, 512 * k:512 * k + 512]
                    for (wop, aop) in ((wh, ah), (wh, al), (wl, ah)):
                        mm(pv, wop, aop, start=(i_mm == 0), stop=(i_mm == n_mm - 1),
                           sem=TEN if i_mm == n_mm - 1 else None)
                        i_mm += 1
                tv = TEN.v
                rb = res_r.rd(res_src[oc, :, 512 * t:512 * t + 512], "vector")
                ob = outb[idx % 2]
                if idx >= 2:
                    g.wait("vector", STQ)
                g.wait("vector", TEN, tv)
                if bias_name is not None:
                    ts(ob, pv, vcol(bias_name, oc), ALU.add)
                    bank_rel(key)
                    tt(ob, ob, rb, ALU.add)
                else:
                    tt(ob, pv, rb, ALU.add)
                    bank_rel(key)
                ev = VEC.v
                g.wait("gpsimd", VEC, ev)
                g.dma("gpsimd", dst_st[oc, :, 512 * t:512 * t + 512], ob, STQ)
                idx += 1

    reset("A1")
    reset("A2")
    XNh = carve("A1", FC * T // 2).bitcast(BF16)
    XNl = carve("A1", FC * T // 2).bitcast(BF16)
    M0 = carve("A2", T)
    R0 = carve("A2", T)
    sqb = carve("A2", 1024)
    rd1 = Reader("A2", 512, n=3)
    ln_stats(lambda fc, t_: rd1.rd(x_fm[fc, :, 512 * t_:512 * t_ + 512], "tensor", "scalar"),
             NT, M0, R0, sqb)

    rd2 = Reader("A2", 512, n=3)
    xtmp = carve("A2", 512)
    for t in range(NT):
        for fc in range(FC):
            ch = rd2.rd(x_fm[fc, :, 512 * t:512 * t + 512], "vector")
            c = slice(fc * T + 512 * t, fc * T + 512 * t + 512)
            tt(xtmp, ch, M0[:, 512 * t:512 * t + 512], ALU.subtract)
            tt(xtmp, xtmp, R0[:, 512 * t:512 * t + 512], ALU.mult)
            split_hl(XNh[:, c], XNl[:, c], xtmp)

    def xn_rhs(k, t):
        c = slice(k * T + 512 * t, k * T + 512 * t + 512)
        return (XNh[:, c], XNl[:, c])

    if stop_phase <= 1:
        return finish(nc, g, ctx)
    g.wait("sync", VEC)
    reset("A2")
    def stage_gemm(wsrc, n_oc, rhs_fn, ntt, evac_fn, wb=None, wsem=None, wtmp=None):
        if wb is None:
            wb = [carve("A2", FC * 128) for _ in range(2)]
        if wtmp is None:
            wtmp = carve("A2", FC * 128)
        if wsem is None:
            wsem = [mk_sem(), mk_sem()]
        phase_barrier()
        wsnap = []
        idx = 0
        for oc in range(n_oc):
            wsnap.append(TEN.v)
            if oc >= 2:
                g.wait("sync", TEN, wsnap[oc - 1])
            g.wait("sync", VEC)
            vldw = g.dma("sync", wtmp.rearrange("p (k c) -> p k c", c=128),
                         wsrc[oc], wsem[oc % 2])
            whl = wb[oc % 2].bitcast(BF16)
            whi = whl[:, 0:FC * 128]
            wlo = whl[:, FC * 128:2 * FC * 128]
            g.wait("vector", wsem[oc % 2], vldw)
            split_hl(whi, wlo, wtmp)
            vsp = VEC.v
            for t in range(ntt):
                key = f"P{idx % 2}a"
                pv = PB[idx % 2][:][:, 0:512]
                g.wait("tensor", VEC, vsp)
                bank_wait("tensor", key)
                n_mm = 3 * FC
                i_mm = 0
                for k in range(FC):
                    rh, rl = rhs_fn(k, t)
                    for (wop, aop) in ((whi, rh), (whi, rl), (wlo, rh)):
                        mm(pv, wop[:, 128 * k:128 * k + 128], aop,
                           start=(i_mm == 0), stop=(i_mm == n_mm - 1),
                           sem=TEN if i_mm == n_mm - 1 else None)
                        i_mm += 1
                tv = TEN.v
                g.wait("vector", TEN, tv)
                bank_rel(key, (VEC, evac_fn(oc, t, pv)))
                idx += 1

    def mk_copy_evac(dst_st):
        bufs = [carve("A2", 512) for _ in range(2)]
        state = {"i": 0}

        def evac(oc, t, pv):
            i = state["i"]
            b = bufs[i % 2].bitcast(BF16)
            if i >= 2:
                g.wait("vector", STQ)
            split_hl(b[:, 0:512], b[:, 512:1024], pv)
            ev = VEC.v
            g.wait("gpsimd", VEC, ev)
            g.dma("gpsimd", dst_st[0][oc, :, 512 * t:512 * t + 512], b[:, 0:512], STQ)
            g.dma("gpsimd", dst_st[1][oc, :, 512 * t:512 * t + 512], b[:, 512:1024], STQ)
            state["i"] = i + 1
            return ev
        return evac

    def mk_v_evac(dst_st, bias_name=None, bias_base=0):
        vb = carve("A2", 512)
        tb = carve("A2", 128)

        def evac(oc, t, pv):
            waitall("vector")
            if bias_name is None:
                vcopy(vb, pv, sem=VEC)
            else:
                ts(vb, pv, vcol(bias_name, bias_base + oc), ALU.add, sem=VEC)
            ev0 = VEC.v
            for i in range(4):
                g.wait("tensor", VEC)
                tv = transpose(PB[2][:][:, 0:128], vb[:, 128 * i:128 * i + 128], sem=TEN)
                g.wait("vector", TEN, tv)
                g.wait("vector", STQ)
                vcopy(tb, PB[2][:][:, 0:128], sem=VEC)
                g.wait("gpsimd", VEC)
                g.dma("gpsimd", dst_st[oc, 512 * t + 128 * i:512 * t + 128 * (i + 1), :],
                      tb, STQ)
            return ev0
        return evac

    wbA = [carve("A2", FC * 128) for _ in range(2)]
    wtmpA = carve("A2", FC * 128)
    wsemA = [mk_sem(), mk_sem()]
    stage_gemm(wq_s, FC, xn_rhs, NT, mk_copy_evac(qst), wb=wbA, wsem=wsemA, wtmp=wtmpA)
    stage_gemm(wk_s, FC, xn_rhs, NT, mk_copy_evac(kst), wb=wbA, wsem=wsemA, wtmp=wtmpA)
    stage_gemm(wv_s, FC, xn_rhs, NT, mk_v_evac(vst), wb=wbA, wsem=wsemA, wtmp=wtmpA)

    def att_slots(mode):
        slots = []
        if mode == "L0":
            for j in range(NSLOT):
                kts = [(kt, 0, None, None) for kt in range(4 * j)]
                kts += [(4 * j + d2, 128 * d2, 128 * d2, None) for d2 in range(4)]
                slots.append((512 * j, kts))
        else:
            for si in range(4):
                kts = [(kt, 0, None, None) for kt in range(8 * si)]
                kts += [(8 * si + d2, 0, None, 8 * si + d2) for d2 in range(8)]
                slots.append((512 * si, kts))
        return slots

    def attention(qsrc, ksrc, vsrc, astdst, mode):
        phase_barrier()
        bank_wait("tensor", "P0a", "P0b", "P1a", "P1b", "P2a", "P2b", "P3a", "P3b")
        reset("A1")
        reset("A2")
        QCOLS = T if mode == "L0" else TO
        slots = att_slots(mode)
        nslot = len(slots)
        cw = T // nslot
        Kbh = [carve("A1", T // 2).bitcast(BF16) for _ in range(2)]
        Kbl = [carve("A1", T // 2).bitcast(BF16) for _ in range(2)]
        Qbh = [carve("A1", QCOLS // 2).bitcast(BF16) for _ in range(2)]
        Qbl = [carve("A1", QCOLS // 2).bitcast(BF16) for _ in range(2)]
        Vb = [carve("A1", T) for _ in range(2)]
        Eb = [[carve("A2", 1024), carve("A2", 1024)] for _ in range(2)]
        accb = [carve("A2", 1024) for _ in range(2)]
        acchl = [carve("A2", 1024).bitcast(BF16) for _ in range(2)]
        rclb = [carve("A2", 512) for _ in range(2)]
        aob = [carve("A2", 512) for _ in range(2)]
        aosp = [carve("A2", 512).bitcast(BF16) for _ in range(2)]
        if not hasattr(attention, "KCS"):
            attention.KCS = [[mk_sem(), mk_sem()] for _ in range(2)]
        KCS = attention.KCS
        if mode == "L1":
            mskb = carve("A1", 4096).bitcast(I8)
            g.wait("sync", LDQ)
            vm = g.dma("sync", mskb, mask8, LDQ)
            g.wait("vector", LDQ, vm)

        def load_chunk(pp, s, ci):
            p = 2 * pp + s
            sem = KCS[s][ci % 2]
            g.wait("sync", sem)
            kc = slice(cw * ci, cw * (ci + 1))
            qc = slice(512 * ci, 512 * ci + 512)
            g.dma("sync", Kbh[s][:, kc], ksrc[0][p][:, kc], sem)
            g.dma("sync", Kbl[s][:, kc], ksrc[1][p][:, kc], sem)
            g.dma("sync", Qbh[s][:, qc], qsrc[0][p][:, qc], sem)
            g.dma("sync", Qbl[s][:, qc], qsrc[1][p][:, qc], sem)
            g.dma("sync", Vb[s][:, kc].rearrange("p (n c) -> p n c", c=128),
                  vsrc[p][kc, :].rearrange("(n p2) c -> p2 n c", p2=128), sem)

        prev_vav = [None, None]
        prev_ev = [None, None]
        prev_vr = [None, None]
        for pp in range(NPAIR // 2):
            waitall("sync")
            g.wait("sync", STQ)
            g.wait("sync", LDQ)
            snaps = []
            for ci in range(nslot):
                for s in (0, 1):
                    load_chunk(pp, s, ci)
                snaps.append([[(KCS[s][0], KCS[s][0].v), (KCS[s][1], KCS[s][1].v)]
                              for s in (0, 1)])
            for si, (q0, kts) in enumerate(slots):
                n_kt = len(kts)
                s_exp = [{}, {}]
                t_av = [{}, {}]
                a_v = [{}, {}]
                m_v = [{}, {}]

                def emit_av(s, ki, kts=kts, n_kt=n_kt, s_exp=s_exp, t_av=t_av,
                            m_v=m_v):
                    kt, off, _, _ = kts[ki]
                    E = Eb[s][ki % 2]
                    g.wait("tensor", SCA, s_exp[s][ki])
                    if off > 0:
                        g.wait("tensor", VEC, m_v[s][ki])
                    if ki == 0 and prev_ev[s] is not None:
                        g.wait("tensor", VEC, prev_ev[s])
                    vb = Vb[s][:, 128 * kt:128 * kt + 128]
                    mm(PB[s][:][0:64, 0:512], vb[:, 0:64], E[:, 0:512],
                       start=(ki == 0), stop=(ki == n_kt - 1), tile_position=(0, 0))
                    t_av[s][ki] = mm(PB[s][:][64:128, 512:1024], vb[:, 64:128],
                                     E[:, 512:1024], start=(ki == 0),
                                     stop=(ki == n_kt - 1), tile_position=(0, 64),
                                     sem=TEN)

                for s in (0, 1):
                    g.op("vector", lambda e, a=accb[s]: e.memset(a[:], 0.0), VEC)
                    for (sm, v) in snaps[si][s]:
                        g.wait("tensor", sm, v)
                for ki, (kt, off, trio, mki) in enumerate(kts):
                    for s in (0, 1):
                        if ki >= 1:
                            g.wait("tensor", SCA, s_exp[s][ki - 1])
                        elif prev_vr[s] is not None:
                            g.wait("tensor", SCA, prev_vr[s])
                        ks = slice(128 * kt, 128 * kt + 128)
                        qs = slice(q0 + off, q0 + 512)
                        pb = PB[2 + s][:]
                        for i_c, (kk, qq) in enumerate(
                                ((Kbh[s], Qbh[s]), (Kbh[s], Qbl[s]), (Kbl[s], Qbh[s]))):
                            mm(pb[:, off:512], kk[0:64, ks], qq[0:64, qs],
                               start=(i_c == 0), stop=(i_c == 2))
                            mm(pb[:, 512 + off:1024], kk[64:128, ks], qq[64:128, qs],
                               start=(i_c == 0), stop=(i_c == 2),
                               sem=TEN if i_c == 2 else None)
                        t_sc = TEN.v
                        pb3 = pb[:, 0:1024].rearrange("p (h q) -> p h q", q=512)
                        if mki is not None:
                            g.wait("vector", TEN, t_sc)
                            stt(pb3, mskb[:, 512 * mki:512 * mki + 512]
                                .unsqueeze(1).broadcast_to((128, 2, 512)),
                                -1e30, pb3, ALU.mult, ALU.add)
                            g.wait("scalar", VEC)
                        elif trio is not None:
                            g.wait("vector", TEN, t_sc)
                            tt(pb3[:, :, trio:trio + 128], pb3[:, :, trio:trio + 128],
                               sb_tri[:].unsqueeze(1).broadcast_to((128, 2, 128)),
                               ALU.add)
                            g.wait("scalar", VEC)
                        else:
                            g.wait("scalar", TEN, t_sc)
                        E = Eb[s][ki % 2]
                        if ki >= 2:
                            g.wait("scalar", TEN, t_av[s][ki - 2])
                            g.wait("scalar", VEC, a_v[s][ki - 2])
                        elif prev_vav[s] is not None:
                            g.wait("scalar", TEN, prev_vav[s])
                        E3 = E.rearrange("p (h q) -> p h q", q=512)
                        s_exp[s][ki] = act(E3[:, :, off:512], pb3[:, :, off:512],
                                           AF.Exp, scale=f32(0.125), sem=SCA)
                        if off > 0:
                            g.op("vector", lambda e, a=E3, o=off:
                                 e.memset(a[:, :, 0:o], 0.0), VEC)
                            m_v[s][ki] = VEC.v
                        g.wait("vector", SCA, s_exp[s][ki])
                        a3 = accb[s].rearrange("p (h q) -> p h q", q=512)
                        tt(a3, a3, E3, ALU.add)
                        a_v[s][ki] = VEC.v
                    for s in (0, 1):
                        if ki >= 1:
                            emit_av(s, ki - 1)
                for s in (0, 1):
                    emit_av(s, n_kt - 1)
                vav = [t_av[s][n_kt - 1] for s in (0, 1)]
                vsp = [0, 0]
                for s in (0, 1):
                    g.wait("vector", STQ)
                    ahl = acchl[s]
                    split_hl(ahl[:, 0:1024], ahl[:, 1024:2048], accb[s])
                    vsp[s] = VEC.v
                vsum = [0, 0]
                for s in (0, 1):
                    g.wait("tensor", VEC, vsp[s])
                    g.wait("tensor", SCA, s_exp[s][n_kt - 1])
                    ahl = acchl[s]
                    pbs = PB[2 + s][:]
                    mm(pbs[0:64, 0:512], sb_ones64b_t[:], ahl[:, 0:512],
                       start=True, stop=False, tile_position=(0, 0))
                    mm(pbs[0:64, 0:512], sb_ones64b_t[:], ahl[:, 1024:1536],
                       start=False, stop=True, tile_position=(0, 0))
                    mm(pbs[64:128, 0:512], sb_ones64b_t[:], ahl[:, 512:1024],
                       start=True, stop=False, tile_position=(0, 64))
                    mm(pbs[64:128, 0:512], sb_ones64b_t[:], ahl[:, 1536:2048],
                       start=False, stop=True, tile_position=(0, 64), sem=TEN)
                    vsum[s] = TEN.v
                vr = [0, 0]
                for s in (0, 1):
                    g.wait("scalar", TEN, vsum[s])
                    act(rclb[s], PB[2 + s][:][:, 0:512], AF.Ln)
                    act(rclb[s], rclb[s], AF.Exp, scale=f32(-1.0), sem=SCA)
                    vr[s] = SCA.v
                for s in (0, 1):
                    g.wait("vector", SCA, vr[s])
                    g.wait("vector", TEN, vav[s])
                    tt(aob[s][0:64, :], PB[s][:][0:64, 0:512], rclb[s][0:64, :],
                       ALU.mult)
                    tt(aob[s][64:128, :], PB[s][:][64:128, 512:1024],
                       rclb[s][64:128, :], ALU.mult)
                    av = aosp[s]
                    split_hl(av[:, 0:512], av[:, 512:1024], aob[s])
                    ev = VEC.v
                    g.wait("gpsimd", VEC, ev)
                    p = 2 * pp + s
                    g.dma("gpsimd", astdst[0][p, :, q0:q0 + 512], av[:, 0:512], STQ)
                    g.dma("gpsimd", astdst[1][p, :, q0:q0 + 512], av[:, 512:1024], STQ)
                    prev_vav[s] = vav[s]
                    prev_ev[s] = ev
                    prev_vr[s] = vr[s]
        for kk in ("P0a", "P0b", "P1a", "P1b", "P2a", "P2b", "P3a", "P3b"):
            bank_rel(kk, (VEC, VEC.v), (SCA, SCA.v))

    attention(qst, kst, vst, ast, "L0")

    if stop_phase <= 3:
        return finish(nc, g, ctx)
    res_gemm(wp_s, FC, ast, x_fm, "bp", y1st, NT)

    natsem = [mk_sem(), mk_sem()]
    qw_stsem = [mk_sem(), mk_sem()]

    def build_qw_steps():
        natb_w = [A2t[:][:, 8704:9728], A2t[:][:, 9728:10752]]
        QSCR.update(A=A2t[:][:, 10752:11776], S1=A2t[:][:, 11776:12800],
                    S2=A2t[:][:, 12800:13824], G=A2t[:][:, 13824:14848],
                    M=A2t[:][:, 14848:14912], SS=A2t[:][:, 14912:14976],
                    R=A2t[:][:, 14976:15040])
        trbig_w = [A2t[:][:, 15040:15552], A2t[:][:, 15552:16064]]
        st = {"lastten": {}, "split": {}, "store": {}, "gi": [0], "first": [True]}

        def mk_load(src, si, w):
            def f():
                sem = natsem[si % 2]
                g.wait("sync", sem)
                lt = st["lastten"].get(si % 2)
                if lt is not None:
                    g.wait("sync", TEN, lt)
                st[("ldv", si % 2)] = g.dma("sync", natb_w[si % 2][:, 0:w], src, sem)
            return f

        def mk_quant(si, w):
            def f():
                sem = natsem[si % 2]
                v = st[("ldv", si % 2)]
                g.wait("vector", sem, v)
                g.wait("scalar", sem, v)
                st[("qv", si)] = quantize(natb_w[si % 2][:, 0:w], w,
                                          natb_w[si % 2][:, 0:w])
            return f

        def mk_group(si, dst, oc, k0g, kloc, kn):
            def f():
                if st["first"][0]:
                    bank_wait("tensor", "P2b", "P3b")
                    st["first"][0] = False
                gp = st["gi"][0] % 2
                st["gi"][0] += 1
                pb = PB[2 + gp][:][:, 512:1024]
                trv = trbig_w[gp].bitcast(BF16)
                ssem = qw_stsem[gp]
                pv = st["split"].get(gp)
                if pv is not None:
                    g.wait("tensor", VEC, pv)
                g.wait("tensor", VEC, st[("qv", si)])
                tv = 0
                for k in range(kn):
                    tv = transpose(pb[:, 128 * k:128 * k + 128],
                                   natb_w[si % 2][:, 128 * (kloc + k):
                                                  128 * (kloc + k) + 128], sem=TEN)
                g.wait("vector", TEN, tv)
                sv_ = st["store"].get(gp)
                if sv_ is not None:
                    g.wait("vector", ssem, sv_)
                split_hl(trv[:, 0:128 * kn], trv[:, 128 * kn:256 * kn],
                         pb[:, 0:128 * kn])
                st["split"][gp] = VEC.v
                g.wait("gpsimd", VEC, VEC.v)
                g.dma("gpsimd", dst[0][oc, :, k0g:k0g + kn, :]
                      .rearrange("p k c -> p (k c)"), trv[:, 0:128 * kn], ssem)
                g.dma("gpsimd", dst[1][oc, :, k0g:k0g + kn, :]
                      .rearrange("p k c -> p (k c)"), trv[:, 128 * kn:256 * kn], ssem)
                st["store"][gp] = ssem.v
                st["lastten"][si % 2] = TEN.v
            return f

        si = 0
        for (nat, dst, n_och, icols) in ((wg_nat, wgq_s, DIC, D),
                                         (wu_nat, wuq_s, DIC, D),
                                         (wd_nat, wdq_s, FC, DI)):
            for oc in range(n_och):
                for off in range(0, icols, 1024):
                    w = min(1024, icols - off)
                    qw_steps.append(mk_load(nat[oc][:, off:off + w], si, w))
                    qw_steps.append(mk_quant(si, w))
                    for kloc in range(0, w // 128, 4):
                        kn = min(4, w // 128 - kloc)
                        qw_steps.append(mk_group(si, dst, oc, off // 128 + kloc,
                                                 kloc, kn))
                    si += 1

    build_qw_steps()

    if stop_phase <= 4:
        return finish(nc, g, ctx)
    reset("A1")
    ZNh = carve("A1", FC * T // 2).bitcast(BF16)
    ZNl = carve("A1", FC * T // 2).bitcast(BF16)
    reset("A2")
    M1 = carve("A2", 512)
    R1 = carve("A2", 512)
    sqb = carve("A2", 1024)
    rd3 = Reader("A2", 512, n=3)
    rd4 = Reader("A2", 512, n=3)
    ztmp = carve("A2", 512)
    for t in range(NT):
        ln_stats(lambda fc, t_, t=t: rd3.rd(y1st[fc, :, 512 * t:512 * t + 512],
                                            "tensor", "scalar"),
                 1, M1, R1, sqb, barrier=(t == 0))
        for fc in range(FC):
            ch = rd4.rd(y1st[fc, :, 512 * t:512 * t + 512], "vector")
            c = slice(fc * T + 512 * t, fc * T + 512 * t + 512)
            tt(ztmp, ch, M1, ALU.subtract)
            tt(ztmp, ztmp, R1, ALU.mult)
            split_hl(ZNh[:, c], ZNl[:, c], ztmp)
            tick(2)

    def res_gemm_sbuf(wsrc, rhs_fn, res_src, bias_name, dst_st, ntt, sign_w=False):
        phase_barrier()
        bank_wait("tensor", "P0a", "P1a")
        reset("A2")
        res_r = Reader("A2", 512, n=3)
        outb = [carve("A2", 512) for _ in range(2)]
        wallsem = mk_sem()
        if sign_w:
            wsb = carve("A2", FC * FC * 64).bitcast(BF16)
            wtmp = carve("A2", FC * 128)
            for oc in range(FC):
                g.wait("sync", SCA)
                v = g.dma("sync", wtmp.rearrange("p (k c) -> p k c", c=128),
                          wsrc[oc], wallsem)
                g.wait("scalar", wallsem, v)
                act(wsb[:, 1024 * oc:1024 * (oc + 1)], wtmp, AF.Sign, sem=SCA)
            g.wait("tensor", SCA)
        else:
            wsb = carve("A2", FC * FC * 128)
            wtmp = carve("A2", FC * 128)
            for oc in range(FC):
                g.wait("sync", VEC)
                v = g.dma("sync", wtmp.rearrange("p (k c) -> p k c", c=128),
                          wsrc[oc], wallsem)
                g.wait("vector", wallsem, v)
                wv = wsb[:, 1024 * oc:1024 * (oc + 1)].bitcast(BF16)
                split_hl(wv[:, 0:1024], wv[:, 1024:2048], wtmp)
            g.wait("tensor", VEC)
        idx = 0
        for t in range(ntt):
            for oc in range(FC):
                key = f"P{idx % 2}a"
                pv = PB[idx % 2][:][:, 0:512]
                bank_wait("tensor", key)
                if sign_w:
                    n_mm = 2 * FC
                    i_mm = 0
                    for k in range(FC):
                        sw = wsb[:, 1024 * oc + 128 * k:1024 * oc + 128 * (k + 1)]
                        rh, rl = rhs_fn(k, t)
                        for aop in (rh, rl):
                            mm(pv, sw, aop, start=(i_mm == 0), stop=(i_mm == n_mm - 1),
                               sem=TEN if i_mm == n_mm - 1 else None)
                            i_mm += 1
                else:
                    wv = wsb[:, 1024 * oc:1024 * (oc + 1)].bitcast(BF16)
                    n_mm = 3 * FC
                    i_mm = 0
                    for k in range(FC):
                        wh = wv[:, 128 * k:128 * k + 128]
                        wl = wv[:, 1024 + 128 * k:1024 + 128 * (k + 1)]
                        rh, rl = rhs_fn(k, t)
                        for (wop, aop) in ((wh, rh), (wh, rl), (wl, rh)):
                            mm(pv, wop, aop, start=(i_mm == 0), stop=(i_mm == n_mm - 1),
                               sem=TEN if i_mm == n_mm - 1 else None)
                            i_mm += 1
                tv = TEN.v
                rb = res_r.rd(res_src[oc, :, 512 * t:512 * t + 512], "vector")
                ob = outb[idx % 2]
                if idx >= 2:
                    g.wait("vector", STQ)
                g.wait("vector", TEN, tv)
                if bias_name is not None:
                    ts(ob, pv, vcol(bias_name, oc), ALU.add)
                    bank_rel(key)
                    tt(ob, ob, rb, ALU.add)
                else:
                    tt(ob, pv, rb, ALU.add)
                    bank_rel(key)
                ev = VEC.v
                g.wait("gpsimd", VEC, ev)
                g.dma("gpsimd", dst_st[oc, :, 512 * t:512 * t + 512], ob, STQ)
                idx += 1
                tick(2)

    def zn_rhs(k, t):
        c = slice(k * T + 512 * t, k * T + 512 * t + 512)
        return (ZNh[:, c], ZNl[:, c])

    res_gemm_sbuf(w0t_s, zn_rhs, y1st, None, y2st, NT, sign_w=True)

    if stop_phase <= 5:
        return finish(nc, g, ctx)
    phase_barrier()
    reset("A2")
    rda = Reader("A2", 256, n=3)
    rdb = Reader("A2", 256, n=3)
    bo_ = [carve("A2", 256) for _ in range(2)]
    st_hist = []
    ii = 0
    for oc in range(FC):
        for j in range(NSLOT):
            ba = rda.rd(y2st[oc, :, 256 * OWN_A[j]:256 * OWN_A[j] + 256], "vector")
            bb = rdb.rd(y2st[oc, :, 256 * OWN_B[j]:256 * OWN_B[j] + 256], "vector")
            ob = bo_[ii % 2]
            if ii >= 2:
                g.wait("vector", STQ)
            ts(ob, ba, vcol("selA"), ALU.mult)
            stt(ob, bb, vcol("selB"), ob, ALU.mult, ALU.add, sem=VEC)
            g.wait("gpsimd", VEC)
            st_hist.append(g.dma("gpsimd", y2own[oc, :, 256 * j:256 * j + 256], ob, STQ))
            ii += 1
            tick(2)

    if stop_phase <= 6:
        return finish(nc, g, ctx)
    reset("A1")
    M2 = carve("A1", T)
    R2 = carve("A1", T)
    P2 = R2
    reset("A2")
    sqb = carve("A2", 1024)
    rd5 = Reader("A2", 512, n=3)
    ln_stats(lambda fc, t_: rd5.rd(y2st[fc, :, 512 * t_:512 * t_ + 512], "tensor", "scalar"),
             NT, M2, R2, sqb)
    tmpc = carve("A2", 512)
    for t in range(NT):
        c = slice(512 * t, 512 * t + 512)
        g.wait("vector", SCA)
        tt(tmpc, R2[:, c], R2[:, c], ALU.mult)
        ts(tmpc, tmpc, -float(EPS), ALU.mult, float(1.0 + EPS), ALU.add, sem=VEC)
        g.wait("scalar", VEC)
        act(tmpc, tmpc, AF.Ln)
        act(tmpc, tmpc, AF.Exp, scale=f32(-0.5), sem=SCA)
        g.wait("vector", SCA)
        tt(P2[:, c], R2[:, c], tmpc, ALU.mult, sem=VEC)
    g.wait("scalar", VEC)

    rd6 = Reader("A2", 512, n=3)
    n1b = [carve("A2", 512) for _ in range(2)]
    st_hist = []
    ii = 0
    for t in range(NT):
        for fc in range(FC):
            ch = rd6.rd(y2st[fc, :, 512 * t:512 * t + 512], "vector")
            nb = n1b[ii % 2]
            if ii >= 2:
                g.wait("vector", STQ)
            tt(nb, ch, M2[:, 512 * t:512 * t + 512], ALU.subtract)
            tt(nb, nb, P2[:, 512 * t:512 * t + 512], ALU.mult)
            ts(nb, nb, vcol("g1", fc), ALU.mult, vcol("b1", fc), ALU.add, sem=VEC)
            g.wait("gpsimd", VEC)
            st_hist.append(g.dma("gpsimd", n1st[fc, :, 512 * t:512 * t + 512], nb, STQ))
            ii += 1
            tick(2)

    while qw_steps:
        qw_steps.pop(0)()
    for s_ in qw_stsem:
        for eng in ("vector", "scalar", "tensor"):
            g.wait(eng, s_)

    if stop_phase <= 7:
        return finish(nc, g, ctx)
    reset("A1")
    N1h = carve("A1", FC * T // 2).bitcast(BF16)
    N1l = carve("A1", FC * T // 2).bitcast(BF16)
    reset("A2")
    phase_barrier()
    rd7 = Reader("A2", 512, n=1)
    g.wait("sync", STQ)
    for t in range(NT):
        for fc in range(FC):
            ch = rd7.rd(n1st[fc, :, 512 * t:512 * t + 512], "vector")
            c = slice(fc * T + 512 * t, fc * T + 512 * t + 512)
            split_hl(N1h[:, c], N1l[:, c], ch)

    def n1_rhs(k, t):
        c = slice(k * T + 512 * t, k * T + 512 * t + 512)
        return (N1h[:, c], N1l[:, c])

    C2b_ = carve("A2", T)
    S2b_ = carve("A2", T)
    g.wait("sync", LDQ)
    v1_ = g.dma("sync", C2b_, c2t, LDQ)
    v2_ = g.dma("sync", S2b_, s2t, LDQ)
    g.wait("vector", LDQ)

    def mk_rope_evac(dst_st, ctab, stab, bias_base):
        kpre = [carve("A2", 512) for _ in range(2)]
        krot = [carve("A2", 512) for _ in range(2)]
        kout1 = carve("A2", 512)
        kout = [kout1, kout1]
        ksp = [carve("A2", 512) for _ in range(2)]
        rsem = mk_sem()
        state = {"i": 0}

        def evac(oc, t, pv):
            i = state["i"]
            b0, b1, b2 = kpre[i % 2], krot[i % 2], kout[i % 2]
            if i >= 2:
                g.wait("vector", rsem, 64 * (i - 1))
                g.wait("vector", STQ)
            ts(b0, pv, vcol("bqkv", bias_base + oc), ALU.add)
            ev0 = VEC.v
            g.wait("gpsimd", VEC, ev0)
            for (d0, s0) in ((0, 32), (32, 0), (64, 96), (96, 64)):
                g.dma("gpsimd", b1[d0:d0 + 32, :], b0[s0:s0 + 32, :], rsem)
            c = slice(512 * t, 512 * t + 512)
            g.wait("vector", rsem, 64 * (i + 1))
            tt(b2, b0, ctab[:, c], ALU.mult)
            tt(b1, b1, stab[:, c], ALU.mult)
            tt(b2, b2, b1, ALU.add)
            bh = ksp[i % 2].bitcast(BF16)
            split_hl(bh[:, 0:512], bh[:, 512:1024], b2)
            ev = VEC.v
            g.wait("gpsimd", VEC, ev)
            g.dma("gpsimd", dst_st[0][oc, :, c], bh[:, 0:512], STQ)
            g.dma("gpsimd", dst_st[1][oc, :, c], bh[:, 512:1024], STQ)
            state["i"] = i + 1
            return ev0
        return evac

    class WView:
        def __init__(self, base):
            self.base = base

        def __getitem__(self, i):
            return wqkv_s[self.base + i]

    wb_c2 = [carve("A2", FC * 128) for _ in range(2)]
    wtmp_c2 = carve("A2", FC * 128)
    wsem_c2 = [mk_sem(), mk_sem()]
    stage_gemm(WView(FC), FC, n1_rhs, NT, mk_rope_evac(kst, C2b_, S2b_, FC),
               wb=wb_c2, wsem=wsem_c2, wtmp=wtmp_c2)
    stage_gemm(WView(2 * FC), FC, n1_rhs, NT, mk_v_evac(vst, "bqkv", 2 * FC),
               wb=wb_c2, wsem=wsem_c2, wtmp=wtmp_c2)

    if stop_phase <= 8:
        return finish(nc, g, ctx)
    reset("A1")
    M2o = carve("A1", TO)
    R2o = carve("A1", TO)
    P2o = R2o
    N1oh = carve("A1", FC * TO // 2).bitcast(BF16)
    N1ol = carve("A1", FC * TO // 2).bitcast(BF16)
    reset("A2")
    sqb = carve("A2", 1024)
    g.wait("sync", STQ)
    rd8 = Reader("A2", 512, n=3)
    ln_stats(lambda fc, t_: rd8.rd(y2own[fc, :, 512 * t_:512 * t_ + 512], "tensor", "scalar"),
             NTO, M2o, R2o, sqb)
    tmpo = carve("A2", 512)
    for t in range(NTO):
        c = slice(512 * t, 512 * t + 512)
        g.wait("vector", SCA)
        tt(tmpo, R2o[:, c], R2o[:, c], ALU.mult)
        ts(tmpo, tmpo, -float(EPS), ALU.mult, float(1.0 + EPS), ALU.add, sem=VEC)
        g.wait("scalar", VEC)
        act(tmpo, tmpo, AF.Ln)
        act(tmpo, tmpo, AF.Exp, scale=f32(-0.5), sem=SCA)
        g.wait("vector", SCA)
        tt(P2o[:, c], R2o[:, c], tmpo, ALU.mult, sem=VEC)
    g.wait("scalar", VEC)
    rd9 = Reader("A2", 512, n=3)
    ntmp = carve("A2", 512)
    for t in range(NTO):
        for fc in range(FC):
            ch = rd9.rd(y2own[fc, :, 512 * t:512 * t + 512], "vector")
            c = slice(fc * TO + 512 * t, fc * TO + 512 * t + 512)
            tt(ntmp, ch, M2o[:, 512 * t:512 * t + 512], ALU.subtract)
            tt(ntmp, ntmp, P2o[:, 512 * t:512 * t + 512], ALU.mult)
            ts(ntmp, ntmp, vcol("g1", fc), ALU.mult, vcol("b1", fc), ALU.add)
            split_hl(N1oh[:, c], N1ol[:, c], ntmp)

    def n1o_rhs(k, t):
        c = slice(k * TO + 512 * t, k * TO + 512 * t + 512)
        return (N1oh[:, c], N1ol[:, c])

    C2ob = carve("A2", TO)
    S2ob = carve("A2", TO)
    g.wait("sync", LDQ)
    v1_ = g.dma("sync", C2ob, c2o, LDQ)
    v2_ = g.dma("sync", S2ob, s2o, LDQ)
    g.wait("vector", LDQ)
    stage_gemm(WView(0), FC, n1o_rhs, NTO, mk_rope_evac(qst1, C2ob, S2ob, 0))

    if stop_phase <= 9:
        return finish(nc, g, ctx)
    reset("A1")
    attention(qst1, kst, vst, ast1, "L1")

    if stop_phase <= 10:
        return finish(nc, g, ctx)
    reset("A1")
    Z2 = carve("A1", FC * TO)
    M2r = carve("A1", TO)
    R2r = carve("A1", TO)
    reset("A2")
    sqb = carve("A2", 1024)
    rd10 = Reader("A2", 512, n=3)
    ln_stats(lambda fc, t_: rd10.rd(y2own[fc, :, 512 * t_:512 * t_ + 512], "tensor", "scalar"),
             NTO, M2r, R2r, sqb)

    def wo_gemm():
        phase_barrier()
        bank_wait("tensor", "P0a", "P1a")
        whl = carve("A1", FC * FC * 128)
        wtmp = carve("A2", FC * 128)
        wallsem = mk_sem()
        rhs_rh = Reader("A2", FC * 256, n=1)
        rhs_rl = Reader("A2", FC * 256, n=1)
        res_r = Reader("A2", 512, n=3)
        zb = [carve("A2", 512) for _ in range(2)]
        for oc in range(FC):
            g.wait("sync", VEC)
            v = g.dma("sync", wtmp.rearrange("p (k c) -> p k c", c=128),
                      wo_s[oc], wallsem)
            g.wait("vector", wallsem, v)
            wv = whl[:, 1024 * oc:1024 * (oc + 1)].bitcast(BF16)
            split_hl(wv[:, 0:1024], wv[:, 1024:2048], wtmp)
        g.wait("tensor", VEC)
        idx = 0
        for t in range(NTO):
            rbh = rhs_rh.rd(ast1[0][:, :, 512 * t:512 * t + 512], "tensor").bitcast(BF16)
            rbl = rhs_rl.rd(ast1[1][:, :, 512 * t:512 * t + 512], "tensor").bitcast(BF16)
            for oc in range(FC):
                key = f"P{idx % 2}a"
                pv = PB[idx % 2][:][:, 0:512]
                bank_wait("tensor", key)
                wv = whl[:, 1024 * oc:1024 * (oc + 1)].bitcast(BF16)
                i_mm = 0
                for k in range(FC):
                    wh = wv[:, 128 * k:128 * k + 128]
                    wl = wv[:, 1024 + 128 * k:1024 + 128 * (k + 1)]
                    ah = rbh[:, 512 * k:512 * k + 512]
                    al = rbl[:, 512 * k:512 * k + 512]
                    for (wop, aop) in ((wh, ah), (wh, al), (wl, ah)):
                        mm(pv, wop, aop, start=(i_mm == 0), stop=(i_mm == 3 * FC - 1),
                           sem=TEN if i_mm == 3 * FC - 1 else None)
                        i_mm += 1
                tv = TEN.v
                rb = res_r.rd(y2own[oc, :, 512 * t:512 * t + 512], "vector")
                zbb = zb[idx % 2]
                c = slice(512 * t, 512 * t + 512)
                g.wait("vector", TEN, tv)
                tt(zbb, rb, M2r[:, c], ALU.subtract)
                tt(zbb, zbb, R2r[:, c], ALU.mult)
                dst = Z2[:, oc * TO + 512 * t:oc * TO + 512 * t + 512]
                ts(dst, pv, vcol("bo", oc), ALU.add)
                bank_rel(key)
                tt(dst, dst, zbb, ALU.add)
                idx += 1

    wo_gemm()
    phase_barrier()
    for fc in range(FC):
        g.wait("gpsimd", VEC)
        g.dma("gpsimd", z2dbg[fc], Z2[:, fc * TO:(fc + 1) * TO], STQ)

    if stop_phase <= 11:
        return finish(nc, g, ctx)
    w1sem = [mk_sem(), mk_sem()]
    w2sem = [mk_sem(), mk_sem()]
    wstg_sem = mk_sem()

    def stage_whl():
        phase_barrier()
        reset("A2")
        wtmp = [carve("A2", DMC * 128) for _ in range(2)]
        whl = [carve("A2", DMC * 128) for _ in range(2)]
        lsem = [wstg_sem, mk_sem()]
        ssem = [mk_sem(), mk_sem()]
        split_v = {}
        store_v = {}
        ii = 0
        for (wsrc, dst, n_oc, n_k) in ((w1_s, w1hl_s, DMC, FC),
                                       (w2_s, w2hl_s, FC, DMC)):
            for oc in range(n_oc):
                bi = ii % 2
                g.wait("sync", lsem[bi])
                sv = split_v.get(bi)
                if sv is not None:
                    g.wait("sync", VEC, sv)
                v = g.dma("sync", wtmp[bi][:, 0:n_k * 128]
                          .rearrange("p (k c) -> p k c", c=128), wsrc[oc], lsem[bi])
                g.wait("vector", lsem[bi], v)
                stv = store_v.get(bi)
                if stv is not None:
                    g.wait("vector", ssem[bi], stv)
                wv = whl[bi][:, 0:n_k * 128].bitcast(BF16)
                split_hl(wv[:, 0:n_k * 128], wv[:, n_k * 128:2 * n_k * 128],
                         wtmp[bi][:, 0:n_k * 128])
                split_v[bi] = VEC.v
                g.wait("gpsimd", VEC, VEC.v)
                g.dma("gpsimd", dst[oc], wv[:, 0:2 * n_k * 128], ssem[bi])
                store_v[bi] = ssem[bi].v
                ii += 1
        for s_ in ssem:
            for eng in ("vector", "scalar", "tensor"):
                g.wait(eng, s_)

    stage_whl()

    reset("A2")
    M3 = carve("A2", TO)
    R3 = carve("A2", TO)
    sqb = carve("A2", 1024)

    def z2_chunk(fc, t):
        return Z2[:, fc * TO + 512 * t:fc * TO + 512 * t + 512]

    ln_stats(z2_chunk, NTO, M3, R3, sqb)

    n2b = carve("A2", FC * 512)
    w1b = [carve("A2", FC * 128) for _ in range(2)]
    gtmp = carve("A2", 512)
    ob2 = carve("A2", 256)
    res_r2 = Reader("A2", 256, n=3)

    def mlp():
        phase_barrier()
        bank_wait("tensor", "P0a", "P1a", "P2a")
        w2b = [A1t[:][:, 16384:16384 + 4096], A1t[:][:, 20480:20480 + 4096]]
        hb = A1t[:][:, 24576:24576 + DMC * 256].bitcast(BF16)
        hh = hb[:, 0:DMC * 256]
        hl = hb[:, DMC * 256:2 * DMC * 256]
        g.wait("sync", STQ)
        for t in range(NTO):
            g.wait("vector", TEN)
            for fc in range(FC):
                c = slice(512 * t, 512 * t + 512)
                tt(gtmp, z2_chunk(fc, t), M3[:, c], ALU.subtract)
                tt(gtmp, gtmp, R3[:, c], ALU.mult)
                ts(gtmp, gtmp, vcol("g2", fc), ALU.mult, vcol("b2", fc), ALU.add)
                nv = n2b.bitcast(BF16)
                split_hl(nv[:, 512 * fc:512 * fc + 512],
                         nv[:, FC * 512 + 512 * fc:FC * 512 + 512 * (fc + 1)], gtmp)
            vn2 = VEC.v
            g.wait("tensor", VEC, vn2)
            nv = n2b.bitcast(BF16)
            for half in range(2):
                hrel = []
                wsnap = []
                for oc in range(DMC):
                    wsnap.append(TEN.v)
                    if oc >= 2:
                        g.wait("sync", TEN, wsnap[oc - 1])
                    vldw = g.dma("sync", w1b[oc % 2].bitcast(BF16), w1hl_s[oc],
                                 w1sem[oc % 2])
                    pv = PB[oc % 2][:][:, 0:256]
                    g.wait("tensor", w1sem[oc % 2], vldw)
                    if oc >= 2:
                        g.wait("tensor", SCA, hrel[oc - 2])
                    wv = w1b[oc % 2].bitcast(BF16)
                    i_mm = 0
                    for k in range(FC):
                        wh = wv[:, 128 * k:128 * k + 128]
                        wl = wv[:, FC * 128 + 128 * k:FC * 128 + 128 * (k + 1)]
                        cq = slice(512 * k + 256 * half, 512 * k + 256 * half + 256)
                        ah = nv[:, cq]
                        al = nv[:, FC * 512:2 * FC * 512][:, cq]
                        for (wop, aop) in ((wh, ah), (wh, al), (wl, ah)):
                            mm(pv, wop, aop, start=(i_mm == 0), stop=(i_mm == 3 * FC - 1),
                               sem=TEN if i_mm == 3 * FC - 1 else None)
                            i_mm += 1
                    g.wait("scalar", TEN)
                    g.wait("scalar", VEC)
                    act(gtmp[:, 0:256], pv, AF.Gelu, bias=vcol("b1m", oc), sem=SCA)
                    hrel.append(SCA.v)
                    g.wait("vector", SCA)
                    split_hl(hh[:, 256 * oc:256 * oc + 256],
                             hl[:, 256 * oc:256 * oc + 256], gtmp[:, 0:256])
                g.wait("tensor", VEC)
                wsnap2 = []
                for oc in range(FC):
                    wsnap2.append(TEN.v)
                    if oc >= 2:
                        g.wait("sync", TEN, wsnap2[oc - 1])
                    elif oc == 0:
                        g.wait("sync", TEN)
                    vldw = g.dma("sync", w2b[oc % 2].bitcast(BF16), w2hl_s[oc],
                                 w2sem[oc % 2])
                    pv = PB[2][:][:, 0:256]
                    g.wait("tensor", w2sem[oc % 2], vldw)
                    g.wait("tensor", VEC)
                    wv = w2b[oc % 2].bitcast(BF16)
                    i_mm = 0
                    for k in range(DMC):
                        wh = wv[:, 128 * k:128 * k + 128]
                        wl = wv[:, DMC * 128 + 128 * k:DMC * 128 + 128 * (k + 1)]
                        ah = hh[:, 256 * k:256 * k + 256]
                        al = hl[:, 256 * k:256 * k + 256]
                        for (wop, aop) in ((wh, ah), (wh, al), (wl, ah)):
                            mm(pv, wop, aop, start=(i_mm == 0), stop=(i_mm == 3 * DMC - 1),
                               sem=TEN if i_mm == 3 * DMC - 1 else None)
                            i_mm += 1
                    tv = TEN.v
                    cq = slice(512 * t + 256 * half, 512 * t + 256 * half + 256)
                    rb = res_r2.rd(y2own[oc, :, cq], "vector")
                    g.wait("vector", TEN, tv)
                    g.wait("vector", STQ)
                    ts(ob2, pv, vcol("b2m", oc), ALU.add)
                    tt(ob2, ob2, Z2[:, oc * TO + 256 * (2 * t + half):
                                    oc * TO + 256 * (2 * t + half) + 256], ALU.add)
                    tt(ob2, ob2, rb, ALU.add)
                    ev = VEC.v
                    g.wait("gpsimd", VEC, ev)
                    g.dma("gpsimd", y3own[oc, :, cq], ob2, STQ)
                    g.wait("tensor", VEC, ev)

    mlp()

    if stop_phase <= 12:
        return finish(nc, g, ctx)
    reset("A1")
    QSCR.update(A=carve("A1", 1024), S1=carve("A1", 1024), S2=carve("A1", 1024),
                G=carve("A1", 1024), M=carve("A1", 64), SS=carve("A1", 64),
                R=carve("A1", 64))
    carve("A1", 1216)

    if stop_phase <= 13:
        return finish(nc, g, ctx)
    reset("A2")
    M4 = carve("A2", TO)
    R4 = carve("A2", TO)
    a2_mark = arena_off["A2"]
    sqb = carve("A2", 1024)
    g.wait("sync", STQ)
    rd11 = Reader("A2", 512, n=3)
    ln_stats(lambda fc, t_: rd11.rd(y3own[fc, :, 512 * t_:512 * t_ + 512], "tensor", "scalar"),
             NTO, M4, R4, sqb)
    arena_off["A2"] = a2_mark

    XQb = A1t[:][:, 5440:5440 + 4096].bitcast(BF16)
    XTM = A1t[:][:, 9536:9536 + 4096]
    HTM = A1t[:][:, 9536:9536 + 4 * DI]
    HQFb = A1t[:][:, 20800:20800 + DIC * 512].bitcast(BF16)
    xb = A1t[:][:, 32064:32064 + 512]
    wgb = [carve("A2", FC * 128) for _ in range(2)]
    wub = [carve("A2", FC * 128) for _ in range(2)]
    wdb = [carve("A2", DIC * 128) for _ in range(2)]
    ub = carve("A2", 512)
    rd12 = Reader("A2", 512, n=2)
    rd13 = Reader("A2", 512, n=2)
    gusem = [mk_sem(), mk_sem()]
    dsem2 = [mk_sem(), mk_sem()]

    def ffn():
        phase_barrier()
        bank_wait("tensor", "P0a", "P0b", "P1a", "P1b", "P2a", "P2b")
        gu_guard = {}
        d_guard = {}
        tp_guard = {}
        pg_guard = {}
        def tp_slot(j):
            return PB[2][:][:, 512 * (j % 2):512 * (j % 2) + 128]
        XQH = XQb[:, 0:FC * 512]
        XQL = XQb[:, FC * 512:2 * FC * 512]
        HQH = HQFb[:, 0:DIC * 512]
        HQL = HQFb[:, DIC * 512:2 * DIC * 512]

        def x_stage(t):
            g.wait("vector", TEN)
            g.wait("scalar", TEN)
            g.wait("vector", STQ)
            for fc in range(FC):
                ch = rd12.rd(y3own[fc, :, 512 * t:512 * t + 512], "vector")
                c = slice(512 * t, 512 * t + 512)
                g.wait("vector", TEN)
                tt(xb, ch, M4[:, c], ALU.subtract)
                tt(xb, xb, R4[:, c], ALU.mult, sem=VEC)
                vx = VEC.v
                g.wait("tensor", VEC, vx)
                for i in range(4):
                    pv = tp_guard.get(i % 2)
                    if pv is not None:
                        g.wait("tensor", VEC, pv)
                    tv = transpose(tp_slot(i), xb[:, 128 * i:128 * i + 128], sem=TEN)
                    g.wait("vector", TEN, tv)
                    vcopy(XTM[:, 1024 * i + 128 * fc:1024 * i + 128 * fc + 128],
                          tp_slot(i), sem=VEC)
                    tp_guard[i % 2] = VEC.v
            for i in range(4):
                qv = quantize(XTM[:, 1024 * i:1024 * (i + 1)], 1024,
                              XTM[:, 1024 * i:1024 * (i + 1)], xv=VEC.v)
                g.wait("tensor", VEC, qv)
                for fc in range(FC):
                    j = 4 * i + fc
                    pv = tp_guard.get(j % 2)
                    if pv is not None:
                        g.wait("tensor", VEC, pv)
                    tv = transpose(tp_slot(j),
                                   XTM[:, 1024 * i + 128 * fc:1024 * i + 128 * fc + 128],
                                   sem=TEN)
                    g.wait("vector", TEN, tv)
                    cc = slice(512 * fc + 128 * i, 512 * fc + 128 * i + 128)
                    split_hl(XQH[:, cc], XQL[:, cc], tp_slot(j))
                    tp_guard[j % 2] = VEC.v

        def gh_stage(t):
            for g0 in range(0, DIC, 8):
                gn = min(8, DIC - g0)
                for oc in range(g0, g0 + gn):
                    bi = oc % 2
                    wgv = wgb[bi].bitcast(BF16)
                    wuv = wub[bi].bitcast(BF16)
                    g.wait("sync", gusem[bi])
                    if gu_guard.get(bi) is not None:
                        g.wait("sync", TEN, gu_guard[bi])
                    g.dma("sync", wgv[:, 0:FC * 128].rearrange("p (k c) -> p k c", c=128),
                          wgq_s[0][oc], gusem[bi])
                    g.dma("sync", wgv[:, FC * 128:2 * FC * 128].rearrange("p (k c) -> p k c", c=128),
                          wgq_s[1][oc], gusem[bi])
                    g.dma("sync", wuv[:, 0:FC * 128].rearrange("p (k c) -> p k c", c=128),
                          wuq_s[0][oc], gusem[bi])
                    v2 = g.dma("sync", wuv[:, FC * 128:2 * FC * 128].rearrange("p (k c) -> p k c", c=128),
                               wuq_s[1][oc], gusem[bi])
                    g.wait("tensor", gusem[bi], v2)
                    pg = PB[0][:][:, 512 * bi:512 * bi + 512]
                    pu = PB[1][:][:, 512 * bi:512 * bi + 512]
                    if oc == 0:
                        g.wait("tensor", VEC)
                        g.wait("tensor", SCA)
                    pgv = pg_guard.get(bi)
                    if pgv is not None:
                        g.wait("tensor", SCA, pgv[0])
                        g.wait("tensor", VEC, pgv[1])
                    for (pp, wv) in ((pg, wgv), (pu, wuv)):
                        i_mm = 0
                        for k in range(FC):
                            wh = wv[:, 128 * k:128 * k + 128]
                            wl = wv[:, FC * 128 + 128 * k:FC * 128 + 128 * (k + 1)]
                            ah = XQH[:, 512 * k:512 * k + 512]
                            al = XQL[:, 512 * k:512 * k + 512]
                            for (wop, aop) in ((wh, ah), (wh, al), (wl, ah)):
                                mm(pp, wop, aop, start=(i_mm == 0), stop=(i_mm == 3 * FC - 1),
                                   sem=TEN if i_mm == 3 * FC - 1 else None)
                                i_mm += 1
                    tv = TEN.v
                    gu_guard[bi] = tv
                    g.wait("scalar", TEN, tv)
                    act(ub, pg, AF.Silu, sem=SCA)
                    pg_sca = SCA.v
                    g.wait("vector", SCA, pg_sca)
                    tt(ub, ub, pu, ALU.mult, sem=VEC)
                    vv = VEC.v
                    pg_guard[bi] = (pg_sca, vv)
                    g.wait("tensor", VEC, vv)
                    for i in range(4):
                        pv = tp_guard.get(i % 2)
                        if pv is not None:
                            g.wait("tensor", VEC, pv)
                        tv2 = transpose(tp_slot(i), ub[:, 128 * i:128 * i + 128], sem=TEN)
                        g.wait("vector", TEN, tv2)
                        vcopy(HTM[:, DI * i + 128 * oc:DI * i + 128 * oc + 128],
                              tp_slot(i), sem=VEC)
                        tp_guard[i % 2] = VEC.v
                for i in range(4):
                    hs = slice(DI * i + 128 * g0, DI * i + 128 * (g0 + gn))
                    qv = quantize(HTM[:, hs], 128 * gn, HTM[:, hs], xv=VEC.v)
                    g.wait("tensor", VEC, qv)
                    for kc in range(g0, g0 + gn):
                        pv = tp_guard.get(kc % 2)
                        if pv is not None:
                            g.wait("tensor", VEC, pv)
                        tv = transpose(tp_slot(kc),
                                       HTM[:, DI * i + 128 * kc:DI * i + 128 * kc + 128],
                                       sem=TEN)
                        g.wait("vector", TEN, tv)
                        cc = slice(512 * kc + 128 * i, 512 * kc + 128 * i + 128)
                        split_hl(HQH[:, cc], HQL[:, cc], tp_slot(kc))
                        tp_guard[kc % 2] = VEC.v
            return (SCA.v, VEC.v)

        def d_stage(t, g_end):
            for oc in range(FC):
                bi = oc % 2
                wdv = wdb[bi].bitcast(BF16)
                g.wait("sync", dsem2[bi])
                if d_guard.get(bi) is not None:
                    g.wait("sync", TEN, d_guard[bi])
                g.dma("sync", wdv[:, 0:DIC * 128].rearrange("p (k c) -> p k c", c=128),
                      wdq_s[0][oc], dsem2[bi])
                v1 = g.dma("sync", wdv[:, DIC * 128:2 * DIC * 128].rearrange("p (k c) -> p k c", c=128),
                           wdq_s[1][oc], dsem2[bi])
                pv = PB[0][:][:, 512 * (oc % 2):512 * (oc % 2) + 512]
                g.wait("tensor", dsem2[bi], v1)
                if oc == 0:
                    g.wait("tensor", SCA, g_end[0])
                    g.wait("tensor", VEC, g_end[1])
                pgv = pg_guard.get(2 + oc % 2)
                if pgv is not None:
                    g.wait("tensor", VEC, pgv)
                i_mm = 0
                for k in range(DIC):
                    wh = wdv[:, 128 * k:128 * k + 128]
                    wl = wdv[:, DIC * 128 + 128 * k:DIC * 128 + 128 * (k + 1)]
                    ah = HQH[:, 512 * k:512 * k + 512]
                    al = HQL[:, 512 * k:512 * k + 512]
                    for (wop, aop) in ((wh, ah), (wh, al), (wl, ah)):
                        mm(pv, wop, aop, start=(i_mm == 0), stop=(i_mm == 3 * DIC - 1),
                           sem=TEN if i_mm == 3 * DIC - 1 else None)
                        i_mm += 1
                tv = TEN.v
                d_guard[bi] = tv
                ch = rd13.rd(y3own[oc, :, 512 * t:512 * t + 512], "vector")
                g.wait("vector", TEN, tv)
                g.wait("vector", STQ)
                tt(xb, pv, ch, ALU.add, sem=VEC)
                pg_guard[2 + oc % 2] = VEC.v
                g.wait("gpsimd", VEC)
                g.dma("gpsimd", y_out[oc, :, 512 * t:512 * t + 512], xb, STQ)

        x_stage(0)
        for t in range(NTO):
            g_end = gh_stage(t)
            if t + 1 < NTO:
                x_stage(t + 1)
            d_stage(t, g_end)

    ffn()

    return finish(nc, g, ctx)


def finish(nc, g, ctx):
    g.wait("sync", g.fin_STQ)
    g.wait("sync", g.fin_LDQ)
    with ctx:
        blk = ctx.enter_context(nc.Block())
        for eng in ENGS:
            def make_body(eng):
                def body(e):
                    for fn in g.ops[eng]:
                        fn(e)
                return body
            getattr(blk, eng)(make_body(eng))
    return nc



def pack_vecs(ins, selA, selB):
    out = np.zeros((128, 128), dtype=np.float32)
    col = 0
    for key, n in (("L0_attn_bp", FC), ("L1_attn_bqkv", 3 * FC), ("L1_attn_ln1_g", FC),
                   ("L1_attn_ln1_b", FC), ("L1_attn_bo", FC), ("L1_attn_ln2_g", FC),
                   ("L1_attn_ln2_b", FC), ("L1_attn_b1", DMC), ("L1_attn_b2", FC)):
        arr = np.asarray(ins[key], dtype=np.float32)
        out[:, col:col + n] = arr.reshape(n, 128).T
        col += n
    out[:, col] = selA
    out[:, col + 1] = selB
    return out


def kernel(**inputs):
    ins = {k: np.asarray(v) for k, v in inputs.items()}
    x = np.asarray(ins["L0_attn_x"], dtype=np.float32)

    C2, S2 = build_rope_tables()
    tri = build_tri()
    identity = np.ascontiguousarray(np.eye(128, dtype=np.float32))
    maskA = build_masks(OWN_A)
    maskB = build_masks(OWN_B)

    wq_s = swizzle_strips(ins["L0_attn_Wq"])
    wk_s = swizzle_strips(ins["L0_attn_Wk"])
    wv_s = swizzle_strips(ins["L0_attn_Wv"])
    wp_s = swizzle_strips(ins["L0_attn_Wp"])
    w0t_s = swizzle_strips(np.ascontiguousarray(np.asarray(ins["L0_ffn_weight"]).T))
    wqkv_s = swizzle_strips(ins["L1_attn_Wqkv"])
    wo_s = swizzle_strips(ins["L1_attn_Wo"])
    w1_s = swizzle_strips(ins["L1_attn_W1"])
    w2_s = swizzle_strips(ins["L1_attn_W2"])
    wg_nat = np.ascontiguousarray(
        np.asarray(ins["L1_ffn_w_gate"], dtype=np.float32).reshape(DIC, 128, D))
    wu_nat = np.ascontiguousarray(
        np.asarray(ins["L1_ffn_w_up"], dtype=np.float32).reshape(DIC, 128, D))
    wd_nat = np.ascontiguousarray(
        np.asarray(ins["L1_ffn_w_down"], dtype=np.float32).reshape(FC, 128, DI))

    nc = build_program()

    in_maps = []
    for core in range(8):
        b, s = core // 2, core % 2
        own = OWN_A if s == 0 else OWN_B
        own_tok = np.concatenate([np.arange(256 * a, 256 * a + 256) for a in own])
        xb = np.ascontiguousarray(x[b].T)
        im = dict(
            x_fm=np.ascontiguousarray(xb.reshape(FC, 128, T)),
            wq_s=wq_s, wk_s=wk_s, wv_s=wv_s, wp_s=wp_s, w0t_s=w0t_s,
            wqkv_s=wqkv_s, wo_s=wo_s, w1_s=w1_s, w2_s=w2_s,
            wg_nat=wg_nat, wu_nat=wu_nat, wd_nat=wd_nat,
            vecs=pack_vecs(ins, 1.0 if s == 0 else 0.0, 0.0 if s == 0 else 1.0),
            c2t=C2, s2t=S2,
            c2o=np.ascontiguousarray(C2[:, own_tok]),
            s2o=np.ascontiguousarray(S2[:, own_tok]),
            iden=identity, mtri=tri,
            mask8=maskA if s == 0 else maskB,
        )
        in_maps.append(im)

    res = run_bass_kernel_spmd(nc, in_maps, core_ids=list(range(8)))

    out = np.empty((B, T, D), dtype=np.float32)
    for core in range(8):
        b, s = core // 2, core % 2
        own = OWN_A if s == 0 else OWN_B
        y = np.asarray(res.results[core]["y_out"]).reshape(D, TO).T
        for j, a in enumerate(own):
            out[b, 256 * a:256 * a + 256, :] = y[256 * j:256 * j + 256, :]
    return out


# version 13 (final)
# speedup vs baseline: 1.0511x; 1.0025x over previous
import numpy as np
from contextlib import ExitStack

import concourse.bass as bass
import concourse.mybir as mybir
from concourse.bass_utils import run_bass_kernel_spmd

F32 = mybir.dt.float32
BF16 = mybir.dt.bfloat16
I8 = mybir.dt.int8
AF = mybir.ActivationFunctionType
ALU = mybir.AluOpType
AX = mybir.AxisListType
f32 = np.float32

B, T, D, H, HD = 4, 4096, 1024, 16, 64
NPAIR = 8
TO = T // 2
OWN_A = [0, 3, 4, 7, 8, 11, 12, 15]
OWN_B = [1, 2, 5, 6, 9, 10, 13, 14]
NSLOT = 8
N_MSK = 32
FC = D // 128
NT = T // 512
NTO = TO // 512
EPS = 1e-5
DI = 2816
DIC = DI // 128
DM = 4096
DMC = DM // 128
C6_1 = float(f32(1.0 / 6.0))
C6_2 = float(f32(1.0 / 6.0 - float(f32(1.0 / 6.0))))

ENGS = ("sync", "tensor", "vector", "scalar", "gpsimd")
DBG = {}


class Sem:
    __slots__ = ("name", "h", "v")

    def __init__(self, name):
        self.name, self.h, self.v = name, None, 0


class Gen:
    def __init__(self):
        self.ops = {e: [] for e in ENGS}

    def emit(self, eng, fn):
        self.ops[eng].append(fn)

    def wait(self, eng, sem, val=None):
        v = sem.v if val is None else val
        assert v <= sem.v, (sem.name, v, sem.v)
        if v > 0:
            self.emit(eng, lambda e, h=sem, vv=v: e.wait_ge(h.h, vv))

    def op(self, eng, build, sem=None, inc=1):
        own = getattr(self, "own_sems", {}).get(eng)
        if own is not None:
            self.wait(eng, own)
            sem, inc = own, 1
        if sem is not None:
            sem.v += inc

        def fn(e, build=build, sem=sem, inc=inc):
            r = build(e)
            if sem is not None:
                r.then_inc(sem.h, inc)
        self.emit(eng, fn)
        return None if sem is None else sem.v

    def dma(self, eng, out, in_, sem):
        sem.v += 16
        self.emit(eng, lambda e, o=out, i=in_, h=sem: e.dma_start(out=o, in_=i).then_inc(h.h, 16))
        return sem.v



def swizzle_strips(W):
    K, M = W.shape
    return np.ascontiguousarray(
        W.reshape(K // 128, 128, M // 128, 128).transpose(2, 1, 0, 3)).astype(np.float32)


def build_rope_tables():
    inv_freq = 1.0 / (10000.0 ** (np.arange(0, HD, 2, dtype=np.float64) / HD))
    t = np.arange(T, dtype=np.float64)
    fr = t[:, None] * inv_freq[None, :]
    cos = np.cos(fr).T
    sin = np.sin(fr).T
    C2 = np.concatenate([cos, cos, cos, cos], axis=0).astype(np.float32)
    S2 = np.concatenate([-sin, sin, -sin, sin], axis=0).astype(np.float32)
    return np.ascontiguousarray(C2), np.ascontiguousarray(S2)


def build_masks(own):
    out = np.zeros((N_MSK, 128, 512), dtype=np.int8)
    for si in range(4):
        qg = np.concatenate([256 * own[2 * si] + np.arange(256),
                             256 * own[2 * si + 1] + np.arange(256)])[None, :]
        for d2 in range(8):
            k = 128 * (8 * si + d2) + np.arange(128)[:, None]
            out[8 * si + d2] = (k > qg).astype(np.int8)
    return np.ascontiguousarray(out.transpose(1, 0, 2).reshape(128, N_MSK * 512))


def build_tri():
    k = np.arange(128)[:, None]
    q = np.arange(128)[None, :]
    return np.ascontiguousarray(np.where(k <= q, 0.0, -1e30).astype(np.float32))



def build_program(stop_phase=99):
    nc = bass.Bass("TRN2", target_bir_lowering=False)
    g = Gen()
    ctx = ExitStack()

    din = {}

    def inp(name, shape, dtype=F32):
        din[name] = nc.dram_tensor(name, list(shape), dtype, kind="ExternalInput").ap()
        return din[name]

    x_fm = inp("x_fm", [FC, 128, T])
    wq_s = inp("wq_s", [FC, 128, FC, 128])
    wk_s = inp("wk_s", [FC, 128, FC, 128])
    wv_s = inp("wv_s", [FC, 128, FC, 128])
    wp_s = inp("wp_s", [FC, 128, FC, 128])
    w0t_s = inp("w0t_s", [FC, 128, FC, 128])
    wqkv_s = inp("wqkv_s", [3 * FC, 128, FC, 128])
    wo_s = inp("wo_s", [FC, 128, FC, 128])
    w1_s = inp("w1_s", [DMC, 128, FC, 128])
    w2_s = inp("w2_s", [FC, 128, DMC, 128])
    wg_nat = inp("wg_nat", [DIC, 128, D])
    wu_nat = inp("wu_nat", [DIC, 128, D])
    wd_nat = inp("wd_nat", [FC, 128, DI])
    vecs = inp("vecs", [128, 128])
    c2t = inp("c2t", [128, T])
    s2t = inp("s2t", [128, T])
    c2o = inp("c2o", [128, TO])
    s2o = inp("s2o", [128, TO])
    iden = inp("iden", [128, 128])
    mtri = inp("mtri", [128, 128])
    mask8 = inp("mask8", [128, N_MSK * 512], I8)

    y_out = nc.dram_tensor("y_out", [FC, 128, TO], F32, kind="ExternalOutput").ap()

    def scratch(name, shape, dtype=F32):
        return nc.dram_tensor(name, list(shape), dtype, kind="Internal").ap()

    qst = [scratch("qst_h", [NPAIR, 128, T], BF16), scratch("qst_l", [NPAIR, 128, T], BF16)]
    kst = [scratch("kst_h", [NPAIR, 128, T], BF16), scratch("kst_l", [NPAIR, 128, T], BF16)]
    vst = scratch("vst", [NPAIR, T, 128])
    ast = [scratch("ast_h", [NPAIR, 128, T], BF16), scratch("ast_l", [NPAIR, 128, T], BF16)]
    qst1 = [scratch("qst1_h", [NPAIR, 128, TO], BF16), scratch("qst1_l", [NPAIR, 128, TO], BF16)]
    ast1 = [scratch("ast1_h", [NPAIR, 128, TO], BF16), scratch("ast1_l", [NPAIR, 128, TO], BF16)]
    y1st = scratch("y1st", [FC, 128, T])
    y2st = scratch("y2st", [FC, 128, T])
    y2own = scratch("y2own", [FC, 128, TO])
    y3own = scratch("y3own", [FC, 128, TO])
    n1st = scratch("n1st", [FC, 128, T])
    z2dbg = scratch("z2dbg", [FC, 128, TO])
    w1hl_s = scratch("w1hl_s", [DMC, 128, 2 * FC * 128], BF16)
    w2hl_s = scratch("w2hl_s", [FC, 128, 2 * DMC * 128], BF16)
    wgq_s = [scratch("wgq_sh", [DIC, 128, FC, 128], BF16),
             scratch("wgq_sl", [DIC, 128, FC, 128], BF16)]
    wuq_s = [scratch("wuq_sh", [DIC, 128, FC, 128], BF16),
             scratch("wuq_sl", [DIC, 128, FC, 128], BF16)]
    wdq_s = [scratch("wdq_sh", [FC, 128, DIC, 128], BF16),
             scratch("wdq_sl", [FC, 128, DIC, 128], BF16)]

    A1t = ctx.enter_context(nc.sbuf_tensor("A1", [128, 32768], F32))
    A2t = ctx.enter_context(nc.sbuf_tensor("A2", [128, 16384], F32))
    sb_vecs = ctx.enter_context(nc.sbuf_tensor("sb_vecs", [128, 128], F32))
    sb_iden = ctx.enter_context(nc.sbuf_tensor("sb_iden", [128, 128], F32))
    sb_tri = ctx.enter_context(nc.sbuf_tensor("sb_tri", [128, 128], F32))
    sb_ones64 = ctx.enter_context(nc.sbuf_tensor("sb_ones64", [128, 64], F32))
    sb_ones64b_t = ctx.enter_context(nc.sbuf_tensor("sb_ones64b", [128, 64], BF16))
    sb_ones1 = ctx.enter_context(nc.sbuf_tensor("sb_ones1", [128, 1], F32))
    sb_onesk = ctx.enter_context(nc.sbuf_tensor("sb_onesk", [1, 128], F32))
    sb_rows = ctx.enter_context(nc.sbuf_tensor("sb_rows", [1, 3072], F32))
    PB = [ctx.enter_context(nc.psum_tensor(f"PB{i}", [128, 1024], F32)) for i in range(4)]

    sems = {}
    for name in ("TEN", "VEC", "SCA", "LDQ", "STQ", "GPC"):
        s = Sem(name)
        s.h = ctx.enter_context(nc.semaphore(name))
        sems[name] = s
    TEN, VEC, SCA, LDQ, STQ, GPC = (sems[n] for n in ("TEN", "VEC", "SCA", "LDQ", "STQ", "GPC"))
    CSEMS = (TEN, VEC, SCA)
    g.fin_STQ = STQ
    g.fin_LDQ = LDQ
    g.own_sems = {"vector": VEC, "scalar": SCA}
    _semctr = [0]

    def mk_sem():
        s = Sem(f"S{_semctr[0]}")
        _semctr[0] += 1
        s.h = ctx.enter_context(nc.semaphore(s.name))
        g.all_slot_sems.append(s)
        return s
    g.all_slot_sems = []

    bankrel = {}

    def bank_wait(eng, *keys):
        for k in keys:
            for s, v in bankrel.get(k, ()):
                g.wait(eng, s, v)

    def bank_rel(key, *svs):
        bankrel[key] = list(svs) if svs else [(VEC, VEC.v)]

    def phase_barrier():
        g.wait("tensor", VEC)
        g.wait("tensor", SCA)
        g.wait("tensor", STQ)
        g.wait("vector", TEN)
        g.wait("vector", SCA)
        g.wait("vector", STQ)
        g.wait("scalar", TEN)
        g.wait("scalar", VEC)
        g.wait("scalar", STQ)
        g.wait("sync", TEN)
        g.wait("sync", VEC)
        g.wait("sync", SCA)
        g.wait("sync", STQ)
        g.wait("sync", LDQ)
        for s_ in g.all_slot_sems:
            g.wait("sync", s_)
        g.wait("gpsimd", VEC)
        g.wait("gpsimd", SCA)

    arena_off = {"A1": 0, "A2": 0}

    def carve(which, cols):
        o = arena_off[which]
        lim = 32768 if which == "A1" else 16384
        assert o + cols <= lim, (which, o, cols)
        arena_off[which] = o + cols
        t_ = A1t if which == "A1" else A2t
        return t_[:][:, o:o + cols]

    def reset(which):
        arena_off[which] = 0

    VC = {}
    off = 0
    for nm, n in (("bp", FC), ("bqkv", 3 * FC), ("g1", FC), ("b1", FC), ("bo", FC),
                  ("g2", FC), ("b2", FC), ("b1m", DMC), ("b2m", FC), ("selA", 1), ("selB", 1)):
        VC[nm] = off
        off += n
    assert off <= 128

    def vcol(nm, i=0):
        return sb_vecs[:][:, VC[nm] + i:VC[nm] + i + 1]

    def tt(out, in0, in1, op, sem=None, eng="vector"):
        return g.op(eng, lambda e, o=out, a=in0, b=in1, p=op:
                    e.tensor_tensor(out=o, in0=a, in1=b, op=p), sem)

    def ts(out, in0, s1, op0, s2=None, op1=None, sem=None):
        def bld(e, o=out, a=in0, s1=s1, s2=s2, op0=op0, op1=op1):
            kw = dict(out=o, in0=a, scalar1=s1, scalar2=s2, op0=op0)
            if op1 is not None:
                kw["op1"] = op1
            return e.tensor_scalar(**kw)
        return g.op("vector", bld, sem)

    def stt(out, in0, scalar, in1, op0, op1, sem=None):
        return g.op("vector", lambda e, o=out, a=in0, s=scalar, b=in1, p0=op0, p1=op1:
                    e.scalar_tensor_tensor(out=o, in0=a, scalar=s, in1=b, op0=p0, op1=p1), sem)

    def act(out, in_, func, scale=None, bias=None, sem=None):
        def bld(e, o=out, i=in_, f=func, sc=scale, bi=bias):
            kw = dict(out=o, in_=i, func=f)
            if sc is not None:
                kw["scale"] = sc
            if bi is not None:
                kw["bias"] = bi
            return e.activation(**kw)
        return g.op("scalar", bld, sem)

    def vcopy(out, in_, sem=None):
        return g.op("vector", lambda e, o=out, i=in_: e.tensor_copy(out=o, in_=i), sem)

    def mm(out, lhsT, rhs, start, stop, tile_position=None, sem=None):
        def bld(e, o=out, l=lhsT, r=rhs, st=start, sp=stop, tp=tile_position):
            kw = dict(out=o, lhsT=l, rhs=r, start=st, stop=sp)
            if tp is not None:
                kw["tile_position"] = tp
            return e.matmul(**kw)
        return g.op("tensor", bld, sem)

    def split_hl(hi, lo, src):
        vcopy(hi, src)
        tt(lo, src, hi, ALU.subtract)

    def transpose(out, in_, sem=None):
        return g.op("tensor", lambda e, o=out, i=in_:
                    e.transpose(out=o, in_=i, identity=sb_iden[:]), sem)

    def waitall(eng):
        for s in CSEMS:
            g.wait(eng, s)

    class Reader:
        def __init__(self, which, cols, n=3):
            self.bufs = [carve(which, cols) for _ in range(n)]
            self.sems = [mk_sem() for _ in range(n)]
            self.n = n
            self.i = 0
            self.snaps = []

        def rd(self, src, *consumers):
            i = self.i
            self.snaps.append(tuple((s, s.v) for s in CSEMS))
            if i >= self.n:
                for s, v in self.snaps[i - self.n + 1]:
                    g.wait("sync", s, v)
            b = self.bufs[i % self.n]
            out = b
            if src.dtype == BF16:
                out = out.bitcast(BF16)
            if len(src.shape) == 3:
                out = out.rearrange("p (k c) -> p k c", c=src.shape[2])
                src = src.rearrange("k p c -> p k c")
            v = g.dma("sync", out, src, self.sems[i % self.n])
            for c in consumers:
                g.wait(c, self.sems[i % self.n], v)
            self.i = i + 1
            return b

    qsl_state = [None]
    QSCR = {}

    def quantize_slice(X, Fcols, out, xv=None):
        nb = Fcols // 16
        assert Fcols <= 1024
        a = QSCR["A"][:, 0:Fcols]
        s1 = QSCR["S1"][:, 0:Fcols]
        s2 = QSCR["S2"][:, 0:Fcols]
        gsn = QSCR["G"][:, 0:Fcols]
        m = QSCR["M"][:, 0:nb]
        ss = QSCR["SS"][:, 0:nb]
        r = QSCR["R"][:, 0:nb]
        if qsl_state[0] is None:
            g.wait("scalar", VEC)
        else:
            g.wait("scalar", VEC, qsl_state[0])
        if xv is not None:
            g.wait("scalar", VEC, xv)
        act(a, X, AF.Abs)
        act(gsn, X, AF.Sign, sem=SCA)
        g.wait("vector", SCA)
        g.op("vector", lambda e, o=m, i=a: e.tensor_reduce(
            out=o, in_=i.rearrange("p (b k) -> p b k", k=16), axis=AX.X, op=ALU.max))
        ts(ss, m, C6_1, ALU.mult)
        stt(ss, m, C6_2, ss, ALU.mult, ALU.add)
        ts(ss, ss, float(f32(1e-12)), ALU.max, sem=VEC)
        g.wait("scalar", VEC)
        act(r, ss, AF.Ln)
        act(r, r, AF.Exp, scale=f32(-1.0), sem=SCA)
        g.wait("vector", SCA)
        rbc = r.unsqueeze(-1).broadcast_to((128, nb, 16))
        sbc = ss.unsqueeze(-1).broadcast_to((128, nb, 16))
        a3d = a.rearrange("p (b k) -> p b k", k=16)
        tt(a3d, a3d, rbc, ALU.mult)
        ts(s1, a, 0.25, ALU.is_gt)
        stt(s1, a, 0.75, s1, ALU.is_gt, ALU.add)
        stt(s1, a, 1.25, s1, ALU.is_gt, ALU.add)
        stt(s1, a, 1.75, s1, ALU.is_gt, ALU.add)
        ts(s2, a, 2.5, ALU.is_gt)
        stt(s2, a, 3.5, s2, ALU.is_gt, ALU.add)
        stt(s2, a, 5.0, s2, ALU.is_gt, ALU.add)
        stt(s2, a, 5.0, s2, ALU.is_gt, ALU.add)
        stt(s2, s1, 0.5, s2, ALU.mult, ALU.add)
        tt(s2.rearrange("p (b k) -> p b k", k=16),
           s2.rearrange("p (b k) -> p b k", k=16), sbc, ALU.mult)
        tt(out, s2, gsn, ALU.mult, sem=VEC)
        qsl_state[0] = VEC.v
        return VEC.v

    def quantize(X, Fcols, out, xv=None):
        v = None
        o = 0
        while o < Fcols:
            w = min(1024, Fcols - o)
            v = quantize_slice(X[:, o:o + w], w, out[:, o:o + w], xv=xv)
            o += w
        return v

    qw_steps = []

    def tick(n=1):
        for _ in range(n):
            if qw_steps:
                qw_steps.pop(0)()

    g.dma("sync", sb_vecs[:], vecs, LDQ)
    g.dma("sync", sb_iden[:], iden, LDQ)
    g.dma("sync", sb_tri[:], mtri, LDQ)
    v0 = LDQ.v
    g.op("gpsimd", lambda e: e.memset(sb_ones64[:], 1.0), GPC)
    g.op("gpsimd", lambda e: e.memset(sb_ones64b_t[:], 1.0), GPC)
    g.op("gpsimd", lambda e: e.memset(sb_ones1[:], 1.0), GPC)
    g.op("gpsimd", lambda e: e.memset(sb_onesk[:], 1.0), GPC)
    for eng in ("tensor", "vector", "scalar"):
        g.wait(eng, LDQ, v0)
        g.wait(eng, GPC)

    def ln_stats(read_chunk, ntt, dst_M, dst_R, sq_buf, barrier=True):
        if barrier:
            phase_barrier()
        bank_wait("tensor", "P2a", "P3a")
        for t in range(ntt):
            for fc in range(FC):
                ch = read_chunk(fc, t)
                sq = sq_buf[:, 512 * (fc % 2):512 * (fc % 2) + 512]
                if fc >= 2:
                    g.wait("scalar", TEN)
                sv = act(sq, ch, AF.Square, sem=SCA)
                mm(PB[2][:][0:1, 0:512], sb_ones1[:], ch, start=(fc == 0),
                   stop=(fc == FC - 1), sem=TEN)
                g.wait("tensor", SCA, sv)
                mm(PB[3][:][0:1, 0:512], sb_ones1[:], sq, start=(fc == 0),
                   stop=(fc == FC - 1), sem=TEN)
            vmm = TEN.v
            g.wait("vector", TEN, vmm)
            r_sum = sb_rows[:][0:1, 0:512]
            r_sq = sb_rows[:][0:1, 512:1024]
            r_mean = sb_rows[:][0:1, 1024:1536]
            r_var = sb_rows[:][0:1, 1536:2048]
            r_rstd = sb_rows[:][0:1, 2048:2560]
            vcopy(r_sum, PB[2][:][0:1, 0:512])
            vcopy(r_sq, PB[3][:][0:1, 0:512])
            ts(r_mean, r_sum, 1.0 / D, ALU.mult)
            tt(r_var, r_mean, r_mean, ALU.mult)
            stt(r_var, r_sq, 1.0 / D, r_var, ALU.mult, ALU.subtract, sem=VEC)
            vv = VEC.v
            g.wait("scalar", VEC, vv)
            act(r_rstd, r_var, AF.Ln, scale=f32(1.0), bias=f32(EPS))
            act(r_rstd, r_rstd, AF.Exp, scale=f32(-0.5), sem=SCA)
            vs = SCA.v
            g.wait("tensor", SCA, vs)
            g.wait("tensor", VEC, vv)
            mm(PB[2][:][:, 0:512], sb_onesk[:], r_mean, start=True, stop=True)
            mm(PB[3][:][:, 0:512], sb_onesk[:], r_rstd, start=True, stop=True, sem=TEN)
            vb = TEN.v
            g.wait("vector", TEN, vb)
            vcopy(dst_M[:, 512 * t:512 * t + 512], PB[2][:][:, 0:512])
            vcopy(dst_R[:, 512 * t:512 * t + 512], PB[3][:][:, 0:512], sem=VEC)
            g.wait("tensor", VEC)
            tick()
        bank_rel("P2a")
        bank_rel("P3a")

    def res_gemm(wsrc, n_k, rhs_src, res_src, bias_name, dst_st, ntt):
        phase_barrier()
        bank_wait("tensor", "P0a", "P1a")
        reset("A2")
        whl = carve("A2", FC * n_k * 128)
        wtmp = carve("A2", n_k * 128)
        wallsem = mk_sem()
        rhs_rh = Reader("A2", 256 * n_k, n=1)
        rhs_rl = Reader("A2", 256 * n_k, n=1)
        res_r = Reader("A2", 512, n=3)
        outb = [carve("A2", 512) for _ in range(2)]
        for oc in range(FC):
            g.wait("sync", VEC)
            v = g.dma("sync", wtmp.rearrange("p (k c) -> p k c", c=128),
                      wsrc[oc], wallsem)
            g.wait("vector", wallsem, v)
            wv = whl[:, n_k * 128 * oc:n_k * 128 * (oc + 1)].bitcast(BF16)
            split_hl(wv[:, 0:n_k * 128], wv[:, n_k * 128:2 * n_k * 128], wtmp)
        g.wait("tensor", VEC)
        idx = 0
        for t in range(ntt):
            rbh = rhs_rh.rd(rhs_src[0][:, :, 512 * t:512 * t + 512], "tensor").bitcast(BF16)
            rbl = rhs_rl.rd(rhs_src[1][:, :, 512 * t:512 * t + 512], "tensor").bitcast(BF16)
            for oc in range(FC):
                key = f"P{idx % 2}a"
                pv = PB[idx % 2][:][:, 0:512]
                bank_wait("tensor", key)
                wv = whl[:, n_k * 128 * oc:n_k * 128 * (oc + 1)].bitcast(BF16)
                n_mm = 3 * n_k
                i_mm = 0
                for k in range(n_k):
                    wh = wv[:, 128 * k:128 * k + 128]
                    wl = wv[:, n_k * 128 + 128 * k:n_k * 128 + 128 * (k + 1)]
                    ah = rbh[:, 512 * k:512 * k + 512]
                    al = rbl[:, 512 * k:512 * k + 512]
                    for (wop, aop) in ((wh, ah), (wh, al), (wl, ah)):
                        mm(pv, wop, aop, start=(i_mm == 0), stop=(i_mm == n_mm - 1),
                           sem=TEN if i_mm == n_mm - 1 else None)
                        i_mm += 1
                tv = TEN.v
                rb = res_r.rd(res_src[oc, :, 512 * t:512 * t + 512], "vector")
                ob = outb[idx % 2]
                if idx >= 2:
                    g.wait("vector", STQ)
                g.wait("vector", TEN, tv)
                if bias_name is not None:
                    ts(ob, pv, vcol(bias_name, oc), ALU.add)
                    bank_rel(key)
                    tt(ob, ob, rb, ALU.add)
                else:
                    tt(ob, pv, rb, ALU.add)
                    bank_rel(key)
                ev = VEC.v
                g.wait("gpsimd", VEC, ev)
                g.dma("gpsimd", dst_st[oc, :, 512 * t:512 * t + 512], ob, STQ)
                idx += 1

    reset("A1")
    reset("A2")
    XNh = carve("A1", FC * T // 2).bitcast(BF16)
    XNl = carve("A1", FC * T // 2).bitcast(BF16)
    M0 = carve("A2", T)
    R0 = carve("A2", T)
    sqb = carve("A2", 1024)
    rd1 = Reader("A2", 512, n=3)
    ln_stats(lambda fc, t_: rd1.rd(x_fm[fc, :, 512 * t_:512 * t_ + 512], "tensor", "scalar"),
             NT, M0, R0, sqb)

    rd2 = Reader("A2", 512, n=3)
    xtmp = carve("A2", 512)
    for t in range(NT):
        for fc in range(FC):
            ch = rd2.rd(x_fm[fc, :, 512 * t:512 * t + 512], "vector")
            c = slice(fc * T + 512 * t, fc * T + 512 * t + 512)
            tt(xtmp, ch, M0[:, 512 * t:512 * t + 512], ALU.subtract)
            tt(xtmp, xtmp, R0[:, 512 * t:512 * t + 512], ALU.mult)
            split_hl(XNh[:, c], XNl[:, c], xtmp)

    def xn_rhs(k, t):
        c = slice(k * T + 512 * t, k * T + 512 * t + 512)
        return (XNh[:, c], XNl[:, c])

    if stop_phase <= 1:
        return finish(nc, g, ctx)
    g.wait("sync", VEC)
    reset("A2")
    def stage_gemm(wsrc, n_oc, rhs_fn, ntt, evac_fn, wb=None, wsem=None, wtmp=None):
        if wb is None:
            wb = [carve("A2", FC * 128) for _ in range(2)]
        if wtmp is None:
            wtmp = carve("A2", FC * 128)
        if wsem is None:
            wsem = [mk_sem(), mk_sem()]
        phase_barrier()
        wsnap = []
        idx = 0
        for oc in range(n_oc):
            wsnap.append(TEN.v)
            if oc >= 2:
                g.wait("sync", TEN, wsnap[oc - 1])
            g.wait("sync", VEC)
            vldw = g.dma("sync", wtmp.rearrange("p (k c) -> p k c", c=128),
                         wsrc[oc], wsem[oc % 2])
            whl = wb[oc % 2].bitcast(BF16)
            whi = whl[:, 0:FC * 128]
            wlo = whl[:, FC * 128:2 * FC * 128]
            g.wait("vector", wsem[oc % 2], vldw)
            split_hl(whi, wlo, wtmp)
            vsp = VEC.v
            for t in range(ntt):
                key = f"P{idx % 2}a"
                pv = PB[idx % 2][:][:, 0:512]
                g.wait("tensor", VEC, vsp)
                bank_wait("tensor", key)
                n_mm = 3 * FC
                i_mm = 0
                for k in range(FC):
                    rh, rl = rhs_fn(k, t)
                    for (wop, aop) in ((whi, rh), (whi, rl), (wlo, rh)):
                        mm(pv, wop[:, 128 * k:128 * k + 128], aop,
                           start=(i_mm == 0), stop=(i_mm == n_mm - 1),
                           sem=TEN if i_mm == n_mm - 1 else None)
                        i_mm += 1
                tv = TEN.v
                g.wait("vector", TEN, tv)
                bank_rel(key, (VEC, evac_fn(oc, t, pv)))
                idx += 1

    def mk_copy_evac(dst_st):
        bufs = [carve("A2", 512) for _ in range(2)]
        state = {"i": 0}

        def evac(oc, t, pv):
            i = state["i"]
            b = bufs[i % 2].bitcast(BF16)
            if i >= 2:
                g.wait("vector", STQ)
            split_hl(b[:, 0:512], b[:, 512:1024], pv)
            ev = VEC.v
            g.wait("gpsimd", VEC, ev)
            g.dma("gpsimd", dst_st[0][oc, :, 512 * t:512 * t + 512], b[:, 0:512], STQ)
            g.dma("gpsimd", dst_st[1][oc, :, 512 * t:512 * t + 512], b[:, 512:1024], STQ)
            state["i"] = i + 1
            return ev
        return evac

    def mk_v_evac(dst_st, bias_name=None, bias_base=0):
        vb = carve("A2", 512)
        tb = carve("A2", 128)

        def evac(oc, t, pv):
            waitall("vector")
            if bias_name is None:
                vcopy(vb, pv, sem=VEC)
            else:
                ts(vb, pv, vcol(bias_name, bias_base + oc), ALU.add, sem=VEC)
            ev0 = VEC.v
            for i in range(4):
                g.wait("tensor", VEC)
                tv = transpose(PB[2][:][:, 0:128], vb[:, 128 * i:128 * i + 128], sem=TEN)
                g.wait("vector", TEN, tv)
                g.wait("vector", STQ)
                vcopy(tb, PB[2][:][:, 0:128], sem=VEC)
                g.wait("gpsimd", VEC)
                g.dma("gpsimd", dst_st[oc, 512 * t + 128 * i:512 * t + 128 * (i + 1), :],
                      tb, STQ)
            return ev0
        return evac

    wbA = [carve("A2", FC * 128) for _ in range(2)]
    wtmpA = carve("A2", FC * 128)
    wsemA = [mk_sem(), mk_sem()]
    stage_gemm(wq_s, FC, xn_rhs, NT, mk_copy_evac(qst), wb=wbA, wsem=wsemA, wtmp=wtmpA)
    stage_gemm(wk_s, FC, xn_rhs, NT, mk_copy_evac(kst), wb=wbA, wsem=wsemA, wtmp=wtmpA)
    stage_gemm(wv_s, FC, xn_rhs, NT, mk_v_evac(vst), wb=wbA, wsem=wsemA, wtmp=wtmpA)

    def att_slots(mode):
        slots = []
        if mode == "L0":
            for j in range(NSLOT):
                kts = [(kt, 0, None, None) for kt in range(4 * j)]
                kts += [(4 * j + d2, 128 * d2, 128 * d2, None) for d2 in range(4)]
                slots.append((512 * j, kts))
        else:
            for si in range(4):
                kts = [(kt, 0, None, None) for kt in range(8 * si)]
                kts += [(8 * si + d2, 0, None, 8 * si + d2) for d2 in range(8)]
                slots.append((512 * si, kts))
        return slots

    def attention(qsrc, ksrc, vsrc, astdst, mode):
        phase_barrier()
        bank_wait("tensor", "P0a", "P0b", "P1a", "P1b", "P2a", "P2b", "P3a", "P3b")
        reset("A1")
        reset("A2")
        QCOLS = T if mode == "L0" else TO
        slots = att_slots(mode)
        nslot = len(slots)
        cw = T // nslot
        Kbh = [carve("A1", T // 2).bitcast(BF16) for _ in range(2)]
        Kbl = [carve("A1", T // 2).bitcast(BF16) for _ in range(2)]
        Qbh = [carve("A1", QCOLS // 2).bitcast(BF16) for _ in range(2)]
        Qbl = [carve("A1", QCOLS // 2).bitcast(BF16) for _ in range(2)]
        Vb = [carve("A1", T) for _ in range(2)]
        Eb = [[carve("A2", 1024), carve("A2", 1024)] for _ in range(2)]
        accb = [carve("A2", 1024) for _ in range(2)]
        acchl = [carve("A2", 1024).bitcast(BF16) for _ in range(2)]
        rclb = [carve("A2", 512) for _ in range(2)]
        aob = [carve("A2", 512) for _ in range(2)]
        aosp = [carve("A2", 512).bitcast(BF16) for _ in range(2)]
        if not hasattr(attention, "KCS"):
            attention.KCS = [[mk_sem(), mk_sem()] for _ in range(2)]
        KCS = attention.KCS
        if mode == "L1":
            mskb = carve("A1", 4096).bitcast(I8)
            g.wait("sync", LDQ)
            vm = g.dma("sync", mskb, mask8, LDQ)
            g.wait("vector", LDQ, vm)

        def load_chunk(pp, s, ci):
            p = 2 * pp + s
            sem = KCS[s][ci % 2]
            g.wait("sync", sem)
            kc = slice(cw * ci, cw * (ci + 1))
            qc = slice(512 * ci, 512 * ci + 512)
            g.dma("sync", Kbh[s][:, kc], ksrc[0][p][:, kc], sem)
            g.dma("sync", Kbl[s][:, kc], ksrc[1][p][:, kc], sem)
            g.dma("sync", Qbh[s][:, qc], qsrc[0][p][:, qc], sem)
            g.dma("sync", Qbl[s][:, qc], qsrc[1][p][:, qc], sem)
            g.dma("sync", Vb[s][:, kc].rearrange("p (n c) -> p n c", c=128),
                  vsrc[p][kc, :].rearrange("(n p2) c -> p2 n c", p2=128), sem)

        prev_vav = [None, None]
        prev_ev = [None, None]
        prev_vr = [None, None]
        for pp in range(NPAIR // 2):
            waitall("sync")
            g.wait("sync", STQ)
            g.wait("sync", LDQ)
            snaps = []
            for ci in range(nslot):
                for s in (0, 1):
                    load_chunk(pp, s, ci)
                snaps.append([[(KCS[s][0], KCS[s][0].v), (KCS[s][1], KCS[s][1].v)]
                              for s in (0, 1)])
            for si, (q0, kts) in enumerate(slots):
                n_kt = len(kts)
                s_exp = [{}, {}]
                t_av = [{}, {}]
                a_v = [{}, {}]
                m_v = [{}, {}]

                def emit_av(s, ki, kts=kts, n_kt=n_kt, s_exp=s_exp, t_av=t_av,
                            m_v=m_v):
                    kt, off, _, _ = kts[ki]
                    E = Eb[s][ki % 2]
                    g.wait("tensor", SCA, s_exp[s][ki])
                    if off > 0:
                        g.wait("tensor", VEC, m_v[s][ki])
                    if ki == 0 and prev_ev[s] is not None:
                        g.wait("tensor", VEC, prev_ev[s])
                    vb = Vb[s][:, 128 * kt:128 * kt + 128]
                    mm(PB[s][:][0:64, 0:512], vb[:, 0:64], E[:, 0:512],
                       start=(ki == 0), stop=(ki == n_kt - 1), tile_position=(0, 0))
                    t_av[s][ki] = mm(PB[s][:][64:128, 512:1024], vb[:, 64:128],
                                     E[:, 512:1024], start=(ki == 0),
                                     stop=(ki == n_kt - 1), tile_position=(0, 64),
                                     sem=TEN)

                for s in (0, 1):
                    g.op("vector", lambda e, a=accb[s]: e.memset(a[:], 0.0), VEC)
                    for (sm, v) in snaps[si][s]:
                        g.wait("tensor", sm, v)
                for ki, (kt, off, trio, mki) in enumerate(kts):
                    for s in (0, 1):
                        if ki >= 1:
                            g.wait("tensor", SCA, s_exp[s][ki - 1])
                        elif prev_vr[s] is not None:
                            g.wait("tensor", SCA, prev_vr[s])
                        ks = slice(128 * kt, 128 * kt + 128)
                        qs = slice(q0 + off, q0 + 512)
                        pb = PB[2 + s][:]
                        for i_c, (kk, qq) in enumerate(
                                ((Kbh[s], Qbh[s]), (Kbh[s], Qbl[s]), (Kbl[s], Qbh[s]))):
                            mm(pb[:, off:512], kk[0:64, ks], qq[0:64, qs],
                               start=(i_c == 0), stop=(i_c == 2))
                            mm(pb[:, 512 + off:1024], kk[64:128, ks], qq[64:128, qs],
                               start=(i_c == 0), stop=(i_c == 2),
                               sem=TEN if i_c == 2 else None)
                        t_sc = TEN.v
                        pb3 = pb[:, 0:1024].rearrange("p (h q) -> p h q", q=512)
                        if mki is not None:
                            g.wait("vector", TEN, t_sc)
                            stt(pb3, mskb[:, 512 * mki:512 * mki + 512]
                                .unsqueeze(1).broadcast_to((128, 2, 512)),
                                -1e30, pb3, ALU.mult, ALU.add)
                            g.wait("scalar", VEC)
                        elif trio is not None:
                            g.wait("vector", TEN, t_sc)
                            tt(pb3[:, :, trio:trio + 128], pb3[:, :, trio:trio + 128],
                               sb_tri[:].unsqueeze(1).broadcast_to((128, 2, 128)),
                               ALU.add)
                            g.wait("scalar", VEC)
                        else:
                            g.wait("scalar", TEN, t_sc)
                        E = Eb[s][ki % 2]
                        if ki >= 2:
                            g.wait("scalar", TEN, t_av[s][ki - 2])
                            g.wait("scalar", VEC, a_v[s][ki - 2])
                        elif prev_vav[s] is not None:
                            g.wait("scalar", TEN, prev_vav[s])
                        E3 = E.rearrange("p (h q) -> p h q", q=512)
                        s_exp[s][ki] = act(E3[:, :, off:512], pb3[:, :, off:512],
                                           AF.Exp, scale=f32(0.125), sem=SCA)
                        if off > 0:
                            g.op("vector", lambda e, a=E3, o=off:
                                 e.memset(a[:, :, 0:o], 0.0), VEC)
                            m_v[s][ki] = VEC.v
                        g.wait("vector", SCA, s_exp[s][ki])
                        a3 = accb[s].rearrange("p (h q) -> p h q", q=512)
                        tt(a3, a3, E3, ALU.add)
                        a_v[s][ki] = VEC.v
                    for s in (0, 1):
                        if ki >= 1:
                            emit_av(s, ki - 1)
                for s in (0, 1):
                    emit_av(s, n_kt - 1)
                vav = [t_av[s][n_kt - 1] for s in (0, 1)]
                vsp = [0, 0]
                for s in (0, 1):
                    g.wait("vector", STQ)
                    ahl = acchl[s]
                    split_hl(ahl[:, 0:1024], ahl[:, 1024:2048], accb[s])
                    vsp[s] = VEC.v
                vsum = [0, 0]
                for s in (0, 1):
                    g.wait("tensor", VEC, vsp[s])
                    g.wait("tensor", SCA, s_exp[s][n_kt - 1])
                    ahl = acchl[s]
                    pbs = PB[2 + s][:]
                    mm(pbs[0:64, 0:512], sb_ones64b_t[:], ahl[:, 0:512],
                       start=True, stop=False, tile_position=(0, 0))
                    mm(pbs[0:64, 0:512], sb_ones64b_t[:], ahl[:, 1024:1536],
                       start=False, stop=True, tile_position=(0, 0))
                    mm(pbs[64:128, 0:512], sb_ones64b_t[:], ahl[:, 512:1024],
                       start=True, stop=False, tile_position=(0, 64))
                    mm(pbs[64:128, 0:512], sb_ones64b_t[:], ahl[:, 1536:2048],
                       start=False, stop=True, tile_position=(0, 64), sem=TEN)
                    vsum[s] = TEN.v
                vr = [0, 0]
                for s in (0, 1):
                    g.wait("scalar", TEN, vsum[s])
                    act(rclb[s], PB[2 + s][:][:, 0:512], AF.Ln)
                    act(rclb[s], rclb[s], AF.Exp, scale=f32(-1.0), sem=SCA)
                    vr[s] = SCA.v
                for s in (0, 1):
                    g.wait("vector", SCA, vr[s])
                    g.wait("vector", TEN, vav[s])
                    tt(aob[s][0:64, :], PB[s][:][0:64, 0:512], rclb[s][0:64, :],
                       ALU.mult)
                    tt(aob[s][64:128, :], PB[s][:][64:128, 512:1024],
                       rclb[s][64:128, :], ALU.mult)
                    av = aosp[s]
                    split_hl(av[:, 0:512], av[:, 512:1024], aob[s])
                    ev = VEC.v
                    g.wait("gpsimd", VEC, ev)
                    p = 2 * pp + s
                    g.dma("gpsimd", astdst[0][p, :, q0:q0 + 512], av[:, 0:512], STQ)
                    g.dma("gpsimd", astdst[1][p, :, q0:q0 + 512], av[:, 512:1024], STQ)
                    prev_vav[s] = vav[s]
                    prev_ev[s] = ev
                    prev_vr[s] = vr[s]
        for kk in ("P0a", "P0b", "P1a", "P1b", "P2a", "P2b", "P3a", "P3b"):
            bank_rel(kk, (VEC, VEC.v), (SCA, SCA.v))

    attention(qst, kst, vst, ast, "L0")

    if stop_phase <= 3:
        return finish(nc, g, ctx)
    res_gemm(wp_s, FC, ast, x_fm, "bp", y1st, NT)

    natsem = [mk_sem(), mk_sem()]
    qw_stsem = [mk_sem(), mk_sem()]

    def build_qw_steps():
        natb_w = [A2t[:][:, 8704:9728], A2t[:][:, 9728:10752]]
        QSCR.update(A=A2t[:][:, 10752:11776], S1=A2t[:][:, 11776:12800],
                    S2=A2t[:][:, 12800:13824], G=A2t[:][:, 13824:14848],
                    M=A2t[:][:, 14848:14912], SS=A2t[:][:, 14912:14976],
                    R=A2t[:][:, 14976:15040])
        trbig_w = [A2t[:][:, 15040:15552], A2t[:][:, 15552:16064]]
        st = {"lastten": {}, "split": {}, "store": {}, "gi": [0], "first": [True]}

        def mk_load(src, si, w):
            def f():
                sem = natsem[si % 2]
                g.wait("sync", sem)
                lt = st["lastten"].get(si % 2)
                if lt is not None:
                    g.wait("sync", TEN, lt)
                st[("ldv", si % 2)] = g.dma("sync", natb_w[si % 2][:, 0:w], src, sem)
            return f

        def mk_quant(si, w):
            def f():
                sem = natsem[si % 2]
                v = st[("ldv", si % 2)]
                g.wait("vector", sem, v)
                g.wait("scalar", sem, v)
                st[("qv", si)] = quantize(natb_w[si % 2][:, 0:w], w,
                                          natb_w[si % 2][:, 0:w])
            return f

        def mk_group(si, dst, oc, k0g, kloc, kn):
            def f():
                if st["first"][0]:
                    bank_wait("tensor", "P2b", "P3b")
                    st["first"][0] = False
                gp = st["gi"][0] % 2
                st["gi"][0] += 1
                pb = PB[2 + gp][:][:, 512:1024]
                trv = trbig_w[gp].bitcast(BF16)
                ssem = qw_stsem[gp]
                pv = st["split"].get(gp)
                if pv is not None:
                    g.wait("tensor", VEC, pv)
                g.wait("tensor", VEC, st[("qv", si)])
                tv = 0
                for k in range(kn):
                    tv = transpose(pb[:, 128 * k:128 * k + 128],
                                   natb_w[si % 2][:, 128 * (kloc + k):
                                                  128 * (kloc + k) + 128], sem=TEN)
                g.wait("vector", TEN, tv)
                sv_ = st["store"].get(gp)
                if sv_ is not None:
                    g.wait("vector", ssem, sv_)
                split_hl(trv[:, 0:128 * kn], trv[:, 128 * kn:256 * kn],
                         pb[:, 0:128 * kn])
                st["split"][gp] = VEC.v
                g.wait("gpsimd", VEC, VEC.v)
                g.dma("gpsimd", dst[0][oc, :, k0g:k0g + kn, :]
                      .rearrange("p k c -> p (k c)"), trv[:, 0:128 * kn], ssem)
                g.dma("gpsimd", dst[1][oc, :, k0g:k0g + kn, :]
                      .rearrange("p k c -> p (k c)"), trv[:, 128 * kn:256 * kn], ssem)
                st["store"][gp] = ssem.v
                st["lastten"][si % 2] = TEN.v
            return f

        si = 0
        for (nat, dst, n_och, icols) in ((wg_nat, wgq_s, DIC, D),
                                         (wu_nat, wuq_s, DIC, D),
                                         (wd_nat, wdq_s, FC, DI)):
            for oc in range(n_och):
                for off in range(0, icols, 1024):
                    w = min(1024, icols - off)
                    qw_steps.append(mk_load(nat[oc][:, off:off + w], si, w))
                    qw_steps.append(mk_quant(si, w))
                    for kloc in range(0, w // 128, 4):
                        kn = min(4, w // 128 - kloc)
                        qw_steps.append(mk_group(si, dst, oc, off // 128 + kloc,
                                                 kloc, kn))
                    si += 1

    build_qw_steps()

    if stop_phase <= 4:
        return finish(nc, g, ctx)
    reset("A1")
    ZNh = carve("A1", FC * T // 2).bitcast(BF16)
    ZNl = carve("A1", FC * T // 2).bitcast(BF16)
    reset("A2")
    M1 = carve("A2", 512)
    R1 = carve("A2", 512)
    sqb = carve("A2", 1024)
    rd3 = Reader("A2", 512, n=3)
    rd4 = Reader("A2", 512, n=3)
    ztmp = carve("A2", 512)
    for t in range(NT):
        ln_stats(lambda fc, t_, t=t: rd3.rd(y1st[fc, :, 512 * t:512 * t + 512],
                                            "tensor", "scalar"),
                 1, M1, R1, sqb, barrier=(t == 0))
        for fc in range(FC):
            ch = rd4.rd(y1st[fc, :, 512 * t:512 * t + 512], "vector")
            c = slice(fc * T + 512 * t, fc * T + 512 * t + 512)
            tt(ztmp, ch, M1, ALU.subtract)
            tt(ztmp, ztmp, R1, ALU.mult)
            split_hl(ZNh[:, c], ZNl[:, c], ztmp)
            tick(2)

    def res_gemm_sbuf(wsrc, rhs_fn, res_src, bias_name, dst_st, ntt, sign_w=False):
        phase_barrier()
        bank_wait("tensor", "P0a", "P1a")
        reset("A2")
        res_r = Reader("A2", 512, n=3)
        outb = [carve("A2", 512) for _ in range(2)]
        wallsem = mk_sem()
        if sign_w:
            wsb = carve("A2", FC * FC * 64).bitcast(BF16)
            wtmp = carve("A2", FC * 128)
            for oc in range(FC):
                g.wait("sync", SCA)
                v = g.dma("sync", wtmp.rearrange("p (k c) -> p k c", c=128),
                          wsrc[oc], wallsem)
                g.wait("scalar", wallsem, v)
                act(wsb[:, 1024 * oc:1024 * (oc + 1)], wtmp, AF.Sign, sem=SCA)
            g.wait("tensor", SCA)
        else:
            wsb = carve("A2", FC * FC * 128)
            wtmp = carve("A2", FC * 128)
            for oc in range(FC):
                g.wait("sync", VEC)
                v = g.dma("sync", wtmp.rearrange("p (k c) -> p k c", c=128),
                          wsrc[oc], wallsem)
                g.wait("vector", wallsem, v)
                wv = wsb[:, 1024 * oc:1024 * (oc + 1)].bitcast(BF16)
                split_hl(wv[:, 0:1024], wv[:, 1024:2048], wtmp)
            g.wait("tensor", VEC)
        idx = 0
        for t in range(ntt):
            for oc in range(FC):
                key = f"P{idx % 2}a"
                pv = PB[idx % 2][:][:, 0:512]
                bank_wait("tensor", key)
                if sign_w:
                    n_mm = 2 * FC
                    i_mm = 0
                    for k in range(FC):
                        sw = wsb[:, 1024 * oc + 128 * k:1024 * oc + 128 * (k + 1)]
                        rh, rl = rhs_fn(k, t)
                        for aop in (rh, rl):
                            mm(pv, sw, aop, start=(i_mm == 0), stop=(i_mm == n_mm - 1),
                               sem=TEN if i_mm == n_mm - 1 else None)
                            i_mm += 1
                else:
                    wv = wsb[:, 1024 * oc:1024 * (oc + 1)].bitcast(BF16)
                    n_mm = 3 * FC
                    i_mm = 0
                    for k in range(FC):
                        wh = wv[:, 128 * k:128 * k + 128]
                        wl = wv[:, 1024 + 128 * k:1024 + 128 * (k + 1)]
                        rh, rl = rhs_fn(k, t)
                        for (wop, aop) in ((wh, rh), (wh, rl), (wl, rh)):
                            mm(pv, wop, aop, start=(i_mm == 0), stop=(i_mm == n_mm - 1),
                               sem=TEN if i_mm == n_mm - 1 else None)
                            i_mm += 1
                tv = TEN.v
                rb = res_r.rd(res_src[oc, :, 512 * t:512 * t + 512], "vector")
                ob = outb[idx % 2]
                if idx >= 2:
                    g.wait("vector", STQ)
                g.wait("vector", TEN, tv)
                if bias_name is not None:
                    ts(ob, pv, vcol(bias_name, oc), ALU.add)
                    bank_rel(key)
                    tt(ob, ob, rb, ALU.add)
                else:
                    tt(ob, pv, rb, ALU.add)
                    bank_rel(key)
                ev = VEC.v
                g.wait("gpsimd", VEC, ev)
                g.dma("gpsimd", dst_st[oc, :, 512 * t:512 * t + 512], ob, STQ)
                idx += 1
                tick(2)

    def zn_rhs(k, t):
        c = slice(k * T + 512 * t, k * T + 512 * t + 512)
        return (ZNh[:, c], ZNl[:, c])

    res_gemm_sbuf(w0t_s, zn_rhs, y1st, None, y2st, NT, sign_w=True)

    if stop_phase <= 5:
        return finish(nc, g, ctx)
    phase_barrier()
    reset("A2")
    rda = Reader("A2", 256, n=3)
    rdb = Reader("A2", 256, n=3)
    bo_ = [carve("A2", 256) for _ in range(2)]
    st_hist = []
    ii = 0
    for oc in range(FC):
        for j in range(NSLOT):
            ba = rda.rd(y2st[oc, :, 256 * OWN_A[j]:256 * OWN_A[j] + 256], "vector")
            bb = rdb.rd(y2st[oc, :, 256 * OWN_B[j]:256 * OWN_B[j] + 256], "vector")
            ob = bo_[ii % 2]
            if ii >= 2:
                g.wait("vector", STQ)
            ts(ob, ba, vcol("selA"), ALU.mult)
            stt(ob, bb, vcol("selB"), ob, ALU.mult, ALU.add, sem=VEC)
            g.wait("gpsimd", VEC)
            st_hist.append(g.dma("gpsimd", y2own[oc, :, 256 * j:256 * j + 256], ob, STQ))
            ii += 1
            tick(2)

    if stop_phase <= 6:
        return finish(nc, g, ctx)
    reset("A1")
    M2 = carve("A1", T)
    R2 = carve("A1", T)
    P2 = R2
    reset("A2")
    sqb = carve("A2", 1024)
    rd5 = Reader("A2", 512, n=3)
    ln_stats(lambda fc, t_: rd5.rd(y2st[fc, :, 512 * t_:512 * t_ + 512], "tensor", "scalar"),
             NT, M2, R2, sqb)
    tmpc = carve("A2", 512)
    for t in range(NT):
        c = slice(512 * t, 512 * t + 512)
        g.wait("vector", SCA)
        tt(tmpc, R2[:, c], R2[:, c], ALU.mult)
        ts(tmpc, tmpc, -float(EPS), ALU.mult, float(1.0 + EPS), ALU.add, sem=VEC)
        g.wait("scalar", VEC)
        act(tmpc, tmpc, AF.Ln)
        act(tmpc, tmpc, AF.Exp, scale=f32(-0.5), sem=SCA)
        g.wait("vector", SCA)
        tt(P2[:, c], R2[:, c], tmpc, ALU.mult, sem=VEC)
    g.wait("scalar", VEC)

    rd6 = Reader("A2", 512, n=3)
    n1b = [carve("A2", 512) for _ in range(2)]
    st_hist = []
    ii = 0
    for t in range(NT):
        for fc in range(FC):
            ch = rd6.rd(y2st[fc, :, 512 * t:512 * t + 512], "vector")
            nb = n1b[ii % 2]
            if ii >= 2:
                g.wait("vector", STQ)
            tt(nb, ch, M2[:, 512 * t:512 * t + 512], ALU.subtract)
            tt(nb, nb, P2[:, 512 * t:512 * t + 512], ALU.mult)
            ts(nb, nb, vcol("g1", fc), ALU.mult, vcol("b1", fc), ALU.add, sem=VEC)
            g.wait("gpsimd", VEC)
            st_hist.append(g.dma("gpsimd", n1st[fc, :, 512 * t:512 * t + 512], nb, STQ))
            ii += 1
            tick(2)

    while qw_steps:
        qw_steps.pop(0)()
    for s_ in qw_stsem:
        for eng in ("vector", "scalar", "tensor"):
            g.wait(eng, s_)

    if stop_phase <= 7:
        return finish(nc, g, ctx)
    reset("A1")
    N1h = carve("A1", FC * T // 2).bitcast(BF16)
    N1l = carve("A1", FC * T // 2).bitcast(BF16)
    reset("A2")
    phase_barrier()
    rd7 = Reader("A2", 512, n=1)
    g.wait("sync", STQ)
    for t in range(NT):
        for fc in range(FC):
            ch = rd7.rd(n1st[fc, :, 512 * t:512 * t + 512], "vector")
            c = slice(fc * T + 512 * t, fc * T + 512 * t + 512)
            split_hl(N1h[:, c], N1l[:, c], ch)

    def n1_rhs(k, t):
        c = slice(k * T + 512 * t, k * T + 512 * t + 512)
        return (N1h[:, c], N1l[:, c])

    C2b_ = carve("A2", T)
    S2b_ = carve("A2", T)
    g.wait("sync", LDQ)
    v1_ = g.dma("sync", C2b_, c2t, LDQ)
    v2_ = g.dma("sync", S2b_, s2t, LDQ)
    g.wait("vector", LDQ)

    def mk_rope_evac(dst_st, ctab, stab, bias_base):
        kpre = [carve("A2", 512) for _ in range(2)]
        krot = [carve("A2", 512) for _ in range(2)]
        kout1 = carve("A2", 512)
        kout = [kout1, kout1]
        ksp = [carve("A2", 512) for _ in range(2)]
        rsem = mk_sem()
        state = {"i": 0}

        def evac(oc, t, pv):
            i = state["i"]
            b0, b1, b2 = kpre[i % 2], krot[i % 2], kout[i % 2]
            if i >= 2:
                g.wait("vector", rsem, 64 * (i - 1))
                g.wait("vector", STQ)
            ts(b0, pv, vcol("bqkv", bias_base + oc), ALU.add)
            ev0 = VEC.v
            g.wait("gpsimd", VEC, ev0)
            for (d0, s0) in ((0, 32), (32, 0), (64, 96), (96, 64)):
                g.dma("gpsimd", b1[d0:d0 + 32, :], b0[s0:s0 + 32, :], rsem)
            c = slice(512 * t, 512 * t + 512)
            g.wait("vector", rsem, 64 * (i + 1))
            tt(b2, b0, ctab[:, c], ALU.mult)
            tt(b1, b1, stab[:, c], ALU.mult)
            tt(b2, b2, b1, ALU.add)
            bh = ksp[i % 2].bitcast(BF16)
            split_hl(bh[:, 0:512], bh[:, 512:1024], b2)
            ev = VEC.v
            g.wait("gpsimd", VEC, ev)
            g.dma("gpsimd", dst_st[0][oc, :, c], bh[:, 0:512], STQ)
            g.dma("gpsimd", dst_st[1][oc, :, c], bh[:, 512:1024], STQ)
            state["i"] = i + 1
            return ev0
        return evac

    class WView:
        def __init__(self, base):
            self.base = base

        def __getitem__(self, i):
            return wqkv_s[self.base + i]

    wb_c2 = [carve("A2", FC * 128) for _ in range(2)]
    wtmp_c2 = carve("A2", FC * 128)
    wsem_c2 = [mk_sem(), mk_sem()]
    stage_gemm(WView(FC), FC, n1_rhs, NT, mk_rope_evac(kst, C2b_, S2b_, FC),
               wb=wb_c2, wsem=wsem_c2, wtmp=wtmp_c2)
    stage_gemm(WView(2 * FC), FC, n1_rhs, NT, mk_v_evac(vst, "bqkv", 2 * FC),
               wb=wb_c2, wsem=wsem_c2, wtmp=wtmp_c2)

    if stop_phase <= 8:
        return finish(nc, g, ctx)
    reset("A1")
    M2o = carve("A1", TO)
    R2o = carve("A1", TO)
    P2o = R2o
    N1oh = carve("A1", FC * TO // 2).bitcast(BF16)
    N1ol = carve("A1", FC * TO // 2).bitcast(BF16)
    reset("A2")
    sqb = carve("A2", 1024)
    g.wait("sync", STQ)
    rd8 = Reader("A2", 512, n=3)
    ln_stats(lambda fc, t_: rd8.rd(y2own[fc, :, 512 * t_:512 * t_ + 512], "tensor", "scalar"),
             NTO, M2o, R2o, sqb)
    tmpo = carve("A2", 512)
    for t in range(NTO):
        c = slice(512 * t, 512 * t + 512)
        g.wait("vector", SCA)
        tt(tmpo, R2o[:, c], R2o[:, c], ALU.mult)
        ts(tmpo, tmpo, -float(EPS), ALU.mult, float(1.0 + EPS), ALU.add, sem=VEC)
        g.wait("scalar", VEC)
        act(tmpo, tmpo, AF.Ln)
        act(tmpo, tmpo, AF.Exp, scale=f32(-0.5), sem=SCA)
        g.wait("vector", SCA)
        tt(P2o[:, c], R2o[:, c], tmpo, ALU.mult, sem=VEC)
    g.wait("scalar", VEC)
    rd9 = Reader("A2", 512, n=3)
    ntmp = carve("A2", 512)
    for t in range(NTO):
        for fc in range(FC):
            ch = rd9.rd(y2own[fc, :, 512 * t:512 * t + 512], "vector")
            c = slice(fc * TO + 512 * t, fc * TO + 512 * t + 512)
            tt(ntmp, ch, M2o[:, 512 * t:512 * t + 512], ALU.subtract)
            tt(ntmp, ntmp, P2o[:, 512 * t:512 * t + 512], ALU.mult)
            ts(ntmp, ntmp, vcol("g1", fc), ALU.mult, vcol("b1", fc), ALU.add)
            split_hl(N1oh[:, c], N1ol[:, c], ntmp)

    def n1o_rhs(k, t):
        c = slice(k * TO + 512 * t, k * TO + 512 * t + 512)
        return (N1oh[:, c], N1ol[:, c])

    C2ob = carve("A2", TO)
    S2ob = carve("A2", TO)
    g.wait("sync", LDQ)
    v1_ = g.dma("sync", C2ob, c2o, LDQ)
    v2_ = g.dma("sync", S2ob, s2o, LDQ)
    g.wait("vector", LDQ)
    stage_gemm(WView(0), FC, n1o_rhs, NTO, mk_rope_evac(qst1, C2ob, S2ob, 0))

    if stop_phase <= 9:
        return finish(nc, g, ctx)
    reset("A1")
    attention(qst1, kst, vst, ast1, "L1")

    if stop_phase <= 10:
        return finish(nc, g, ctx)
    reset("A1")
    Z2 = carve("A1", FC * TO)
    M2r = carve("A1", TO)
    R2r = carve("A1", TO)
    reset("A2")
    sqb = carve("A2", 1024)
    rd10 = Reader("A2", 512, n=3)
    ln_stats(lambda fc, t_: rd10.rd(y2own[fc, :, 512 * t_:512 * t_ + 512], "tensor", "scalar"),
             NTO, M2r, R2r, sqb)

    def wo_gemm():
        phase_barrier()
        bank_wait("tensor", "P0a", "P1a")
        whl = carve("A1", FC * FC * 128)
        wtmp = carve("A2", FC * 128)
        wallsem = mk_sem()
        rhs_rh = Reader("A2", FC * 256, n=1)
        rhs_rl = Reader("A2", FC * 256, n=1)
        res_r = Reader("A2", 512, n=3)
        zb = [carve("A2", 512) for _ in range(2)]
        for oc in range(FC):
            g.wait("sync", VEC)
            v = g.dma("sync", wtmp.rearrange("p (k c) -> p k c", c=128),
                      wo_s[oc], wallsem)
            g.wait("vector", wallsem, v)
            wv = whl[:, 1024 * oc:1024 * (oc + 1)].bitcast(BF16)
            split_hl(wv[:, 0:1024], wv[:, 1024:2048], wtmp)
        g.wait("tensor", VEC)
        idx = 0
        for t in range(NTO):
            rbh = rhs_rh.rd(ast1[0][:, :, 512 * t:512 * t + 512], "tensor").bitcast(BF16)
            rbl = rhs_rl.rd(ast1[1][:, :, 512 * t:512 * t + 512], "tensor").bitcast(BF16)
            for oc in range(FC):
                key = f"P{idx % 2}a"
                pv = PB[idx % 2][:][:, 0:512]
                bank_wait("tensor", key)
                wv = whl[:, 1024 * oc:1024 * (oc + 1)].bitcast(BF16)
                i_mm = 0
                for k in range(FC):
                    wh = wv[:, 128 * k:128 * k + 128]
                    wl = wv[:, 1024 + 128 * k:1024 + 128 * (k + 1)]
                    ah = rbh[:, 512 * k:512 * k + 512]
                    al = rbl[:, 512 * k:512 * k + 512]
                    for (wop, aop) in ((wh, ah), (wh, al), (wl, ah)):
                        mm(pv, wop, aop, start=(i_mm == 0), stop=(i_mm == 3 * FC - 1),
                           sem=TEN if i_mm == 3 * FC - 1 else None)
                        i_mm += 1
                tv = TEN.v
                rb = res_r.rd(y2own[oc, :, 512 * t:512 * t + 512], "vector")
                zbb = zb[idx % 2]
                c = slice(512 * t, 512 * t + 512)
                g.wait("vector", TEN, tv)
                tt(zbb, rb, M2r[:, c], ALU.subtract)
                tt(zbb, zbb, R2r[:, c], ALU.mult)
                dst = Z2[:, oc * TO + 512 * t:oc * TO + 512 * t + 512]
                ts(dst, pv, vcol("bo", oc), ALU.add)
                bank_rel(key)
                tt(dst, dst, zbb, ALU.add)
                idx += 1

    wo_gemm()

    if stop_phase <= 11:
        return finish(nc, g, ctx)
    w1sem = [mk_sem(), mk_sem()]
    w2sem = [mk_sem(), mk_sem()]
    wstg_sem = mk_sem()

    def stage_whl():
        phase_barrier()
        reset("A2")
        wtmp = [carve("A2", DMC * 128) for _ in range(2)]
        whl = [carve("A2", DMC * 128) for _ in range(2)]
        lsem = [wstg_sem, mk_sem()]
        ssem = [mk_sem(), mk_sem()]
        split_v = {}
        store_v = {}
        ii = 0
        for (wsrc, dst, n_oc, n_k) in ((w1_s, w1hl_s, DMC, FC),
                                       (w2_s, w2hl_s, FC, DMC)):
            for oc in range(n_oc):
                bi = ii % 2
                g.wait("sync", lsem[bi])
                sv = split_v.get(bi)
                if sv is not None:
                    g.wait("sync", VEC, sv)
                v = g.dma("sync", wtmp[bi][:, 0:n_k * 128]
                          .rearrange("p (k c) -> p k c", c=128), wsrc[oc], lsem[bi])
                g.wait("vector", lsem[bi], v)
                stv = store_v.get(bi)
                if stv is not None:
                    g.wait("vector", ssem[bi], stv)
                wv = whl[bi][:, 0:n_k * 128].bitcast(BF16)
                split_hl(wv[:, 0:n_k * 128], wv[:, n_k * 128:2 * n_k * 128],
                         wtmp[bi][:, 0:n_k * 128])
                split_v[bi] = VEC.v
                g.wait("gpsimd", VEC, VEC.v)
                g.dma("gpsimd", dst[oc], wv[:, 0:2 * n_k * 128], ssem[bi])
                store_v[bi] = ssem[bi].v
                ii += 1
        for s_ in ssem:
            for eng in ("vector", "scalar", "tensor"):
                g.wait(eng, s_)

    stage_whl()

    reset("A2")
    M3 = carve("A2", TO)
    R3 = carve("A2", TO)
    sqb = carve("A2", 1024)

    def z2_chunk(fc, t):
        return Z2[:, fc * TO + 512 * t:fc * TO + 512 * t + 512]

    ln_stats(z2_chunk, NTO, M3, R3, sqb)

    n2b = carve("A2", FC * 512)
    w1b = [carve("A2", FC * 128) for _ in range(2)]
    gtmp = carve("A2", 512)
    ob2 = carve("A2", 256)
    res_r2 = Reader("A2", 256, n=3)

    def mlp():
        phase_barrier()
        bank_wait("tensor", "P0a", "P1a", "P2a")
        w2b = [A1t[:][:, 16384:16384 + 4096], A1t[:][:, 20480:20480 + 4096]]
        hb = A1t[:][:, 24576:24576 + DMC * 256].bitcast(BF16)
        hh = hb[:, 0:DMC * 256]
        hl = hb[:, DMC * 256:2 * DMC * 256]
        g.wait("sync", STQ)
        for t in range(NTO):
            g.wait("vector", TEN)
            for fc in range(FC):
                c = slice(512 * t, 512 * t + 512)
                tt(gtmp, z2_chunk(fc, t), M3[:, c], ALU.subtract)
                tt(gtmp, gtmp, R3[:, c], ALU.mult)
                ts(gtmp, gtmp, vcol("g2", fc), ALU.mult, vcol("b2", fc), ALU.add)
                nv = n2b.bitcast(BF16)
                split_hl(nv[:, 512 * fc:512 * fc + 512],
                         nv[:, FC * 512 + 512 * fc:FC * 512 + 512 * (fc + 1)], gtmp)
            vn2 = VEC.v
            g.wait("tensor", VEC, vn2)
            nv = n2b.bitcast(BF16)
            for half in range(2):
                hrel = []
                wsnap = []
                for oc in range(DMC):
                    wsnap.append(TEN.v)
                    if oc >= 2:
                        g.wait("sync", TEN, wsnap[oc - 1])
                    vldw = g.dma("sync", w1b[oc % 2].bitcast(BF16), w1hl_s[oc],
                                 w1sem[oc % 2])
                    pv = PB[oc % 2][:][:, 0:256]
                    g.wait("tensor", w1sem[oc % 2], vldw)
                    if oc >= 2:
                        g.wait("tensor", SCA, hrel[oc - 2])
                    wv = w1b[oc % 2].bitcast(BF16)
                    i_mm = 0
                    for k in range(FC):
                        wh = wv[:, 128 * k:128 * k + 128]
                        wl = wv[:, FC * 128 + 128 * k:FC * 128 + 128 * (k + 1)]
                        cq = slice(512 * k + 256 * half, 512 * k + 256 * half + 256)
                        ah = nv[:, cq]
                        al = nv[:, FC * 512:2 * FC * 512][:, cq]
                        for (wop, aop) in ((wh, ah), (wh, al), (wl, ah)):
                            mm(pv, wop, aop, start=(i_mm == 0), stop=(i_mm == 3 * FC - 1),
                               sem=TEN if i_mm == 3 * FC - 1 else None)
                            i_mm += 1
                    g.wait("scalar", TEN)
                    g.wait("scalar", VEC)
                    act(gtmp[:, 0:256], pv, AF.Gelu, bias=vcol("b1m", oc), sem=SCA)
                    hrel.append(SCA.v)
                    g.wait("vector", SCA)
                    split_hl(hh[:, 256 * oc:256 * oc + 256],
                             hl[:, 256 * oc:256 * oc + 256], gtmp[:, 0:256])
                g.wait("tensor", VEC)
                wsnap2 = []
                for oc in range(FC):
                    wsnap2.append(TEN.v)
                    if oc >= 2:
                        g.wait("sync", TEN, wsnap2[oc - 1])
                    elif oc == 0:
                        g.wait("sync", TEN)
                    vldw = g.dma("sync", w2b[oc % 2].bitcast(BF16), w2hl_s[oc],
                                 w2sem[oc % 2])
                    pv = PB[2][:][:, 0:256]
                    g.wait("tensor", w2sem[oc % 2], vldw)
                    g.wait("tensor", VEC)
                    wv = w2b[oc % 2].bitcast(BF16)
                    i_mm = 0
                    for k in range(DMC):
                        wh = wv[:, 128 * k:128 * k + 128]
                        wl = wv[:, DMC * 128 + 128 * k:DMC * 128 + 128 * (k + 1)]
                        ah = hh[:, 256 * k:256 * k + 256]
                        al = hl[:, 256 * k:256 * k + 256]
                        for (wop, aop) in ((wh, ah), (wh, al), (wl, ah)):
                            mm(pv, wop, aop, start=(i_mm == 0), stop=(i_mm == 3 * DMC - 1),
                               sem=TEN if i_mm == 3 * DMC - 1 else None)
                            i_mm += 1
                    tv = TEN.v
                    cq = slice(512 * t + 256 * half, 512 * t + 256 * half + 256)
                    rb = res_r2.rd(y2own[oc, :, cq], "vector")
                    g.wait("vector", TEN, tv)
                    g.wait("vector", STQ)
                    ts(ob2, pv, vcol("b2m", oc), ALU.add)
                    tt(ob2, ob2, Z2[:, oc * TO + 256 * (2 * t + half):
                                    oc * TO + 256 * (2 * t + half) + 256], ALU.add)
                    tt(ob2, ob2, rb, ALU.add)
                    ev = VEC.v
                    g.wait("gpsimd", VEC, ev)
                    g.dma("gpsimd", y3own[oc, :, cq], ob2, STQ)
                    g.wait("tensor", VEC, ev)

    mlp()

    if stop_phase <= 12:
        return finish(nc, g, ctx)
    reset("A1")
    QSCR.update(A=carve("A1", 1024), S1=carve("A1", 1024), S2=carve("A1", 1024),
                G=carve("A1", 1024), M=carve("A1", 64), SS=carve("A1", 64),
                R=carve("A1", 64))
    carve("A1", 1216)

    if stop_phase <= 13:
        return finish(nc, g, ctx)
    reset("A2")
    M4 = carve("A2", TO)
    R4 = carve("A2", TO)
    a2_mark = arena_off["A2"]
    sqb = carve("A2", 1024)
    g.wait("sync", STQ)
    rd11 = Reader("A2", 512, n=3)
    ln_stats(lambda fc, t_: rd11.rd(y3own[fc, :, 512 * t_:512 * t_ + 512], "tensor", "scalar"),
             NTO, M4, R4, sqb)
    arena_off["A2"] = a2_mark

    XQb = A1t[:][:, 5440:5440 + 4096].bitcast(BF16)
    XTM = A1t[:][:, 9536:9536 + 4096]
    HTM = A1t[:][:, 9536:9536 + 4 * DI]
    HQFb = A1t[:][:, 20800:20800 + DIC * 512].bitcast(BF16)
    xb = A1t[:][:, 32064:32064 + 512]
    wgb = [carve("A2", FC * 128) for _ in range(2)]
    wub = [carve("A2", FC * 128) for _ in range(2)]
    wdb = [carve("A2", DIC * 128) for _ in range(2)]
    ub = carve("A2", 512)
    rd12 = Reader("A2", 512, n=2)
    rd13 = Reader("A2", 512, n=2)
    gusem = [mk_sem(), mk_sem()]
    dsem2 = [mk_sem(), mk_sem()]

    def ffn():
        phase_barrier()
        bank_wait("tensor", "P0a", "P0b", "P1a", "P1b", "P2a", "P2b")
        gu_guard = {}
        d_guard = {}
        tp_guard = {}
        pg_guard = {}
        def tp_slot(j):
            return PB[2][:][:, 512 * (j % 2):512 * (j % 2) + 128]
        XQH = XQb[:, 0:FC * 512]
        XQL = XQb[:, FC * 512:2 * FC * 512]
        HQH = HQFb[:, 0:DIC * 512]
        HQL = HQFb[:, DIC * 512:2 * DIC * 512]

        def x_stage(t):
            g.wait("vector", TEN)
            g.wait("scalar", TEN)
            g.wait("vector", STQ)
            for fc in range(FC):
                ch = rd12.rd(y3own[fc, :, 512 * t:512 * t + 512], "vector")
                c = slice(512 * t, 512 * t + 512)
                g.wait("vector", TEN)
                tt(xb, ch, M4[:, c], ALU.subtract)
                tt(xb, xb, R4[:, c], ALU.mult, sem=VEC)
                vx = VEC.v
                g.wait("tensor", VEC, vx)
                for i in range(4):
                    pv = tp_guard.get(i % 2)
                    if pv is not None:
                        g.wait("tensor", VEC, pv)
                    tv = transpose(tp_slot(i), xb[:, 128 * i:128 * i + 128], sem=TEN)
                    g.wait("vector", TEN, tv)
                    vcopy(XTM[:, 1024 * i + 128 * fc:1024 * i + 128 * fc + 128],
                          tp_slot(i), sem=VEC)
                    tp_guard[i % 2] = VEC.v
            for i in range(4):
                qv = quantize(XTM[:, 1024 * i:1024 * (i + 1)], 1024,
                              XTM[:, 1024 * i:1024 * (i + 1)], xv=VEC.v)
                g.wait("tensor", VEC, qv)
                for fc in range(FC):
                    j = 4 * i + fc
                    pv = tp_guard.get(j % 2)
                    if pv is not None:
                        g.wait("tensor", VEC, pv)
                    tv = transpose(tp_slot(j),
                                   XTM[:, 1024 * i + 128 * fc:1024 * i + 128 * fc + 128],
                                   sem=TEN)
                    g.wait("vector", TEN, tv)
                    cc = slice(512 * fc + 128 * i, 512 * fc + 128 * i + 128)
                    split_hl(XQH[:, cc], XQL[:, cc], tp_slot(j))
                    tp_guard[j % 2] = VEC.v

        def gh_stage(t):
            for g0 in range(0, DIC, 8):
                gn = min(8, DIC - g0)
                for oc in range(g0, g0 + gn):
                    bi = oc % 2
                    wgv = wgb[bi].bitcast(BF16)
                    wuv = wub[bi].bitcast(BF16)
                    g.wait("sync", gusem[bi])
                    if gu_guard.get(bi) is not None:
                        g.wait("sync", TEN, gu_guard[bi])
                    g.dma("sync", wgv[:, 0:FC * 128].rearrange("p (k c) -> p k c", c=128),
                          wgq_s[0][oc], gusem[bi])
                    g.dma("sync", wgv[:, FC * 128:2 * FC * 128].rearrange("p (k c) -> p k c", c=128),
                          wgq_s[1][oc], gusem[bi])
                    g.dma("sync", wuv[:, 0:FC * 128].rearrange("p (k c) -> p k c", c=128),
                          wuq_s[0][oc], gusem[bi])
                    v2 = g.dma("sync", wuv[:, FC * 128:2 * FC * 128].rearrange("p (k c) -> p k c", c=128),
                               wuq_s[1][oc], gusem[bi])
                    g.wait("tensor", gusem[bi], v2)
                    pg = PB[0][:][:, 512 * bi:512 * bi + 512]
                    pu = PB[1][:][:, 512 * bi:512 * bi + 512]
                    if oc == 0:
                        g.wait("tensor", VEC)
                        g.wait("tensor", SCA)
                    pgv = pg_guard.get(bi)
                    if pgv is not None:
                        g.wait("tensor", SCA, pgv[0])
                        g.wait("tensor", VEC, pgv[1])
                    for (pp, wv) in ((pg, wgv), (pu, wuv)):
                        i_mm = 0
                        for k in range(FC):
                            wh = wv[:, 128 * k:128 * k + 128]
                            wl = wv[:, FC * 128 + 128 * k:FC * 128 + 128 * (k + 1)]
                            ah = XQH[:, 512 * k:512 * k + 512]
                            al = XQL[:, 512 * k:512 * k + 512]
                            for (wop, aop) in ((wh, ah), (wh, al), (wl, ah)):
                                mm(pp, wop, aop, start=(i_mm == 0), stop=(i_mm == 3 * FC - 1),
                                   sem=TEN if i_mm == 3 * FC - 1 else None)
                                i_mm += 1
                    tv = TEN.v
                    gu_guard[bi] = tv
                    g.wait("scalar", TEN, tv)
                    act(ub, pg, AF.Silu, sem=SCA)
                    pg_sca = SCA.v
                    g.wait("vector", SCA, pg_sca)
                    tt(ub, ub, pu, ALU.mult, sem=VEC)
                    vv = VEC.v
                    pg_guard[bi] = (pg_sca, vv)
                    g.wait("tensor", VEC, vv)
                    for i in range(4):
                        pv = tp_guard.get(i % 2)
                        if pv is not None:
                            g.wait("tensor", VEC, pv)
                        tv2 = transpose(tp_slot(i), ub[:, 128 * i:128 * i + 128], sem=TEN)
                        g.wait("vector", TEN, tv2)
                        vcopy(HTM[:, DI * i + 128 * oc:DI * i + 128 * oc + 128],
                              tp_slot(i), sem=VEC)
                        tp_guard[i % 2] = VEC.v
                for i in range(4):
                    hs = slice(DI * i + 128 * g0, DI * i + 128 * (g0 + gn))
                    qv = quantize(HTM[:, hs], 128 * gn, HTM[:, hs], xv=VEC.v)
                    g.wait("tensor", VEC, qv)
                    for kc in range(g0, g0 + gn):
                        pv = tp_guard.get(kc % 2)
                        if pv is not None:
                            g.wait("tensor", VEC, pv)
                        tv = transpose(tp_slot(kc),
                                       HTM[:, DI * i + 128 * kc:DI * i + 128 * kc + 128],
                                       sem=TEN)
                        g.wait("vector", TEN, tv)
                        cc = slice(512 * kc + 128 * i, 512 * kc + 128 * i + 128)
                        split_hl(HQH[:, cc], HQL[:, cc], tp_slot(kc))
                        tp_guard[kc % 2] = VEC.v
            return (SCA.v, VEC.v)

        def d_stage(t, g_end):
            for oc in range(FC):
                bi = oc % 2
                wdv = wdb[bi].bitcast(BF16)
                g.wait("sync", dsem2[bi])
                if d_guard.get(bi) is not None:
                    g.wait("sync", TEN, d_guard[bi])
                g.dma("sync", wdv[:, 0:DIC * 128].rearrange("p (k c) -> p k c", c=128),
                      wdq_s[0][oc], dsem2[bi])
                v1 = g.dma("sync", wdv[:, DIC * 128:2 * DIC * 128].rearrange("p (k c) -> p k c", c=128),
                           wdq_s[1][oc], dsem2[bi])
                pv = PB[0][:][:, 512 * (oc % 2):512 * (oc % 2) + 512]
                g.wait("tensor", dsem2[bi], v1)
                if oc == 0:
                    g.wait("tensor", SCA, g_end[0])
                    g.wait("tensor", VEC, g_end[1])
                pgv = pg_guard.get(2 + oc % 2)
                if pgv is not None:
                    g.wait("tensor", VEC, pgv)
                i_mm = 0
                for k in range(DIC):
                    wh = wdv[:, 128 * k:128 * k + 128]
                    wl = wdv[:, DIC * 128 + 128 * k:DIC * 128 + 128 * (k + 1)]
                    ah = HQH[:, 512 * k:512 * k + 512]
                    al = HQL[:, 512 * k:512 * k + 512]
                    for (wop, aop) in ((wh, ah), (wh, al), (wl, ah)):
                        mm(pv, wop, aop, start=(i_mm == 0), stop=(i_mm == 3 * DIC - 1),
                           sem=TEN if i_mm == 3 * DIC - 1 else None)
                        i_mm += 1
                tv = TEN.v
                d_guard[bi] = tv
                ch = rd13.rd(y3own[oc, :, 512 * t:512 * t + 512], "vector")
                g.wait("vector", TEN, tv)
                g.wait("vector", STQ)
                tt(xb, pv, ch, ALU.add, sem=VEC)
                pg_guard[2 + oc % 2] = VEC.v
                g.wait("gpsimd", VEC)
                g.dma("gpsimd", y_out[oc, :, 512 * t:512 * t + 512], xb, STQ)

        x_stage(0)
        for t in range(NTO):
            g_end = gh_stage(t)
            if t + 1 < NTO:
                x_stage(t + 1)
            d_stage(t, g_end)

    ffn()

    return finish(nc, g, ctx)


def finish(nc, g, ctx):
    g.wait("sync", g.fin_STQ)
    g.wait("sync", g.fin_LDQ)
    with ctx:
        blk = ctx.enter_context(nc.Block())
        for eng in ENGS:
            def make_body(eng):
                def body(e):
                    for fn in g.ops[eng]:
                        fn(e)
                return body
            getattr(blk, eng)(make_body(eng))
    return nc



def pack_vecs(ins, selA, selB):
    out = np.zeros((128, 128), dtype=np.float32)
    col = 0
    for key, n in (("L0_attn_bp", FC), ("L1_attn_bqkv", 3 * FC), ("L1_attn_ln1_g", FC),
                   ("L1_attn_ln1_b", FC), ("L1_attn_bo", FC), ("L1_attn_ln2_g", FC),
                   ("L1_attn_ln2_b", FC), ("L1_attn_b1", DMC), ("L1_attn_b2", FC)):
        arr = np.asarray(ins[key], dtype=np.float32)
        out[:, col:col + n] = arr.reshape(n, 128).T
        col += n
    out[:, col] = selA
    out[:, col + 1] = selB
    return out


def kernel(**inputs):
    ins = {k: np.asarray(v) for k, v in inputs.items()}
    x = np.asarray(ins["L0_attn_x"], dtype=np.float32)

    C2, S2 = build_rope_tables()
    tri = build_tri()
    identity = np.ascontiguousarray(np.eye(128, dtype=np.float32))
    maskA = build_masks(OWN_A)
    maskB = build_masks(OWN_B)

    wq_s = swizzle_strips(ins["L0_attn_Wq"])
    wk_s = swizzle_strips(ins["L0_attn_Wk"])
    wv_s = swizzle_strips(ins["L0_attn_Wv"])
    wp_s = swizzle_strips(ins["L0_attn_Wp"])
    w0t_s = swizzle_strips(np.ascontiguousarray(np.asarray(ins["L0_ffn_weight"]).T))
    wqkv_s = swizzle_strips(ins["L1_attn_Wqkv"])
    wo_s = swizzle_strips(ins["L1_attn_Wo"])
    w1_s = swizzle_strips(ins["L1_attn_W1"])
    w2_s = swizzle_strips(ins["L1_attn_W2"])
    wg_nat = np.ascontiguousarray(
        np.asarray(ins["L1_ffn_w_gate"], dtype=np.float32).reshape(DIC, 128, D))
    wu_nat = np.ascontiguousarray(
        np.asarray(ins["L1_ffn_w_up"], dtype=np.float32).reshape(DIC, 128, D))
    wd_nat = np.ascontiguousarray(
        np.asarray(ins["L1_ffn_w_down"], dtype=np.float32).reshape(FC, 128, DI))

    nc = build_program()

    in_maps = []
    for core in range(8):
        b, s = core // 2, core % 2
        own = OWN_A if s == 0 else OWN_B
        own_tok = np.concatenate([np.arange(256 * a, 256 * a + 256) for a in own])
        xb = np.ascontiguousarray(x[b].T)
        im = dict(
            x_fm=np.ascontiguousarray(xb.reshape(FC, 128, T)),
            wq_s=wq_s, wk_s=wk_s, wv_s=wv_s, wp_s=wp_s, w0t_s=w0t_s,
            wqkv_s=wqkv_s, wo_s=wo_s, w1_s=w1_s, w2_s=w2_s,
            wg_nat=wg_nat, wu_nat=wu_nat, wd_nat=wd_nat,
            vecs=pack_vecs(ins, 1.0 if s == 0 else 0.0, 0.0 if s == 0 else 1.0),
            c2t=C2, s2t=S2,
            c2o=np.ascontiguousarray(C2[:, own_tok]),
            s2o=np.ascontiguousarray(S2[:, own_tok]),
            iden=identity, mtri=tri,
            mask8=maskA if s == 0 else maskB,
        )
        in_maps.append(im)

    res = run_bass_kernel_spmd(nc, in_maps, core_ids=list(range(8)))

    out = np.empty((B, T, D), dtype=np.float32)
    for core in range(8):
        b, s = core // 2, core % 2
        own = OWN_A if s == 0 else OWN_B
        y = np.asarray(res.results[core]["y_out"]).reshape(D, TO).T
        for j, a in enumerate(own):
            out[b, 256 * a:256 * a + 256, :] = y[256 * j:256 * j + 256, :]
    return out


# version 14
# speedup vs baseline: 1.0687x; 1.0168x over previous
import numpy as np
from contextlib import ExitStack

import concourse.bass as bass
import concourse.mybir as mybir
from concourse.bass_utils import run_bass_kernel_spmd

F32 = mybir.dt.float32
BF16 = mybir.dt.bfloat16
I8 = mybir.dt.int8
AF = mybir.ActivationFunctionType
ALU = mybir.AluOpType
AX = mybir.AxisListType
f32 = np.float32

B, T, D, H, HD = 4, 4096, 1024, 16, 64
NPAIR = 8
TO = T // 2
OWN_A = [0, 3, 4, 7, 8, 11, 12, 15]
OWN_B = [1, 2, 5, 6, 9, 10, 13, 14]
NSLOT = 8
N_MSK = 32
FC = D // 128
NT = T // 512
NTO = TO // 512
EPS = 1e-5
DI = 2816
DIC = DI // 128
DM = 4096
DMC = DM // 128
C6_1 = float(f32(1.0 / 6.0))
C6_2 = float(f32(1.0 / 6.0 - float(f32(1.0 / 6.0))))

ENGS = ("sync", "tensor", "vector", "scalar", "gpsimd")
DBG = {}


class Sem:
    __slots__ = ("name", "h", "v")

    def __init__(self, name):
        self.name, self.h, self.v = name, None, 0


class Gen:
    def __init__(self):
        self.ops = {e: [] for e in ENGS}

    def emit(self, eng, fn):
        self.ops[eng].append(fn)

    def wait(self, eng, sem, val=None):
        v = sem.v if val is None else val
        assert v <= sem.v, (sem.name, v, sem.v)
        if v > 0:
            self.emit(eng, lambda e, h=sem, vv=v: e.wait_ge(h.h, vv))

    def op(self, eng, build, sem=None, inc=1):
        own = getattr(self, "own_sems", {}).get(eng)
        if own is not None:
            self.wait(eng, own)
            sem, inc = own, 1
        if sem is not None:
            sem.v += inc

        def fn(e, build=build, sem=sem, inc=inc):
            r = build(e)
            if sem is not None:
                r.then_inc(sem.h, inc)
        self.emit(eng, fn)
        return None if sem is None else sem.v

    def dma(self, eng, out, in_, sem):
        sem.v += 16
        self.emit(eng, lambda e, o=out, i=in_, h=sem: e.dma_start(out=o, in_=i).then_inc(h.h, 16))
        return sem.v



def swizzle_strips(W):
    K, M = W.shape
    return np.ascontiguousarray(
        W.reshape(K // 128, 128, M // 128, 128).transpose(2, 1, 0, 3)).astype(np.float32)


def build_rope_tables():
    inv_freq = 1.0 / (10000.0 ** (np.arange(0, HD, 2, dtype=np.float64) / HD))
    t = np.arange(T, dtype=np.float64)
    fr = t[:, None] * inv_freq[None, :]
    cos = np.cos(fr).T
    sin = np.sin(fr).T
    C2 = np.concatenate([cos, cos, cos, cos], axis=0).astype(np.float32)
    S2 = np.concatenate([-sin, sin, -sin, sin], axis=0).astype(np.float32)
    return np.ascontiguousarray(C2), np.ascontiguousarray(S2)


def build_masks(own):
    out = np.zeros((N_MSK, 128, 512), dtype=np.int8)
    for si in range(4):
        qg = np.concatenate([256 * own[2 * si] + np.arange(256),
                             256 * own[2 * si + 1] + np.arange(256)])[None, :]
        for d2 in range(8):
            k = 128 * (8 * si + d2) + np.arange(128)[:, None]
            out[8 * si + d2] = (k > qg).astype(np.int8)
    return np.ascontiguousarray(out.transpose(1, 0, 2).reshape(128, N_MSK * 512))


def build_tri():
    k = np.arange(128)[:, None]
    q = np.arange(128)[None, :]
    return np.ascontiguousarray(np.where(k <= q, 0.0, -1e30).astype(np.float32))



def build_program(stop_phase=99):
    nc = bass.Bass("TRN2", target_bir_lowering=False)
    g = Gen()
    ctx = ExitStack()

    din = {}

    def inp(name, shape, dtype=F32):
        din[name] = nc.dram_tensor(name, list(shape), dtype, kind="ExternalInput").ap()
        return din[name]

    x_fm = inp("x_fm", [FC, 128, T])
    wq_s = inp("wq_s", [FC, 128, FC, 128])
    wk_s = inp("wk_s", [FC, 128, FC, 128])
    wv_s = inp("wv_s", [FC, 128, FC, 128])
    wp_s = inp("wp_s", [FC, 128, FC, 128])
    w0t_s = inp("w0t_s", [FC, 128, FC, 128])
    wqkv_s = inp("wqkv_s", [3 * FC, 128, FC, 128])
    wo_s = inp("wo_s", [FC, 128, FC, 128])
    w1_s = inp("w1_s", [DMC, 128, FC, 128])
    w2_s = inp("w2_s", [FC, 128, DMC, 128])
    wg_nat = inp("wg_nat", [DIC, 128, D])
    wu_nat = inp("wu_nat", [DIC, 128, D])
    wd_nat = inp("wd_nat", [FC, 128, DI])
    vecs = inp("vecs", [128, 128])
    c2t = inp("c2t", [128, T])
    s2t = inp("s2t", [128, T])
    c2o = inp("c2o", [128, TO])
    s2o = inp("s2o", [128, TO])
    iden = inp("iden", [128, 128])
    mtri = inp("mtri", [128, 128])
    mask8 = inp("mask8", [128, N_MSK * 512], I8)

    y_out = nc.dram_tensor("y_out", [FC, 128, TO], F32, kind="ExternalOutput").ap()

    def scratch(name, shape, dtype=F32):
        return nc.dram_tensor(name, list(shape), dtype, kind="Internal").ap()

    qst = [scratch("qst_h", [NPAIR, 128, T], BF16), scratch("qst_l", [NPAIR, 128, T], BF16)]
    kst = [scratch("kst_h", [NPAIR, 128, T], BF16), scratch("kst_l", [NPAIR, 128, T], BF16)]
    vst = scratch("vst", [NPAIR, T, 128])
    ast = [scratch("ast_h", [NPAIR, 128, T], BF16), scratch("ast_l", [NPAIR, 128, T], BF16)]
    qst1 = [scratch("qst1_h", [NPAIR, 128, TO], BF16), scratch("qst1_l", [NPAIR, 128, TO], BF16)]
    ast1 = [scratch("ast1_h", [NPAIR, 128, TO], BF16), scratch("ast1_l", [NPAIR, 128, TO], BF16)]
    y1st = scratch("y1st", [FC, 128, T])
    n1h_st = scratch("n1h_st", [FC, 128, T], BF16)
    n1l_st = scratch("n1l_st", [FC, 128, T], BF16)
    y2st = scratch("y2st", [FC, 128, T])
    y2own = scratch("y2own", [FC, 128, TO])
    y3own = scratch("y3own", [FC, 128, TO])
    n1st = scratch("n1st", [FC, 128, T])
    z2dbg = scratch("z2dbg", [FC, 128, TO])
    w1hl_s = scratch("w1hl_s", [DMC, 128, 2 * FC * 128], BF16)
    w2hl_s = scratch("w2hl_s", [FC, 128, 2 * DMC * 128], BF16)
    wgq_s = [scratch("wgq_sh", [DIC, 128, FC, 128], BF16),
             scratch("wgq_sl", [DIC, 128, FC, 128], BF16)]
    wuq_s = [scratch("wuq_sh", [DIC, 128, FC, 128], BF16),
             scratch("wuq_sl", [DIC, 128, FC, 128], BF16)]
    wdq_s = [scratch("wdq_sh", [FC, 128, DIC, 128], BF16),
             scratch("wdq_sl", [FC, 128, DIC, 128], BF16)]

    A1t = ctx.enter_context(nc.sbuf_tensor("A1", [128, 32768], F32))
    A2t = ctx.enter_context(nc.sbuf_tensor("A2", [128, 16384], F32))
    sb_vecs = ctx.enter_context(nc.sbuf_tensor("sb_vecs", [128, 128], F32))
    sb_iden = ctx.enter_context(nc.sbuf_tensor("sb_iden", [128, 128], F32))
    sb_tri = ctx.enter_context(nc.sbuf_tensor("sb_tri", [128, 128], F32))
    sb_ones64 = ctx.enter_context(nc.sbuf_tensor("sb_ones64", [128, 64], F32))
    sb_ones64b_t = ctx.enter_context(nc.sbuf_tensor("sb_ones64b", [128, 64], BF16))
    sb_ones1 = ctx.enter_context(nc.sbuf_tensor("sb_ones1", [128, 1], F32))
    sb_onesk = ctx.enter_context(nc.sbuf_tensor("sb_onesk", [1, 128], F32))
    sb_rows = ctx.enter_context(nc.sbuf_tensor("sb_rows", [1, 3072], F32))
    PB = [ctx.enter_context(nc.psum_tensor(f"PB{i}", [128, 1024], F32)) for i in range(4)]

    sems = {}
    for name in ("TEN", "VEC", "SCA", "LDQ", "STQ", "GPC"):
        s = Sem(name)
        s.h = ctx.enter_context(nc.semaphore(name))
        sems[name] = s
    TEN, VEC, SCA, LDQ, STQ, GPC = (sems[n] for n in ("TEN", "VEC", "SCA", "LDQ", "STQ", "GPC"))
    CSEMS = (TEN, VEC, SCA)
    g.fin_STQ = STQ
    g.fin_LDQ = LDQ
    g.own_sems = {"vector": VEC, "scalar": SCA}
    _semctr = [0]

    def mk_sem():
        s = Sem(f"S{_semctr[0]}")
        _semctr[0] += 1
        s.h = ctx.enter_context(nc.semaphore(s.name))
        g.all_slot_sems.append(s)
        return s
    g.all_slot_sems = []

    bankrel = {}

    def bank_wait(eng, *keys):
        for k in keys:
            for s, v in bankrel.get(k, ()):
                g.wait(eng, s, v)

    def bank_rel(key, *svs):
        bankrel[key] = list(svs) if svs else [(VEC, VEC.v)]

    def phase_barrier():
        g.wait("tensor", VEC)
        g.wait("tensor", SCA)
        g.wait("tensor", STQ)
        g.wait("vector", TEN)
        g.wait("vector", SCA)
        g.wait("vector", STQ)
        g.wait("scalar", TEN)
        g.wait("scalar", VEC)
        g.wait("scalar", STQ)
        g.wait("sync", TEN)
        g.wait("sync", VEC)
        g.wait("sync", SCA)
        g.wait("sync", STQ)
        g.wait("sync", LDQ)
        for s_ in g.all_slot_sems:
            g.wait("sync", s_)
        g.wait("gpsimd", VEC)
        g.wait("gpsimd", SCA)

    arena_off = {"A1": 0, "A2": 0}

    def carve(which, cols):
        o = arena_off[which]
        lim = 32768 if which == "A1" else 16384
        assert o + cols <= lim, (which, o, cols)
        arena_off[which] = o + cols
        t_ = A1t if which == "A1" else A2t
        return t_[:][:, o:o + cols]

    def reset(which):
        arena_off[which] = 0

    VC = {}
    off = 0
    for nm, n in (("bp", FC), ("bqkv", 3 * FC), ("g1", FC), ("b1", FC), ("bo", FC),
                  ("g2", FC), ("b2", FC), ("b1m", DMC), ("b2m", FC), ("selA", 1), ("selB", 1)):
        VC[nm] = off
        off += n
    assert off <= 128

    def vcol(nm, i=0):
        return sb_vecs[:][:, VC[nm] + i:VC[nm] + i + 1]

    def tt(out, in0, in1, op, sem=None, eng="vector"):
        return g.op(eng, lambda e, o=out, a=in0, b=in1, p=op:
                    e.tensor_tensor(out=o, in0=a, in1=b, op=p), sem)

    def ts(out, in0, s1, op0, s2=None, op1=None, sem=None):
        def bld(e, o=out, a=in0, s1=s1, s2=s2, op0=op0, op1=op1):
            kw = dict(out=o, in0=a, scalar1=s1, scalar2=s2, op0=op0)
            if op1 is not None:
                kw["op1"] = op1
            return e.tensor_scalar(**kw)
        return g.op("vector", bld, sem)

    def stt(out, in0, scalar, in1, op0, op1, sem=None):
        return g.op("vector", lambda e, o=out, a=in0, s=scalar, b=in1, p0=op0, p1=op1:
                    e.scalar_tensor_tensor(out=o, in0=a, scalar=s, in1=b, op0=p0, op1=p1), sem)

    def act(out, in_, func, scale=None, bias=None, sem=None):
        def bld(e, o=out, i=in_, f=func, sc=scale, bi=bias):
            kw = dict(out=o, in_=i, func=f)
            if sc is not None:
                kw["scale"] = sc
            if bi is not None:
                kw["bias"] = bi
            return e.activation(**kw)
        return g.op("scalar", bld, sem)

    def vcopy(out, in_, sem=None):
        return g.op("vector", lambda e, o=out, i=in_: e.tensor_copy(out=o, in_=i), sem)

    def mm(out, lhsT, rhs, start, stop, tile_position=None, sem=None):
        def bld(e, o=out, l=lhsT, r=rhs, st=start, sp=stop, tp=tile_position):
            kw = dict(out=o, lhsT=l, rhs=r, start=st, stop=sp)
            if tp is not None:
                kw["tile_position"] = tp
            return e.matmul(**kw)
        return g.op("tensor", bld, sem)

    def split_hl(hi, lo, src):
        vcopy(hi, src)
        tt(lo, src, hi, ALU.subtract)

    def transpose(out, in_, sem=None):
        return g.op("tensor", lambda e, o=out, i=in_:
                    e.transpose(out=o, in_=i, identity=sb_iden[:]), sem)

    def waitall(eng):
        for s in CSEMS:
            g.wait(eng, s)

    class Reader:
        def __init__(self, which, cols, n=3):
            self.bufs = [carve(which, cols) for _ in range(n)]
            self.sems = [mk_sem() for _ in range(n)]
            self.n = n
            self.i = 0
            self.snaps = []

        def rd(self, src, *consumers):
            i = self.i
            self.snaps.append(tuple((s, s.v) for s in CSEMS))
            if i >= self.n:
                for s, v in self.snaps[i - self.n + 1]:
                    g.wait("sync", s, v)
            b = self.bufs[i % self.n]
            out = b
            if src.dtype == BF16:
                out = out.bitcast(BF16)
            if len(src.shape) == 3:
                out = out.rearrange("p (k c) -> p k c", c=src.shape[2])
                src = src.rearrange("k p c -> p k c")
            v = g.dma("sync", out, src, self.sems[i % self.n])
            for c in consumers:
                g.wait(c, self.sems[i % self.n], v)
            self.i = i + 1
            return b

    qsl_state = [None]
    QSCR = {}

    def quantize_slice(X, Fcols, out, xv=None):
        nb = Fcols // 16
        assert Fcols <= 1024
        a = QSCR["A"][:, 0:Fcols]
        s1 = QSCR["S1"][:, 0:Fcols]
        s2 = QSCR["S2"][:, 0:Fcols]
        gsn = QSCR["G"][:, 0:Fcols]
        m = QSCR["M"][:, 0:nb]
        ss = QSCR["SS"][:, 0:nb]
        r = QSCR["R"][:, 0:nb]
        if qsl_state[0] is None:
            g.wait("scalar", VEC)
        else:
            g.wait("scalar", VEC, qsl_state[0])
        if xv is not None:
            g.wait("scalar", VEC, xv)
        act(a, X, AF.Abs)
        act(gsn, X, AF.Sign, sem=SCA)
        g.wait("vector", SCA)
        g.op("vector", lambda e, o=m, i=a: e.tensor_reduce(
            out=o, in_=i.rearrange("p (b k) -> p b k", k=16), axis=AX.X, op=ALU.max))
        ts(ss, m, C6_1, ALU.mult)
        stt(ss, m, C6_2, ss, ALU.mult, ALU.add)
        ts(ss, ss, float(f32(1e-12)), ALU.max, sem=VEC)
        g.wait("scalar", VEC)
        act(r, ss, AF.Ln)
        act(r, r, AF.Exp, scale=f32(-1.0), sem=SCA)
        g.wait("vector", SCA)
        rbc = r.unsqueeze(-1).broadcast_to((128, nb, 16))
        sbc = ss.unsqueeze(-1).broadcast_to((128, nb, 16))
        a3d = a.rearrange("p (b k) -> p b k", k=16)
        tt(a3d, a3d, rbc, ALU.mult)
        ts(s1, a, 0.25, ALU.is_gt)
        stt(s1, a, 0.75, s1, ALU.is_gt, ALU.add)
        stt(s1, a, 1.25, s1, ALU.is_gt, ALU.add)
        stt(s1, a, 1.75, s1, ALU.is_gt, ALU.add)
        ts(s2, a, 2.5, ALU.is_gt)
        stt(s2, a, 3.5, s2, ALU.is_gt, ALU.add)
        stt(s2, a, 5.0, s2, ALU.is_gt, ALU.add)
        stt(s2, a, 5.0, s2, ALU.is_gt, ALU.add)
        stt(s2, s1, 0.5, s2, ALU.mult, ALU.add)
        tt(s2.rearrange("p (b k) -> p b k", k=16),
           s2.rearrange("p (b k) -> p b k", k=16), sbc, ALU.mult)
        tt(out, s2, gsn, ALU.mult, sem=VEC)
        qsl_state[0] = VEC.v
        return VEC.v

    def quantize(X, Fcols, out, xv=None):
        v = None
        o = 0
        while o < Fcols:
            w = min(1024, Fcols - o)
            v = quantize_slice(X[:, o:o + w], w, out[:, o:o + w], xv=xv)
            o += w
        return v

    qw_steps = []

    def tick(n=1):
        for _ in range(n):
            if qw_steps:
                qw_steps.pop(0)()

    g.dma("sync", sb_vecs[:], vecs, LDQ)
    g.dma("sync", sb_iden[:], iden, LDQ)
    g.dma("sync", sb_tri[:], mtri, LDQ)
    v0 = LDQ.v
    g.op("gpsimd", lambda e: e.memset(sb_ones64[:], 1.0), GPC)
    g.op("gpsimd", lambda e: e.memset(sb_ones64b_t[:], 1.0), GPC)
    g.op("gpsimd", lambda e: e.memset(sb_ones1[:], 1.0), GPC)
    g.op("gpsimd", lambda e: e.memset(sb_onesk[:], 1.0), GPC)
    for eng in ("tensor", "vector", "scalar"):
        g.wait(eng, LDQ, v0)
        g.wait(eng, GPC)

    def ln_stats(read_chunk, ntt, dst_M, dst_R, sq_buf, barrier=True):
        if barrier:
            phase_barrier()
        bank_wait("tensor", "P2a", "P3a")
        for t in range(ntt):
            for fc in range(FC):
                ch = read_chunk(fc, t)
                sq = sq_buf[:, 512 * (fc % 2):512 * (fc % 2) + 512]
                if fc >= 2:
                    g.wait("scalar", TEN)
                sv = act(sq, ch, AF.Square, sem=SCA)
                mm(PB[2][:][0:1, 0:512], sb_ones1[:], ch, start=(fc == 0),
                   stop=(fc == FC - 1), sem=TEN)
                g.wait("tensor", SCA, sv)
                mm(PB[3][:][0:1, 0:512], sb_ones1[:], sq, start=(fc == 0),
                   stop=(fc == FC - 1), sem=TEN)
            vmm = TEN.v
            g.wait("vector", TEN, vmm)
            r_sum = sb_rows[:][0:1, 0:512]
            r_sq = sb_rows[:][0:1, 512:1024]
            r_mean = sb_rows[:][0:1, 1024:1536]
            r_var = sb_rows[:][0:1, 1536:2048]
            r_rstd = sb_rows[:][0:1, 2048:2560]
            vcopy(r_sum, PB[2][:][0:1, 0:512])
            vcopy(r_sq, PB[3][:][0:1, 0:512])
            ts(r_mean, r_sum, 1.0 / D, ALU.mult)
            tt(r_var, r_mean, r_mean, ALU.mult)
            stt(r_var, r_sq, 1.0 / D, r_var, ALU.mult, ALU.subtract, sem=VEC)
            vv = VEC.v
            g.wait("scalar", VEC, vv)
            act(r_rstd, r_var, AF.Ln, scale=f32(1.0), bias=f32(EPS))
            act(r_rstd, r_rstd, AF.Exp, scale=f32(-0.5), sem=SCA)
            vs = SCA.v
            g.wait("tensor", SCA, vs)
            g.wait("tensor", VEC, vv)
            mm(PB[2][:][:, 0:512], sb_onesk[:], r_mean, start=True, stop=True)
            mm(PB[3][:][:, 0:512], sb_onesk[:], r_rstd, start=True, stop=True, sem=TEN)
            vb = TEN.v
            g.wait("vector", TEN, vb)
            vcopy(dst_M[:, 512 * t:512 * t + 512], PB[2][:][:, 0:512])
            vcopy(dst_R[:, 512 * t:512 * t + 512], PB[3][:][:, 0:512], sem=VEC)
            g.wait("tensor", VEC)
            tick()
        bank_rel("P2a")
        bank_rel("P3a")

    def res_gemm(wsrc, n_k, rhs_src, res_src, bias_name, dst_st, ntt):
        phase_barrier()
        bank_wait("tensor", "P0a", "P1a")
        reset("A2")
        whl = carve("A2", FC * n_k * 128)
        wtmp = carve("A2", n_k * 128)
        wallsem = mk_sem()
        rhs_rh = Reader("A2", 256 * n_k, n=1)
        rhs_rl = Reader("A2", 256 * n_k, n=1)
        res_r = Reader("A2", 512, n=3)
        outb = [carve("A2", 512) for _ in range(2)]
        for oc in range(FC):
            g.wait("sync", VEC)
            v = g.dma("sync", wtmp.rearrange("p (k c) -> p k c", c=128),
                      wsrc[oc], wallsem)
            g.wait("vector", wallsem, v)
            wv = whl[:, n_k * 128 * oc:n_k * 128 * (oc + 1)].bitcast(BF16)
            split_hl(wv[:, 0:n_k * 128], wv[:, n_k * 128:2 * n_k * 128], wtmp)
        g.wait("tensor", VEC)
        idx = 0
        for t in range(ntt):
            rbh = rhs_rh.rd(rhs_src[0][:, :, 512 * t:512 * t + 512], "tensor").bitcast(BF16)
            rbl = rhs_rl.rd(rhs_src[1][:, :, 512 * t:512 * t + 512], "tensor").bitcast(BF16)
            for oc in range(FC):
                key = f"P{idx % 2}a"
                pv = PB[idx % 2][:][:, 0:512]
                bank_wait("tensor", key)
                wv = whl[:, n_k * 128 * oc:n_k * 128 * (oc + 1)].bitcast(BF16)
                n_mm = 3 * n_k
                i_mm = 0
                for k in range(n_k):
                    wh = wv[:, 128 * k:128 * k + 128]
                    wl = wv[:, n_k * 128 + 128 * k:n_k * 128 + 128 * (k + 1)]
                    ah = rbh[:, 512 * k:512 * k + 512]
                    al = rbl[:, 512 * k:512 * k + 512]
                    for (wop, aop) in ((wh, ah), (wh, al), (wl, ah)):
                        mm(pv, wop, aop, start=(i_mm == 0), stop=(i_mm == n_mm - 1),
                           sem=TEN if i_mm == n_mm - 1 else None)
                        i_mm += 1
                tv = TEN.v
                rb = res_r.rd(res_src[oc, :, 512 * t:512 * t + 512], "vector")
                ob = outb[idx % 2]
                if idx >= 2:
                    g.wait("vector", STQ)
                g.wait("vector", TEN, tv)
                if bias_name is not None:
                    ts(ob, pv, vcol(bias_name, oc), ALU.add)
                    bank_rel(key)
                    tt(ob, ob, rb, ALU.add)
                else:
                    tt(ob, pv, rb, ALU.add)
                    bank_rel(key)
                ev = VEC.v
                g.wait("gpsimd", VEC, ev)
                g.dma("gpsimd", dst_st[oc, :, 512 * t:512 * t + 512], ob, STQ)
                idx += 1

    reset("A1")
    reset("A2")
    XNh = carve("A1", FC * T // 2).bitcast(BF16)
    XNl = carve("A1", FC * T // 2).bitcast(BF16)
    M0 = carve("A2", T)
    R0 = carve("A2", T)
    sqb = carve("A2", 1024)
    rd1 = Reader("A2", 512, n=3)
    ln_stats(lambda fc, t_: rd1.rd(x_fm[fc, :, 512 * t_:512 * t_ + 512], "tensor", "scalar"),
             NT, M0, R0, sqb)

    rd2 = Reader("A2", 512, n=3)
    xtmp = carve("A2", 512)
    for t in range(NT):
        for fc in range(FC):
            ch = rd2.rd(x_fm[fc, :, 512 * t:512 * t + 512], "vector")
            c = slice(fc * T + 512 * t, fc * T + 512 * t + 512)
            tt(xtmp, ch, M0[:, 512 * t:512 * t + 512], ALU.subtract)
            tt(xtmp, xtmp, R0[:, 512 * t:512 * t + 512], ALU.mult)
            split_hl(XNh[:, c], XNl[:, c], xtmp)

    def xn_rhs(k, t):
        c = slice(k * T + 512 * t, k * T + 512 * t + 512)
        return (XNh[:, c], XNl[:, c])

    if stop_phase <= 1:
        return finish(nc, g, ctx)
    g.wait("sync", VEC)
    reset("A2")
    def stage_gemm(wsrc, n_oc, rhs_fn, ntt, evac_fn, wb=None, wsem=None, wtmp=None):
        if wb is None:
            wb = [carve("A2", FC * 128) for _ in range(2)]
        if wtmp is None:
            wtmp = carve("A2", FC * 128)
        if wsem is None:
            wsem = [mk_sem(), mk_sem()]
        phase_barrier()
        wsnap = []
        idx = 0
        for oc in range(n_oc):
            wsnap.append(TEN.v)
            if oc >= 2:
                g.wait("sync", TEN, wsnap[oc - 1])
            g.wait("sync", VEC)
            vldw = g.dma("sync", wtmp.rearrange("p (k c) -> p k c", c=128),
                         wsrc[oc], wsem[oc % 2])
            whl = wb[oc % 2].bitcast(BF16)
            whi = whl[:, 0:FC * 128]
            wlo = whl[:, FC * 128:2 * FC * 128]
            g.wait("vector", wsem[oc % 2], vldw)
            split_hl(whi, wlo, wtmp)
            vsp = VEC.v
            for t in range(ntt):
                key = f"P{idx % 2}a"
                pv = PB[idx % 2][:][:, 0:512]
                g.wait("tensor", VEC, vsp)
                bank_wait("tensor", key)
                n_mm = 3 * FC
                i_mm = 0
                for k in range(FC):
                    rh, rl = rhs_fn(k, t)
                    for (wop, aop) in ((whi, rh), (whi, rl), (wlo, rh)):
                        mm(pv, wop[:, 128 * k:128 * k + 128], aop,
                           start=(i_mm == 0), stop=(i_mm == n_mm - 1),
                           sem=TEN if i_mm == n_mm - 1 else None)
                        i_mm += 1
                tv = TEN.v
                g.wait("vector", TEN, tv)
                bank_rel(key, (VEC, evac_fn(oc, t, pv)))
                idx += 1

    def mk_copy_evac(dst_st):
        bufs = [carve("A2", 512) for _ in range(2)]
        state = {"i": 0}

        def evac(oc, t, pv):
            i = state["i"]
            b = bufs[i % 2].bitcast(BF16)
            if i >= 2:
                g.wait("vector", STQ)
            split_hl(b[:, 0:512], b[:, 512:1024], pv)
            ev = VEC.v
            g.wait("gpsimd", VEC, ev)
            g.dma("gpsimd", dst_st[0][oc, :, 512 * t:512 * t + 512], b[:, 0:512], STQ)
            g.dma("gpsimd", dst_st[1][oc, :, 512 * t:512 * t + 512], b[:, 512:1024], STQ)
            state["i"] = i + 1
            return ev
        return evac

    def mk_v_evac(dst_st, bias_name=None, bias_base=0):
        vb = carve("A2", 512)
        tb = carve("A2", 128)

        def evac(oc, t, pv):
            waitall("vector")
            if bias_name is None:
                vcopy(vb, pv, sem=VEC)
            else:
                ts(vb, pv, vcol(bias_name, bias_base + oc), ALU.add, sem=VEC)
            ev0 = VEC.v
            for i in range(4):
                g.wait("tensor", VEC)
                tv = transpose(PB[2][:][:, 0:128], vb[:, 128 * i:128 * i + 128], sem=TEN)
                g.wait("vector", TEN, tv)
                g.wait("vector", STQ)
                vcopy(tb, PB[2][:][:, 0:128], sem=VEC)
                g.wait("gpsimd", VEC)
                g.dma("gpsimd", dst_st[oc, 512 * t + 128 * i:512 * t + 128 * (i + 1), :],
                      tb, STQ)
            return ev0
        return evac

    wbA = [carve("A2", FC * 128) for _ in range(2)]
    wtmpA = carve("A2", FC * 128)
    wsemA = [mk_sem(), mk_sem()]
    stage_gemm(wq_s, FC, xn_rhs, NT, mk_copy_evac(qst), wb=wbA, wsem=wsemA, wtmp=wtmpA)
    stage_gemm(wk_s, FC, xn_rhs, NT, mk_copy_evac(kst), wb=wbA, wsem=wsemA, wtmp=wtmpA)
    stage_gemm(wv_s, FC, xn_rhs, NT, mk_v_evac(vst), wb=wbA, wsem=wsemA, wtmp=wtmpA)

    def att_slots(mode):
        slots = []
        if mode == "L0":
            for j in range(NSLOT):
                kts = [(kt, 0, None, None) for kt in range(4 * j)]
                kts += [(4 * j + d2, 128 * d2, 128 * d2, None) for d2 in range(4)]
                slots.append((512 * j, kts))
        else:
            for si in range(4):
                kts = [(kt, 0, None, None) for kt in range(8 * si)]
                kts += [(8 * si + d2, 0, None, 8 * si + d2) for d2 in range(8)]
                slots.append((512 * si, kts))
        return slots

    def attention(qsrc, ksrc, vsrc, astdst, mode):
        phase_barrier()
        bank_wait("tensor", "P0a", "P0b", "P1a", "P1b", "P2a", "P2b", "P3a", "P3b")
        reset("A1")
        reset("A2")
        QCOLS = T if mode == "L0" else TO
        slots = att_slots(mode)
        nslot = len(slots)
        cw = T // nslot
        Kbh = [carve("A1", T // 2).bitcast(BF16) for _ in range(2)]
        Kbl = [carve("A1", T // 2).bitcast(BF16) for _ in range(2)]
        Qbh = [carve("A1", QCOLS // 2).bitcast(BF16) for _ in range(2)]
        Qbl = [carve("A1", QCOLS // 2).bitcast(BF16) for _ in range(2)]
        Vb = [carve("A1", T) for _ in range(2)]
        Eb = [[carve("A2", 1024), carve("A2", 1024)] for _ in range(2)]
        accb = [carve("A2", 1024) for _ in range(2)]
        acchl = [carve("A2", 1024).bitcast(BF16) for _ in range(2)]
        rclb = [carve("A2", 512) for _ in range(2)]
        aob = [carve("A2", 512) for _ in range(2)]
        aosp = [carve("A2", 512).bitcast(BF16) for _ in range(2)]
        if not hasattr(attention, "KCS"):
            attention.KCS = [[mk_sem(), mk_sem()] for _ in range(2)]
        KCS = attention.KCS
        if mode == "L1":
            mskb = carve("A1", 4096).bitcast(I8)
            g.wait("sync", LDQ)
            vm = g.dma("sync", mskb, mask8, LDQ)
            g.wait("vector", LDQ, vm)

        def load_chunk(pp, s, ci):
            p = 2 * pp + s
            sem = KCS[s][ci % 2]
            g.wait("sync", sem)
            kc = slice(cw * ci, cw * (ci + 1))
            qc = slice(512 * ci, 512 * ci + 512)
            g.dma("sync", Kbh[s][:, kc], ksrc[0][p][:, kc], sem)
            g.dma("sync", Kbl[s][:, kc], ksrc[1][p][:, kc], sem)
            g.dma("sync", Qbh[s][:, qc], qsrc[0][p][:, qc], sem)
            g.dma("sync", Qbl[s][:, qc], qsrc[1][p][:, qc], sem)
            g.dma("sync", Vb[s][:, kc].rearrange("p (n c) -> p n c", c=128),
                  vsrc[p][kc, :].rearrange("(n p2) c -> p2 n c", p2=128), sem)

        prev_vav = [None, None]
        prev_ev = [None, None]
        prev_vr = [None, None]
        for pp in range(NPAIR // 2):
            waitall("sync")
            g.wait("sync", STQ)
            g.wait("sync", LDQ)
            snaps = []
            for ci in range(nslot):
                for s in (0, 1):
                    load_chunk(pp, s, ci)
                snaps.append([[(KCS[s][0], KCS[s][0].v), (KCS[s][1], KCS[s][1].v)]
                              for s in (0, 1)])
            for si, (q0, kts) in enumerate(slots):
                n_kt = len(kts)
                s_exp = [{}, {}]
                t_av = [{}, {}]
                a_v = [{}, {}]
                m_v = [{}, {}]

                def emit_av(s, ki, kts=kts, n_kt=n_kt, s_exp=s_exp, t_av=t_av,
                            m_v=m_v):
                    kt, off, _, _ = kts[ki]
                    E = Eb[s][ki % 2]
                    g.wait("tensor", SCA, s_exp[s][ki])
                    if off > 0:
                        g.wait("tensor", VEC, m_v[s][ki])
                    if ki == 0 and prev_ev[s] is not None:
                        g.wait("tensor", VEC, prev_ev[s])
                    vb = Vb[s][:, 128 * kt:128 * kt + 128]
                    mm(PB[s][:][0:64, 0:512], vb[:, 0:64], E[:, 0:512],
                       start=(ki == 0), stop=(ki == n_kt - 1), tile_position=(0, 0))
                    t_av[s][ki] = mm(PB[s][:][64:128, 512:1024], vb[:, 64:128],
                                     E[:, 512:1024], start=(ki == 0),
                                     stop=(ki == n_kt - 1), tile_position=(0, 64),
                                     sem=TEN)

                for s in (0, 1):
                    g.op("vector", lambda e, a=accb[s]: e.memset(a[:], 0.0), VEC)
                    for (sm, v) in snaps[si][s]:
                        g.wait("tensor", sm, v)
                for ki, (kt, off, trio, mki) in enumerate(kts):
                    for s in (0, 1):
                        if ki >= 1:
                            g.wait("tensor", SCA, s_exp[s][ki - 1])
                        elif prev_vr[s] is not None:
                            g.wait("tensor", SCA, prev_vr[s])
                        ks = slice(128 * kt, 128 * kt + 128)
                        qs = slice(q0 + off, q0 + 512)
                        pb = PB[2 + s][:]
                        for i_c, (kk, qq) in enumerate(
                                ((Kbh[s], Qbh[s]), (Kbh[s], Qbl[s]), (Kbl[s], Qbh[s]))):
                            mm(pb[:, off:512], kk[0:64, ks], qq[0:64, qs],
                               start=(i_c == 0), stop=(i_c == 2))
                            mm(pb[:, 512 + off:1024], kk[64:128, ks], qq[64:128, qs],
                               start=(i_c == 0), stop=(i_c == 2),
                               sem=TEN if i_c == 2 else None)
                        t_sc = TEN.v
                        pb3 = pb[:, 0:1024].rearrange("p (h q) -> p h q", q=512)
                        if mki is not None:
                            g.wait("vector", TEN, t_sc)
                            stt(pb3, mskb[:, 512 * mki:512 * mki + 512]
                                .unsqueeze(1).broadcast_to((128, 2, 512)),
                                -1e30, pb3, ALU.mult, ALU.add)
                            g.wait("scalar", VEC)
                        elif trio is not None:
                            g.wait("vector", TEN, t_sc)
                            tt(pb3[:, :, trio:trio + 128], pb3[:, :, trio:trio + 128],
                               sb_tri[:].unsqueeze(1).broadcast_to((128, 2, 128)),
                               ALU.add)
                            g.wait("scalar", VEC)
                        else:
                            g.wait("scalar", TEN, t_sc)
                        E = Eb[s][ki % 2]
                        if ki >= 2:
                            g.wait("scalar", TEN, t_av[s][ki - 2])
                            g.wait("scalar", VEC, a_v[s][ki - 2])
                        elif prev_vav[s] is not None:
                            g.wait("scalar", TEN, prev_vav[s])
                        E3 = E.rearrange("p (h q) -> p h q", q=512)
                        s_exp[s][ki] = act(E3[:, :, off:512], pb3[:, :, off:512],
                                           AF.Exp, scale=f32(0.125), sem=SCA)
                        if off > 0:
                            g.op("vector", lambda e, a=E3, o=off:
                                 e.memset(a[:, :, 0:o], 0.0), VEC)
                            m_v[s][ki] = VEC.v
                        g.wait("vector", SCA, s_exp[s][ki])
                        a3 = accb[s].rearrange("p (h q) -> p h q", q=512)
                        tt(a3, a3, E3, ALU.add)
                        a_v[s][ki] = VEC.v
                    for s in (0, 1):
                        if ki >= 1:
                            emit_av(s, ki - 1)
                for s in (0, 1):
                    emit_av(s, n_kt - 1)
                vav = [t_av[s][n_kt - 1] for s in (0, 1)]
                vsp = [0, 0]
                for s in (0, 1):
                    g.wait("vector", STQ)
                    ahl = acchl[s]
                    split_hl(ahl[:, 0:1024], ahl[:, 1024:2048], accb[s])
                    vsp[s] = VEC.v
                vsum = [0, 0]
                for s in (0, 1):
                    g.wait("tensor", VEC, vsp[s])
                    g.wait("tensor", SCA, s_exp[s][n_kt - 1])
                    ahl = acchl[s]
                    pbs = PB[2 + s][:]
                    mm(pbs[0:64, 0:512], sb_ones64b_t[:], ahl[:, 0:512],
                       start=True, stop=False, tile_position=(0, 0))
                    mm(pbs[0:64, 0:512], sb_ones64b_t[:], ahl[:, 1024:1536],
                       start=False, stop=True, tile_position=(0, 0))
                    mm(pbs[64:128, 0:512], sb_ones64b_t[:], ahl[:, 512:1024],
                       start=True, stop=False, tile_position=(0, 64))
                    mm(pbs[64:128, 0:512], sb_ones64b_t[:], ahl[:, 1536:2048],
                       start=False, stop=True, tile_position=(0, 64), sem=TEN)
                    vsum[s] = TEN.v
                vr = [0, 0]
                for s in (0, 1):
                    g.wait("scalar", TEN, vsum[s])
                    act(rclb[s], PB[2 + s][:][:, 0:512], AF.Ln)
                    act(rclb[s], rclb[s], AF.Exp, scale=f32(-1.0), sem=SCA)
                    vr[s] = SCA.v
                for s in (0, 1):
                    g.wait("vector", SCA, vr[s])
                    g.wait("vector", TEN, vav[s])
                    tt(aob[s][0:64, :], PB[s][:][0:64, 0:512], rclb[s][0:64, :],
                       ALU.mult)
                    tt(aob[s][64:128, :], PB[s][:][64:128, 512:1024],
                       rclb[s][64:128, :], ALU.mult)
                    av = aosp[s]
                    split_hl(av[:, 0:512], av[:, 512:1024], aob[s])
                    ev = VEC.v
                    g.wait("gpsimd", VEC, ev)
                    p = 2 * pp + s
                    g.dma("gpsimd", astdst[0][p, :, q0:q0 + 512], av[:, 0:512], STQ)
                    g.dma("gpsimd", astdst[1][p, :, q0:q0 + 512], av[:, 512:1024], STQ)
                    prev_vav[s] = vav[s]
                    prev_ev[s] = ev
                    prev_vr[s] = vr[s]
        for kk in ("P0a", "P0b", "P1a", "P1b", "P2a", "P2b", "P3a", "P3b"):
            bank_rel(kk, (VEC, VEC.v), (SCA, SCA.v))

    attention(qst, kst, vst, ast, "L0")

    if stop_phase <= 3:
        return finish(nc, g, ctx)
    res_gemm(wp_s, FC, ast, x_fm, "bp", y1st, NT)

    natsem = [mk_sem(), mk_sem()]
    qw_stsem = [mk_sem(), mk_sem()]

    def build_qw_steps():
        natb_w = [A2t[:][:, 8704:9728], A2t[:][:, 9728:10752]]
        QSCR.update(A=A2t[:][:, 10752:11776], S1=A2t[:][:, 11776:12800],
                    S2=A2t[:][:, 12800:13824], G=A2t[:][:, 13824:14848],
                    M=A2t[:][:, 14848:14912], SS=A2t[:][:, 14912:14976],
                    R=A2t[:][:, 14976:15040])
        trbig_w = [A2t[:][:, 15040:15552], A2t[:][:, 15552:16064]]
        st = {"lastten": {}, "split": {}, "store": {}, "gi": [0], "first": [True]}

        def mk_load(src, si, w):
            def f():
                sem = natsem[si % 2]
                g.wait("sync", sem)
                lt = st["lastten"].get(si % 2)
                if lt is not None:
                    g.wait("sync", TEN, lt)
                st[("ldv", si % 2)] = g.dma("sync", natb_w[si % 2][:, 0:w], src, sem)
            return f

        def mk_quant(si, w):
            def f():
                sem = natsem[si % 2]
                v = st[("ldv", si % 2)]
                g.wait("vector", sem, v)
                g.wait("scalar", sem, v)
                st[("qv", si)] = quantize(natb_w[si % 2][:, 0:w], w,
                                          natb_w[si % 2][:, 0:w])
            return f

        def mk_group(si, dst, oc, k0g, kloc, kn):
            def f():
                if st["first"][0]:
                    bank_wait("tensor", "P2b", "P3b")
                    st["first"][0] = False
                gp = st["gi"][0] % 2
                st["gi"][0] += 1
                pb = PB[2 + gp][:][:, 512:1024]
                trv = trbig_w[gp].bitcast(BF16)
                ssem = qw_stsem[gp]
                pv = st["split"].get(gp)
                if pv is not None:
                    g.wait("tensor", VEC, pv)
                g.wait("tensor", VEC, st[("qv", si)])
                tv = 0
                for k in range(kn):
                    tv = transpose(pb[:, 128 * k:128 * k + 128],
                                   natb_w[si % 2][:, 128 * (kloc + k):
                                                  128 * (kloc + k) + 128], sem=TEN)
                g.wait("vector", TEN, tv)
                sv_ = st["store"].get(gp)
                if sv_ is not None:
                    g.wait("vector", ssem, sv_)
                split_hl(trv[:, 0:128 * kn], trv[:, 128 * kn:256 * kn],
                         pb[:, 0:128 * kn])
                st["split"][gp] = VEC.v
                g.wait("gpsimd", VEC, VEC.v)
                g.dma("gpsimd", dst[0][oc, :, k0g:k0g + kn, :]
                      .rearrange("p k c -> p (k c)"), trv[:, 0:128 * kn], ssem)
                g.dma("gpsimd", dst[1][oc, :, k0g:k0g + kn, :]
                      .rearrange("p k c -> p (k c)"), trv[:, 128 * kn:256 * kn], ssem)
                st["store"][gp] = ssem.v
                st["lastten"][si % 2] = TEN.v
            return f

        si = 0
        for (nat, dst, n_och, icols) in ((wg_nat, wgq_s, DIC, D),
                                         (wu_nat, wuq_s, DIC, D),
                                         (wd_nat, wdq_s, FC, DI)):
            for oc in range(n_och):
                for off in range(0, icols, 1024):
                    w = min(1024, icols - off)
                    qw_steps.append(mk_load(nat[oc][:, off:off + w], si, w))
                    qw_steps.append(mk_quant(si, w))
                    for kloc in range(0, w // 128, 4):
                        kn = min(4, w // 128 - kloc)
                        qw_steps.append(mk_group(si, dst, oc, off // 128 + kloc,
                                                 kloc, kn))
                    si += 1

    build_qw_steps()

    if stop_phase <= 4:
        return finish(nc, g, ctx)
    reset("A1")
    ZNh = carve("A1", FC * T // 2).bitcast(BF16)
    ZNl = carve("A1", FC * T // 2).bitcast(BF16)
    reset("A2")
    M1 = carve("A2", 512)
    R1 = carve("A2", 512)
    sqb = carve("A2", 1024)
    rd3 = Reader("A2", 512, n=3)
    rd4 = Reader("A2", 512, n=3)
    ztmp = carve("A2", 512)
    for t in range(NT):
        ln_stats(lambda fc, t_, t=t: rd3.rd(y1st[fc, :, 512 * t:512 * t + 512],
                                            "tensor", "scalar"),
                 1, M1, R1, sqb, barrier=(t == 0))
        for fc in range(FC):
            ch = rd4.rd(y1st[fc, :, 512 * t:512 * t + 512], "vector")
            c = slice(fc * T + 512 * t, fc * T + 512 * t + 512)
            tt(ztmp, ch, M1, ALU.subtract)
            tt(ztmp, ztmp, R1, ALU.mult)
            split_hl(ZNh[:, c], ZNl[:, c], ztmp)
            tick(2)

    def res_gemm_sbuf(wsrc, rhs_fn, res_src, bias_name, dst_st, ntt, sign_w=False):
        phase_barrier()
        bank_wait("tensor", "P0a", "P1a")
        reset("A2")
        res_r = Reader("A2", 512, n=3)
        outb = [carve("A2", 512) for _ in range(2)]
        wallsem = mk_sem()
        if sign_w:
            wsb = carve("A2", FC * FC * 64).bitcast(BF16)
            wtmp = carve("A2", FC * 128)
            for oc in range(FC):
                g.wait("sync", SCA)
                v = g.dma("sync", wtmp.rearrange("p (k c) -> p k c", c=128),
                          wsrc[oc], wallsem)
                g.wait("scalar", wallsem, v)
                act(wsb[:, 1024 * oc:1024 * (oc + 1)], wtmp, AF.Sign, sem=SCA)
            g.wait("tensor", SCA)
        else:
            wsb = carve("A2", FC * FC * 128)
            wtmp = carve("A2", FC * 128)
            for oc in range(FC):
                g.wait("sync", VEC)
                v = g.dma("sync", wtmp.rearrange("p (k c) -> p k c", c=128),
                          wsrc[oc], wallsem)
                g.wait("vector", wallsem, v)
                wv = wsb[:, 1024 * oc:1024 * (oc + 1)].bitcast(BF16)
                split_hl(wv[:, 0:1024], wv[:, 1024:2048], wtmp)
            g.wait("tensor", VEC)
        idx = 0
        for t in range(ntt):
            for oc in range(FC):
                key = f"P{idx % 2}a"
                pv = PB[idx % 2][:][:, 0:512]
                bank_wait("tensor", key)
                if sign_w:
                    n_mm = 2 * FC
                    i_mm = 0
                    for k in range(FC):
                        sw = wsb[:, 1024 * oc + 128 * k:1024 * oc + 128 * (k + 1)]
                        rh, rl = rhs_fn(k, t)
                        for aop in (rh, rl):
                            mm(pv, sw, aop, start=(i_mm == 0), stop=(i_mm == n_mm - 1),
                               sem=TEN if i_mm == n_mm - 1 else None)
                            i_mm += 1
                else:
                    wv = wsb[:, 1024 * oc:1024 * (oc + 1)].bitcast(BF16)
                    n_mm = 3 * FC
                    i_mm = 0
                    for k in range(FC):
                        wh = wv[:, 128 * k:128 * k + 128]
                        wl = wv[:, 1024 + 128 * k:1024 + 128 * (k + 1)]
                        rh, rl = rhs_fn(k, t)
                        for (wop, aop) in ((wh, rh), (wh, rl), (wl, rh)):
                            mm(pv, wop, aop, start=(i_mm == 0), stop=(i_mm == n_mm - 1),
                               sem=TEN if i_mm == n_mm - 1 else None)
                            i_mm += 1
                tv = TEN.v
                rb = res_r.rd(res_src[oc, :, 512 * t:512 * t + 512], "vector")
                ob = outb[idx % 2]
                if idx >= 2:
                    g.wait("vector", STQ)
                g.wait("vector", TEN, tv)
                if bias_name is not None:
                    ts(ob, pv, vcol(bias_name, oc), ALU.add)
                    bank_rel(key)
                    tt(ob, ob, rb, ALU.add)
                else:
                    tt(ob, pv, rb, ALU.add)
                    bank_rel(key)
                ev = VEC.v
                g.wait("gpsimd", VEC, ev)
                g.dma("gpsimd", dst_st[oc, :, 512 * t:512 * t + 512], ob, STQ)
                idx += 1
                tick(2)

    def zn_rhs(k, t):
        c = slice(k * T + 512 * t, k * T + 512 * t + 512)
        return (ZNh[:, c], ZNl[:, c])

    res_gemm_sbuf(w0t_s, zn_rhs, y1st, None, y2st, NT, sign_w=True)

    if stop_phase <= 5:
        return finish(nc, g, ctx)
    phase_barrier()
    reset("A2")
    rda = Reader("A2", 256, n=3)
    rdb = Reader("A2", 256, n=3)
    bo_ = [carve("A2", 256) for _ in range(2)]
    st_hist = []
    ii = 0
    for oc in range(FC):
        for j in range(NSLOT):
            ba = rda.rd(y2st[oc, :, 256 * OWN_A[j]:256 * OWN_A[j] + 256], "vector")
            bb = rdb.rd(y2st[oc, :, 256 * OWN_B[j]:256 * OWN_B[j] + 256], "vector")
            ob = bo_[ii % 2]
            if ii >= 2:
                g.wait("vector", STQ)
            ts(ob, ba, vcol("selA"), ALU.mult)
            stt(ob, bb, vcol("selB"), ob, ALU.mult, ALU.add, sem=VEC)
            g.wait("gpsimd", VEC)
            st_hist.append(g.dma("gpsimd", y2own[oc, :, 256 * j:256 * j + 256], ob, STQ))
            ii += 1
            tick(2)

    if stop_phase <= 6:
        return finish(nc, g, ctx)
    reset("A1")
    M2 = carve("A1", T)
    R2 = carve("A1", T)
    P2 = R2
    reset("A2")
    sqb = carve("A2", 1024)
    rd5 = Reader("A2", 512, n=3)
    ln_stats(lambda fc, t_: rd5.rd(y2st[fc, :, 512 * t_:512 * t_ + 512], "tensor", "scalar"),
             NT, M2, R2, sqb)
    tmpc = carve("A2", 512)
    for t in range(NT):
        c = slice(512 * t, 512 * t + 512)
        g.wait("vector", SCA)
        tt(tmpc, R2[:, c], R2[:, c], ALU.mult)
        ts(tmpc, tmpc, -float(EPS), ALU.mult, float(1.0 + EPS), ALU.add, sem=VEC)
        g.wait("scalar", VEC)
        act(tmpc, tmpc, AF.Ln)
        act(tmpc, tmpc, AF.Exp, scale=f32(-0.5), sem=SCA)
        g.wait("vector", SCA)
        tt(P2[:, c], R2[:, c], tmpc, ALU.mult, sem=VEC)
    g.wait("scalar", VEC)

    rd6 = Reader("A2", 512, n=3)
    n1b = [carve("A2", 512) for _ in range(2)]
    nsp = [carve("A2", 512).bitcast(BF16) for _ in range(2)]
    ii = 0
    for t in range(NT):
        for fc in range(FC):
            ch = rd6.rd(y2st[fc, :, 512 * t:512 * t + 512], "vector")
            nb = n1b[ii % 2]
            sp = nsp[ii % 2]
            if ii >= 2:
                g.wait("vector", STQ)
            tt(nb, ch, M2[:, 512 * t:512 * t + 512], ALU.subtract)
            tt(nb, nb, P2[:, 512 * t:512 * t + 512], ALU.mult)
            ts(nb, nb, vcol("g1", fc), ALU.mult, vcol("b1", fc), ALU.add)
            split_hl(sp[:, 0:512], sp[:, 512:1024], nb)
            g.wait("gpsimd", VEC)
            g.dma("gpsimd", n1h_st[fc, :, 512 * t:512 * t + 512], sp[:, 0:512], STQ)
            g.dma("gpsimd", n1l_st[fc, :, 512 * t:512 * t + 512], sp[:, 512:1024], STQ)
            ii += 1
            tick(2)

    while qw_steps:
        qw_steps.pop(0)()
    for s_ in qw_stsem:
        for eng in ("vector", "scalar", "tensor"):
            g.wait(eng, s_)

    if stop_phase <= 7:
        return finish(nc, g, ctx)
    reset("A1")
    N1h = carve("A1", FC * T // 2).bitcast(BF16)
    N1l = carve("A1", FC * T // 2).bitcast(BF16)
    reset("A2")
    phase_barrier()
    g.wait("sync", STQ)
    g.wait("sync", LDQ)
    for fc in range(FC):
        g.dma("sync", N1h[:, fc * T:(fc + 1) * T], n1h_st[fc], LDQ)
        vn1 = g.dma("sync", N1l[:, fc * T:(fc + 1) * T], n1l_st[fc], LDQ)
    g.wait("tensor", LDQ, vn1)
    g.wait("vector", LDQ, vn1)

    def n1_rhs(k, t):
        c = slice(k * T + 512 * t, k * T + 512 * t + 512)
        return (N1h[:, c], N1l[:, c])

    C2b_ = carve("A2", T)
    S2b_ = carve("A2", T)
    g.wait("sync", LDQ)
    v1_ = g.dma("sync", C2b_, c2t, LDQ)
    v2_ = g.dma("sync", S2b_, s2t, LDQ)
    g.wait("vector", LDQ)

    def mk_rope_evac(dst_st, ctab, stab, bias_base):
        kpre = [carve("A2", 512) for _ in range(2)]
        krot = [carve("A2", 512) for _ in range(2)]
        kout1 = carve("A2", 512)
        kout = [kout1, kout1]
        ksp = [carve("A2", 512) for _ in range(2)]
        rsem = mk_sem()
        state = {"i": 0}

        def evac(oc, t, pv):
            i = state["i"]
            b0, b1, b2 = kpre[i % 2], krot[i % 2], kout[i % 2]
            if i >= 2:
                g.wait("vector", rsem, 64 * (i - 1))
                g.wait("vector", STQ)
            ts(b0, pv, vcol("bqkv", bias_base + oc), ALU.add)
            ev0 = VEC.v
            g.wait("gpsimd", VEC, ev0)
            for (d0, s0) in ((0, 32), (32, 0), (64, 96), (96, 64)):
                g.dma("gpsimd", b1[d0:d0 + 32, :], b0[s0:s0 + 32, :], rsem)
            c = slice(512 * t, 512 * t + 512)
            g.wait("vector", rsem, 64 * (i + 1))
            tt(b2, b0, ctab[:, c], ALU.mult)
            tt(b1, b1, stab[:, c], ALU.mult)
            tt(b2, b2, b1, ALU.add)
            bh = ksp[i % 2].bitcast(BF16)
            split_hl(bh[:, 0:512], bh[:, 512:1024], b2)
            ev = VEC.v
            g.wait("gpsimd", VEC, ev)
            g.dma("gpsimd", dst_st[0][oc, :, c], bh[:, 0:512], STQ)
            g.dma("gpsimd", dst_st[1][oc, :, c], bh[:, 512:1024], STQ)
            state["i"] = i + 1
            return ev0
        return evac

    class WView:
        def __init__(self, base):
            self.base = base

        def __getitem__(self, i):
            return wqkv_s[self.base + i]

    wb_c2 = [carve("A2", FC * 128) for _ in range(2)]
    wtmp_c2 = carve("A2", FC * 128)
    wsem_c2 = [mk_sem(), mk_sem()]
    stage_gemm(WView(FC), FC, n1_rhs, NT, mk_rope_evac(kst, C2b_, S2b_, FC),
               wb=wb_c2, wsem=wsem_c2, wtmp=wtmp_c2)
    stage_gemm(WView(2 * FC), FC, n1_rhs, NT, mk_v_evac(vst, "bqkv", 2 * FC),
               wb=wb_c2, wsem=wsem_c2, wtmp=wtmp_c2)

    if stop_phase <= 8:
        return finish(nc, g, ctx)
    reset("A1")
    M2o = carve("A1", TO)
    R2o = carve("A1", TO)
    P2o = R2o
    N1oh = carve("A1", FC * TO // 2).bitcast(BF16)
    N1ol = carve("A1", FC * TO // 2).bitcast(BF16)
    reset("A2")
    sqb = carve("A2", 1024)
    g.wait("sync", STQ)
    rd8 = Reader("A2", 512, n=3)
    ln_stats(lambda fc, t_: rd8.rd(y2own[fc, :, 512 * t_:512 * t_ + 512], "tensor", "scalar"),
             NTO, M2o, R2o, sqb)
    tmpo = carve("A2", 512)
    for t in range(NTO):
        c = slice(512 * t, 512 * t + 512)
        g.wait("vector", SCA)
        tt(tmpo, R2o[:, c], R2o[:, c], ALU.mult)
        ts(tmpo, tmpo, -float(EPS), ALU.mult, float(1.0 + EPS), ALU.add, sem=VEC)
        g.wait("scalar", VEC)
        act(tmpo, tmpo, AF.Ln)
        act(tmpo, tmpo, AF.Exp, scale=f32(-0.5), sem=SCA)
        g.wait("vector", SCA)
        tt(P2o[:, c], R2o[:, c], tmpo, ALU.mult, sem=VEC)
    g.wait("scalar", VEC)
    rd9 = Reader("A2", 512, n=3)
    ntmp = carve("A2", 512)
    for t in range(NTO):
        for fc in range(FC):
            ch = rd9.rd(y2own[fc, :, 512 * t:512 * t + 512], "vector")
            c = slice(fc * TO + 512 * t, fc * TO + 512 * t + 512)
            tt(ntmp, ch, M2o[:, 512 * t:512 * t + 512], ALU.subtract)
            tt(ntmp, ntmp, P2o[:, 512 * t:512 * t + 512], ALU.mult)
            ts(ntmp, ntmp, vcol("g1", fc), ALU.mult, vcol("b1", fc), ALU.add)
            split_hl(N1oh[:, c], N1ol[:, c], ntmp)

    def n1o_rhs(k, t):
        c = slice(k * TO + 512 * t, k * TO + 512 * t + 512)
        return (N1oh[:, c], N1ol[:, c])

    C2ob = carve("A2", TO)
    S2ob = carve("A2", TO)
    g.wait("sync", LDQ)
    v1_ = g.dma("sync", C2ob, c2o, LDQ)
    v2_ = g.dma("sync", S2ob, s2o, LDQ)
    g.wait("vector", LDQ)
    stage_gemm(WView(0), FC, n1o_rhs, NTO, mk_rope_evac(qst1, C2ob, S2ob, 0))

    if stop_phase <= 9:
        return finish(nc, g, ctx)
    reset("A1")
    attention(qst1, kst, vst, ast1, "L1")

    if stop_phase <= 10:
        return finish(nc, g, ctx)
    reset("A1")
    Z2 = carve("A1", FC * TO)
    M2r = carve("A1", TO)
    R2r = carve("A1", TO)
    reset("A2")
    sqb = carve("A2", 1024)
    rd10 = Reader("A2", 512, n=3)
    ln_stats(lambda fc, t_: rd10.rd(y2own[fc, :, 512 * t_:512 * t_ + 512], "tensor", "scalar"),
             NTO, M2r, R2r, sqb)

    def wo_gemm():
        phase_barrier()
        bank_wait("tensor", "P0a", "P1a")
        whl = carve("A1", FC * FC * 128)
        wtmp = carve("A2", FC * 128)
        wallsem = mk_sem()
        rhs_rh = Reader("A2", FC * 256, n=1)
        rhs_rl = Reader("A2", FC * 256, n=1)
        res_r = Reader("A2", 512, n=3)
        zb = [carve("A2", 512) for _ in range(2)]
        for oc in range(FC):
            g.wait("sync", VEC)
            v = g.dma("sync", wtmp.rearrange("p (k c) -> p k c", c=128),
                      wo_s[oc], wallsem)
            g.wait("vector", wallsem, v)
            wv = whl[:, 1024 * oc:1024 * (oc + 1)].bitcast(BF16)
            split_hl(wv[:, 0:1024], wv[:, 1024:2048], wtmp)
        g.wait("tensor", VEC)
        idx = 0
        for t in range(NTO):
            rbh = rhs_rh.rd(ast1[0][:, :, 512 * t:512 * t + 512], "tensor").bitcast(BF16)
            rbl = rhs_rl.rd(ast1[1][:, :, 512 * t:512 * t + 512], "tensor").bitcast(BF16)
            for oc in range(FC):
                key = f"P{idx % 2}a"
                pv = PB[idx % 2][:][:, 0:512]
                bank_wait("tensor", key)
                wv = whl[:, 1024 * oc:1024 * (oc + 1)].bitcast(BF16)
                i_mm = 0
                for k in range(FC):
                    wh = wv[:, 128 * k:128 * k + 128]
                    wl = wv[:, 1024 + 128 * k:1024 + 128 * (k + 1)]
                    ah = rbh[:, 512 * k:512 * k + 512]
                    al = rbl[:, 512 * k:512 * k + 512]
                    for (wop, aop) in ((wh, ah), (wh, al), (wl, ah)):
                        mm(pv, wop, aop, start=(i_mm == 0), stop=(i_mm == 3 * FC - 1),
                           sem=TEN if i_mm == 3 * FC - 1 else None)
                        i_mm += 1
                tv = TEN.v
                rb = res_r.rd(y2own[oc, :, 512 * t:512 * t + 512], "vector")
                zbb = zb[idx % 2]
                c = slice(512 * t, 512 * t + 512)
                g.wait("vector", TEN, tv)
                tt(zbb, rb, M2r[:, c], ALU.subtract)
                tt(zbb, zbb, R2r[:, c], ALU.mult)
                dst = Z2[:, oc * TO + 512 * t:oc * TO + 512 * t + 512]
                ts(dst, pv, vcol("bo", oc), ALU.add)
                bank_rel(key)
                tt(dst, dst, zbb, ALU.add)
                idx += 1

    wo_gemm()

    if stop_phase <= 11:
        return finish(nc, g, ctx)
    w1sem = [mk_sem(), mk_sem()]
    w2sem = [mk_sem(), mk_sem()]
    wstg_sem = mk_sem()

    def stage_whl():
        phase_barrier()
        reset("A2")
        wtmp = [carve("A2", DMC * 128) for _ in range(2)]
        whl = [carve("A2", DMC * 128) for _ in range(2)]
        lsem = [wstg_sem, mk_sem()]
        ssem = [mk_sem(), mk_sem()]
        split_v = {}
        store_v = {}
        ii = 0
        for (wsrc, dst, n_oc, n_k) in ((w1_s, w1hl_s, DMC, FC),
                                       (w2_s, w2hl_s, FC, DMC)):
            for oc in range(n_oc):
                bi = ii % 2
                g.wait("sync", lsem[bi])
                sv = split_v.get(bi)
                if sv is not None:
                    g.wait("sync", VEC, sv)
                v = g.dma("sync", wtmp[bi][:, 0:n_k * 128]
                          .rearrange("p (k c) -> p k c", c=128), wsrc[oc], lsem[bi])
                g.wait("vector", lsem[bi], v)
                stv = store_v.get(bi)
                if stv is not None:
                    g.wait("vector", ssem[bi], stv)
                wv = whl[bi][:, 0:n_k * 128].bitcast(BF16)
                split_hl(wv[:, 0:n_k * 128], wv[:, n_k * 128:2 * n_k * 128],
                         wtmp[bi][:, 0:n_k * 128])
                split_v[bi] = VEC.v
                g.wait("gpsimd", VEC, VEC.v)
                g.dma("gpsimd", dst[oc], wv[:, 0:2 * n_k * 128], ssem[bi])
                store_v[bi] = ssem[bi].v
                ii += 1
        for s_ in ssem:
            for eng in ("vector", "scalar", "tensor"):
                g.wait(eng, s_)

    stage_whl()

    reset("A2")
    M3 = carve("A2", TO)
    R3 = carve("A2", TO)
    sqb = carve("A2", 1024)

    def z2_chunk(fc, t):
        return Z2[:, fc * TO + 512 * t:fc * TO + 512 * t + 512]

    ln_stats(z2_chunk, NTO, M3, R3, sqb)

    n2b = carve("A2", FC * 512)
    w1b = [carve("A2", FC * 128) for _ in range(2)]
    gtmp = carve("A2", 512)
    ob2 = carve("A2", 256)
    res_r2 = Reader("A2", 256, n=3)

    def mlp():
        phase_barrier()
        bank_wait("tensor", "P0a", "P1a", "P2a")
        w2b = [A1t[:][:, 16384:16384 + 4096], A1t[:][:, 20480:20480 + 4096]]
        hb = A1t[:][:, 24576:24576 + DMC * 256].bitcast(BF16)
        hh = hb[:, 0:DMC * 256]
        hl = hb[:, DMC * 256:2 * DMC * 256]
        g.wait("sync", STQ)
        for t in range(NTO):
            g.wait("vector", TEN)
            for fc in range(FC):
                c = slice(512 * t, 512 * t + 512)
                tt(gtmp, z2_chunk(fc, t), M3[:, c], ALU.subtract)
                tt(gtmp, gtmp, R3[:, c], ALU.mult)
                ts(gtmp, gtmp, vcol("g2", fc), ALU.mult, vcol("b2", fc), ALU.add)
                nv = n2b.bitcast(BF16)
                split_hl(nv[:, 512 * fc:512 * fc + 512],
                         nv[:, FC * 512 + 512 * fc:FC * 512 + 512 * (fc + 1)], gtmp)
            vn2 = VEC.v
            g.wait("tensor", VEC, vn2)
            nv = n2b.bitcast(BF16)
            for half in range(2):
                hrel = []
                wsnap = []
                for oc in range(DMC):
                    wsnap.append(TEN.v)
                    if oc >= 2:
                        g.wait("sync", TEN, wsnap[oc - 1])
                    vldw = g.dma("sync", w1b[oc % 2].bitcast(BF16), w1hl_s[oc],
                                 w1sem[oc % 2])
                    pv = PB[oc % 2][:][:, 0:256]
                    g.wait("tensor", w1sem[oc % 2], vldw)
                    if oc >= 2:
                        g.wait("tensor", SCA, hrel[oc - 2])
                    wv = w1b[oc % 2].bitcast(BF16)
                    i_mm = 0
                    for k in range(FC):
                        wh = wv[:, 128 * k:128 * k + 128]
                        wl = wv[:, FC * 128 + 128 * k:FC * 128 + 128 * (k + 1)]
                        cq = slice(512 * k + 256 * half, 512 * k + 256 * half + 256)
                        ah = nv[:, cq]
                        al = nv[:, FC * 512:2 * FC * 512][:, cq]
                        for (wop, aop) in ((wh, ah), (wh, al), (wl, ah)):
                            mm(pv, wop, aop, start=(i_mm == 0), stop=(i_mm == 3 * FC - 1),
                               sem=TEN if i_mm == 3 * FC - 1 else None)
                            i_mm += 1
                    g.wait("scalar", TEN)
                    g.wait("scalar", VEC)
                    act(gtmp[:, 0:256], pv, AF.Gelu, bias=vcol("b1m", oc), sem=SCA)
                    hrel.append(SCA.v)
                    g.wait("vector", SCA)
                    split_hl(hh[:, 256 * oc:256 * oc + 256],
                             hl[:, 256 * oc:256 * oc + 256], gtmp[:, 0:256])
                g.wait("tensor", VEC)
                wsnap2 = []
                for oc in range(FC):
                    wsnap2.append(TEN.v)
                    if oc >= 2:
                        g.wait("sync", TEN, wsnap2[oc - 1])
                    elif oc == 0:
                        g.wait("sync", TEN)
                    vldw = g.dma("sync", w2b[oc % 2].bitcast(BF16), w2hl_s[oc],
                                 w2sem[oc % 2])
                    pv = PB[2][:][:, 0:256]
                    g.wait("tensor", w2sem[oc % 2], vldw)
                    g.wait("tensor", VEC)
                    wv = w2b[oc % 2].bitcast(BF16)
                    i_mm = 0
                    for k in range(DMC):
                        wh = wv[:, 128 * k:128 * k + 128]
                        wl = wv[:, DMC * 128 + 128 * k:DMC * 128 + 128 * (k + 1)]
                        ah = hh[:, 256 * k:256 * k + 256]
                        al = hl[:, 256 * k:256 * k + 256]
                        for (wop, aop) in ((wh, ah), (wh, al), (wl, ah)):
                            mm(pv, wop, aop, start=(i_mm == 0), stop=(i_mm == 3 * DMC - 1),
                               sem=TEN if i_mm == 3 * DMC - 1 else None)
                            i_mm += 1
                    tv = TEN.v
                    cq = slice(512 * t + 256 * half, 512 * t + 256 * half + 256)
                    rb = res_r2.rd(y2own[oc, :, cq], "vector")
                    g.wait("vector", TEN, tv)
                    g.wait("vector", STQ)
                    ts(ob2, pv, vcol("b2m", oc), ALU.add)
                    tt(ob2, ob2, Z2[:, oc * TO + 256 * (2 * t + half):
                                    oc * TO + 256 * (2 * t + half) + 256], ALU.add)
                    tt(ob2, ob2, rb, ALU.add)
                    ev = VEC.v
                    g.wait("gpsimd", VEC, ev)
                    g.dma("gpsimd", y3own[oc, :, cq], ob2, STQ)
                    g.wait("tensor", VEC, ev)

    mlp()

    if stop_phase <= 12:
        return finish(nc, g, ctx)
    reset("A1")
    QSCR.update(A=carve("A1", 1024), S1=carve("A1", 1024), S2=carve("A1", 1024),
                G=carve("A1", 1024), M=carve("A1", 64), SS=carve("A1", 64),
                R=carve("A1", 64))
    carve("A1", 1216)

    if stop_phase <= 13:
        return finish(nc, g, ctx)
    reset("A2")
    M4 = carve("A2", TO)
    R4 = carve("A2", TO)
    a2_mark = arena_off["A2"]
    sqb = carve("A2", 1024)
    g.wait("sync", STQ)
    rd11 = Reader("A2", 512, n=3)
    ln_stats(lambda fc, t_: rd11.rd(y3own[fc, :, 512 * t_:512 * t_ + 512], "tensor", "scalar"),
             NTO, M4, R4, sqb)
    arena_off["A2"] = a2_mark

    XQb = A1t[:][:, 5440:5440 + 4096].bitcast(BF16)
    XTM = A1t[:][:, 9536:9536 + 4096]
    HTM = A1t[:][:, 9536:9536 + 4 * DI]
    HQFb = A1t[:][:, 20800:20800 + DIC * 512].bitcast(BF16)
    xb = A1t[:][:, 32064:32064 + 512]
    wgb = [carve("A2", FC * 128) for _ in range(2)]
    wub = [carve("A2", FC * 128) for _ in range(2)]
    wdb = [carve("A2", DIC * 128) for _ in range(2)]
    ub = carve("A2", 512)
    rd12 = Reader("A2", 512, n=2)
    rd13 = Reader("A2", 512, n=2)
    gusem = [mk_sem(), mk_sem()]
    dsem2 = [mk_sem(), mk_sem()]

    def ffn():
        phase_barrier()
        bank_wait("tensor", "P0a", "P0b", "P1a", "P1b", "P2a", "P2b")
        gu_guard = {}
        d_guard = {}
        tp_guard = {}
        pg_guard = {}
        def tp_slot(j):
            return PB[2][:][:, 512 * (j % 2):512 * (j % 2) + 128]
        XQH = XQb[:, 0:FC * 512]
        XQL = XQb[:, FC * 512:2 * FC * 512]
        HQH = HQFb[:, 0:DIC * 512]
        HQL = HQFb[:, DIC * 512:2 * DIC * 512]

        def x_stage(t):
            g.wait("vector", TEN)
            g.wait("scalar", TEN)
            g.wait("vector", STQ)
            for fc in range(FC):
                ch = rd12.rd(y3own[fc, :, 512 * t:512 * t + 512], "vector")
                c = slice(512 * t, 512 * t + 512)
                g.wait("vector", TEN)
                tt(xb, ch, M4[:, c], ALU.subtract)
                tt(xb, xb, R4[:, c], ALU.mult, sem=VEC)
                vx = VEC.v
                g.wait("tensor", VEC, vx)
                for i in range(4):
                    pv = tp_guard.get(i % 2)
                    if pv is not None:
                        g.wait("tensor", VEC, pv)
                    tv = transpose(tp_slot(i), xb[:, 128 * i:128 * i + 128], sem=TEN)
                    g.wait("vector", TEN, tv)
                    vcopy(XTM[:, 1024 * i + 128 * fc:1024 * i + 128 * fc + 128],
                          tp_slot(i), sem=VEC)
                    tp_guard[i % 2] = VEC.v
            for i in range(4):
                qv = quantize(XTM[:, 1024 * i:1024 * (i + 1)], 1024,
                              XTM[:, 1024 * i:1024 * (i + 1)], xv=VEC.v)
                g.wait("tensor", VEC, qv)
                for fc in range(FC):
                    j = 4 * i + fc
                    pv = tp_guard.get(j % 2)
                    if pv is not None:
                        g.wait("tensor", VEC, pv)
                    tv = transpose(tp_slot(j),
                                   XTM[:, 1024 * i + 128 * fc:1024 * i + 128 * fc + 128],
                                   sem=TEN)
                    g.wait("vector", TEN, tv)
                    cc = slice(512 * fc + 128 * i, 512 * fc + 128 * i + 128)
                    split_hl(XQH[:, cc], XQL[:, cc], tp_slot(j))
                    tp_guard[j % 2] = VEC.v

        def gh_stage(t):
            for g0 in range(0, DIC, 8):
                gn = min(8, DIC - g0)
                for oc in range(g0, g0 + gn):
                    bi = oc % 2
                    wgv = wgb[bi].bitcast(BF16)
                    wuv = wub[bi].bitcast(BF16)
                    g.wait("sync", gusem[bi])
                    if gu_guard.get(bi) is not None:
                        g.wait("sync", TEN, gu_guard[bi])
                    g.dma("sync", wgv[:, 0:FC * 128].rearrange("p (k c) -> p k c", c=128),
                          wgq_s[0][oc], gusem[bi])
                    g.dma("sync", wgv[:, FC * 128:2 * FC * 128].rearrange("p (k c) -> p k c", c=128),
                          wgq_s[1][oc], gusem[bi])
                    g.dma("sync", wuv[:, 0:FC * 128].rearrange("p (k c) -> p k c", c=128),
                          wuq_s[0][oc], gusem[bi])
                    v2 = g.dma("sync", wuv[:, FC * 128:2 * FC * 128].rearrange("p (k c) -> p k c", c=128),
                               wuq_s[1][oc], gusem[bi])
                    g.wait("tensor", gusem[bi], v2)
                    pg = PB[0][:][:, 512 * bi:512 * bi + 512]
                    pu = PB[1][:][:, 512 * bi:512 * bi + 512]
                    if oc == 0:
                        g.wait("tensor", VEC)
                        g.wait("tensor", SCA)
                    pgv = pg_guard.get(bi)
                    if pgv is not None:
                        g.wait("tensor", SCA, pgv[0])
                        g.wait("tensor", VEC, pgv[1])
                    for (pp, wv) in ((pg, wgv), (pu, wuv)):
                        i_mm = 0
                        for k in range(FC):
                            wh = wv[:, 128 * k:128 * k + 128]
                            wl = wv[:, FC * 128 + 128 * k:FC * 128 + 128 * (k + 1)]
                            ah = XQH[:, 512 * k:512 * k + 512]
                            al = XQL[:, 512 * k:512 * k + 512]
                            for (wop, aop) in ((wh, ah), (wh, al), (wl, ah)):
                                mm(pp, wop, aop, start=(i_mm == 0), stop=(i_mm == 3 * FC - 1),
                                   sem=TEN if i_mm == 3 * FC - 1 else None)
                                i_mm += 1
                    tv = TEN.v
                    gu_guard[bi] = tv
                    g.wait("scalar", TEN, tv)
                    act(ub, pg, AF.Silu, sem=SCA)
                    pg_sca = SCA.v
                    g.wait("vector", SCA, pg_sca)
                    tt(ub, ub, pu, ALU.mult, sem=VEC)
                    vv = VEC.v
                    pg_guard[bi] = (pg_sca, vv)
                    g.wait("tensor", VEC, vv)
                    for i in range(4):
                        pv = tp_guard.get(i % 2)
                        if pv is not None:
                            g.wait("tensor", VEC, pv)
                        tv2 = transpose(tp_slot(i), ub[:, 128 * i:128 * i + 128], sem=TEN)
                        g.wait("vector", TEN, tv2)
                        vcopy(HTM[:, DI * i + 128 * oc:DI * i + 128 * oc + 128],
                              tp_slot(i), sem=VEC)
                        tp_guard[i % 2] = VEC.v
                for i in range(4):
                    hs = slice(DI * i + 128 * g0, DI * i + 128 * (g0 + gn))
                    qv = quantize(HTM[:, hs], 128 * gn, HTM[:, hs], xv=VEC.v)
                    g.wait("tensor", VEC, qv)
                    for kc in range(g0, g0 + gn):
                        pv = tp_guard.get(kc % 2)
                        if pv is not None:
                            g.wait("tensor", VEC, pv)
                        tv = transpose(tp_slot(kc),
                                       HTM[:, DI * i + 128 * kc:DI * i + 128 * kc + 128],
                                       sem=TEN)
                        g.wait("vector", TEN, tv)
                        cc = slice(512 * kc + 128 * i, 512 * kc + 128 * i + 128)
                        split_hl(HQH[:, cc], HQL[:, cc], tp_slot(kc))
                        tp_guard[kc % 2] = VEC.v
            return (SCA.v, VEC.v)

        def d_stage(t, g_end):
            for oc in range(FC):
                bi = oc % 2
                wdv = wdb[bi].bitcast(BF16)
                g.wait("sync", dsem2[bi])
                if d_guard.get(bi) is not None:
                    g.wait("sync", TEN, d_guard[bi])
                g.dma("sync", wdv[:, 0:DIC * 128].rearrange("p (k c) -> p k c", c=128),
                      wdq_s[0][oc], dsem2[bi])
                v1 = g.dma("sync", wdv[:, DIC * 128:2 * DIC * 128].rearrange("p (k c) -> p k c", c=128),
                           wdq_s[1][oc], dsem2[bi])
                pv = PB[0][:][:, 512 * (oc % 2):512 * (oc % 2) + 512]
                g.wait("tensor", dsem2[bi], v1)
                if oc == 0:
                    g.wait("tensor", SCA, g_end[0])
                    g.wait("tensor", VEC, g_end[1])
                pgv = pg_guard.get(2 + oc % 2)
                if pgv is not None:
                    g.wait("tensor", VEC, pgv)
                i_mm = 0
                for k in range(DIC):
                    wh = wdv[:, 128 * k:128 * k + 128]
                    wl = wdv[:, DIC * 128 + 128 * k:DIC * 128 + 128 * (k + 1)]
                    ah = HQH[:, 512 * k:512 * k + 512]
                    al = HQL[:, 512 * k:512 * k + 512]
                    for (wop, aop) in ((wh, ah), (wh, al), (wl, ah)):
                        mm(pv, wop, aop, start=(i_mm == 0), stop=(i_mm == 3 * DIC - 1),
                           sem=TEN if i_mm == 3 * DIC - 1 else None)
                        i_mm += 1
                tv = TEN.v
                d_guard[bi] = tv
                ch = rd13.rd(y3own[oc, :, 512 * t:512 * t + 512], "vector")
                g.wait("vector", TEN, tv)
                g.wait("vector", STQ)
                tt(xb, pv, ch, ALU.add, sem=VEC)
                pg_guard[2 + oc % 2] = VEC.v
                g.wait("gpsimd", VEC)
                g.dma("gpsimd", y_out[oc, :, 512 * t:512 * t + 512], xb, STQ)

        x_stage(0)
        for t in range(NTO):
            g_end = gh_stage(t)
            if t + 1 < NTO:
                x_stage(t + 1)
            d_stage(t, g_end)

    ffn()

    return finish(nc, g, ctx)


def finish(nc, g, ctx):
    g.wait("sync", g.fin_STQ)
    g.wait("sync", g.fin_LDQ)
    with ctx:
        blk = ctx.enter_context(nc.Block())
        for eng in ENGS:
            def make_body(eng):
                def body(e):
                    for fn in g.ops[eng]:
                        fn(e)
                return body
            getattr(blk, eng)(make_body(eng))
    return nc



def pack_vecs(ins, selA, selB):
    out = np.zeros((128, 128), dtype=np.float32)
    col = 0
    for key, n in (("L0_attn_bp", FC), ("L1_attn_bqkv", 3 * FC), ("L1_attn_ln1_g", FC),
                   ("L1_attn_ln1_b", FC), ("L1_attn_bo", FC), ("L1_attn_ln2_g", FC),
                   ("L1_attn_ln2_b", FC), ("L1_attn_b1", DMC), ("L1_attn_b2", FC)):
        arr = np.asarray(ins[key], dtype=np.float32)
        out[:, col:col + n] = arr.reshape(n, 128).T
        col += n
    out[:, col] = selA
    out[:, col + 1] = selB
    return out


def kernel(**inputs):
    ins = {k: np.asarray(v) for k, v in inputs.items()}
    x = np.asarray(ins["L0_attn_x"], dtype=np.float32)

    C2, S2 = build_rope_tables()
    tri = build_tri()
    identity = np.ascontiguousarray(np.eye(128, dtype=np.float32))
    maskA = build_masks(OWN_A)
    maskB = build_masks(OWN_B)

    wq_s = swizzle_strips(ins["L0_attn_Wq"])
    wk_s = swizzle_strips(ins["L0_attn_Wk"])
    wv_s = swizzle_strips(ins["L0_attn_Wv"])
    wp_s = swizzle_strips(ins["L0_attn_Wp"])
    w0t_s = swizzle_strips(np.ascontiguousarray(np.asarray(ins["L0_ffn_weight"]).T))
    wqkv_s = swizzle_strips(ins["L1_attn_Wqkv"])
    wo_s = swizzle_strips(ins["L1_attn_Wo"])
    w1_s = swizzle_strips(ins["L1_attn_W1"])
    w2_s = swizzle_strips(ins["L1_attn_W2"])
    wg_nat = np.ascontiguousarray(
        np.asarray(ins["L1_ffn_w_gate"], dtype=np.float32).reshape(DIC, 128, D))
    wu_nat = np.ascontiguousarray(
        np.asarray(ins["L1_ffn_w_up"], dtype=np.float32).reshape(DIC, 128, D))
    wd_nat = np.ascontiguousarray(
        np.asarray(ins["L1_ffn_w_down"], dtype=np.float32).reshape(FC, 128, DI))

    nc = build_program()

    in_maps = []
    for core in range(8):
        b, s = core // 2, core % 2
        own = OWN_A if s == 0 else OWN_B
        own_tok = np.concatenate([np.arange(256 * a, 256 * a + 256) for a in own])
        xb = np.ascontiguousarray(x[b].T)
        im = dict(
            x_fm=np.ascontiguousarray(xb.reshape(FC, 128, T)),
            wq_s=wq_s, wk_s=wk_s, wv_s=wv_s, wp_s=wp_s, w0t_s=w0t_s,
            wqkv_s=wqkv_s, wo_s=wo_s, w1_s=w1_s, w2_s=w2_s,
            wg_nat=wg_nat, wu_nat=wu_nat, wd_nat=wd_nat,
            vecs=pack_vecs(ins, 1.0 if s == 0 else 0.0, 0.0 if s == 0 else 1.0),
            c2t=C2, s2t=S2,
            c2o=np.ascontiguousarray(C2[:, own_tok]),
            s2o=np.ascontiguousarray(S2[:, own_tok]),
            iden=identity, mtri=tri,
            mask8=maskA if s == 0 else maskB,
        )
        in_maps.append(im)

    res = run_bass_kernel_spmd(nc, in_maps, core_ids=list(range(8)))

    out = np.empty((B, T, D), dtype=np.float32)
    for core in range(8):
        b, s = core // 2, core % 2
        own = OWN_A if s == 0 else OWN_B
        y = np.asarray(res.results[core]["y_out"]).reshape(D, TO).T
        for j, a in enumerate(own):
            out[b, 256 * a:256 * a + 256, :] = y[256 * j:256 * j + 256, :]
    return out
